# version 01
# baseline (speedup 1.0000x reference)
import numpy as np
import ml_dtypes
import concourse.bass as bass
import concourse.mybir as mybir
from concourse.bass_utils import run_bass_kernel_spmd

F32 = mybir.dt.float32
BF16 = mybir.dt.bfloat16
AF = mybir.ActivationFunctionType
ALU = mybir.AluOpType
AXX = mybir.AxisListType.X
BFNP = ml_dtypes.bfloat16

DM = 1024
NH = 16
DH = 64
DEPTH = 6
DI = 2730
W1C = 5460
WIN = 256
NEG = -1.0e38
LNEPS = 1e-5
RMSEPS = float(np.finfo(np.float32).eps)
C_ATTN_A = 30.0

ENGS = ("sync", "tensor", "vector", "scalar", "gpsimd")


def host_tables(ntok):
    t = {}
    t["ident"] = np.eye(128, dtype=BFNP)
    t["ones_row"] = np.ones((1, 512), dtype=BFNP)
    p = np.arange(128)[:, None]
    c = np.arange(384)[None, :]
    t["maskb"] = np.where((c >= p) & (c <= p + WIN), 0.0, NEG).astype(np.float32)
    u = np.arange(2 * ntok - 1)[None, :]
    t["dist"] = np.abs(u - (ntok - 1) - np.arange(128)[:, None]).astype(np.float32)
    half = np.arange(0, DH, 2, dtype=np.float64)
    inv_freq = 1.0 / 10000.0 ** (half / DH)
    tt = np.arange(ntok, dtype=np.float64)
    freqs = tt[:, None] * inv_freq[None, :]
    freqs = np.concatenate([freqs, freqs], -1)
    base_scale = (half + 0.4 * DH) / (1.4 * DH)
    power = (tt - ntok // 2) / (WIN // 2)
    scale = base_scale[None, :] ** power[:, None]
    scale = np.concatenate([scale, scale], -1)
    cos = np.cos(freqs)
    sin = np.sin(freqs)
    sgn = np.where(np.arange(DH) < DH // 2, -1.0, 1.0)[None, :]
    t["cs_q"] = (cos * scale).astype(np.float32)
    t["ss_q"] = (sin * scale * sgn).astype(np.float32)
    t["cs_k"] = (cos / scale).astype(np.float32)
    t["ss_k"] = (sin / scale * sgn).astype(np.float32)
    return t


def prep_weights(inp, depth=DEPTH):
    w = {}
    cont = np.ascontiguousarray

    def pmaj(a):
        E, F = a.shape
        return cont(a.reshape(E // 128, 128, F).transpose(1, 0, 2))

    w["aw_qkv"] = pmaj(np.asarray(inp["L0_attn_a_Wqkv_w"]).T.copy())
    w["ab_qkv"] = cont(np.asarray(inp["L0_attn_a_Wqkv_b"])[None, :])
    w["aw_out"] = pmaj(np.asarray(inp["L0_attn_a_out_w"]).T.copy())
    w["ab_out"] = cont(np.asarray(inp["L0_attn_a_out_b"])[None, :])
    w["a_dd"] = cont(np.asarray(inp["L0_attn_a_distance_decay"])[None, :])
    w["bw_qkv"] = pmaj(np.concatenate(
        [np.asarray(inp["L0_attn_b_Wq"]).T, np.asarray(inp["L0_attn_b_Wk"]).T,
         np.asarray(inp["L0_attn_b_Wv"]).T], axis=1))
    w["bw_out"] = pmaj(np.asarray(inp["L0_attn_b_Wo"]).T.copy())
    w["b_qn"] = cont(np.asarray(inp["L0_attn_b_qn_w"])[None, :])
    w["b_kn"] = cont(np.asarray(inp["L0_attn_b_kn_w"])[None, :])
    w["f_wqkv"] = np.stack([pmaj(np.asarray(inp["L0_ffn_Wqkv"])[i]) for i in range(depth)])
    w["f_wo"] = np.stack([pmaj(np.asarray(inp["L0_ffn_Wo"])[i]) for i in range(depth)])
    w["f_w1"] = np.stack([pmaj(np.asarray(inp["L0_ffn_W1"])[i]) for i in range(depth)])
    w2 = np.asarray(inp["L0_ffn_W2"])
    w2p = np.zeros((depth, 2816, DM), np.float32)
    w2p[:, :DI] = w2[:depth]
    w["f_w2"] = np.stack([pmaj(w2p[i]) for i in range(depth)])
    w["f_qs"] = cont(np.asarray(inp["L0_ffn_q_scale"])[:depth])
    w["f_ks"] = cont(np.asarray(inp["L0_ffn_k_scale"])[:depth])
    w["f_lng"] = cont(np.asarray(inp["L0_ffn_ln_g"])[:depth])
    w["f_lnb"] = cont(np.asarray(inp["L0_ffn_ln_b"])[:depth])
    w["f_b1"] = cont(np.asarray(inp["L0_ffn_b1"])[:depth])
    w["f_b2"] = cont(np.asarray(inp["L0_ffn_b2"])[:depth])
    return w


class Prog:
    DPOOL = {"sync": 8, "gpsimd": 4, "scalar": 4}
    SEMN = (["sync", "tensor", "vector", "scalar", "gpsimd"]
            + [f"d{q}{i}" for q, n in DPOOL.items() for i in range(n)])
    SAFE_SAME_ENGINE = True
    TENSOR_SERIAL = True

    def __init__(self):
        self.q = {e: [] for e in ENGS}
        self.cnt = {}
        self.waited = {e: {} for e in ENGS}
        self.sems = {}
        self.ninst = 0
        self.scr_rd = {}

    def _dma_sem(self, eng):
        if not hasattr(self, "_dma_rr"):
            self._dma_rr = {}
        i = self._dma_rr.get(eng, 0)
        self._dma_rr[eng] = (i + 1) % self.DPOOL[eng]
        return f"d{eng}{i}"

    def wait(self, eng, deps):
        for d in deps:
            if d is None:
                continue
            s, v = d
            if self.waited[eng].get(s, 0) < v:
                self.waited[eng][s] = v

                def mk(s=s, v=v):
                    return lambda e: e.wait_ge(self.sems[s], v)
                self.q[eng].append(mk())
                self.ninst += 1

    def on(self, eng, fn, deps=(), tok=False, dma=False):
        if eng == "tensor" and self.TENSOR_SERIAL and not dma:
            deps = list(deps) + [getattr(self, "_last_pe", None)]
            tok = True
        self.wait(eng, deps)
        self.ninst += 1
        if not hasattr(self, "last_tok"):
            self.last_tok = {}
            self.clean = {}
        if not tok:
            self.q[eng].append(fn)
            self.clean[eng] = False
            return None
        s = self._dma_sem(eng) if dma else eng
        inc = 16 if dma else 1
        if dma:
            self.wait(eng, [(s, self.cnt.get(s, 0))])
        self.cnt[s] = self.cnt.get(s, 0) + inc
        v = self.cnt[s]

        def mk(fn=fn, s=s, inc=inc):
            return lambda e: fn(e).then_inc(self.sems[s], inc)
        self.q[eng].append(mk())
        if not dma and not self.SAFE_SAME_ENGINE:
            self.waited[eng][s] = v
        if eng == "tensor" and not dma:
            self._last_pe = (s, v)
        if not dma:
            self.last_tok[eng] = (s, v)
            self.clean[eng] = True
        else:
            self.last_tok[eng] = (s, v)
            self.clean[eng] = True
        return (s, v)

    def dma(self, eng, out, in_, deps=()):
        return self.on(eng, lambda e, o=out, i=in_: e.dma_start(out=o, in_=i),
                       deps=deps, tok=True, dma=True)

    def rd(self, name, tok):
        self.scr_rd.setdefault(name, []).append(tok)
        return tok

    def wdeps(self, name):
        return self.scr_rd.pop(name, [])

    def barrier(self):
        if not hasattr(self, "last_tok"):
            self.last_tok = {}
            self.clean = {}
        for eng in ENGS:
            if self.q[eng] and not self.clean.get(eng, True):
                self.on(eng, lambda e: e.nop(), tok=True)
        snap = [(s, self.cnt.get(s, 0)) for s in self.SEMN if self.cnt.get(s, 0) > 0]
        for eng in ENGS:
            self.wait(eng, snap)

    def finish(self, blk):
        for name in ENGS:
            ops = self.q[name]

            def mksec(ops=ops):
                def section(e):
                    for f in ops:
                        f(e)
                return section
            getattr(blk, name)(mksec())


def build(ntok=4096, depth=DEPTH, parts=("a", "m1", "b", "m2"), dbg_out=(), stop=None):
    T = ntok // 128
    nc = bass.Bass("TRN2", target_bir_lowering=False)
    P = Prog()
    din = {}

    def dt_in(name, shape, dtype=F32):
        din[name] = nc.dram_tensor(name, list(shape), dtype, kind="ExternalInput").ap()

    dt_in("x0", (ntok, DM))
    dt_in("aw_qkv", (128, 8, 3 * DM)); dt_in("ab_qkv", (1, 3 * DM))
    dt_in("aw_out", (128, 8, DM)); dt_in("ab_out", (1, DM)); dt_in("a_dd", (1, 1))
    dt_in("bw_qkv", (128, 8, 3 * DM)); dt_in("bw_out", (128, 8, DM))
    dt_in("b_qn", (1, DH)); dt_in("b_kn", (1, DH))
    dt_in("f_wqkv", (depth, 128, 8, 3 * DM)); dt_in("f_wo", (depth, 128, 8, DM))
    dt_in("f_w1", (depth, 128, 8, W1C)); dt_in("f_w2", (depth, 128, 22, DM))
    dt_in("f_qs", (depth, DH)); dt_in("f_ks", (depth, DH))
    dt_in("f_lng", (depth, DM)); dt_in("f_lnb", (depth, DM))
    dt_in("f_b1", (depth, W1C)); dt_in("f_b2", (depth, DM))
    dt_in("ident", (128, 128), BF16); dt_in("ones_row", (1, 512), BF16)
    dt_in("maskb", (128, 384)); dt_in("dist", (128, 2 * ntok - 1))
    dt_in("cs_q", (ntok, DH)); dt_in("ss_q", (ntok, DH))
    dt_in("cs_k", (ntok, DH)); dt_in("ss_k", (ntok, DH))
    dout = nc.dram_tensor("out", [ntok, DM], F32, kind="ExternalOutput").ap()

    def dt_sc(name, shape, dtype=F32):
        kind = "ExternalOutput" if name in dbg_out else "Internal"
        return nc.dram_tensor(name, list(shape), dtype, kind=kind).ap()

    st = dict(ntok=ntok, T=T, depth=depth, nc=nc, din=din, dout=dout, stop=stop)
    st["X1"] = dt_sc("X1", (ntok, DM))
    st["X2"] = dt_sc("X2", (ntok, DM))
    st["X3"] = dt_sc("X3", (ntok, DM))
    st["YA"] = dt_sc("YA", (ntok, DM))
    st["YB"] = dt_sc("YB", (ntok, DM))
    st["QF"] = dt_sc("QF", (8, 128, ntok), BF16)
    st["KF"] = dt_sc("KF", (8, 128, ntok), BF16)
    st["VSP"] = dt_sc("VSP", (T, 128, NH * 65), BF16)
    st["OF"] = dt_sc("OF", (8, 128, ntok), BF16)
    st["BNC"] = dt_sc("BNC", (8, ntok))
    st["WRD"] = dt_sc("WRD", (depth, W1C), BF16)

    from contextlib import ExitStack
    ctx = ExitStack()
    sb = lambda name, shape, dtype: ctx.enter_context(nc.sbuf_tensor("s_" + name, list(shape), dtype))
    with ctx:
        st["AR1"] = sb("AR1", [128, 44544], BF16)
        st["AR2"] = sb("AR2", [128, 30208], BF16)
        st["ident"] = sb("ident", [128, 128], BF16)
        st["ones_row"] = sb("ones_row", [1, 512], BF16)
        st["maskb"] = sb("maskb", [128, 384], F32)
        st["wa"] = sb("wa", [128, 1024], F32)
        st["wb"] = sb("wb", [128, 1024], F32)
        st["wc"] = sb("wc", [128, 1024], F32)
        st["wd"] = sb("wd", [128, 1024], F32)
        st["we"] = sb("we", [128, 1024], F32)
        st["wbf"] = sb("wbf", [128, 1024], BF16)
        st["wbf3"] = sb("wbf3", [128, 1024], BF16)
        st["wbf4"] = sb("wbf4", [128, 1024], BF16)
        st["wbf5"] = sb("wbf5", [128, 1024], BF16)
        st["vst"] = sb("vst", [128, NH * 65], BF16)
        st["gsb"] = sb("gsb", [128, 512], F32)
        st["stat"] = sb("stat", [128, 32], F32)
        st["sml"] = sb("sml", [128, 160], F32)
        st["ptb"] = sb("ptb", [128, 4 * 512], BF16)
        st["wrow"] = sb("wrow", [1, W1C], BF16)
        st["b2row"] = sb("b2row", [1, DM], BF16)
        st["absl"] = sb("absl", [1, 3 * DM], BF16)
        st["lnbc"] = sb("lnbc", [128, 8], BF16)
        st["qnc"] = sb("qnc", [128, 1], F32)
        st["knc"] = sb("knc", [128, 1], F32)
        st["negC"] = sb("negC", [128, 1], F32)
        st["onec"] = sb("onec", [128, 1], BF16)
        st["rr"] = sb("rr", [1, 512], F32)
        st["sconst"] = sb("sconst", [128, 4], F32)
        for col, val in enumerate([LNEPS, RMSEPS, -C_ATTN_A]):
            nc.const_aps.register(F32, float(val), st["sconst"].ap()[:, col:col + 1])
        st["PB"] = [ctx.enter_context(nc.psum_tensor(f"pb{i}", [128, 512], F32))
                    for i in range(8)]
        st["pt_rd"] = {}
        for sname in Prog.SEMN:
            P.sems[sname] = ctx.enter_context(nc.semaphore(sname))
        blk = ctx.enter_context(nc.Block())
        emit_program(P, st, parts)
        P.finish(blk)
    return nc, P


def f32v(arena, c0f, ncols, rows=128, p0=0):
    return arena.ap().bitcast(F32)[p0:p0 + rows, c0f:c0f + ncols]


def bf16v(arena, c0, ncols, rows=128, p0=0):
    return arena[p0:p0 + rows, c0:c0 + ncols]


def ln_stats(P, st, xt, eps, deps):
    stat = st["stat"]; we = st["we"]
    t1 = P.on("vector", lambda e: e.tensor_reduce(stat[:, 16:17], xt[:], AXX, ALU.add),
              deps=deps, tok=True)
    t2 = P.on("scalar", lambda e: e.activation(we[:], xt[:], AF.Square), deps=deps, tok=True)
    t3 = P.on("vector", lambda e: e.tensor_reduce(stat[:, 17:18], we[:], AXX, ALU.add),
              deps=[t2], tok=True)
    t4 = P.on("vector", lambda e: e.tensor_scalar(stat[:, 18:19], stat[:, 16:17],
                                                  -1.0 / 1024, None, ALU.mult, ALU.bypass),
              deps=[t1], tok=True)
    t5 = P.on("vector", lambda e: e.tensor_tensor(stat[:, 19:20], stat[:, 18:19],
                                                  stat[:, 18:19], ALU.mult), deps=[t4], tok=True)
    t6 = P.on("vector", lambda e: e.tensor_scalar(stat[:, 20:21], stat[:, 17:18],
                                                  1.0 / 1024, None, ALU.mult, ALU.bypass),
              deps=[t3], tok=True)
    t7 = P.on("vector", lambda e: e.tensor_tensor(stat[:, 21:22], stat[:, 20:21],
                                                  stat[:, 19:20], ALU.subtract),
              deps=[t5, t6], tok=True)
    t8 = P.on("scalar", lambda e: e.activation(stat[:, 22:23], stat[:, 21:22], AF.Ln,
                                               bias=float(eps), scale=1.0), deps=[t7], tok=True)
    t9 = P.on("scalar", lambda e: e.activation(stat[:, 23:24], stat[:, 22:23], AF.Exp,
                                               scale=-0.5), deps=[t8], tok=True)
    return stat[:, 18:19], stat[:, 23:24], [t4, t9]


def transpose_tile(P, st, src_bf, dst_strided, bank_idx, deps, nchunk=8, dst_chunks=None):
    bank = st["PB"][bank_idx]
    pv = bank.ap().bitcast(BF16)
    alldeps = list(deps) + [st["pt_rd"].get(bank_idx)]
    tl = None
    for c in range(nchunk):
        tl = P.on("tensor", lambda e, c=c: e.transpose(
            pv[:, c * 128:(c + 1) * 128], src_bf[:, c * 128:(c + 1) * 128], st["ident"][:]),
            deps=alldeps if c == 0 else (), tok=(c == nchunk - 1))
    pvv = pv.rearrange("p (c n) -> p c n", c=8)[:, 0:nchunk, :]
    tev = P.on("scalar", lambda e: e.copy(dst_strided, pvv), deps=[tl], tok=True)
    st["pt_rd"][bank_idx] = tev
    return tev


def shifted_load(P, st, ysrc, yname, tt, dst, deps=()):
    r0 = tt * 128
    t1 = P.rd(yname, P.dma("sync", dst[:, 0:512], ysrc[r0:r0 + 128, 0:512], deps=deps))
    if tt == 0:
        t2 = P.rd(yname, P.dma("sync", dst[1:128, 512:1024], ysrc[0:127, 512:1024]))
        t3 = P.on("gpsimd", lambda e: e.memset(dst[0:1, 512:1024], 0.0), deps=deps, tok=True)
        return [t1, t2, t3]
    t2 = P.rd(yname, P.dma("sync", dst[:, 512:1024], ysrc[r0 - 1:r0 + 127, 512:1024]))
    return [t1, t2]


def bcast_col(P, st, dram_row_ap, dst, deps=()):
    return P.dma("sync", dst, dram_row_ap.partition_broadcast(dst.shape[0]), deps=deps)


def emit_program(P, st, parts):
    din = st["din"]
    P.dma("sync", st["ident"][:], din["ident"][:])
    P.dma("sync", st["ones_row"][:], din["ones_row"][:])
    P.dma("sync", st["maskb"][:], din["maskb"][:])
    P.on("gpsimd", lambda e: e.memset(st["onec"][:], 1.0), tok=True)
    for col, val in enumerate([LNEPS, RMSEPS, -C_ATTN_A]):
        P.on("vector", lambda e, col=col, val=val: e.memset(st["sconst"][:, col:col + 1],
                                                            float(val)), tok=True)
    P.barrier()

    x_cur, x_name = din["x0"], "x0"
    if "a" in parts:
        ph_dense(P, st, "a", x_cur, x_name, st["X1"], "X1")
        x_cur, x_name = st["X1"], "X1"
    if "m1" in parts:
        ph_ffnmod(P, st, x_cur, x_name, st["X2"], "X2", inv=0)
        x_cur, x_name = st["X2"], "X2"
    if "b" in parts:
        ph_dense(P, st, "b", x_cur, x_name, st["X3"], "X3")
        x_cur, x_name = st["X3"], "X3"
    if "m2" in parts:
        ph_ffnmod(P, st, x_cur, x_name, st["dout"], "out", inv=1)
        x_cur = st["dout"]
    if x_cur is not st["dout"]:
        P.barrier()
        prev = []
        for tt in range(st["T"]):
            r = slice(tt * 128, tt * 128 + 128)
            t = P.dma("sync", st["wc"][:], x_cur[r, :], deps=prev)
            tc = P.on("vector", lambda e: e.tensor_copy(st["wd"][:], st["wc"][:]),
                      deps=[t], tok=True)
            prev = [tc, P.dma("sync", st["dout"][r, :], st["wd"][:], deps=[tc])]
    P.wait("tensor", [(s, P.cnt.get(s, 0)) for s in Prog.SEMN if s.startswith("d")])
    P.on("tensor", lambda e: e.nop())


def ph_dense(P, st, var, x_in, x_name, x_out, xo_name):
    din = st["din"]; T = st["T"]; ntok = st["ntok"]
    AR1, AR2, PB = st["AR1"], st["AR2"], st["PB"]
    wa, wb, wc, wd = st["wa"], st["wb"], st["wc"], st["wd"]
    wbf, vst, ptb = st["wbf"], st["vst"], st["ptb"]
    negC, sml, rr = st["negC"], st["sml"], st["rr"]
    P.barrier()

    lnfm = bf16v(AR1, 0, 8 * ntok)
    lnfm3 = lnfm.rearrange("p (c n) -> p c n", c=8)
    wv = bf16v(AR1, 8 * ntok, 8 * DM)
    wv3 = wv.rearrange("p (c n) -> p c n", c=8)
    wslq = bf16v(AR1, 8 * ntok + 8 * DM, 8 * 128)
    wslk = bf16v(AR1, 8 * ntok + 8 * DM + 8 * 128, 8 * 128)
    wslq3 = wslq.rearrange("p (c n) -> p c n", c=8)
    wslk3 = wslk.rearrange("p (c n) -> p c n", c=8)
    ofm = bf16v(AR1, 8 * ntok + 8 * DM + 16 * 128, 8 * 128)
    ofm3 = ofm.rearrange("p (c n) -> p c n", c=8)
    if var == "a":
        dec = f32v(AR2, 0, 2 * ntok - 1)
        c0 = 2 * (2 * ntok - 1)
        rstdb = None
    else:
        dec = None
        rstdb = f32v(AR2, 0, ntok)
        c0 = 2 * ntok
    q2 = bf16v(AR2, c0, ntok); c0 += ntok
    k2 = bf16v(AR2, c0, ntok); c0 += ntok
    if var == "b":
        tmpn = bf16v(AR2, c0, ntok); c0 += ntok
    else:
        tmpn = None
    v2p = bf16v(AR2, c0, T * 130); c0 += T * 130
    v2p3 = v2p.rearrange("p (t n) -> p t n", t=T)
    rbq = st["wbf3"].ap().bitcast(F32)[0:64, 0:512]
    obf = st["wbf"][0:64, 0:512]
    assert c0 <= 30208, c0

    wq_d = din["aw_qkv" if var == "a" else "bw_qkv"]
    wo_d = din["aw_out" if var == "a" else "bw_out"]

    tbias = None
    if var == "a":
        tdd = bcast_col(P, st, din["a_dd"][0:1, :], sml[:, 8:9])
        tng = P.on("vector", lambda e: e.tensor_scalar(sml[:, 9:10], sml[:, 8:9], -1.0, None,
                                                       ALU.mult, ALU.bypass), deps=[tdd], tok=True)
        tna = P.on("vector", lambda e: e.tensor_tensor(sml[:, 10:11], sml[:, 8:9],
                                                       sml[:, 9:10], ALU.min), deps=[tng], tok=True)
        tdist = P.dma("sync", dec, din["dist"][:])
        tdec = P.on("scalar", lambda e: e.activation(dec, dec, AF.Exp, scale=sml[:, 10:11]),
                    deps=[tna, tdist], tok=True)
        tbias = P.dma("gpsimd", st["absl"][:], din["ab_qkv"][:])
        tbo = P.dma("gpsimd", st["b2row"][:], din["ab_out"][:])
        tnc = None
    else:
        tq1 = P.dma("sync", st["qnc"][0:64, :],
                    din["b_qn"].rearrange("o (p i) -> (o p) i", i=1))
        tq2 = P.dma("sync", st["qnc"][64:128, :],
                    din["b_qn"].rearrange("o (p i) -> (o p) i", i=1))
        tk1 = P.dma("sync", st["knc"][0:64, :],
                    din["b_kn"].rearrange("o (p i) -> (o p) i", i=1))
        tk2 = P.dma("sync", st["knc"][64:128, :],
                    din["b_kn"].rearrange("o (p i) -> (o p) i", i=1))
        tqb = bcast_col(P, st, din["b_qn"][0:1, :], sml[:, 16:80])
        tkb = bcast_col(P, st, din["b_kn"][0:1, :], sml[:, 80:144])
        t1 = P.on("vector", lambda e: e.tensor_reduce(sml[:, 0:1], sml[:, 16:80], AXX, ALU.max,
                                                      apply_absolute_value=True),
                  deps=[tqb], tok=True)
        t2 = P.on("vector", lambda e: e.tensor_reduce(sml[:, 1:2], sml[:, 80:144], AXX, ALU.max,
                                                      apply_absolute_value=True),
                  deps=[tkb], tok=True)
        t3m = P.on("vector", lambda e: e.tensor_tensor(sml[:, 2:3], sml[:, 0:1], sml[:, 1:2],
                                                       ALU.mult), deps=[t1, t2], tok=True)
        tnc = P.on("vector", lambda e: e.tensor_scalar(negC[:], sml[:, 2:3], -8.0, None,
                                                       ALU.mult, ALU.bypass), deps=[t3m], tok=True)
        tdec = None
        st["_rmsdep"] = [tq1, tq2, tk1, tk2]

    prev_tn = None
    prev_ev = None
    for tt in range(T):
        r = slice(tt * 128, tt * 128 + 128)
        tld = P.rd(x_name, P.dma("sync", wa[:], x_in[r, :], deps=[prev_tn]))
        negmu, rstd, tst = ln_stats(P, st, wa, LNEPS, [tld])
        prev_tn = P.on("vector", lambda e: e.tensor_scalar(wbf[:], wa[:], negmu, rstd,
                                                           ALU.add, ALU.mult),
                       deps=tst + [prev_ev], tok=True)
        prev_ev = transpose_tile(P, st, wbf, lnfm3[:, :, tt * 128:(tt + 1) * 128],
                                 6, [prev_tn])

    if st["stop"] == "ph1":
        return
    twv = P.dma("gpsimd", wv3, wq_d[:, :, 2 * DM:3 * DM])
    vst3 = vst.ap().rearrange("p (h n) -> p h n", h=NH)
    tmem = P.on("gpsimd", lambda e: e.memset(vst3[:, :, 64:65], 1.0), tok=True)
    prev_sp = None
    prev_ev = [tmem, prev_ev]
    for tt in range(T):
        tm = {}
        for half in range(2):
            bank = PB[half]
            nmm = 8
            for c in range(8):
                lastmm = (c == 7) and var != "a"
                tm[half] = P.on("tensor", lambda e, c=c, half=half, bank=bank, tt=tt, lastmm=lastmm: e.matmul(
                    bank[:], lnfm3[:, c, tt * 128:(tt + 1) * 128],
                    wv3[:, c, half * 512:(half + 1) * 512],
                    start=(c == 0), stop=lastmm),
                    deps=([twv] + prev_ev) if c == 0 else (), tok=lastmm)
            if var == "a":
                tm[half] = P.on("tensor", lambda e, half=half, bank=bank: e.matmul(
                    bank[:], st["ones_row"][0:1, 0:128],
                    st["absl"][0:1, 2 * DM + half * 512:2 * DM + (half + 1) * 512],
                    start=False, stop=True),
                    deps=[tbias], tok=True)
        prev_ev = []
        for half in range(2):
            pv = PB[half].ap().rearrange("p (h n) -> p h n", h=8)
            prev_ev.append(P.on("scalar", lambda e, half=half, pv=pv: e.copy(
                vst3[:, half * 8:(half + 1) * 8, 0:64], pv),
                deps=[tm[half], prev_sp], tok=True))
        prev_sp = P.dma("sync", st["VSP"][tt], vst[:], deps=prev_ev + P.wdeps("VSP"))

    if st["stop"] == "ph2":
        return
    nqg = ntok // 512
    oev = [prev_sp]
    for p in list(range(8))[st.get('_ponly', 0):]:
        tq = P.dma("gpsimd", wslq3, wq_d[:, :, 128 * p:128 * (p + 1)], deps=oev)
        tk = P.dma("gpsimd", wslk3, wq_d[:, :, DM + 128 * p:DM + 128 * (p + 1)], deps=oev)
        tev_q = list(oev)
        for (wsl3, tw_, dst, bofs) in ((wslq3, tq, q2, 128 * p), (wslk3, tk, k2, DM + 128 * p)):
            new_ev = []
            for qg in range(nqg):
                bank = PB[qg % 2]
                tmq = None
                for c in range(8):
                    lastmm = (c == 7) and var != "a"
                    tmq = P.on("tensor", lambda e, c=c, qg=qg, bank=bank, wsl3=wsl3, lastmm=lastmm: e.matmul(
                        bank[:], wsl3[:, c, :], lnfm3[:, c, qg * 512:(qg + 1) * 512],
                        start=(c == 0), stop=lastmm),
                        deps=([tw_] + tev_q) if c == 0 else (), tok=lastmm)
                if var == "a":
                    tmq = P.on("tensor", lambda e, qg=qg, bank=bank, bofs=bofs: e.matmul(
                        bank[:], st["absl"][0:1, bofs:bofs + 128],
                        st["ones_row"][0:1, 0:512], start=False, stop=True),
                        deps=[tbias], tok=True)
                new_ev.append(P.on("scalar", lambda e, qg=qg, bank=bank, dst=dst: e.copy(
                    dst[:, qg * 512:(qg + 1) * 512], bank[:]), deps=[tmq], tok=True))
            tev_q = new_ev
        qS, kS = q2, k2
        if st["stop"] == "proj" and p == 0:
            return
        if var == "b":
            t_rms1 = rms_norm_pair(P, st, q2, tmpn, st["qnc"], 2, rstdb, tev_q)
            t_rms2 = rms_norm_pair(P, st, k2, q2, st["knc"], 4, rstdb, [t_rms1])
            qS, kS = tmpn, q2
            tev_q = [t_rms2]
        tv = P.dma("sync", v2p3,
                   st["VSP"][:, :, 130 * p:130 * (p + 1)].rearrange("t p n -> p t n"),
                   deps=tev_q + P.wdeps("v2p"))
        P.rd("VSP", tv)
        slot_rd = {}
        sbank_rd = {}
        for qg in range(nqg):
            ob = (PB[4], PB[5]) if qg % 2 == 0 else (PB[6], PB[7])
            tdone = [None, None]
            for t in range(T):
                for h in range(2):
                    sb_ = PB[2 * h + t % 2]
                    rows = slice(64 * h, 64 * h + 64)
                    tS = P.on("tensor", lambda e, t=t, h=h, qg=qg, sb_=sb_, rows=rows: e.matmul(
                        sb_[:], kS[rows, t * 128:(t + 1) * 128],
                        qS[rows, qg * 512:(qg + 1) * 512], start=True, stop=True),
                        deps=([tv, tnc, tdec] + tev_q + st.get("_oev_par", [])
                              if (t < 2 and qg < 2) else []) + [sbank_rd.get((h, t % 2))],
                        tok=True)
                    slot = 2 * (t % 2) + h
                    pslot = ptb[:, slot * 512:(slot + 1) * 512]
                    if var == "a":
                        u0 = (ntok - 1) - 128 * t + 512 * qg
                        wsf = (wa, wb)[h][:, (t % 2) * 512:(t % 2) * 512 + 512]
                        trd = P.on("vector", lambda e, sb_=sb_, wsf=wsf, u0=u0:
                                   e.scalar_tensor_tensor(wsf, sb_[:], 0.125,
                                                          dec[:, u0:u0 + 512],
                                                          ALU.mult, ALU.mult),
                                   deps=[tS, slot_rd.get(slot)], tok=True)
                        sbank_rd[(h, t % 2)] = trd
                        tx = P.on("scalar", lambda e, pslot=pslot, wsf=wsf: e.activation(
                            pslot, wsf, AF.Exp, bias=-C_ATTN_A, scale=1.0),
                            deps=[trd], tok=True)
                    else:
                        tx = P.on("scalar", lambda e, pslot=pslot, sb_=sb_: e.activation(
                            pslot, sb_[:], AF.Exp, bias=negC[:], scale=0.125),
                            deps=[tS, slot_rd.get(slot)], tok=True)
                        sbank_rd[(h, t % 2)] = tx
                    tdone[h] = P.on("tensor", lambda e, t=t, h=h, ob=ob, pslot=pslot: e.matmul(
                        ob[h][0:65, :], v2p3[:, t, 65 * h:65 * h + 65],
                        pslot, start=(t == 0), stop=(t == T - 1)),
                        deps=[tx] + (st.get("_oev_par", []) if t == 0 else []),
                        tok=True)
                    slot_rd[slot] = tdone[h]
            oev2 = []
            for h in range(2):
                rrh = rr[0:1, 0:512] if h == 0 else st["wc"].ap().bitcast(F32)[0:1, 0:512]
                rbqh = rbq if h == 0 else st["wd"].ap().bitcast(F32)[0:64, 0:512]
                obfh = obf if h == 0 else st["wbf4"][0:64, 0:512]
                t1 = P.on("vector", lambda e, ob=ob, h=h, rrh=rrh: e.reciprocal(
                    rrh, ob[h][64:65, :]), deps=[tdone[h]] + P.wdeps(f"rr{h}"), tok=True)
                t2 = P.rd(f"rr{h}", P.dma("sync", st["BNC"][h:h + 1, 0:512], rrh,
                                          deps=[t1] + P.wdeps(f"BNC{h}")))
                t3 = P.rd(f"BNC{h}", P.dma("sync", rbqh,
                                           st["BNC"][h:h + 1, 0:512].partition_broadcast(64),
                                           deps=[t2] + P.wdeps(f"rbq{h}")))
                t4 = P.rd(f"rbq{h}", P.on("vector", lambda e, ob=ob, h=h, obfh=obfh, rbqh=rbqh:
                                          e.tensor_tensor(obfh, ob[h][0:64, :], rbqh, ALU.mult),
                                          deps=[t3] + P.wdeps(f"obf{h}"), tok=True))
                t5 = P.rd(f"obf{h}", P.dma("sync",
                                           st["OF"][p, 64 * h:64 * h + 64, qg * 512:(qg + 1) * 512],
                                           obfh, deps=[t4] + P.wdeps("OF")))
                oev2 += [t4, t5]
            st["_oev_par"] = oev2
            oev = oev2
    if st["stop"] == "ph3":
        return
    P.barrier()
    two = P.dma("gpsimd", wv3, wo_d[:, :, :])
    tbo2 = P.dma("gpsimd", st["b2row"][:], din["ab_out"][:]) if var == "a" else None
    prev = []
    for tt in range(T):
        r = slice(tt * 128, tt * 128 + 128)
        tof = P.rd("OF", P.dma("sync", ofm3,
                               st["OF"][:, :, tt * 128:(tt + 1) * 128].rearrange("c p n -> p c n"),
                               deps=prev))
        txl = P.rd(x_name, P.dma("sync", wc[:], x_in[r, :], deps=prev))
        tmm = {}
        for half in range(2):
            bank = PB[half]
            for c in range(8):
                lastmm = (c == 7) and var != "a"
                tmm[half] = P.on("tensor", lambda e, c=c, half=half, bank=bank, lastmm=lastmm: e.matmul(
                    bank[:], ofm3[:, c, :], wv3[:, c, half * 512:(half + 1) * 512],
                    start=(c == 0), stop=lastmm),
                    deps=([tof, two] + prev) if c == 0 else (), tok=lastmm)
            if var == "a":
                tmm[half] = P.on("tensor", lambda e, half=half, bank=bank: e.matmul(
                    bank[:], st["ones_row"][0:1, 0:128],
                    st["b2row"][0:1, half * 512:(half + 1) * 512], start=False, stop=True),
                    deps=[tbo2], tok=True)
        prev = []
        for half in range(2):
            tr = P.on("vector", lambda e, half=half: e.tensor_tensor(
                wd[:, half * 512:(half + 1) * 512], PB[half][:],
                wc[:, half * 512:(half + 1) * 512], ALU.add),
                deps=[tmm[half], txl], tok=True)
            prev.append(tr)
        prev.append(P.dma("sync", x_out[r, :], wd[:], deps=prev + P.wdeps(xo_name)))


def rms_norm_pair(P, st, src, dst, ncol, bnc0, rstdb, deps):
    ntok = st["ntok"]; PB = st["PB"]; rr = st["rr"]
    nqg = ntok // 512
    tsq = P.on("scalar", lambda e: e.activation(dst, src, AF.Square), deps=deps, tok=True)
    dmas = []
    prev5 = {0: None, 1: None}
    for h in range(2):
        rows = slice(64 * h, 64 * h + 64)
        for qg in range(nqg):
            tm = P.on("tensor", lambda e, h=h, qg=qg, rows=rows: e.matmul(
                PB[2 + h][0:1, :], st["onec"][rows, :], dst[rows, qg * 512:(qg + 1) * 512],
                start=True, stop=True), deps=[tsq, prev5[h]] + (dmas[-1:] if dmas else []),
                tok=True)
            t5 = P.on("scalar", lambda e, h=h: e.activation(rr[:, 0:512], PB[2 + h][0:1, :],
                                                            AF.Ln, bias=RMSEPS, scale=1.0 / 64),
                      deps=[tm], tok=True)
            t6 = P.on("scalar", lambda e: e.activation(rr[:, 0:512], rr[:, 0:512], AF.Exp,
                                                       scale=-0.5), deps=[t5], tok=True)
            prev5[h] = t6
            dmas.append(P.dma("sync", st["BNC"][bnc0 + h:bnc0 + h + 1, qg * 512:(qg + 1) * 512],
                              rr[0:1, 0:512], deps=[t6] + P.wdeps(f"BNCrms{bnc0 + h}")))
    tb = []
    for h in range(2):
        tb.append(P.rd(f"BNCrms{bnc0 + h}",
                       P.dma("sync", rstdb[64 * h:64 * h + 64, :],
                             st["BNC"][bnc0 + h:bnc0 + h + 1, :].partition_broadcast(64),
                             deps=dmas + P.wdeps("rstdb"))))
    tn = P.rd("rstdb", P.on("vector", lambda e: e.scalar_tensor_tensor(dst, src, ncol[:], rstdb,
                                                                      ALU.mult, ALU.mult),
                            deps=tb + st.get("_rmsdep", []), tok=True))
    return tn


def ph_ffnmod(P, st, x_in, x_name, x_out, xo_name, inv):
    T = st["T"]
    ycur, yname = st["YA"], "YA"
    ynext, ynname = st["YB"], "YB"
    P.barrier()
    prev = []
    for tt in range(T):
        r = slice(tt * 128, tt * 128 + 128)
        tld = P.rd(x_name, P.dma("sync", st["wa"][:], x_in[r, :], deps=prev))
        negmu, rstd, tst = ln_stats(P, st, st["wa"], LNEPS, [tld])
        tn = P.on("vector", lambda e: e.tensor_scalar(st["wb"][:], st["wa"][:], negmu, rstd,
                                                      ALU.add, ALU.mult), deps=tst, tok=True)
        prev = [tn, P.dma("sync", ycur[r, :], st["wb"][:], deps=[tn] + P.wdeps(yname))]
    if st["stop"] == "y0":
        return
    for i in range(st["depth"]):
        ph_lattn(P, st, i, ycur, yname, ynext, ynname)
        if st["stop"] in ("lattnA", "lattnB", "lattnC"):
            return
        ycur, yname, ynext, ynname = ynext, ynname, ycur, yname
        ph_lffn(P, st, i, ycur, yname, ynext, ynname, inv)
        if st["stop"] == "lffn":
            return
        ycur, yname, ynext, ynname = ynext, ynname, ycur, yname
    P.barrier()
    prev = []
    for tt in range(T):
        r = slice(tt * 128, tt * 128 + 128)
        t1 = P.rd(x_name, P.dma("sync", st["wa"][:], x_in[r, :], deps=prev))
        t2 = P.rd(yname, P.dma("sync", st["wb"][:], ycur[r, :], deps=prev))
        t3 = P.on("vector", lambda e: e.tensor_tensor(st["wc"][:], st["wa"][:], st["wb"][:],
                                                      ALU.add), deps=[t1, t2], tok=True)
        prev = [t3, P.dma("sync", x_out[r, :], st["wc"][:], deps=[t3] + P.wdeps(xo_name))]


def ph_lattn(P, st, li, yin, yname, yout, yoname):
    din = st["din"]; T = st["T"]; ntok = st["ntok"]
    AR1, AR2, PB = st["AR1"], st["AR2"], st["PB"]
    wa, wb, wc, wd, we = st["wa"], st["wb"], st["wc"], st["wd"], st["we"]
    wbf, vst, ptb = st["wbf"], st["vst"], st["ptb"]
    sml, rr, negC = st["sml"], st["rr"], st["negC"]
    P.barrier()

    wqkv = bf16v(AR1, 0, 8 * 3 * DM)
    wqkv3 = wqkv.rearrange("p (c n) -> p c n", c=8)
    rotc = 8 * 3 * DM
    CS = {}; SS = {}
    for nm in ("q", "k"):
        CS[nm] = f32v(AR1, rotc // 2, T * DH).rearrange("p (t d) -> p t d", d=DH)
        rotc += 2 * T * DH
        SS[nm] = f32v(AR1, rotc // 2, T * DH).rearrange("p (t d) -> p t d", d=DH)
        rotc += 2 * T * DH
    assert rotc <= 44544
    lnf3 = st["wbf3"].ap().rearrange("p (c n) -> p c n", c=8)
    vst3 = vst.ap().rearrange("p (h n) -> p h n", h=NH)

    tw = P.dma("gpsimd", wqkv3, din["f_wqkv"][li])
    tcs = {}
    for nm in ("q", "k"):
        tcs[nm] = [P.dma("sync", CS[nm], din[f"cs_{nm}"].rearrange("(t p) d -> p t d", p=128)),
                   P.dma("sync", SS[nm], din[f"ss_{nm}"].rearrange("(t p) d -> p t d", p=128))]
    tqsb = bcast_col(P, st, din["f_qs"][li:li + 1, :], sml[:, 16:80])
    tksb = bcast_col(P, st, din["f_ks"][li:li + 1, :], sml[:, 80:144])
    tsw = [P.on("vector", lambda e: e.tensor_copy(we[:, 0:32], sml[:, 48:80]),
                deps=[tqsb], tok=True),
           P.on("vector", lambda e: e.tensor_copy(we[:, 32:64], sml[:, 16:48]),
                deps=[tqsb], tok=True),
           P.on("vector", lambda e: e.tensor_copy(we[:, 64:96], sml[:, 112:144]),
                deps=[tksb], tok=True),
           P.on("vector", lambda e: e.tensor_copy(we[:, 96:128], sml[:, 80:112]),
                deps=[tksb], tok=True)]
    tf = {}
    for j, nm in enumerate(("q", "k")):
        scol = sml[:, 16:80] if nm == "q" else sml[:, 80:144]
        swv = we[:, 0:64] if nm == "q" else we[:, 64:128]
        tf[nm] = [P.on("vector", lambda e, nm=nm, scol=scol: e.tensor_tensor(
            CS[nm], CS[nm], scol.unsqueeze(1).broadcast_to((128, T, DH)), ALU.mult),
            deps=[tcs[nm][0], tqsb, tksb], tok=True),
            P.on("vector", lambda e, nm=nm, swv=swv: e.tensor_tensor(
                SS[nm], SS[nm], swv.unsqueeze(1).broadcast_to((128, T, DH)), ALU.mult),
                deps=[tcs[nm][1]] + tsw, tok=True)]
    t1 = P.on("vector", lambda e: e.tensor_reduce(sml[:, 0:1], sml[:, 16:80], AXX, ALU.max,
                                                  apply_absolute_value=True),
              deps=[tqsb] + tf["q"], tok=True)
    t2 = P.on("vector", lambda e: e.tensor_reduce(sml[:, 1:2], sml[:, 80:144], AXX, ALU.max,
                                                  apply_absolute_value=True),
              deps=[tksb] + tf["k"], tok=True)
    t3m = P.on("vector", lambda e: e.tensor_tensor(sml[:, 2:3], sml[:, 0:1], sml[:, 1:2],
                                                   ALU.mult), deps=[t1, t2], tok=True)
    tnc = P.on("vector", lambda e: e.tensor_scalar(negC[:], sml[:, 2:3], -8.0, None,
                                                   ALU.mult, ALU.bypass), deps=[t3m], tok=True)
    tmem = P.on("gpsimd", lambda e: e.memset(vst3[:, :, 64:65], 1.0), tok=True)

    prev_sp = [tmem]
    prev_rot = {"q": None, "k": None}
    for tt in range(T):
        tls = shifted_load(P, st, yin, yname, tt, wa, deps=[prev_rot["k"]])
        tcst = P.on("scalar", lambda e: e.copy(wbf[:], wa[:]), deps=tls, tok=True)
        tsf = transpose_tile(P, st, wbf, lnf3, 6, [tcst])
        tmv = {}
        for g in range(6):
            bank = PB[g]
            for c in range(8):
                tmv[g] = P.on("tensor", lambda e, g=g, c=c, bank=bank: e.matmul(
                    bank[:], lnf3[:, c, :], wqkv3[:, c, g * 512:(g + 1) * 512],
                    start=(c == 0), stop=(c == 7)),
                    deps=([tsf, tw] + prev_sp) if c == 0 else (), tok=(c == 7))
        tev = []
        for half in range(2):
            pv = PB[4 + half].ap().rearrange("p (h n) -> p h n", h=8)
            tev.append(P.on("scalar", lambda e, half=half, pv=pv: e.copy(
                vst3[:, half * 8:(half + 1) * 8, 0:64], pv),
                deps=[tmv[4 + half]] + prev_sp, tok=True))
        tvs = P.dma("sync", st["VSP"][tt], vst[:], deps=tev + P.wdeps("VSP"))
        tqr = rotary(P, st, tt, 0, CS["q"], SS["q"], st["QF"], "wbf4",
                     [tmv[0], tmv[1], prev_rot["q"]])
        tkr = rotary(P, st, tt, 2, CS["k"], SS["k"], st["KF"], "wbf5",
                     [tmv[2], tmv[3], tqr["ev"], prev_rot["k"]])
        prev_rot = {"q": tqr["sp"], "k": tkr["sp"]}
        prev_sp = [tvs, tqr["sp"], tkr["sp"]]

    if st["stop"] == "lattnA":
        return
    P.barrier()
    O1 = f32v(AR1, 0, ntok, rows=65)
    O2 = f32v(AR1, ntok, ntok, rows=65)
    rbig = f32v(AR1, 2 * ntok, ntok, rows=64)
    q2 = bf16v(AR2, 0, ntok)
    k2 = bf16v(AR2, ntok, ntok)
    v2p = bf16v(AR2, 2 * ntok, T * 130)
    v2p3 = v2p.rearrange("p (t n) -> p t n", t=T)
    obig = bf16v(AR2, 2 * ntok + T * 130, ntok, rows=64)
    oev = []
    for p in range(8):
        tqd = P.rd("QF", P.dma("sync", q2, st["QF"][p], deps=oev + P.wdeps("q2")))
        tkd = P.rd("KF", P.dma("sync", k2, st["KF"][p], deps=oev + P.wdeps("k2")))
        tvd = P.rd("VSP", P.dma("sync", v2p3,
                                st["VSP"][:, :, 130 * p:130 * (p + 1)].rearrange("t p n -> p t n"),
                                deps=oev + P.wdeps("v2p")))
        owtok = {}
        evtok = {}
        sbank_rd = [None, None]
        slot_rd = {}
        for t in range(T):
            W = min(384, ntok - 128 * t)
            nw = (W + 127) // 128
            for h in range(2):
                sb_ = PB[h]
                rows = slice(64 * h, 64 * h + 64)
                tS = P.on("tensor", lambda e, t=t, h=h, sb_=sb_, rows=rows, W=W: e.matmul(
                    sb_[:, 0:W], k2[rows, t * 128:(t + 1) * 128],
                    q2[rows, t * 128:t * 128 + W], start=True, stop=True),
                    deps=([tqd, tkd, tvd, tnc] if t == 0 else []) + [sbank_rd[h]], tok=True)
                wsf = (wa, wb)[h][:, (t % 2) * 512:(t % 2) * 512 + 384]
                sbank_rd[h] = P.on("vector", lambda e, sb_=sb_, wsf=wsf, W=W:
                                   e.scalar_tensor_tensor(wsf[:, 0:W], sb_[:, 0:W], 8.0,
                                                          st["maskb"][:, 0:W],
                                                          ALU.mult, ALU.add),
                                   deps=[tS, slot_rd.get((h, t % 2))], tok=True)
                slot = 2 * (t % 2) + h
                pslot = ptb[:, slot * 512:slot * 512 + 384]
                tx = P.on("scalar", lambda e, pslot=pslot, wsf=wsf, W=W: e.activation(
                    pslot[:, 0:W], wsf[:, 0:W], AF.Exp, bias=negC[:], scale=1.0),
                    deps=[sbank_rd[h], slot_rd.get((h, t % 2))], tok=True)
                ws = [w for w in range(t, t + nw) if w < T]
                for j, w in enumerate(ws):
                    bank = PB[2 + 3 * h + (w % 3)]
                    first = (t == max(0, w - 2))
                    last = (t == w)
                    dep = [tx]
                    if first and w >= 3:
                        dep.append(evtok[(h, w - 3)])
                    tk_ = P.on("tensor", lambda e, bank=bank, h=h, t=t, j=j, first=first,
                               last=last, pslot=pslot: e.matmul(
                        bank[0:65, 0:128], v2p3[:, t, 65 * h:65 * h + 65],
                        pslot[:, j * 128:(j + 1) * 128], start=first, stop=last),
                        deps=dep, tok=(last or j == len(ws) - 1))
                    if last:
                        owtok[(h, w)] = tk_
                    if j == len(ws) - 1:
                        slot_rd[(h, t % 2)] = tk_
                w = t
                bank = PB[2 + 3 * h + (w % 3)]
                Odst = O1 if h == 0 else O2
                evtok[(h, w)] = P.on("vector", lambda e, bank=bank, w=w, Odst=Odst:
                                     e.tensor_copy(Odst[:, w * 128:(w + 1) * 128],
                                                   bank[0:65, 0:128]),
                                     deps=[owtok[(h, w)]], tok=True)
        oev = []
        for h in range(2):
            Osrc = O1 if h == 0 else O2
            tch = []
            for qg in range(ntok // 512):
                tt1 = P.on("vector", lambda e, Osrc=Osrc, qg=qg: e.reciprocal(
                    rr[:, 0:512], Osrc[64:65, qg * 512:(qg + 1) * 512]),
                    deps=[evtok[(h, w)] for w in range(4 * qg, min(T, 4 * qg + 4))]
                    + P.wdeps("lrr"), tok=True)
                tch.append(P.rd("lrr",
                                P.dma("sync", st["BNC"][h:h + 1, qg * 512:(qg + 1) * 512],
                                      rr[0:1, 0:512], deps=[tt1] + P.wdeps(f"LBNC{h}"))))
            tb = P.rd(f"LBNC{h}", P.dma("sync", rbig,
                                        st["BNC"][h:h + 1, :].partition_broadcast(64),
                                        deps=tch + P.wdeps("rbig")))
            t4 = P.rd("rbig", P.on("vector", lambda e, Osrc=Osrc: e.tensor_tensor(
                obig, Osrc[0:64, :], rbig, ALU.mult),
                deps=[tb] + P.wdeps("obig"), tok=True))
            t5 = P.rd("obig", P.dma("sync", st["OF"][p, 64 * h:64 * h + 64, :], obig,
                                    deps=[t4] + P.wdeps("OF")))
            oev += [t4, t5]

    if st["stop"] == "lattnB":
        return
    P.barrier()
    wo = bf16v(AR2, 0, 8 * DM)
    wo3 = wo.rearrange("p (c n) -> p c n", c=8)
    ofm3 = st["wbf3"].ap().rearrange("p (c n) -> p c n", c=8)
    two = P.dma("gpsimd", wo3, din["f_wo"][li])
    prev = []
    for tt in range(T):
        r = slice(tt * 128, tt * 128 + 128)
        tof = P.rd("OF", P.dma("sync", ofm3,
                               st["OF"][:, :, tt * 128:(tt + 1) * 128].rearrange("c p n -> p c n"),
                               deps=prev))
        txl = P.rd(yname, P.dma("sync", wc[:], yin[r, :], deps=prev))
        tmm = {}
        for half in range(2):
            bank = PB[half]
            for c in range(8):
                tmm[half] = P.on("tensor", lambda e, c=c, half=half, bank=bank: e.matmul(
                    bank[:], ofm3[:, c, :], wo3[:, c, half * 512:(half + 1) * 512],
                    start=(c == 0), stop=(c == 7)),
                    deps=([tof, two] + prev) if c == 0 else (), tok=(c == 7))
        prev = []
        for half in range(2):
            tr = P.on("vector", lambda e, half=half: e.tensor_tensor(
                wd[:, half * 512:(half + 1) * 512], PB[half][:],
                wc[:, half * 512:(half + 1) * 512], ALU.add),
                deps=[tmm[half], txl], tok=True)
            prev.append(tr)
        prev.append(P.dma("sync", yout[r, :], wd[:], deps=prev + P.wdeps(yoname)))


def rotary(P, st, tt, bank0, CS3, SS3, dstF, stg_name, deps):
    wa, wc, wd, we = st["wa"], st["wc"], st["wd"], st["we"]
    stat = st["stat"]; PB = st["PB"]
    stg = st[stg_name]
    t1 = P.on("scalar", lambda e: e.copy(wc[:, 0:512], PB[bank0][:]), deps=deps, tok=True)
    t2 = P.on("scalar", lambda e: e.copy(wc[:, 512:1024], PB[bank0 + 1][:]), tok=True)
    t3 = P.on("scalar", lambda e: e.activation(wd[:, 0:512], PB[bank0][:], AF.Square), tok=True)
    t4 = P.on("scalar", lambda e: e.activation(wd[:, 512:1024], PB[bank0 + 1][:], AF.Square),
              tok=True)
    wc3 = wc.ap().rearrange("p (h d) -> p h d", d=DH)
    wd3 = wd.ap().rearrange("p (h d) -> p h d", d=DH)
    t5 = P.on("vector", lambda e: e.tensor_reduce(stat[:, 0:16], wd3, AXX, ALU.add),
              deps=[t3, t4], tok=True)
    t6 = P.on("vector", lambda e: e.tensor_scalar(stat[:, 0:16], stat[:, 0:16], 1.42e-24,
                                                  None, ALU.max, ALU.bypass), deps=[t5], tok=True)
    t7 = P.on("scalar", lambda e: e.activation(stat[:, 0:16], stat[:, 0:16], AF.Ln,
                                               scale=1.0), deps=[t6], tok=True)
    t8 = P.on("scalar", lambda e: e.activation(stat[:, 0:16], stat[:, 0:16], AF.Exp,
                                               scale=-0.5), deps=[t7], tok=True)
    rb = stat[:, 0:16].unsqueeze(2).broadcast_to((128, 16, DH))
    tu = P.on("vector", lambda e: e.tensor_tensor(wd3, wc3, rb, ALU.mult),
              deps=[t1, t2, t8], tok=True)
    wd4 = wd.ap().rearrange("p (h s d2) -> p h s d2", s=2, d2=32)
    wa4 = wa.ap().rearrange("p (h s d2) -> p h s d2", s=2, d2=32)
    g1 = P.on("vector", lambda e: e.tensor_tensor(
        we.ap().rearrange("p (h d) -> p h d", d=DH), wd3,
        CS3[:, tt, :].unsqueeze(1).broadcast_to((128, 16, DH)), ALU.mult),
        deps=[tu], tok=True)
    g2 = P.on("vector", lambda e: e.tensor_tensor(
        wa4[:, :, 0, :], wd4[:, :, 1, :],
        SS3[:, tt, 0:32].unsqueeze(1).broadcast_to((128, 16, 32)), ALU.mult),
        deps=[g1], tok=True)
    g3 = P.on("vector", lambda e: e.tensor_tensor(
        wa4[:, :, 1, :], wd4[:, :, 0, :],
        SS3[:, tt, 32:64].unsqueeze(1).broadcast_to((128, 16, 32)), ALU.mult),
        deps=[g2], tok=True)
    g4 = P.on("vector", lambda e: e.tensor_tensor(st["wbf"][:], we[:], wa[:], ALU.add),
              deps=[g1, g2, g3], tok=True)
    tev = transpose_tile(P, st, st["wbf"], stg.ap().rearrange("p (c n) -> p c n", c=8),
                         7, [g4])
    tsp = P.dma("sync", dstF.rearrange("c p n -> p c n")[:, :, tt * 128:(tt + 1) * 128],
                stg.ap().rearrange("p (c n) -> p c n", c=8), deps=[tev])
    return {"sp": tsp, "mid": g4, "ev": tev}


def ph_lffn(P, st, li, yin, yname, yout, yoname, inv):
    din = st["din"]; T = st["T"]; ntok = st["ntok"]
    AR1, AR2, PB = st["AR1"], st["AR2"], st["PB"]
    wa, wb, wc, wd = st["wa"], st["wb"], st["wc"], st["wd"]
    wbf, gsb = st["wbf"], st["gsb"]
    P.barrier()

    w1 = bf16v(AR1, 0, 8 * W1C)
    w13 = w1.rearrange("p (c n) -> p c n", c=8)
    w2 = bf16v(AR2, 0, 22 * DM)
    w23 = w2.rearrange("p (c n) -> p c n", c=22)
    lngb = f32v(AR2, (22 * DM) // 2, DM)
    mst = bf16v(AR2, 22 * DM + 2 * DM, DI)
    mfst = bf16v(AR2, 22 * DM + 2 * DM + DI, 22 * 128)
    assert 22 * DM + 2 * DM + DI + 22 * 128 <= 30208
    lnf3 = st["wbf3"].ap().rearrange("p (c n) -> p c n", c=8)
    mfst3 = mfst.rearrange("p (c n) -> p c n", c=22)

    tw1 = P.dma("gpsimd", w13, din["f_w1"][li])
    tw2 = P.dma("gpsimd", w23, din["f_w2"][li])
    tb2 = P.dma("gpsimd", st["b2row"][:], din["f_b2"][li:li + 1, :])
    tlg = bcast_col(P, st, din["f_lng"][li:li + 1, :], lngb)
    nch = (W1C + 511) // 512
    if inv == 0:
        lnb_src = din["f_lnb"][li:li + 1, :].rearrange("o (c p) -> p (o c)", p=128)
        tlb = P.on("gpsimd", lambda e: e.dma_start(out=st["lnbc"][:], in_=lnb_src,
                                                   allow_slow_non_contiguous=True),
                   tok=True, dma=True)
        twr_last = None
        twr = []
        for j in range(nch):
            Wj = min(512, W1C - 512 * j)
            tbr = P.dma("sync", st["rr"][0:1, 0:Wj],
                        din["f_b1"][li:li + 1, 512 * j:512 * j + Wj], deps=[twr_last])
            tmj = None
            for c in range(8):
                tmj = P.on("tensor", lambda e, j=j, c=c, Wj=Wj: e.matmul(
                    PB[0][0:1, 0:Wj], st["lnbc"][:, c:c + 1],
                    w13[:, c, 512 * j:512 * j + Wj], start=(c == 0), stop=(c == 7)),
                    deps=([tlb, tw1, twr_last]) if c == 0 else (), tok=(c == 7))
            twr_last = P.on("vector", lambda e, j=j, Wj=Wj: e.tensor_tensor(
                st["wrow"][0:1, 512 * j:512 * j + Wj], PB[0][0:1, 0:Wj],
                st["rr"][0:1, 0:Wj], ALU.add), deps=[tmj, tbr], tok=True)
            twr.append(twr_last)
        twr = [P.dma("sync", st["WRD"][li], st["wrow"][:], deps=twr)]
    else:
        twr = [P.dma("sync", st["wrow"][:], st["WRD"][li])]

    prev = []
    prev_aj = []
    prev_mev = []
    for tt in range(T):
        r = slice(tt * 128, tt * 128 + 128)
        tls = shifted_load(P, st, yin, yname, tt, wa, deps=prev)
        negmu, rstd, tst = ln_stats(P, st, wa, LNEPS, tls)
        t1 = P.on("vector", lambda e: e.tensor_scalar(wb[:], wa[:], negmu, rstd,
                                                      ALU.add, ALU.mult), deps=tst, tok=True)
        t2 = P.on("vector", lambda e: e.scalar_tensor_tensor(wbf[:], wb[:], 1.0, lngb,
                                                             ALU.mult, ALU.mult),
                  deps=[t1, tlg], tok=True)
        tsf = transpose_tile(P, st, wbf, lnf3, 6, [t2])
        ajs = []
        for j in range(6):
            a0 = 512 * j
            aW = min(512, DI - a0)
            tmj = {}
            for (idx, c0_) in ((0, a0), (1, DI + a0)):
                bank = PB[2 * (j % 2) + idx]
                for c in range(8):
                    tmj[idx] = P.on("tensor", lambda e, c=c, c0_=c0_, aW=aW, bank=bank: e.matmul(
                        bank[:, 0:aW], lnf3[:, c, :], w13[:, c, c0_:c0_ + aW],
                        start=(c == 0), stop=False),
                        deps=([tsf, tw1] + twr + prev + ajs[-4:] + prev_aj[-4:])
                        if c == 0 else ())
                tmj[idx] = P.on("tensor", lambda e, c0_=c0_, aW=aW, bank=bank: e.matmul(
                    bank[:, 0:aW], st["ones_row"][0:1, 0:128], st["wrow"][0:1, c0_:c0_ + aW],
                    start=False, stop=True), deps=twr, tok=True)
            gb = PB[2 * (j % 2) + 1]
            ab = PB[2 * (j % 2)]
            tg = P.on("scalar", lambda e, gb=gb, aW=aW: e.activation(
                gsb[:, 0:aW], gb[:, 0:aW], AF.Gelu), deps=[tmj[1]], tok=True)
            ajs.append(P.on("vector", lambda e, ab=ab, a0=a0, aW=aW: e.tensor_tensor(
                mst[:, a0:a0 + aW], ab[:, 0:aW], gsb[:, 0:aW], ALU.mult),
                deps=[tmj[0], tg], tok=True))
        prev_aj = ajs
        tmev = []
        pv = PB[7].ap().bitcast(BF16)
        for grp in range(3):
            ch0 = grp * 8
            nc_ = min(8, 22 - ch0)
            tlast = None
            for cc in range(nc_):
                c2 = ch0 + cc
                cw = min(128, DI - c2 * 128)
                tlast = P.on("tensor", lambda e, cc=cc, c2=c2, cw=cw, pv=pv: e.transpose(
                    pv[0:cw, cc * 128:(cc + 1) * 128], mst[:, c2 * 128:c2 * 128 + cw],
                    st["ident"][:]),
                    deps=(ajs + [st["pt_rd"].get(7)]) if cc == 0 else (), tok=(cc == nc_ - 1))
            pvv = pv.rearrange("p (c n) -> p c n", c=8)[:, 0:nc_, :]
            tmev.append(P.on("scalar", lambda e, ch0=ch0, nc_=nc_, pvv=pvv: e.copy(
                mfst3[:, ch0:ch0 + nc_, :], pvv), deps=[tlast], tok=True))
            st["pt_rd"][7] = tmev[-1]
        prev_mev = tmev
        txl = P.rd(yname, P.dma("sync", wc[:], yin[r, :], deps=prev))
        tm2 = {}
        for half in range(2):
            bank = PB[4 + half]
            for c2 in range(22):
                tm2[half] = P.on("tensor", lambda e, c2=c2, half=half, bank=bank: e.matmul(
                    bank[:], mfst[:, c2 * 128:(c2 + 1) * 128],
                    w23[:, c2, half * 512:(half + 1) * 512],
                    start=(c2 == 0), stop=False),
                    deps=(tmev + [tw2] + prev) if c2 == 0 else ())
            tm2[half] = P.on("tensor", lambda e, half=half, bank=bank: e.matmul(
                bank[:], st["ones_row"][0:1, 0:128],
                st["b2row"][0:1, half * 512:(half + 1) * 512],
                start=False, stop=True), deps=[tb2], tok=True)
        prev = []
        for half in range(2):
            tr = P.on("vector", lambda e, half=half: e.tensor_tensor(
                wd[:, half * 512:(half + 1) * 512], PB[4 + half][:],
                wc[:, half * 512:(half + 1) * 512], ALU.add),
                deps=[tm2[half], txl], tok=True)
            prev.append(tr)
        prev.append(P.dma("sync", yout[r, :], wd[:], deps=prev + P.wdeps(yoname)))


_CACHE = {}


def kernel(**inputs):
    x = np.asarray(inputs["L0_attn_a_x"])
    B, N, E = x.shape
    key = (N,)
    if key not in _CACHE:
        nc, P = build(ntok=N)
        _CACHE[key] = (nc, host_tables(N))
    nc, tables = _CACHE[key]
    w = prep_weights(inputs)
    base = {}
    base.update(w)
    base.update(tables)
    in_maps = []
    for c in range(B):
        m = dict(base)
        m["x0"] = np.ascontiguousarray(x[c])
        in_maps.append(m)
    res = run_bass_kernel_spmd(nc, in_maps, core_ids=list(range(B)))
    return np.stack([np.asarray(r["out"]) for r in res.results], axis=0)


# version 02
# speedup vs baseline: 1.6295x; 1.6295x over previous
import numpy as np
import ml_dtypes
import concourse.bass as bass
import concourse.mybir as mybir
from concourse.bass_utils import run_bass_kernel_spmd

F32 = mybir.dt.float32
BF16 = mybir.dt.bfloat16
AF = mybir.ActivationFunctionType
ALU = mybir.AluOpType
AXX = mybir.AxisListType.X
BFNP = ml_dtypes.bfloat16

DM = 1024
NH = 16
DH = 64
DEPTH = 6
DI = 2730
W1C = 5460
WIN = 256
NEG = -1.0e38
LNEPS = 1e-5
RMSEPS = float(np.finfo(np.float32).eps)
C_ATTN_A = 30.0

ENGS = ("sync", "tensor", "vector", "scalar", "gpsimd")


def host_tables(ntok):
    t = {}
    t["ident"] = np.eye(128, dtype=BFNP)
    t["ones_row"] = np.ones((1, 512), dtype=BFNP)
    p = np.arange(128)[:, None]
    c = np.arange(384)[None, :]
    t["maskb"] = np.where((c >= p) & (c <= p + WIN), 0.0, NEG).astype(np.float32)
    u = np.arange(2 * ntok - 1)[None, :]
    t["dist"] = np.abs(u - (ntok - 1) - np.arange(128)[:, None]).astype(np.float32)
    half = np.arange(0, DH, 2, dtype=np.float64)
    inv_freq = 1.0 / 10000.0 ** (half / DH)
    tt = np.arange(ntok, dtype=np.float64)
    freqs = tt[:, None] * inv_freq[None, :]
    freqs = np.concatenate([freqs, freqs], -1)
    base_scale = (half + 0.4 * DH) / (1.4 * DH)
    power = (tt - ntok // 2) / (WIN // 2)
    scale = base_scale[None, :] ** power[:, None]
    scale = np.concatenate([scale, scale], -1)
    cos = np.cos(freqs)
    sin = np.sin(freqs)
    sgn = np.where(np.arange(DH) < DH // 2, -1.0, 1.0)[None, :]
    t["cs_q"] = (cos * scale).astype(np.float32)
    t["ss_q"] = (sin * scale * sgn).astype(np.float32)
    t["cs_k"] = (cos / scale).astype(np.float32)
    t["ss_k"] = (sin / scale * sgn).astype(np.float32)
    return t


def prep_weights(inp, depth=DEPTH):
    w = {}
    cont = np.ascontiguousarray

    def pmaj(a):
        E, F = a.shape
        return cont(a.reshape(E // 128, 128, F).transpose(1, 0, 2))

    w["aw_qkv"] = pmaj(np.asarray(inp["L0_attn_a_Wqkv_w"]).T.copy())
    w["ab_qkv"] = cont(np.asarray(inp["L0_attn_a_Wqkv_b"])[None, :])
    w["aw_out"] = pmaj(np.asarray(inp["L0_attn_a_out_w"]).T.copy())
    w["ab_out"] = cont(np.asarray(inp["L0_attn_a_out_b"])[None, :])
    w["a_dd"] = cont(np.asarray(inp["L0_attn_a_distance_decay"])[None, :])
    w["bw_qkv"] = pmaj(np.concatenate(
        [np.asarray(inp["L0_attn_b_Wq"]).T, np.asarray(inp["L0_attn_b_Wk"]).T,
         np.asarray(inp["L0_attn_b_Wv"]).T], axis=1))
    w["bw_out"] = pmaj(np.asarray(inp["L0_attn_b_Wo"]).T.copy())
    w["b_qn"] = cont(np.asarray(inp["L0_attn_b_qn_w"])[None, :])
    w["b_kn"] = cont(np.asarray(inp["L0_attn_b_kn_w"])[None, :])
    w["f_wqkv"] = np.stack([pmaj(np.asarray(inp["L0_ffn_Wqkv"])[i]) for i in range(depth)])
    w["f_wo"] = np.stack([pmaj(np.asarray(inp["L0_ffn_Wo"])[i]) for i in range(depth)])
    w["f_w1"] = np.stack([pmaj(np.asarray(inp["L0_ffn_W1"])[i]) for i in range(depth)])
    w2 = np.asarray(inp["L0_ffn_W2"])
    w2p = np.zeros((depth, 2816, DM), np.float32)
    w2p[:, :DI] = w2[:depth]
    w["f_w2"] = np.stack([pmaj(w2p[i]) for i in range(depth)])
    w["f_qs"] = cont(np.asarray(inp["L0_ffn_q_scale"])[:depth])
    w["f_ks"] = cont(np.asarray(inp["L0_ffn_k_scale"])[:depth])
    w["f_lng"] = cont(np.asarray(inp["L0_ffn_ln_g"])[:depth])
    w["f_lnb"] = cont(np.asarray(inp["L0_ffn_ln_b"])[:depth])
    w["f_b1"] = cont(np.asarray(inp["L0_ffn_b1"])[:depth])
    w["f_b2"] = cont(np.asarray(inp["L0_ffn_b2"])[:depth])
    return w


class Prog:
    DPOOL = {"sync": 8, "gpsimd": 4, "scalar": 4}
    SEMN = (["sync", "tensor", "vector", "scalar", "gpsimd"]
            + [f"d{q}{i}" for q, n in DPOOL.items() for i in range(n)])
    SAFE_SAME_ENGINE = True

    def __init__(self):
        self.q = {e: [] for e in ENGS}
        self.cnt = {}
        self.waited = {e: {} for e in ENGS}
        self.sems = {}
        self.ninst = 0
        self.scr_rd = {}

    def _dma_sem(self, eng):
        if not hasattr(self, "_dma_rr"):
            self._dma_rr = {}
        i = self._dma_rr.get(eng, 0)
        self._dma_rr[eng] = (i + 1) % self.DPOOL[eng]
        return f"d{eng}{i}"

    def wait(self, eng, deps):
        for d in deps:
            if d is None:
                continue
            s, v = d
            if self.waited[eng].get(s, 0) < v:
                self.waited[eng][s] = v

                def mk(s=s, v=v):
                    return lambda e: e.wait_ge(self.sems[s], v)
                self.q[eng].append(mk())
                self.ninst += 1

    def on(self, eng, fn, deps=(), tok=False, dma=False, mode=None):
        if eng == "tensor" and not dma:
            assert mode is not None, "tensor ops must declare a tiling mode"
            if mode != getattr(self, "_pe_mode", None):
                deps = list(deps) + [getattr(self, "_last_pe", None)]
            self._pe_mode = mode
            tok = True
        self.wait(eng, deps)
        self.ninst += 1
        if not hasattr(self, "last_tok"):
            self.last_tok = {}
            self.clean = {}
        if not tok:
            self.q[eng].append(fn)
            self.clean[eng] = False
            return None
        s = self._dma_sem(eng) if dma else eng
        inc = 16 if dma else 1
        if dma:
            self.wait(eng, [(s, self.cnt.get(s, 0))])
        self.cnt[s] = self.cnt.get(s, 0) + inc
        v = self.cnt[s]

        def mk(fn=fn, s=s, inc=inc):
            return lambda e: fn(e).then_inc(self.sems[s], inc)
        self.q[eng].append(mk())
        if not dma and not self.SAFE_SAME_ENGINE:
            self.waited[eng][s] = v
        if eng == "tensor" and not dma:
            self._last_pe = (s, v)
        if not dma:
            self.last_tok[eng] = (s, v)
            self.clean[eng] = True
        else:
            self.last_tok[eng] = (s, v)
            self.clean[eng] = True
        return (s, v)

    def dma(self, eng, out, in_, deps=()):
        return self.on(eng, lambda e, o=out, i=in_: e.dma_start(out=o, in_=i),
                       deps=deps, tok=True, dma=True)

    def rd(self, name, tok):
        self.scr_rd.setdefault(name, []).append(tok)
        return tok

    def wdeps(self, name):
        return self.scr_rd.pop(name, [])

    def barrier(self):
        if not hasattr(self, "last_tok"):
            self.last_tok = {}
            self.clean = {}
        for eng in ENGS:
            if self.q[eng] and not self.clean.get(eng, True):
                self.on(eng, lambda e: e.nop(), tok=True)
        snap = [(s, self.cnt.get(s, 0)) for s in self.SEMN if self.cnt.get(s, 0) > 0]
        for eng in ENGS:
            self.wait(eng, snap)

    def finish(self, blk):
        for name in ENGS:
            ops = self.q[name]

            def mksec(ops=ops):
                def section(e):
                    for f in ops:
                        f(e)
                return section
            getattr(blk, name)(mksec())


def build(ntok=4096, depth=DEPTH, parts=("a", "m1", "b", "m2"), dbg_out=(), stop=None):
    T = ntok // 128
    nc = bass.Bass("TRN2", target_bir_lowering=False)
    P = Prog()
    din = {}

    def dt_in(name, shape, dtype=F32):
        din[name] = nc.dram_tensor(name, list(shape), dtype, kind="ExternalInput").ap()

    dt_in("x0", (ntok, DM))
    dt_in("aw_qkv", (128, 8, 3 * DM)); dt_in("ab_qkv", (1, 3 * DM))
    dt_in("aw_out", (128, 8, DM)); dt_in("ab_out", (1, DM)); dt_in("a_dd", (1, 1))
    dt_in("bw_qkv", (128, 8, 3 * DM)); dt_in("bw_out", (128, 8, DM))
    dt_in("b_qn", (1, DH)); dt_in("b_kn", (1, DH))
    dt_in("f_wqkv", (depth, 128, 8, 3 * DM)); dt_in("f_wo", (depth, 128, 8, DM))
    dt_in("f_w1", (depth, 128, 8, W1C)); dt_in("f_w2", (depth, 128, 22, DM))
    dt_in("f_qs", (depth, DH)); dt_in("f_ks", (depth, DH))
    dt_in("f_lng", (depth, DM)); dt_in("f_lnb", (depth, DM))
    dt_in("f_b1", (depth, W1C)); dt_in("f_b2", (depth, DM))
    dt_in("ident", (128, 128), BF16); dt_in("ones_row", (1, 512), BF16)
    dt_in("maskb", (128, 384)); dt_in("dist", (128, 2 * ntok - 1))
    dt_in("cs_q", (ntok, DH)); dt_in("ss_q", (ntok, DH))
    dt_in("cs_k", (ntok, DH)); dt_in("ss_k", (ntok, DH))
    dout = nc.dram_tensor("out", [ntok, DM], F32, kind="ExternalOutput").ap()

    def dt_sc(name, shape, dtype=F32):
        kind = "ExternalOutput" if name in dbg_out else "Internal"
        return nc.dram_tensor(name, list(shape), dtype, kind=kind).ap()

    st = dict(ntok=ntok, T=T, depth=depth, nc=nc, din=din, dout=dout, stop=stop)
    st["X1"] = dt_sc("X1", (ntok, DM))
    st["X2"] = dt_sc("X2", (ntok, DM))
    st["X3"] = dt_sc("X3", (ntok, DM))
    st["YA"] = dt_sc("YA", (ntok, DM))
    st["YB"] = dt_sc("YB", (ntok, DM))
    st["QF"] = dt_sc("QF", (8, 128, ntok), BF16)
    st["KF"] = dt_sc("KF", (8, 128, ntok), BF16)
    st["VSP"] = dt_sc("VSP", (T, 128, NH * 65), BF16)
    st["OF"] = dt_sc("OF", (8, 128, ntok), BF16)
    st["BNC"] = dt_sc("BNC", (8, ntok))
    st["WRD"] = dt_sc("WRD", (depth, W1C), BF16)

    from contextlib import ExitStack
    ctx = ExitStack()
    sb = lambda name, shape, dtype: ctx.enter_context(nc.sbuf_tensor("s_" + name, list(shape), dtype))
    with ctx:
        st["AR1"] = sb("AR1", [128, 44544], BF16)
        st["AR2"] = sb("AR2", [128, 30208], BF16)
        st["ident"] = sb("ident", [128, 128], BF16)
        st["ones_row"] = sb("ones_row", [1, 512], BF16)
        st["maskb"] = sb("maskb", [128, 384], F32)
        st["wa"] = sb("wa", [128, 1024], F32)
        st["wb"] = sb("wb", [128, 1024], F32)
        st["wc"] = sb("wc", [128, 1024], F32)
        st["wd"] = sb("wd", [128, 1024], F32)
        st["we"] = sb("we", [128, 1024], F32)
        st["wbf"] = sb("wbf", [128, 1024], BF16)
        st["wbf3"] = sb("wbf3", [128, 1024], BF16)
        st["wbf4"] = sb("wbf4", [128, 1024], BF16)
        st["wbf5"] = sb("wbf5", [128, 1024], BF16)
        st["vst"] = sb("vst", [128, NH * 65], BF16)
        st["gsb"] = sb("gsb", [128, 512], F32)
        st["stat"] = sb("stat", [128, 32], F32)
        st["sml"] = sb("sml", [128, 160], F32)
        st["ptb"] = sb("ptb", [128, 4 * 512], BF16)
        st["wrow"] = sb("wrow", [1, W1C], BF16)
        st["b2row"] = sb("b2row", [1, DM], BF16)
        st["absl"] = sb("absl", [1, 3 * DM], BF16)
        st["lnbc"] = sb("lnbc", [128, 8], BF16)
        st["qnc"] = sb("qnc", [128, 1], F32)
        st["knc"] = sb("knc", [128, 1], F32)
        st["negC"] = sb("negC", [128, 1], F32)
        st["onec"] = sb("onec", [128, 1], BF16)
        st["rr"] = sb("rr", [1, 512], F32)
        st["sconst"] = sb("sconst", [128, 4], F32)
        for col, val in enumerate([LNEPS, RMSEPS, -C_ATTN_A]):
            nc.const_aps.register(F32, float(val), st["sconst"].ap()[:, col:col + 1])
        st["PB"] = [ctx.enter_context(nc.psum_tensor(f"pb{i}", [128, 512], F32))
                    for i in range(8)]
        st["pt_rd"] = {}
        for sname in Prog.SEMN:
            P.sems[sname] = ctx.enter_context(nc.semaphore(sname))
        blk = ctx.enter_context(nc.Block())
        emit_program(P, st, parts)
        P.finish(blk)
    return nc, P


def f32v(arena, c0f, ncols, rows=128, p0=0):
    return arena.ap().bitcast(F32)[p0:p0 + rows, c0f:c0f + ncols]


def bf16v(arena, c0, ncols, rows=128, p0=0):
    return arena[p0:p0 + rows, c0:c0 + ncols]


def ln_stats(P, st, xt, eps, deps):
    stat = st["stat"]; we = st["we"]
    t1 = P.on("vector", lambda e: e.tensor_reduce(stat[:, 16:17], xt[:], AXX, ALU.add),
              deps=deps, tok=True)
    t2 = P.on("scalar", lambda e: e.activation(we[:], xt[:], AF.Square), deps=deps, tok=True)
    t3 = P.on("vector", lambda e: e.tensor_reduce(stat[:, 17:18], we[:], AXX, ALU.add),
              deps=[t2], tok=True)
    t4 = P.on("vector", lambda e: e.tensor_scalar(stat[:, 18:19], stat[:, 16:17],
                                                  -1.0 / 1024, None, ALU.mult, ALU.bypass),
              deps=[t1], tok=True)
    t5 = P.on("vector", lambda e: e.tensor_tensor(stat[:, 19:20], stat[:, 18:19],
                                                  stat[:, 18:19], ALU.mult), deps=[t4], tok=True)
    t6 = P.on("vector", lambda e: e.tensor_scalar(stat[:, 20:21], stat[:, 17:18],
                                                  1.0 / 1024, None, ALU.mult, ALU.bypass),
              deps=[t3], tok=True)
    t7 = P.on("vector", lambda e: e.tensor_tensor(stat[:, 21:22], stat[:, 20:21],
                                                  stat[:, 19:20], ALU.subtract),
              deps=[t5, t6], tok=True)
    t8 = P.on("scalar", lambda e: e.activation(stat[:, 22:23], stat[:, 21:22], AF.Ln,
                                               bias=float(eps), scale=1.0), deps=[t7], tok=True)
    t9 = P.on("scalar", lambda e: e.activation(stat[:, 23:24], stat[:, 22:23], AF.Exp,
                                               scale=-0.5), deps=[t8], tok=True)
    return stat[:, 18:19], stat[:, 23:24], [t4, t9]


def transpose_tile(P, st, src_bf, dst_strided, bank_idx, deps, nchunk=8, dst_chunks=None):
    bank = st["PB"][bank_idx]
    pv = bank.ap().bitcast(BF16)
    alldeps = list(deps) + [st["pt_rd"].get(bank_idx)]
    tl = None
    for c in range(nchunk):
        tl = P.on("tensor", lambda e, c=c: e.transpose(
            pv[:, c * 128:(c + 1) * 128], src_bf[:, c * 128:(c + 1) * 128], st["ident"][:]),
            deps=alldeps if c == 0 else (), tok=(c == nchunk - 1), mode="tr")
    pvv = pv.rearrange("p (c n) -> p c n", c=8)[:, 0:nchunk, :]
    tev = P.on("scalar", lambda e: e.copy(dst_strided, pvv), deps=[tl], tok=True)
    st["pt_rd"][bank_idx] = tev
    return tev


def shifted_load(P, st, ysrc, yname, tt, dst, deps=()):
    r0 = tt * 128
    t1 = P.rd(yname, P.dma("sync", dst[:, 0:512], ysrc[r0:r0 + 128, 0:512], deps=deps))
    if tt == 0:
        t2 = P.rd(yname, P.dma("sync", dst[1:128, 512:1024], ysrc[0:127, 512:1024]))
        t3 = P.on("gpsimd", lambda e: e.memset(dst[0:1, 512:1024], 0.0), deps=deps, tok=True)
        return [t1, t2, t3]
    t2 = P.rd(yname, P.dma("sync", dst[:, 512:1024], ysrc[r0 - 1:r0 + 127, 512:1024]))
    return [t1, t2]


def bcast_col(P, st, dram_row_ap, dst, deps=()):
    return P.dma("sync", dst, dram_row_ap.partition_broadcast(dst.shape[0]), deps=deps)


def emit_program(P, st, parts):
    din = st["din"]
    P.dma("sync", st["ident"][:], din["ident"][:])
    P.dma("sync", st["ones_row"][:], din["ones_row"][:])
    P.dma("sync", st["maskb"][:], din["maskb"][:])
    P.on("gpsimd", lambda e: e.memset(st["onec"][:], 1.0), tok=True)
    for col, val in enumerate([LNEPS, RMSEPS, -C_ATTN_A]):
        P.on("vector", lambda e, col=col, val=val: e.memset(st["sconst"][:, col:col + 1],
                                                            float(val)), tok=True)
    P.barrier()

    x_cur, x_name = din["x0"], "x0"
    if "a" in parts:
        ph_dense(P, st, "a", x_cur, x_name, st["X1"], "X1")
        x_cur, x_name = st["X1"], "X1"
    if "m1" in parts:
        ph_ffnmod(P, st, x_cur, x_name, st["X2"], "X2", inv=0)
        x_cur, x_name = st["X2"], "X2"
    if "b" in parts:
        ph_dense(P, st, "b", x_cur, x_name, st["X3"], "X3")
        x_cur, x_name = st["X3"], "X3"
    if "m2" in parts:
        ph_ffnmod(P, st, x_cur, x_name, st["dout"], "out", inv=1)
        x_cur = st["dout"]
    if x_cur is not st["dout"]:
        P.barrier()
        prev = []
        for tt in range(st["T"]):
            r = slice(tt * 128, tt * 128 + 128)
            t = P.dma("sync", st["wc"][:], x_cur[r, :], deps=prev)
            tc = P.on("vector", lambda e: e.tensor_copy(st["wd"][:], st["wc"][:]),
                      deps=[t], tok=True)
            prev = [tc, P.dma("sync", st["dout"][r, :], st["wd"][:], deps=[tc])]
    P.wait("tensor", [(s, P.cnt.get(s, 0)) for s in Prog.SEMN if s.startswith("d")])
    P.on("tensor", lambda e: e.nop(), mode="nop")


def ph_dense(P, st, var, x_in, x_name, x_out, xo_name):
    din = st["din"]; T = st["T"]; ntok = st["ntok"]
    AR1, AR2, PB = st["AR1"], st["AR2"], st["PB"]
    wa, wb, wc, wd = st["wa"], st["wb"], st["wc"], st["wd"]
    wbf, vst, ptb = st["wbf"], st["vst"], st["ptb"]
    negC, sml, rr = st["negC"], st["sml"], st["rr"]
    P.barrier()

    lnfm = bf16v(AR1, 0, 8 * ntok)
    lnfm3 = lnfm.rearrange("p (c n) -> p c n", c=8)
    wv = bf16v(AR1, 8 * ntok, 8 * DM)
    wv3 = wv.rearrange("p (c n) -> p c n", c=8)
    wslq = bf16v(AR1, 8 * ntok + 8 * DM, 8 * 128)
    wslk = bf16v(AR1, 8 * ntok + 8 * DM + 8 * 128, 8 * 128)
    wslq3 = wslq.rearrange("p (c n) -> p c n", c=8)
    wslk3 = wslk.rearrange("p (c n) -> p c n", c=8)
    ofm = bf16v(AR1, 8 * ntok + 8 * DM + 16 * 128, 8 * 128)
    ofm3 = ofm.rearrange("p (c n) -> p c n", c=8)
    if var == "a":
        dec = f32v(AR2, 0, 2 * ntok - 1)
        c0 = 2 * (2 * ntok - 1)
        rstdb = None
    else:
        dec = None
        rstdb = f32v(AR2, 0, ntok)
        c0 = 2 * ntok
    q2 = bf16v(AR2, c0, ntok); c0 += ntok
    k2 = bf16v(AR2, c0, ntok); c0 += ntok
    if var == "b":
        tmpn = bf16v(AR2, c0, ntok); c0 += ntok
    else:
        tmpn = None
    v2p = bf16v(AR2, c0, T * 130); c0 += T * 130
    v2p3 = v2p.rearrange("p (t n) -> p t n", t=T)
    rbq = st["wbf3"].ap().bitcast(F32)[0:64, 0:512]
    obf = st["wbf"][0:64, 0:512]
    assert c0 <= 30208, c0

    wq_d = din["aw_qkv" if var == "a" else "bw_qkv"]
    wo_d = din["aw_out" if var == "a" else "bw_out"]

    tbias = None
    if var == "a":
        tdd = bcast_col(P, st, din["a_dd"][0:1, :], sml[:, 8:9])
        tng = P.on("vector", lambda e: e.tensor_scalar(sml[:, 9:10], sml[:, 8:9], -1.0, None,
                                                       ALU.mult, ALU.bypass), deps=[tdd], tok=True)
        tna = P.on("vector", lambda e: e.tensor_tensor(sml[:, 10:11], sml[:, 8:9],
                                                       sml[:, 9:10], ALU.min), deps=[tng], tok=True)
        tdist = P.dma("sync", dec, din["dist"][:])
        tdec = P.on("scalar", lambda e: e.activation(dec, dec, AF.Exp, scale=sml[:, 10:11]),
                    deps=[tna, tdist], tok=True)
        tbias = P.dma("gpsimd", st["absl"][:], din["ab_qkv"][:])
        tbo = P.dma("gpsimd", st["b2row"][:], din["ab_out"][:])
        tnc = None
    else:
        tq1 = P.dma("sync", st["qnc"][0:64, :],
                    din["b_qn"].rearrange("o (p i) -> (o p) i", i=1))
        tq2 = P.dma("sync", st["qnc"][64:128, :],
                    din["b_qn"].rearrange("o (p i) -> (o p) i", i=1))
        tk1 = P.dma("sync", st["knc"][0:64, :],
                    din["b_kn"].rearrange("o (p i) -> (o p) i", i=1))
        tk2 = P.dma("sync", st["knc"][64:128, :],
                    din["b_kn"].rearrange("o (p i) -> (o p) i", i=1))
        tqb = bcast_col(P, st, din["b_qn"][0:1, :], sml[:, 16:80])
        tkb = bcast_col(P, st, din["b_kn"][0:1, :], sml[:, 80:144])
        t1 = P.on("vector", lambda e: e.tensor_reduce(sml[:, 0:1], sml[:, 16:80], AXX, ALU.max,
                                                      apply_absolute_value=True),
                  deps=[tqb], tok=True)
        t2 = P.on("vector", lambda e: e.tensor_reduce(sml[:, 1:2], sml[:, 80:144], AXX, ALU.max,
                                                      apply_absolute_value=True),
                  deps=[tkb], tok=True)
        t3m = P.on("vector", lambda e: e.tensor_tensor(sml[:, 2:3], sml[:, 0:1], sml[:, 1:2],
                                                       ALU.mult), deps=[t1, t2], tok=True)
        tnc = P.on("vector", lambda e: e.tensor_scalar(negC[:], sml[:, 2:3], -8.0, None,
                                                       ALU.mult, ALU.bypass), deps=[t3m], tok=True)
        tdec = None
        st["_rmsdep"] = [tq1, tq2, tk1, tk2]

    prev_tn = None
    prev_ev = None
    for tt in range(T):
        r = slice(tt * 128, tt * 128 + 128)
        tld = P.rd(x_name, P.dma("sync", wa[:], x_in[r, :], deps=[prev_tn]))
        negmu, rstd, tst = ln_stats(P, st, wa, LNEPS, [tld])
        prev_tn = P.on("vector", lambda e: e.tensor_scalar(wbf[:], wa[:], negmu, rstd,
                                                           ALU.add, ALU.mult),
                       deps=tst + [prev_ev], tok=True)
        prev_ev = transpose_tile(P, st, wbf, lnfm3[:, :, tt * 128:(tt + 1) * 128],
                                 6, [prev_tn])

    if st["stop"] == "ph1":
        return
    twv = P.dma("gpsimd", wv3, wq_d[:, :, 2 * DM:3 * DM])
    vst3 = vst.ap().rearrange("p (h n) -> p h n", h=NH)
    tmem = P.on("gpsimd", lambda e: e.memset(vst3[:, :, 64:65], 1.0), tok=True)
    prev_sp = None
    prev_ev = [tmem, prev_ev]
    for tt in range(T):
        tm = {}
        for half in range(2):
            bank = PB[half]
            nmm = 8
            for c in range(8):
                lastmm = (c == 7) and var != "a"
                tm[half] = P.on("tensor", lambda e, c=c, half=half, bank=bank, tt=tt, lastmm=lastmm: e.matmul(
                    bank[:], lnfm3[:, c, tt * 128:(tt + 1) * 128],
                    wv3[:, c, half * 512:(half + 1) * 512],
                    start=(c == 0), stop=lastmm),
                    deps=([twv] + prev_ev) if c == 0 else (), tok=lastmm, mode="full")
            if var == "a":
                tm[half] = P.on("tensor", lambda e, half=half, bank=bank: e.matmul(
                    bank[:], st["ones_row"][0:1, 0:128],
                    st["absl"][0:1, 2 * DM + half * 512:2 * DM + (half + 1) * 512],
                    start=False, stop=True),
                    deps=[tbias], tok=True, mode="k1")
        prev_ev = []
        for half in range(2):
            pv = PB[half].ap().rearrange("p (h n) -> p h n", h=8)
            prev_ev.append(P.on("scalar", lambda e, half=half, pv=pv: e.copy(
                vst3[:, half * 8:(half + 1) * 8, 0:64], pv),
                deps=[tm[half], prev_sp], tok=True))
        prev_sp = P.dma("sync", st["VSP"][tt], vst[:], deps=prev_ev + P.wdeps("VSP"))

    if st["stop"] == "ph2":
        return
    nqg = ntok // 512
    oev = [prev_sp]
    for p in list(range(8))[st.get('_ponly', 0):]:
        tq = P.dma("gpsimd", wslq3, wq_d[:, :, 128 * p:128 * (p + 1)], deps=oev)
        tk = P.dma("gpsimd", wslk3, wq_d[:, :, DM + 128 * p:DM + 128 * (p + 1)], deps=oev)
        tev_q = list(oev)
        for (wsl3, tw_, dst, bofs) in ((wslq3, tq, q2, 128 * p), (wslk3, tk, k2, DM + 128 * p)):
            new_ev = []
            for qg in range(nqg):
                bank = PB[qg % 2]
                tmq = None
                for c in range(8):
                    lastmm = (c == 7) and var != "a"
                    tmq = P.on("tensor", lambda e, c=c, qg=qg, bank=bank, wsl3=wsl3, lastmm=lastmm: e.matmul(
                        bank[:], wsl3[:, c, :], lnfm3[:, c, qg * 512:(qg + 1) * 512],
                        start=(c == 0), stop=lastmm),
                        deps=([tw_] + tev_q) if c == 0 else (), tok=lastmm, mode="full")
                if var == "a":
                    tmq = P.on("tensor", lambda e, qg=qg, bank=bank, bofs=bofs: e.matmul(
                        bank[:], st["absl"][0:1, bofs:bofs + 128],
                        st["ones_row"][0:1, 0:512], start=False, stop=True),
                        deps=[tbias], tok=True, mode="k1")
                new_ev.append(P.on("scalar", lambda e, qg=qg, bank=bank, dst=dst: e.copy(
                    dst[:, qg * 512:(qg + 1) * 512], bank[:]), deps=[tmq], tok=True))
            tev_q = new_ev
        qS, kS = q2, k2
        if st["stop"] == "proj" and p == 0:
            return
        if var == "b":
            t_rms1 = rms_norm_pair(P, st, q2, tmpn, st["qnc"], 2, rstdb, tev_q)
            t_rms2 = rms_norm_pair(P, st, k2, q2, st["knc"], 4, rstdb, [t_rms1])
            qS, kS = tmpn, q2
            tev_q = [t_rms2]
        tv = P.dma("sync", v2p3,
                   st["VSP"][:, :, 130 * p:130 * (p + 1)].rearrange("t p n -> p t n"),
                   deps=tev_q + P.wdeps("v2p"))
        P.rd("VSP", tv)
        slot_rd = {}
        sbank_rd = {}
        for qg in range(nqg):
            ob = (PB[4], PB[5]) if qg % 2 == 0 else (PB[6], PB[7])
            tdone = [None, None]
            for t in range(T):
                for h in range(2):
                    sb_ = PB[2 * h + t % 2]
                    rows = slice(64 * h, 64 * h + 64)
                    tS = P.on("tensor", lambda e, t=t, h=h, qg=qg, sb_=sb_, rows=rows: e.matmul(
                        sb_[:], kS[rows, t * 128:(t + 1) * 128],
                        qS[rows, qg * 512:(qg + 1) * 512], start=True, stop=True),
                        deps=([tv, tnc, tdec] + tev_q + st.get("_oev_par", [])
                              if (t < 2 and qg < 2) else []) + [sbank_rd.get((h, t % 2))],
                        tok=True, mode="r64")
                    slot = 2 * (t % 2) + h
                    pslot = ptb[:, slot * 512:(slot + 1) * 512]
                    if var == "a":
                        u0 = (ntok - 1) - 128 * t + 512 * qg
                        wsf = (wa, wb)[h][:, (t % 2) * 512:(t % 2) * 512 + 512]
                        trd = P.on("vector", lambda e, sb_=sb_, wsf=wsf, u0=u0:
                                   e.scalar_tensor_tensor(wsf, sb_[:], 0.125,
                                                          dec[:, u0:u0 + 512],
                                                          ALU.mult, ALU.mult),
                                   deps=[tS, slot_rd.get(slot)], tok=True)
                        sbank_rd[(h, t % 2)] = trd
                        tx = P.on("scalar", lambda e, pslot=pslot, wsf=wsf: e.activation(
                            pslot, wsf, AF.Exp, bias=-C_ATTN_A, scale=1.0),
                            deps=[trd], tok=True)
                    else:
                        tx = P.on("scalar", lambda e, pslot=pslot, sb_=sb_: e.activation(
                            pslot, sb_[:], AF.Exp, bias=negC[:], scale=0.125),
                            deps=[tS, slot_rd.get(slot)], tok=True)
                        sbank_rd[(h, t % 2)] = tx
                    tdone[h] = P.on("tensor", lambda e, t=t, h=h, ob=ob, pslot=pslot: e.matmul(
                        ob[h][0:65, :], v2p3[:, t, 65 * h:65 * h + 65],
                        pslot, start=(t == 0), stop=(t == T - 1)),
                        deps=[tx] + (st.get("_oev_par", []) if t == 0 else []),
                        tok=True, mode="full")
                    slot_rd[slot] = tdone[h]
            oev2 = []
            for h in range(2):
                rrh = rr[0:1, 0:512] if h == 0 else st["wc"].ap().bitcast(F32)[0:1, 0:512]
                rbqh = rbq if h == 0 else st["wd"].ap().bitcast(F32)[0:64, 0:512]
                obfh = obf if h == 0 else st["wbf4"][0:64, 0:512]
                t1 = P.on("vector", lambda e, ob=ob, h=h, rrh=rrh: e.reciprocal(
                    rrh, ob[h][64:65, :]), deps=[tdone[h]] + P.wdeps(f"rr{h}"), tok=True)
                t2 = P.rd(f"rr{h}", P.dma("sync", st["BNC"][h:h + 1, 0:512], rrh,
                                          deps=[t1] + P.wdeps(f"BNC{h}")))
                t3 = P.rd(f"BNC{h}", P.dma("sync", rbqh,
                                           st["BNC"][h:h + 1, 0:512].partition_broadcast(64),
                                           deps=[t2] + P.wdeps(f"rbq{h}")))
                t4 = P.rd(f"rbq{h}", P.on("vector", lambda e, ob=ob, h=h, obfh=obfh, rbqh=rbqh:
                                          e.tensor_tensor(obfh, ob[h][0:64, :], rbqh, ALU.mult),
                                          deps=[t3] + P.wdeps(f"obf{h}"), tok=True))
                t5 = P.rd(f"obf{h}", P.dma("sync",
                                           st["OF"][p, 64 * h:64 * h + 64, qg * 512:(qg + 1) * 512],
                                           obfh, deps=[t4] + P.wdeps("OF")))
                oev2 += [t4, t5]
            st["_oev_par"] = oev2
            oev = oev2
    if st["stop"] == "ph3":
        return
    P.barrier()
    two = P.dma("gpsimd", wv3, wo_d[:, :, :])
    tbo2 = P.dma("gpsimd", st["b2row"][:], din["ab_out"][:]) if var == "a" else None
    prev = []
    for tt in range(T):
        r = slice(tt * 128, tt * 128 + 128)
        tof = P.rd("OF", P.dma("sync", ofm3,
                               st["OF"][:, :, tt * 128:(tt + 1) * 128].rearrange("c p n -> p c n"),
                               deps=prev))
        txl = P.rd(x_name, P.dma("sync", wc[:], x_in[r, :], deps=prev))
        tmm = {}
        for half in range(2):
            bank = PB[half]
            for c in range(8):
                lastmm = (c == 7) and var != "a"
                tmm[half] = P.on("tensor", lambda e, c=c, half=half, bank=bank, lastmm=lastmm: e.matmul(
                    bank[:], ofm3[:, c, :], wv3[:, c, half * 512:(half + 1) * 512],
                    start=(c == 0), stop=lastmm),
                    deps=([tof, two] + prev) if c == 0 else (), tok=lastmm, mode="full")
            if var == "a":
                tmm[half] = P.on("tensor", lambda e, half=half, bank=bank: e.matmul(
                    bank[:], st["ones_row"][0:1, 0:128],
                    st["b2row"][0:1, half * 512:(half + 1) * 512], start=False, stop=True),
                    deps=[tbo2], tok=True, mode="k1")
        prev = []
        for half in range(2):
            tr = P.on("vector", lambda e, half=half: e.tensor_tensor(
                wd[:, half * 512:(half + 1) * 512], PB[half][:],
                wc[:, half * 512:(half + 1) * 512], ALU.add),
                deps=[tmm[half], txl], tok=True)
            prev.append(tr)
        prev.append(P.dma("sync", x_out[r, :], wd[:], deps=prev + P.wdeps(xo_name)))


def rms_norm_pair(P, st, src, dst, ncol, bnc0, rstdb, deps):
    ntok = st["ntok"]; PB = st["PB"]; rr = st["rr"]
    nqg = ntok // 512
    tsq = P.on("scalar", lambda e: e.activation(dst, src, AF.Square), deps=deps, tok=True)
    dmas = []
    prev5 = {0: None, 1: None}
    for h in range(2):
        rows = slice(64 * h, 64 * h + 64)
        for qg in range(nqg):
            tm = P.on("tensor", lambda e, h=h, qg=qg, rows=rows: e.matmul(
                PB[2 + h][0:1, :], st["onec"][rows, :], dst[rows, qg * 512:(qg + 1) * 512],
                start=True, stop=True), deps=[tsq, prev5[h]] + (dmas[-1:] if dmas else []),
                tok=True, mode="rms")
            t5 = P.on("scalar", lambda e, h=h: e.activation(rr[:, 0:512], PB[2 + h][0:1, :],
                                                            AF.Ln, bias=RMSEPS, scale=1.0 / 64),
                      deps=[tm], tok=True)
            t6 = P.on("scalar", lambda e: e.activation(rr[:, 0:512], rr[:, 0:512], AF.Exp,
                                                       scale=-0.5), deps=[t5], tok=True)
            prev5[h] = t6
            dmas.append(P.dma("sync", st["BNC"][bnc0 + h:bnc0 + h + 1, qg * 512:(qg + 1) * 512],
                              rr[0:1, 0:512], deps=[t6] + P.wdeps(f"BNCrms{bnc0 + h}")))
    tb = []
    for h in range(2):
        tb.append(P.rd(f"BNCrms{bnc0 + h}",
                       P.dma("sync", rstdb[64 * h:64 * h + 64, :],
                             st["BNC"][bnc0 + h:bnc0 + h + 1, :].partition_broadcast(64),
                             deps=dmas + P.wdeps("rstdb"))))
    tn = P.rd("rstdb", P.on("vector", lambda e: e.scalar_tensor_tensor(dst, src, ncol[:], rstdb,
                                                                      ALU.mult, ALU.mult),
                            deps=tb + st.get("_rmsdep", []), tok=True))
    return tn


def ph_ffnmod(P, st, x_in, x_name, x_out, xo_name, inv):
    T = st["T"]
    ycur, yname = st["YA"], "YA"
    ynext, ynname = st["YB"], "YB"
    P.barrier()
    prev = []
    for tt in range(T):
        r = slice(tt * 128, tt * 128 + 128)
        tld = P.rd(x_name, P.dma("sync", st["wa"][:], x_in[r, :], deps=prev))
        negmu, rstd, tst = ln_stats(P, st, st["wa"], LNEPS, [tld])
        tn = P.on("vector", lambda e: e.tensor_scalar(st["wb"][:], st["wa"][:], negmu, rstd,
                                                      ALU.add, ALU.mult), deps=tst, tok=True)
        prev = [tn, P.dma("sync", ycur[r, :], st["wb"][:], deps=[tn] + P.wdeps(yname))]
    if st["stop"] == "y0":
        return
    for i in range(st["depth"]):
        ph_lattn(P, st, i, ycur, yname, ynext, ynname)
        if st["stop"] in ("lattnA", "lattnB", "lattnC"):
            return
        ycur, yname, ynext, ynname = ynext, ynname, ycur, yname
        ph_lffn(P, st, i, ycur, yname, ynext, ynname, inv)
        if st["stop"] == "lffn":
            return
        ycur, yname, ynext, ynname = ynext, ynname, ycur, yname
    P.barrier()
    prev = []
    for tt in range(T):
        r = slice(tt * 128, tt * 128 + 128)
        t1 = P.rd(x_name, P.dma("sync", st["wa"][:], x_in[r, :], deps=prev))
        t2 = P.rd(yname, P.dma("sync", st["wb"][:], ycur[r, :], deps=prev))
        t3 = P.on("vector", lambda e: e.tensor_tensor(st["wc"][:], st["wa"][:], st["wb"][:],
                                                      ALU.add), deps=[t1, t2], tok=True)
        prev = [t3, P.dma("sync", x_out[r, :], st["wc"][:], deps=[t3] + P.wdeps(xo_name))]


def ph_lattn(P, st, li, yin, yname, yout, yoname):
    din = st["din"]; T = st["T"]; ntok = st["ntok"]
    AR1, AR2, PB = st["AR1"], st["AR2"], st["PB"]
    wa, wb, wc, wd, we = st["wa"], st["wb"], st["wc"], st["wd"], st["we"]
    wbf, vst, ptb = st["wbf"], st["vst"], st["ptb"]
    sml, rr, negC = st["sml"], st["rr"], st["negC"]
    P.barrier()

    wqkv = bf16v(AR1, 0, 8 * 3 * DM)
    wqkv3 = wqkv.rearrange("p (c n) -> p c n", c=8)
    rotc = 8 * 3 * DM
    CS = {}; SS = {}
    for nm in ("q", "k"):
        CS[nm] = f32v(AR1, rotc // 2, T * DH).rearrange("p (t d) -> p t d", d=DH)
        rotc += 2 * T * DH
        SS[nm] = f32v(AR1, rotc // 2, T * DH).rearrange("p (t d) -> p t d", d=DH)
        rotc += 2 * T * DH
    assert rotc <= 44544
    lnf3 = st["wbf3"].ap().rearrange("p (c n) -> p c n", c=8)
    vst3 = vst.ap().rearrange("p (h n) -> p h n", h=NH)

    tw = P.dma("gpsimd", wqkv3, din["f_wqkv"][li])
    tcs = {}
    for nm in ("q", "k"):
        tcs[nm] = [P.dma("sync", CS[nm], din[f"cs_{nm}"].rearrange("(t p) d -> p t d", p=128)),
                   P.dma("sync", SS[nm], din[f"ss_{nm}"].rearrange("(t p) d -> p t d", p=128))]
    tqsb = bcast_col(P, st, din["f_qs"][li:li + 1, :], sml[:, 16:80])
    tksb = bcast_col(P, st, din["f_ks"][li:li + 1, :], sml[:, 80:144])
    tsw = [P.on("vector", lambda e: e.tensor_copy(we[:, 0:32], sml[:, 48:80]),
                deps=[tqsb], tok=True),
           P.on("vector", lambda e: e.tensor_copy(we[:, 32:64], sml[:, 16:48]),
                deps=[tqsb], tok=True),
           P.on("vector", lambda e: e.tensor_copy(we[:, 64:96], sml[:, 112:144]),
                deps=[tksb], tok=True),
           P.on("vector", lambda e: e.tensor_copy(we[:, 96:128], sml[:, 80:112]),
                deps=[tksb], tok=True)]
    tf = {}
    for j, nm in enumerate(("q", "k")):
        scol = sml[:, 16:80] if nm == "q" else sml[:, 80:144]
        swv = we[:, 0:64] if nm == "q" else we[:, 64:128]
        tf[nm] = [P.on("vector", lambda e, nm=nm, scol=scol: e.tensor_tensor(
            CS[nm], CS[nm], scol.unsqueeze(1).broadcast_to((128, T, DH)), ALU.mult),
            deps=[tcs[nm][0], tqsb, tksb], tok=True),
            P.on("vector", lambda e, nm=nm, swv=swv: e.tensor_tensor(
                SS[nm], SS[nm], swv.unsqueeze(1).broadcast_to((128, T, DH)), ALU.mult),
                deps=[tcs[nm][1]] + tsw, tok=True)]
    t1 = P.on("vector", lambda e: e.tensor_reduce(sml[:, 0:1], sml[:, 16:80], AXX, ALU.max,
                                                  apply_absolute_value=True),
              deps=[tqsb] + tf["q"], tok=True)
    t2 = P.on("vector", lambda e: e.tensor_reduce(sml[:, 1:2], sml[:, 80:144], AXX, ALU.max,
                                                  apply_absolute_value=True),
              deps=[tksb] + tf["k"], tok=True)
    t3m = P.on("vector", lambda e: e.tensor_tensor(sml[:, 2:3], sml[:, 0:1], sml[:, 1:2],
                                                   ALU.mult), deps=[t1, t2], tok=True)
    tnc = P.on("vector", lambda e: e.tensor_scalar(negC[:], sml[:, 2:3], -8.0, None,
                                                   ALU.mult, ALU.bypass), deps=[t3m], tok=True)
    tmem = P.on("gpsimd", lambda e: e.memset(vst3[:, :, 64:65], 1.0), tok=True)

    prev_sp = [tmem]
    prev_rot = {"q": None, "k": None}
    for tt in range(T):
        tls = shifted_load(P, st, yin, yname, tt, wa, deps=[prev_rot["k"]])
        tcst = P.on("scalar", lambda e: e.copy(wbf[:], wa[:]), deps=tls, tok=True)
        tsf = transpose_tile(P, st, wbf, lnf3, 6, [tcst])
        tmv = {}
        for g in range(6):
            bank = PB[g]
            for c in range(8):
                tmv[g] = P.on("tensor", lambda e, g=g, c=c, bank=bank: e.matmul(
                    bank[:], lnf3[:, c, :], wqkv3[:, c, g * 512:(g + 1) * 512],
                    start=(c == 0), stop=(c == 7)),
                    deps=([tsf, tw] + prev_sp) if c == 0 else (), tok=(c == 7), mode="full")
        tev = []
        for half in range(2):
            pv = PB[4 + half].ap().rearrange("p (h n) -> p h n", h=8)
            tev.append(P.on("scalar", lambda e, half=half, pv=pv: e.copy(
                vst3[:, half * 8:(half + 1) * 8, 0:64], pv),
                deps=[tmv[4 + half]] + prev_sp, tok=True))
        tvs = P.dma("sync", st["VSP"][tt], vst[:], deps=tev + P.wdeps("VSP"))
        tqr = rotary(P, st, tt, 0, CS["q"], SS["q"], st["QF"], "wbf4",
                     [tmv[0], tmv[1], prev_rot["q"]])
        tkr = rotary(P, st, tt, 2, CS["k"], SS["k"], st["KF"], "wbf5",
                     [tmv[2], tmv[3], tqr["ev"], prev_rot["k"]])
        prev_rot = {"q": tqr["sp"], "k": tkr["sp"]}
        prev_sp = [tvs, tqr["sp"], tkr["sp"]]

    if st["stop"] == "lattnA":
        return
    P.barrier()
    O1 = f32v(AR1, 0, ntok, rows=65)
    O2 = f32v(AR1, ntok, ntok, rows=65)
    rbig = f32v(AR1, 2 * ntok, ntok, rows=64)
    q2 = bf16v(AR2, 0, ntok)
    k2 = bf16v(AR2, ntok, ntok)
    v2p = bf16v(AR2, 2 * ntok, T * 130)
    v2p3 = v2p.rearrange("p (t n) -> p t n", t=T)
    obig = bf16v(AR2, 2 * ntok + T * 130, ntok, rows=64)
    oev = []
    for p in range(8):
        tqd = P.rd("QF", P.dma("sync", q2, st["QF"][p], deps=oev + P.wdeps("q2")))
        tkd = P.rd("KF", P.dma("sync", k2, st["KF"][p], deps=oev + P.wdeps("k2")))
        tvd = P.rd("VSP", P.dma("sync", v2p3,
                                st["VSP"][:, :, 130 * p:130 * (p + 1)].rearrange("t p n -> p t n"),
                                deps=oev + P.wdeps("v2p")))
        owtok = {}
        evtok = {}
        sbank_rd = [None, None]
        slot_rd = {}
        for t in range(T):
            W = min(384, ntok - 128 * t)
            nw = (W + 127) // 128
            for h in range(2):
                sb_ = PB[h]
                rows = slice(64 * h, 64 * h + 64)
                tS = P.on("tensor", lambda e, t=t, h=h, sb_=sb_, rows=rows, W=W: e.matmul(
                    sb_[:, 0:W], k2[rows, t * 128:(t + 1) * 128],
                    q2[rows, t * 128:t * 128 + W], start=True, stop=True),
                    deps=([tqd, tkd, tvd, tnc] if t == 0 else []) + [sbank_rd[h]],
                    tok=True, mode="r64")
                wsf = (wa, wb)[h][:, (t % 2) * 512:(t % 2) * 512 + 384]
                sbank_rd[h] = P.on("vector", lambda e, sb_=sb_, wsf=wsf, W=W:
                                   e.scalar_tensor_tensor(wsf[:, 0:W], sb_[:, 0:W], 8.0,
                                                          st["maskb"][:, 0:W],
                                                          ALU.mult, ALU.add),
                                   deps=[tS, slot_rd.get((h, t % 2))], tok=True)
                slot = 2 * (t % 2) + h
                pslot = ptb[:, slot * 512:slot * 512 + 384]
                tx = P.on("scalar", lambda e, pslot=pslot, wsf=wsf, W=W: e.activation(
                    pslot[:, 0:W], wsf[:, 0:W], AF.Exp, bias=negC[:], scale=1.0),
                    deps=[sbank_rd[h], slot_rd.get((h, t % 2))], tok=True)
                ws = [w for w in range(t, t + nw) if w < T]
                for j, w in enumerate(ws):
                    bank = PB[2 + 3 * h + (w % 3)]
                    first = (t == max(0, w - 2))
                    last = (t == w)
                    dep = [tx]
                    if first and w >= 3:
                        dep.append(evtok[(h, w - 3)])
                    tk_ = P.on("tensor", lambda e, bank=bank, h=h, t=t, j=j, first=first,
                               last=last, pslot=pslot: e.matmul(
                        bank[0:65, 0:128], v2p3[:, t, 65 * h:65 * h + 65],
                        pslot[:, j * 128:(j + 1) * 128], start=first, stop=last),
                        deps=dep, tok=(last or j == len(ws) - 1), mode="full")
                    if last:
                        owtok[(h, w)] = tk_
                    if j == len(ws) - 1:
                        slot_rd[(h, t % 2)] = tk_
                w = t
                bank = PB[2 + 3 * h + (w % 3)]
                Odst = O1 if h == 0 else O2
                evtok[(h, w)] = P.on("vector", lambda e, bank=bank, w=w, Odst=Odst:
                                     e.tensor_copy(Odst[:, w * 128:(w + 1) * 128],
                                                   bank[0:65, 0:128]),
                                     deps=[owtok[(h, w)]], tok=True)
        oev = []
        for h in range(2):
            Osrc = O1 if h == 0 else O2
            tch = []
            for qg in range(ntok // 512):
                tt1 = P.on("vector", lambda e, Osrc=Osrc, qg=qg: e.reciprocal(
                    rr[:, 0:512], Osrc[64:65, qg * 512:(qg + 1) * 512]),
                    deps=[evtok[(h, w)] for w in range(4 * qg, min(T, 4 * qg + 4))]
                    + P.wdeps("lrr"), tok=True)
                tch.append(P.rd("lrr",
                                P.dma("sync", st["BNC"][h:h + 1, qg * 512:(qg + 1) * 512],
                                      rr[0:1, 0:512], deps=[tt1] + P.wdeps(f"LBNC{h}"))))
            tb = P.rd(f"LBNC{h}", P.dma("sync", rbig,
                                        st["BNC"][h:h + 1, :].partition_broadcast(64),
                                        deps=tch + P.wdeps("rbig")))
            t4 = P.rd("rbig", P.on("vector", lambda e, Osrc=Osrc: e.tensor_tensor(
                obig, Osrc[0:64, :], rbig, ALU.mult),
                deps=[tb] + P.wdeps("obig"), tok=True))
            t5 = P.rd("obig", P.dma("sync", st["OF"][p, 64 * h:64 * h + 64, :], obig,
                                    deps=[t4] + P.wdeps("OF")))
            oev += [t4, t5]

    if st["stop"] == "lattnB":
        return
    P.barrier()
    wo = bf16v(AR2, 0, 8 * DM)
    wo3 = wo.rearrange("p (c n) -> p c n", c=8)
    ofm3 = st["wbf3"].ap().rearrange("p (c n) -> p c n", c=8)
    two = P.dma("gpsimd", wo3, din["f_wo"][li])
    prev = []
    for tt in range(T):
        r = slice(tt * 128, tt * 128 + 128)
        tof = P.rd("OF", P.dma("sync", ofm3,
                               st["OF"][:, :, tt * 128:(tt + 1) * 128].rearrange("c p n -> p c n"),
                               deps=prev))
        txl = P.rd(yname, P.dma("sync", wc[:], yin[r, :], deps=prev))
        tmm = {}
        for half in range(2):
            bank = PB[half]
            for c in range(8):
                tmm[half] = P.on("tensor", lambda e, c=c, half=half, bank=bank: e.matmul(
                    bank[:], ofm3[:, c, :], wo3[:, c, half * 512:(half + 1) * 512],
                    start=(c == 0), stop=(c == 7)),
                    deps=([tof, two] + prev) if c == 0 else (), tok=(c == 7), mode="full")
        prev = []
        for half in range(2):
            tr = P.on("vector", lambda e, half=half: e.tensor_tensor(
                wd[:, half * 512:(half + 1) * 512], PB[half][:],
                wc[:, half * 512:(half + 1) * 512], ALU.add),
                deps=[tmm[half], txl], tok=True)
            prev.append(tr)
        prev.append(P.dma("sync", yout[r, :], wd[:], deps=prev + P.wdeps(yoname)))


def rotary(P, st, tt, bank0, CS3, SS3, dstF, stg_name, deps):
    wa, wc, wd, we = st["wa"], st["wc"], st["wd"], st["we"]
    stat = st["stat"]; PB = st["PB"]
    stg = st[stg_name]
    t1 = P.on("scalar", lambda e: e.copy(wc[:, 0:512], PB[bank0][:]), deps=deps, tok=True)
    t2 = P.on("scalar", lambda e: e.copy(wc[:, 512:1024], PB[bank0 + 1][:]), tok=True)
    t3 = P.on("scalar", lambda e: e.activation(wd[:, 0:512], PB[bank0][:], AF.Square), tok=True)
    t4 = P.on("scalar", lambda e: e.activation(wd[:, 512:1024], PB[bank0 + 1][:], AF.Square),
              tok=True)
    wc3 = wc.ap().rearrange("p (h d) -> p h d", d=DH)
    wd3 = wd.ap().rearrange("p (h d) -> p h d", d=DH)
    t5 = P.on("vector", lambda e: e.tensor_reduce(stat[:, 0:16], wd3, AXX, ALU.add),
              deps=[t3, t4], tok=True)
    t6 = P.on("vector", lambda e: e.tensor_scalar(stat[:, 0:16], stat[:, 0:16], 1.42e-24,
                                                  None, ALU.max, ALU.bypass), deps=[t5], tok=True)
    t7 = P.on("scalar", lambda e: e.activation(stat[:, 0:16], stat[:, 0:16], AF.Ln,
                                               scale=1.0), deps=[t6], tok=True)
    t8 = P.on("scalar", lambda e: e.activation(stat[:, 0:16], stat[:, 0:16], AF.Exp,
                                               scale=-0.5), deps=[t7], tok=True)
    rb = stat[:, 0:16].unsqueeze(2).broadcast_to((128, 16, DH))
    tu = P.on("vector", lambda e: e.tensor_tensor(wd3, wc3, rb, ALU.mult),
              deps=[t1, t2, t8], tok=True)
    wd4 = wd.ap().rearrange("p (h s d2) -> p h s d2", s=2, d2=32)
    wa4 = wa.ap().rearrange("p (h s d2) -> p h s d2", s=2, d2=32)
    g1 = P.on("vector", lambda e: e.tensor_tensor(
        we.ap().rearrange("p (h d) -> p h d", d=DH), wd3,
        CS3[:, tt, :].unsqueeze(1).broadcast_to((128, 16, DH)), ALU.mult),
        deps=[tu], tok=True)
    g2 = P.on("vector", lambda e: e.tensor_tensor(
        wa4[:, :, 0, :], wd4[:, :, 1, :],
        SS3[:, tt, 0:32].unsqueeze(1).broadcast_to((128, 16, 32)), ALU.mult),
        deps=[g1], tok=True)
    g3 = P.on("vector", lambda e: e.tensor_tensor(
        wa4[:, :, 1, :], wd4[:, :, 0, :],
        SS3[:, tt, 32:64].unsqueeze(1).broadcast_to((128, 16, 32)), ALU.mult),
        deps=[g2], tok=True)
    g4 = P.on("vector", lambda e: e.tensor_tensor(st["wbf"][:], we[:], wa[:], ALU.add),
              deps=[g1, g2, g3], tok=True)
    tev = transpose_tile(P, st, st["wbf"], stg.ap().rearrange("p (c n) -> p c n", c=8),
                         7, [g4])
    tsp = P.dma("sync", dstF.rearrange("c p n -> p c n")[:, :, tt * 128:(tt + 1) * 128],
                stg.ap().rearrange("p (c n) -> p c n", c=8), deps=[tev])
    return {"sp": tsp, "mid": g4, "ev": tev}


def ph_lffn(P, st, li, yin, yname, yout, yoname, inv):
    din = st["din"]; T = st["T"]; ntok = st["ntok"]
    AR1, AR2, PB = st["AR1"], st["AR2"], st["PB"]
    wa, wb, wc, wd = st["wa"], st["wb"], st["wc"], st["wd"]
    wbf, gsb = st["wbf"], st["gsb"]
    P.barrier()

    w1 = bf16v(AR1, 0, 8 * W1C)
    w13 = w1.rearrange("p (c n) -> p c n", c=8)
    w2 = bf16v(AR2, 0, 22 * DM)
    w23 = w2.rearrange("p (c n) -> p c n", c=22)
    lngb = f32v(AR2, (22 * DM) // 2, DM)
    mst = bf16v(AR2, 22 * DM + 2 * DM, DI)
    mfst = bf16v(AR2, 22 * DM + 2 * DM + DI, 22 * 128)
    assert 22 * DM + 2 * DM + DI + 22 * 128 <= 30208
    lnf3 = st["wbf3"].ap().rearrange("p (c n) -> p c n", c=8)
    mfst3 = mfst.rearrange("p (c n) -> p c n", c=22)

    tw1 = P.dma("gpsimd", w13, din["f_w1"][li])
    tw2 = P.dma("gpsimd", w23, din["f_w2"][li])
    tb2 = P.dma("gpsimd", st["b2row"][:], din["f_b2"][li:li + 1, :])
    tlg = bcast_col(P, st, din["f_lng"][li:li + 1, :], lngb)
    nch = (W1C + 511) // 512
    if inv == 0:
        lnb_src = din["f_lnb"][li:li + 1, :].rearrange("o (c p) -> p (o c)", p=128)
        tlb = P.on("gpsimd", lambda e: e.dma_start(out=st["lnbc"][:], in_=lnb_src,
                                                   allow_slow_non_contiguous=True),
                   tok=True, dma=True)
        twr_last = None
        twr = []
        for j in range(nch):
            Wj = min(512, W1C - 512 * j)
            tbr = P.dma("sync", st["rr"][0:1, 0:Wj],
                        din["f_b1"][li:li + 1, 512 * j:512 * j + Wj], deps=[twr_last])
            tmj = None
            for c in range(8):
                tmj = P.on("tensor", lambda e, j=j, c=c, Wj=Wj: e.matmul(
                    PB[0][0:1, 0:Wj], st["lnbc"][:, c:c + 1],
                    w13[:, c, 512 * j:512 * j + Wj], start=(c == 0), stop=(c == 7)),
                    deps=([tlb, tw1, twr_last]) if c == 0 else (), tok=(c == 7), mode="m1")
            twr_last = P.on("vector", lambda e, j=j, Wj=Wj: e.tensor_tensor(
                st["wrow"][0:1, 512 * j:512 * j + Wj], PB[0][0:1, 0:Wj],
                st["rr"][0:1, 0:Wj], ALU.add), deps=[tmj, tbr], tok=True)
            twr.append(twr_last)
        twr = [P.dma("sync", st["WRD"][li], st["wrow"][:], deps=twr)]
    else:
        twr = [P.dma("sync", st["wrow"][:], st["WRD"][li])]

    prev = []
    prev_aj = []
    prev_mev = []
    for tt in range(T):
        r = slice(tt * 128, tt * 128 + 128)
        tls = shifted_load(P, st, yin, yname, tt, wa, deps=prev)
        negmu, rstd, tst = ln_stats(P, st, wa, LNEPS, tls)
        t1 = P.on("vector", lambda e: e.tensor_scalar(wb[:], wa[:], negmu, rstd,
                                                      ALU.add, ALU.mult), deps=tst, tok=True)
        t2 = P.on("vector", lambda e: e.scalar_tensor_tensor(wbf[:], wb[:], 1.0, lngb,
                                                             ALU.mult, ALU.mult),
                  deps=[t1, tlg], tok=True)
        tsf = transpose_tile(P, st, wbf, lnf3, 6, [t2])
        ajs = []
        for j in range(6):
            a0 = 512 * j
            aW = min(512, DI - a0)
            tmj = {}
            for (idx, c0_) in ((0, a0), (1, DI + a0)):
                bank = PB[2 * (j % 2) + idx]
                for c in range(8):
                    tmj[idx] = P.on("tensor", lambda e, c=c, c0_=c0_, aW=aW, bank=bank: e.matmul(
                        bank[:, 0:aW], lnf3[:, c, :], w13[:, c, c0_:c0_ + aW],
                        start=(c == 0), stop=False),
                        deps=([tsf, tw1] + twr + prev + ajs[-4:] + prev_aj[-4:])
                        if c == 0 else (), mode="full")
                tmj[idx] = P.on("tensor", lambda e, c0_=c0_, aW=aW, bank=bank: e.matmul(
                    bank[:, 0:aW], st["ones_row"][0:1, 0:128], st["wrow"][0:1, c0_:c0_ + aW],
                    start=False, stop=True), deps=twr, tok=True, mode="k1")
            gb = PB[2 * (j % 2) + 1]
            ab = PB[2 * (j % 2)]
            tg = P.on("scalar", lambda e, gb=gb, aW=aW: e.activation(
                gsb[:, 0:aW], gb[:, 0:aW], AF.Gelu), deps=[tmj[1]], tok=True)
            ajs.append(P.on("vector", lambda e, ab=ab, a0=a0, aW=aW: e.tensor_tensor(
                mst[:, a0:a0 + aW], ab[:, 0:aW], gsb[:, 0:aW], ALU.mult),
                deps=[tmj[0], tg], tok=True))
        prev_aj = ajs
        tmev = []
        pv = PB[7].ap().bitcast(BF16)
        for grp in range(3):
            ch0 = grp * 8
            nc_ = min(8, 22 - ch0)
            tlast = None
            for cc in range(nc_):
                c2 = ch0 + cc
                cw = min(128, DI - c2 * 128)
                tlast = P.on("tensor", lambda e, cc=cc, c2=c2, cw=cw, pv=pv: e.transpose(
                    pv[0:cw, cc * 128:(cc + 1) * 128], mst[:, c2 * 128:c2 * 128 + cw],
                    st["ident"][:]),
                    deps=(ajs + [st["pt_rd"].get(7)]) if cc == 0 else (),
                    tok=(cc == nc_ - 1), mode="tr")
            pvv = pv.rearrange("p (c n) -> p c n", c=8)[:, 0:nc_, :]
            tmev.append(P.on("scalar", lambda e, ch0=ch0, nc_=nc_, pvv=pvv: e.copy(
                mfst3[:, ch0:ch0 + nc_, :], pvv), deps=[tlast], tok=True))
            st["pt_rd"][7] = tmev[-1]
        prev_mev = tmev
        txl = P.rd(yname, P.dma("sync", wc[:], yin[r, :], deps=prev))
        tm2 = {}
        for half in range(2):
            bank = PB[4 + half]
            for c2 in range(22):
                tm2[half] = P.on("tensor", lambda e, c2=c2, half=half, bank=bank: e.matmul(
                    bank[:], mfst[:, c2 * 128:(c2 + 1) * 128],
                    w23[:, c2, half * 512:(half + 1) * 512],
                    start=(c2 == 0), stop=False),
                    deps=(tmev + [tw2] + prev) if c2 == 0 else (), mode="full")
            tm2[half] = P.on("tensor", lambda e, half=half, bank=bank: e.matmul(
                bank[:], st["ones_row"][0:1, 0:128],
                st["b2row"][0:1, half * 512:(half + 1) * 512],
                start=False, stop=True), deps=[tb2], tok=True, mode="k1")
        prev = []
        for half in range(2):
            tr = P.on("vector", lambda e, half=half: e.tensor_tensor(
                wd[:, half * 512:(half + 1) * 512], PB[4 + half][:],
                wc[:, half * 512:(half + 1) * 512], ALU.add),
                deps=[tm2[half], txl], tok=True)
            prev.append(tr)
        prev.append(P.dma("sync", yout[r, :], wd[:], deps=prev + P.wdeps(yoname)))


_CACHE = {}


def kernel(**inputs):
    x = np.asarray(inputs["L0_attn_a_x"])
    B, N, E = x.shape
    key = (N,)
    if key not in _CACHE:
        nc, P = build(ntok=N)
        _CACHE[key] = (nc, host_tables(N))
    nc, tables = _CACHE[key]
    w = prep_weights(inputs)
    base = {}
    base.update(w)
    base.update(tables)
    in_maps = []
    for c in range(B):
        m = dict(base)
        m["x0"] = np.ascontiguousarray(x[c])
        in_maps.append(m)
    res = run_bass_kernel_spmd(nc, in_maps, core_ids=list(range(B)))
    return np.stack([np.asarray(r["out"]) for r in res.results], axis=0)


# version 03
# speedup vs baseline: 2.1232x; 1.3029x over previous
import numpy as np
import ml_dtypes
import concourse.bass as bass
import concourse.mybir as mybir
from concourse.bass_utils import run_bass_kernel_spmd

F32 = mybir.dt.float32
BF16 = mybir.dt.bfloat16
AF = mybir.ActivationFunctionType
ALU = mybir.AluOpType
AXX = mybir.AxisListType.X
BFNP = ml_dtypes.bfloat16

DM = 1024
NH = 16
DH = 64
DEPTH = 6
DI = 2730
W1C = 5460
WIN = 256
NEG = -1.0e38
LNEPS = 1e-5
RMSEPS = float(np.finfo(np.float32).eps)
C_ATTN_A = 30.0

ENGS = ("sync", "tensor", "vector", "scalar", "gpsimd")


def host_tables(ntok):
    t = {}
    t["ident"] = np.eye(128, dtype=BFNP)
    t["ones_row"] = np.ones((1, 512), dtype=BFNP)
    p = np.arange(128)[:, None]
    c = np.arange(384)[None, :]
    t["maskb"] = np.where((c >= p) & (c <= p + WIN), 0.0, NEG).astype(np.float32)
    u = np.arange(2 * ntok - 1)[None, :]
    t["dist"] = np.abs(u - (ntok - 1) - np.arange(128)[:, None]).astype(np.float32)
    half = np.arange(0, DH, 2, dtype=np.float64)
    inv_freq = 1.0 / 10000.0 ** (half / DH)
    tt = np.arange(ntok, dtype=np.float64)
    freqs = tt[:, None] * inv_freq[None, :]
    freqs = np.concatenate([freqs, freqs], -1)
    base_scale = (half + 0.4 * DH) / (1.4 * DH)
    power = (tt - ntok // 2) / (WIN // 2)
    scale = base_scale[None, :] ** power[:, None]
    scale = np.concatenate([scale, scale], -1)
    cos = np.cos(freqs)
    sin = np.sin(freqs)
    sgn = np.where(np.arange(DH) < DH // 2, -1.0, 1.0)[None, :]
    t["cs_q"] = (cos * scale).astype(np.float32)
    t["ss_q"] = (sin * scale * sgn).astype(np.float32)
    t["cs_k"] = (cos / scale).astype(np.float32)
    t["ss_k"] = (sin / scale * sgn).astype(np.float32)
    return t


def prep_weights(inp, depth=DEPTH):
    w = {}
    cont = np.ascontiguousarray

    def pmaj(a):
        E, F = a.shape
        return cont(a.reshape(E // 128, 128, F).transpose(1, 0, 2))

    w["aw_qkv"] = pmaj(np.asarray(inp["L0_attn_a_Wqkv_w"]).T.copy())
    w["ab_qkv"] = cont(np.asarray(inp["L0_attn_a_Wqkv_b"])[None, :])
    w["aw_out"] = pmaj(np.asarray(inp["L0_attn_a_out_w"]).T.copy())
    w["ab_out"] = cont(np.asarray(inp["L0_attn_a_out_b"])[None, :])
    w["a_dd"] = cont(np.asarray(inp["L0_attn_a_distance_decay"])[None, :])
    w["bw_qkv"] = pmaj(np.concatenate(
        [np.asarray(inp["L0_attn_b_Wq"]).T, np.asarray(inp["L0_attn_b_Wk"]).T,
         np.asarray(inp["L0_attn_b_Wv"]).T], axis=1))
    w["bw_out"] = pmaj(np.asarray(inp["L0_attn_b_Wo"]).T.copy())
    w["b_qn"] = cont(np.asarray(inp["L0_attn_b_qn_w"])[None, :])
    w["b_kn"] = cont(np.asarray(inp["L0_attn_b_kn_w"])[None, :])
    w["f_wqkv"] = np.stack([pmaj(np.asarray(inp["L0_ffn_Wqkv"])[i]) for i in range(depth)])
    w["f_wo"] = np.stack([pmaj(np.asarray(inp["L0_ffn_Wo"])[i]) for i in range(depth)])
    w["f_w1"] = np.stack([pmaj(np.asarray(inp["L0_ffn_W1"])[i]) for i in range(depth)])
    w2 = np.asarray(inp["L0_ffn_W2"])
    w2p = np.zeros((depth, 2816, DM), np.float32)
    w2p[:, :DI] = w2[:depth]
    w["f_w2"] = np.stack([pmaj(w2p[i]) for i in range(depth)])
    w["f_qs"] = cont(np.asarray(inp["L0_ffn_q_scale"])[:depth])
    w["f_ks"] = cont(np.asarray(inp["L0_ffn_k_scale"])[:depth])
    w["f_lng"] = cont(np.asarray(inp["L0_ffn_ln_g"])[:depth])
    w["f_lnb"] = cont(np.asarray(inp["L0_ffn_ln_b"])[:depth])
    w["f_b1"] = cont(np.asarray(inp["L0_ffn_b1"])[:depth])
    w["f_b2"] = cont(np.asarray(inp["L0_ffn_b2"])[:depth])
    return w


class Prog:
    DPOOL = {"sync": 8, "gpsimd": 4, "scalar": 4}
    SEMN = (["sync", "tensor", "vector", "scalar", "gpsimd"]
            + [f"d{q}{i}" for q, n in DPOOL.items() for i in range(n)])
    SAFE_SAME_ENGINE = True

    def __init__(self):
        self.q = {e: [] for e in ENGS}
        self.cnt = {}
        self.waited = {e: {} for e in ENGS}
        self.sems = {}
        self.ninst = 0
        self.scr_rd = {}

    def _dma_sem(self, eng):
        if not hasattr(self, "_dma_rr"):
            self._dma_rr = {}
        i = self._dma_rr.get(eng, 0)
        self._dma_rr[eng] = (i + 1) % self.DPOOL[eng]
        return f"d{eng}{i}"

    def wait(self, eng, deps):
        for d in deps:
            if d is None:
                continue
            s, v = d
            if self.waited[eng].get(s, 0) < v:
                self.waited[eng][s] = v

                def mk(s=s, v=v):
                    return lambda e: e.wait_ge(self.sems[s], v)
                self.q[eng].append(mk())
                self.ninst += 1

    def on(self, eng, fn, deps=(), tok=False, dma=False, mode=None):
        if eng == "tensor" and not dma:
            assert mode is not None, "tensor ops must declare a tiling mode"
            if mode != getattr(self, "_pe_mode", None):
                deps = list(deps) + [getattr(self, "_last_pe", None)]
            self._pe_mode = mode
            tok = True
        self.wait(eng, deps)
        self.ninst += 1
        if not hasattr(self, "last_tok"):
            self.last_tok = {}
            self.clean = {}
        if not tok:
            self.q[eng].append(fn)
            self.clean[eng] = False
            return None
        s = self._dma_sem(eng) if dma else eng
        inc = 16 if dma else 1
        if dma:
            self.wait(eng, [(s, self.cnt.get(s, 0))])
        self.cnt[s] = self.cnt.get(s, 0) + inc
        v = self.cnt[s]

        def mk(fn=fn, s=s, inc=inc):
            return lambda e: fn(e).then_inc(self.sems[s], inc)
        self.q[eng].append(mk())
        if not dma and not self.SAFE_SAME_ENGINE:
            self.waited[eng][s] = v
        if eng == "tensor" and not dma:
            self._last_pe = (s, v)
        if not dma:
            self.last_tok[eng] = (s, v)
            self.clean[eng] = True
        else:
            self.last_tok[eng] = (s, v)
            self.clean[eng] = True
        return (s, v)

    def dma(self, eng, out, in_, deps=()):
        return self.on(eng, lambda e, o=out, i=in_: e.dma_start(out=o, in_=i),
                       deps=deps, tok=True, dma=True)

    def rd(self, name, tok):
        self.scr_rd.setdefault(name, []).append(tok)
        return tok

    def wdeps(self, name):
        return self.scr_rd.pop(name, [])

    def barrier(self):
        if not hasattr(self, "last_tok"):
            self.last_tok = {}
            self.clean = {}
        for eng in ENGS:
            if self.q[eng] and not self.clean.get(eng, True):
                self.on(eng, lambda e: e.nop(), tok=True)
        snap = [(s, self.cnt.get(s, 0)) for s in self.SEMN if self.cnt.get(s, 0) > 0]
        for eng in ENGS:
            self.wait(eng, snap)

    def finish(self, blk):
        for name in ENGS:
            ops = self.q[name]

            def mksec(ops=ops):
                def section(e):
                    for f in ops:
                        f(e)
                return section
            getattr(blk, name)(mksec())


def build(ntok=4096, depth=DEPTH, parts=("a", "m1", "b", "m2"), dbg_out=(), stop=None):
    T = ntok // 128
    nc = bass.Bass("TRN2", target_bir_lowering=False)
    P = Prog()
    din = {}

    def dt_in(name, shape, dtype=F32):
        din[name] = nc.dram_tensor(name, list(shape), dtype, kind="ExternalInput").ap()

    dt_in("x0", (ntok, DM))
    dt_in("aw_qkv", (128, 8, 3 * DM)); dt_in("ab_qkv", (1, 3 * DM))
    dt_in("aw_out", (128, 8, DM)); dt_in("ab_out", (1, DM)); dt_in("a_dd", (1, 1))
    dt_in("bw_qkv", (128, 8, 3 * DM)); dt_in("bw_out", (128, 8, DM))
    dt_in("b_qn", (1, DH)); dt_in("b_kn", (1, DH))
    dt_in("f_wqkv", (depth, 128, 8, 3 * DM)); dt_in("f_wo", (depth, 128, 8, DM))
    dt_in("f_w1", (depth, 128, 8, W1C)); dt_in("f_w2", (depth, 128, 22, DM))
    dt_in("f_qs", (depth, DH)); dt_in("f_ks", (depth, DH))
    dt_in("f_lng", (depth, DM)); dt_in("f_lnb", (depth, DM))
    dt_in("f_b1", (depth, W1C)); dt_in("f_b2", (depth, DM))
    dt_in("ident", (128, 128), BF16); dt_in("ones_row", (1, 512), BF16)
    dt_in("maskb", (128, 384)); dt_in("dist", (128, 2 * ntok - 1))
    dt_in("cs_q", (ntok, DH)); dt_in("ss_q", (ntok, DH))
    dt_in("cs_k", (ntok, DH)); dt_in("ss_k", (ntok, DH))
    dout = nc.dram_tensor("out", [ntok, DM], F32, kind="ExternalOutput").ap()

    def dt_sc(name, shape, dtype=F32):
        kind = "ExternalOutput" if name in dbg_out else "Internal"
        return nc.dram_tensor(name, list(shape), dtype, kind=kind).ap()

    st = dict(ntok=ntok, T=T, depth=depth, nc=nc, din=din, dout=dout, stop=stop)
    st["X1"] = dt_sc("X1", (ntok, DM))
    st["X2"] = dt_sc("X2", (ntok, DM))
    st["X3"] = dt_sc("X3", (ntok, DM))
    st["YA"] = dt_sc("YA", (ntok, DM))
    st["YB"] = dt_sc("YB", (ntok, DM))
    st["QF"] = dt_sc("QF", (8, 128, ntok), BF16)
    st["KF"] = dt_sc("KF", (8, 128, ntok), BF16)
    st["VSP"] = dt_sc("VSP", (T, 128, NH * 65), BF16)
    st["OF"] = dt_sc("OF", (8, 128, ntok), BF16)
    st["BNC"] = dt_sc("BNC", (8, ntok))
    st["WRD"] = dt_sc("WRD", (depth, W1C), BF16)

    from contextlib import ExitStack
    ctx = ExitStack()
    sb = lambda name, shape, dtype: ctx.enter_context(nc.sbuf_tensor("s_" + name, list(shape), dtype))
    with ctx:
        st["AR1"] = sb("AR1", [128, 44544], BF16)
        st["AR2"] = sb("AR2", [128, 30208], BF16)
        st["ident"] = sb("ident", [128, 128], BF16)
        st["ones_row"] = sb("ones_row", [1, 512], BF16)
        st["maskb"] = sb("maskb", [128, 384], F32)
        st["wa"] = sb("wa", [128, 1024], F32)
        st["wb"] = sb("wb", [128, 1024], F32)
        st["wc"] = sb("wc", [128, 1024], F32)
        st["wd"] = sb("wd", [128, 1024], F32)
        st["we"] = sb("we", [128, 1024], F32)
        st["wbf"] = sb("wbf", [128, 1024], BF16)
        st["wbf3"] = sb("wbf3", [128, 1024], BF16)
        st["wbf4"] = sb("wbf4", [128, 1024], BF16)
        st["wbf5"] = sb("wbf5", [128, 1024], BF16)
        st["vst"] = sb("vst", [128, NH * 65], BF16)
        st["gsb"] = sb("gsb", [128, 512], F32)
        st["stat"] = sb("stat", [128, 32], F32)
        st["sml"] = sb("sml", [128, 160], F32)
        st["ptb"] = sb("ptb", [128, 4 * 512], BF16)
        st["wrow"] = sb("wrow", [1, W1C], BF16)
        st["b2row"] = sb("b2row", [1, DM], BF16)
        st["absl"] = sb("absl", [1, 3 * DM], BF16)
        st["lnbc"] = sb("lnbc", [128, 8], BF16)
        st["qnc"] = sb("qnc", [128, 1], F32)
        st["knc"] = sb("knc", [128, 1], F32)
        st["negC"] = sb("negC", [128, 1], F32)
        st["onec"] = sb("onec", [128, 1], BF16)
        st["rr"] = sb("rr", [1, 512], F32)
        st["sconst"] = sb("sconst", [128, 4], F32)
        for col, val in enumerate([LNEPS, RMSEPS, -C_ATTN_A]):
            nc.const_aps.register(F32, float(val), st["sconst"].ap()[:, col:col + 1])
        st["PB"] = [ctx.enter_context(nc.psum_tensor(f"pb{i}", [128, 512], F32))
                    for i in range(8)]
        st["pt_rd"] = {}
        for sname in Prog.SEMN:
            P.sems[sname] = ctx.enter_context(nc.semaphore(sname))
        blk = ctx.enter_context(nc.Block())
        emit_program(P, st, parts)
        P.finish(blk)
    return nc, P


def f32v(arena, c0f, ncols, rows=128, p0=0):
    return arena.ap().bitcast(F32)[p0:p0 + rows, c0f:c0f + ncols]


def bf16v(arena, c0, ncols, rows=128, p0=0):
    return arena[p0:p0 + rows, c0:c0 + ncols]


def ln_stats(P, st, xt, eps, deps):
    stat = st["stat"]; we = st["we"]
    t1 = P.on("vector", lambda e: e.tensor_reduce(stat[:, 16:17], xt[:], AXX, ALU.add),
              deps=deps, tok=True)
    t2 = P.on("scalar", lambda e: e.activation(we[:], xt[:], AF.Square), deps=deps, tok=True)
    t3 = P.on("vector", lambda e: e.tensor_reduce(stat[:, 17:18], we[:], AXX, ALU.add),
              deps=[t2], tok=True)
    t4 = P.on("vector", lambda e: e.tensor_scalar(stat[:, 18:19], stat[:, 16:17],
                                                  -1.0 / 1024, None, ALU.mult, ALU.bypass),
              deps=[t1], tok=True)
    t5 = P.on("vector", lambda e: e.tensor_tensor(stat[:, 19:20], stat[:, 18:19],
                                                  stat[:, 18:19], ALU.mult), deps=[t4], tok=True)
    t6 = P.on("vector", lambda e: e.tensor_scalar(stat[:, 20:21], stat[:, 17:18],
                                                  1.0 / 1024, None, ALU.mult, ALU.bypass),
              deps=[t3], tok=True)
    t7 = P.on("vector", lambda e: e.tensor_tensor(stat[:, 21:22], stat[:, 20:21],
                                                  stat[:, 19:20], ALU.subtract),
              deps=[t5, t6], tok=True)
    t8 = P.on("scalar", lambda e: e.activation(stat[:, 22:23], stat[:, 21:22], AF.Ln,
                                               bias=float(eps), scale=1.0), deps=[t7], tok=True)
    t9 = P.on("scalar", lambda e: e.activation(stat[:, 23:24], stat[:, 22:23], AF.Exp,
                                               scale=-0.5), deps=[t8], tok=True)
    return stat[:, 18:19], stat[:, 23:24], [t4, t9]


def transpose_tile(P, st, src_bf, dst_strided, bank_idx, deps, nchunk=8, dst_chunks=None):
    bank = st["PB"][bank_idx]
    pv = bank.ap().bitcast(BF16)
    alldeps = list(deps) + [st["pt_rd"].get(bank_idx)]
    tl = None
    for c in range(nchunk):
        tl = P.on("tensor", lambda e, c=c: e.transpose(
            pv[:, c * 128:(c + 1) * 128], src_bf[:, c * 128:(c + 1) * 128], st["ident"][:]),
            deps=alldeps if c == 0 else (), tok=(c == nchunk - 1), mode="tr")
    pvv = pv.rearrange("p (c n) -> p c n", c=8)[:, 0:nchunk, :]
    tev = P.on("scalar", lambda e: e.copy(dst_strided, pvv), deps=[tl], tok=True)
    st["pt_rd"][bank_idx] = tev
    return tev


def shifted_load(P, st, ysrc, yname, tt, dst, deps=()):
    r0 = tt * 128
    t1 = P.rd(yname, P.dma("sync", dst[:, 0:512], ysrc[r0:r0 + 128, 0:512], deps=deps))
    if tt == 0:
        t2 = P.rd(yname, P.dma("sync", dst[1:128, 512:1024], ysrc[0:127, 512:1024]))
        t3 = P.on("gpsimd", lambda e: e.memset(dst[0:1, 512:1024], 0.0), deps=deps, tok=True)
        return [t1, t2, t3]
    t2 = P.rd(yname, P.dma("sync", dst[:, 512:1024], ysrc[r0 - 1:r0 + 127, 512:1024]))
    return [t1, t2]


def bcast_col(P, st, dram_row_ap, dst, deps=()):
    return P.dma("sync", dst, dram_row_ap.partition_broadcast(dst.shape[0]), deps=deps)


def emit_program(P, st, parts):
    din = st["din"]
    P.dma("sync", st["ident"][:], din["ident"][:])
    P.dma("sync", st["ones_row"][:], din["ones_row"][:])
    P.dma("sync", st["maskb"][:], din["maskb"][:])
    P.on("gpsimd", lambda e: e.memset(st["onec"][:], 1.0), tok=True)
    for col, val in enumerate([LNEPS, RMSEPS, -C_ATTN_A]):
        P.on("vector", lambda e, col=col, val=val: e.memset(st["sconst"][:, col:col + 1],
                                                            float(val)), tok=True)
    P.barrier()

    x_cur, x_name = din["x0"], "x0"
    if "a" in parts:
        ph_dense(P, st, "a", x_cur, x_name, st["X1"], "X1")
        x_cur, x_name = st["X1"], "X1"
    if "m1" in parts:
        ph_ffnmod(P, st, x_cur, x_name, st["X2"], "X2", inv=0)
        x_cur, x_name = st["X2"], "X2"
    if "b" in parts:
        ph_dense(P, st, "b", x_cur, x_name, st["X3"], "X3")
        x_cur, x_name = st["X3"], "X3"
    if "m2" in parts:
        ph_ffnmod(P, st, x_cur, x_name, st["dout"], "out", inv=1)
        x_cur = st["dout"]
    if x_cur is not st["dout"]:
        P.barrier()
        prev = []
        for tt in range(st["T"]):
            r = slice(tt * 128, tt * 128 + 128)
            t = P.dma("sync", st["wc"][:], x_cur[r, :], deps=prev)
            tc = P.on("vector", lambda e: e.tensor_copy(st["wd"][:], st["wc"][:]),
                      deps=[t], tok=True)
            prev = [tc, P.dma("sync", st["dout"][r, :], st["wd"][:], deps=[tc])]
    P.wait("tensor", [(s, P.cnt.get(s, 0)) for s in Prog.SEMN if s.startswith("d")])
    P.on("tensor", lambda e: e.nop(), mode="nop")


def ph_dense(P, st, var, x_in, x_name, x_out, xo_name):
    din = st["din"]; T = st["T"]; ntok = st["ntok"]
    AR1, AR2, PB = st["AR1"], st["AR2"], st["PB"]
    wa, wb, wc, wd = st["wa"], st["wb"], st["wc"], st["wd"]
    wbf, vst, ptb = st["wbf"], st["vst"], st["ptb"]
    negC, sml, rr = st["negC"], st["sml"], st["rr"]
    P.barrier()

    lnfm = bf16v(AR1, 0, 8 * ntok)
    lnfm3 = lnfm.rearrange("p (c n) -> p c n", c=8)
    wv = bf16v(AR1, 8 * ntok, 8 * DM)
    wv3 = wv.rearrange("p (c n) -> p c n", c=8)
    wslq = bf16v(AR1, 8 * ntok + 8 * DM, 8 * 128)
    wslk = bf16v(AR1, 8 * ntok + 8 * DM + 8 * 128, 8 * 128)
    wslq3 = wslq.rearrange("p (c n) -> p c n", c=8)
    wslk3 = wslk.rearrange("p (c n) -> p c n", c=8)
    ofm = bf16v(AR1, 8 * ntok + 8 * DM + 16 * 128, 8 * 128)
    ofm3 = ofm.rearrange("p (c n) -> p c n", c=8)
    if var == "a":
        dec = f32v(AR2, 0, 2 * ntok - 1)
        c0 = 2 * (2 * ntok - 1)
        rstdb = None
    else:
        dec = None
        rstdb = f32v(AR2, 0, ntok)
        c0 = 2 * ntok
    q2 = bf16v(AR2, c0, ntok); c0 += ntok
    k2 = bf16v(AR2, c0, ntok); c0 += ntok
    if var == "b":
        tmpn = bf16v(AR2, c0, ntok); c0 += ntok
    else:
        tmpn = None
    v2p = bf16v(AR2, c0, T * 130); c0 += T * 130
    v2p3 = v2p.rearrange("p (t n) -> p t n", t=T)
    rbq = st["wbf3"].ap().bitcast(F32)[0:64, 0:512]
    obf = st["wbf"][0:64, 0:512]
    assert c0 <= 30208, c0

    wq_d = din["aw_qkv" if var == "a" else "bw_qkv"]
    wo_d = din["aw_out" if var == "a" else "bw_out"]

    tbias = None
    if var == "a":
        tdd = bcast_col(P, st, din["a_dd"][0:1, :], sml[:, 8:9])
        tng = P.on("vector", lambda e: e.tensor_scalar(sml[:, 9:10], sml[:, 8:9], -1.0, None,
                                                       ALU.mult, ALU.bypass), deps=[tdd], tok=True)
        tna = P.on("vector", lambda e: e.tensor_tensor(sml[:, 10:11], sml[:, 8:9],
                                                       sml[:, 9:10], ALU.min), deps=[tng], tok=True)
        tdist = P.dma("sync", dec, din["dist"][:])
        tdec = P.on("scalar", lambda e: e.activation(dec, dec, AF.Exp, scale=sml[:, 10:11]),
                    deps=[tna, tdist], tok=True)
        tbias = P.dma("gpsimd", st["absl"][:], din["ab_qkv"][:])
        tbo = P.dma("gpsimd", st["b2row"][:], din["ab_out"][:])
        tnc = None
    else:
        tq1 = P.dma("sync", st["qnc"][0:64, :],
                    din["b_qn"].rearrange("o (p i) -> (o p) i", i=1))
        tq2 = P.dma("sync", st["qnc"][64:128, :],
                    din["b_qn"].rearrange("o (p i) -> (o p) i", i=1))
        tk1 = P.dma("sync", st["knc"][0:64, :],
                    din["b_kn"].rearrange("o (p i) -> (o p) i", i=1))
        tk2 = P.dma("sync", st["knc"][64:128, :],
                    din["b_kn"].rearrange("o (p i) -> (o p) i", i=1))
        tqb = bcast_col(P, st, din["b_qn"][0:1, :], sml[:, 16:80])
        tkb = bcast_col(P, st, din["b_kn"][0:1, :], sml[:, 80:144])
        t1 = P.on("vector", lambda e: e.tensor_reduce(sml[:, 0:1], sml[:, 16:80], AXX, ALU.max,
                                                      apply_absolute_value=True),
                  deps=[tqb], tok=True)
        t2 = P.on("vector", lambda e: e.tensor_reduce(sml[:, 1:2], sml[:, 80:144], AXX, ALU.max,
                                                      apply_absolute_value=True),
                  deps=[tkb], tok=True)
        t3m = P.on("vector", lambda e: e.tensor_tensor(sml[:, 2:3], sml[:, 0:1], sml[:, 1:2],
                                                       ALU.mult), deps=[t1, t2], tok=True)
        tnc = P.on("vector", lambda e: e.tensor_scalar(negC[:], sml[:, 2:3], -8.0, None,
                                                       ALU.mult, ALU.bypass), deps=[t3m], tok=True)
        tdec = None
        st["_rmsdep"] = [tq1, tq2, tk1, tk2]

    prev_tn = None
    prev_ev = None
    for tt in range(T):
        r = slice(tt * 128, tt * 128 + 128)
        tld = P.rd(x_name, P.dma("sync", wa[:], x_in[r, :], deps=[prev_tn]))
        negmu, rstd, tst = ln_stats(P, st, wa, LNEPS, [tld])
        prev_tn = P.on("vector", lambda e: e.tensor_scalar(wbf[:], wa[:], negmu, rstd,
                                                           ALU.add, ALU.mult),
                       deps=tst + [prev_ev], tok=True)
        prev_ev = transpose_tile(P, st, wbf, lnfm3[:, :, tt * 128:(tt + 1) * 128],
                                 6, [prev_tn])

    if st["stop"] == "ph1":
        return
    twv = P.dma("gpsimd", wv3, wq_d[:, :, 2 * DM:3 * DM])
    vst3 = vst.ap().rearrange("p (h n) -> p h n", h=NH)
    tmem = P.on("gpsimd", lambda e: e.memset(vst3[:, :, 64:65], 1.0), tok=True)
    prev_sp = None
    prev_ev = [tmem, prev_ev]
    for tt in range(T):
        tm = {}
        for half in range(2):
            bank = PB[half]
            nmm = 8
            for c in range(8):
                lastmm = (c == 7) and var != "a"
                tm[half] = P.on("tensor", lambda e, c=c, half=half, bank=bank, tt=tt, lastmm=lastmm: e.matmul(
                    bank[:], lnfm3[:, c, tt * 128:(tt + 1) * 128],
                    wv3[:, c, half * 512:(half + 1) * 512],
                    start=(c == 0), stop=lastmm),
                    deps=([twv] + prev_ev) if c == 0 else (), tok=lastmm, mode="full")
            if var == "a":
                tm[half] = P.on("tensor", lambda e, half=half, bank=bank: e.matmul(
                    bank[:], st["ones_row"][0:1, 0:128],
                    st["absl"][0:1, 2 * DM + half * 512:2 * DM + (half + 1) * 512],
                    start=False, stop=True),
                    deps=[tbias], tok=True, mode="k1")
        prev_ev = []
        for half in range(2):
            pv = PB[half].ap().rearrange("p (h n) -> p h n", h=8)
            prev_ev.append(P.on("scalar", lambda e, half=half, pv=pv: e.copy(
                vst3[:, half * 8:(half + 1) * 8, 0:64], pv),
                deps=[tm[half], prev_sp], tok=True))
        prev_sp = P.dma("sync", st["VSP"][tt], vst[:], deps=prev_ev + P.wdeps("VSP"))

    if st["stop"] == "ph2":
        return
    nqg = ntok // 512
    oev = [prev_sp]
    for p in list(range(8))[st.get('_ponly', 0):]:
        tq = P.dma("gpsimd", wslq3, wq_d[:, :, 128 * p:128 * (p + 1)], deps=oev)
        tk = P.dma("gpsimd", wslk3, wq_d[:, :, DM + 128 * p:DM + 128 * (p + 1)], deps=oev)
        tev_q = list(oev)
        for (wsl3, tw_, dst, bofs) in ((wslq3, tq, q2, 128 * p), (wslk3, tk, k2, DM + 128 * p)):
            new_ev = []
            for qg in range(nqg):
                bank = PB[qg % 2]
                tmq = None
                for c in range(8):
                    lastmm = (c == 7) and var != "a"
                    tmq = P.on("tensor", lambda e, c=c, qg=qg, bank=bank, wsl3=wsl3, lastmm=lastmm: e.matmul(
                        bank[:], wsl3[:, c, :], lnfm3[:, c, qg * 512:(qg + 1) * 512],
                        start=(c == 0), stop=lastmm),
                        deps=([tw_] + tev_q) if c == 0 else (), tok=lastmm, mode="full")
                if var == "a":
                    tmq = P.on("tensor", lambda e, qg=qg, bank=bank, bofs=bofs: e.matmul(
                        bank[:], st["absl"][0:1, bofs:bofs + 128],
                        st["ones_row"][0:1, 0:512], start=False, stop=True),
                        deps=[tbias], tok=True, mode="k1")
                new_ev.append(P.on("scalar", lambda e, qg=qg, bank=bank, dst=dst: e.copy(
                    dst[:, qg * 512:(qg + 1) * 512], bank[:]), deps=[tmq], tok=True))
            tev_q = new_ev
        qS, kS = q2, k2
        if st["stop"] == "proj" and p == 0:
            return
        if var == "b":
            t_rms1 = rms_norm_pair(P, st, q2, tmpn, st["qnc"], 2, rstdb, tev_q)
            t_rms2 = rms_norm_pair(P, st, k2, q2, st["knc"], 4, rstdb, [t_rms1])
            qS, kS = tmpn, q2
            tev_q = [t_rms2]
        tv = P.dma("sync", v2p3,
                   st["VSP"][:, :, 130 * p:130 * (p + 1)].rearrange("t p n -> p t n"),
                   deps=tev_q + P.wdeps("v2p"))
        P.rd("VSP", tv)
        slot_rd = {}
        sbank_rd = {}
        for qg in range(nqg):
            ob = (PB[4], PB[5]) if qg % 2 == 0 else (PB[6], PB[7])
            tdone = [None, None]
            txp = {}
            for t in range(T + 1):
                if t < T:
                    for h in range(2):
                        sb_ = PB[2 * h + t % 2]
                        rows = slice(64 * h, 64 * h + 64)
                        tS = P.on("tensor", lambda e, t=t, h=h, qg=qg, sb_=sb_, rows=rows:
                                  e.matmul(sb_[:], kS[rows, t * 128:(t + 1) * 128],
                                           qS[rows, qg * 512:(qg + 1) * 512],
                                           start=True, stop=True),
                                  deps=([tv, tnc, tdec] + tev_q + st.get("_oev_par", [])
                                        if (t < 2 and qg < 2) else [])
                                  + [sbank_rd.get((h, t % 2))],
                                  tok=True, mode="r64")
                        sbank_rd[("S", h, t % 2)] = tS
                if t >= 1:
                    for h in range(2):
                        tdone[h] = P.on("tensor", lambda e, t=t, h=h, ob=ob:
                                        e.matmul(ob[h][0:65, :],
                                                 v2p3[:, t - 1, 65 * h:65 * h + 65],
                                                 ptb[:, (2 * ((t - 1) % 2) + h) * 512:
                                                     (2 * ((t - 1) % 2) + h + 1) * 512],
                                                 start=(t - 1 == 0), stop=(t - 1 == T - 1)),
                                        deps=[txp[(h, (t - 1) % 2)]]
                                        + (st.get("_oev_par", []) if t == 1 else []),
                                        tok=True, mode="full")
                        slot_rd[2 * ((t - 1) % 2) + h] = tdone[h]
                if t < T:
                    for h in range(2):
                        sb_ = PB[2 * h + t % 2]
                        tS = sbank_rd[("S", h, t % 2)]
                        slot = 2 * (t % 2) + h
                        pslot = ptb[:, slot * 512:(slot + 1) * 512]
                        if var == "a":
                            u0 = (ntok - 1) - 128 * t + 512 * qg
                            wsf = (wa, wb)[h][:, (t % 2) * 512:(t % 2) * 512 + 512]
                            trd = P.on("vector", lambda e, sb_=sb_, wsf=wsf, u0=u0:
                                       e.scalar_tensor_tensor(wsf, sb_[:], 0.125,
                                                              dec[:, u0:u0 + 512],
                                                              ALU.mult, ALU.mult),
                                       deps=[tS, slot_rd.get(slot)], tok=True)
                            sbank_rd[(h, t % 2)] = trd
                            tx = P.on("scalar", lambda e, pslot=pslot, wsf=wsf: e.activation(
                                pslot, wsf, AF.Exp, bias=-C_ATTN_A, scale=1.0),
                                deps=[trd], tok=True)
                        else:
                            tx = P.on("scalar", lambda e, pslot=pslot, sb_=sb_: e.activation(
                                pslot, sb_[:], AF.Exp, bias=negC[:], scale=0.125),
                                deps=[tS, slot_rd.get(slot)], tok=True)
                            sbank_rd[(h, t % 2)] = tx
                        txp[(h, t % 2)] = tx
            oev2 = []
            for h in range(2):
                rrh = rr[0:1, 0:512] if h == 0 else st["wc"].ap().bitcast(F32)[0:1, 0:512]
                rbqh = rbq if h == 0 else st["wd"].ap().bitcast(F32)[0:64, 0:512]
                obfh = obf if h == 0 else st["wbf4"][0:64, 0:512]
                t1 = P.on("vector", lambda e, ob=ob, h=h, rrh=rrh: e.reciprocal(
                    rrh, ob[h][64:65, :]), deps=[tdone[h]] + P.wdeps(f"rr{h}"), tok=True)
                t2 = P.rd(f"rr{h}", P.dma("sync", st["BNC"][h:h + 1, 0:512], rrh,
                                          deps=[t1] + P.wdeps(f"BNC{h}")))
                t3 = P.rd(f"BNC{h}", P.dma("sync", rbqh,
                                           st["BNC"][h:h + 1, 0:512].partition_broadcast(64),
                                           deps=[t2] + P.wdeps(f"rbq{h}")))
                t4 = P.rd(f"rbq{h}", P.on("vector", lambda e, ob=ob, h=h, obfh=obfh, rbqh=rbqh:
                                          e.tensor_tensor(obfh, ob[h][0:64, :], rbqh, ALU.mult),
                                          deps=[t3] + P.wdeps(f"obf{h}"), tok=True))
                t5 = P.rd(f"obf{h}", P.dma("sync",
                                           st["OF"][p, 64 * h:64 * h + 64, qg * 512:(qg + 1) * 512],
                                           obfh, deps=[t4] + P.wdeps("OF")))
                oev2 += [t4, t5]
            st["_oev_par"] = oev2
            oev = oev2
    if st["stop"] == "ph3":
        return
    P.barrier()
    two = P.dma("gpsimd", wv3, wo_d[:, :, :])
    tbo2 = P.dma("gpsimd", st["b2row"][:], din["ab_out"][:]) if var == "a" else None
    prev = []
    for tt in range(T):
        r = slice(tt * 128, tt * 128 + 128)
        tof = P.rd("OF", P.dma("sync", ofm3,
                               st["OF"][:, :, tt * 128:(tt + 1) * 128].rearrange("c p n -> p c n"),
                               deps=prev))
        txl = P.rd(x_name, P.dma("sync", wc[:], x_in[r, :], deps=prev))
        tmm = {}
        for half in range(2):
            bank = PB[half]
            for c in range(8):
                lastmm = (c == 7) and var != "a"
                tmm[half] = P.on("tensor", lambda e, c=c, half=half, bank=bank, lastmm=lastmm: e.matmul(
                    bank[:], ofm3[:, c, :], wv3[:, c, half * 512:(half + 1) * 512],
                    start=(c == 0), stop=lastmm),
                    deps=([tof, two] + prev) if c == 0 else (), tok=lastmm, mode="full")
            if var == "a":
                tmm[half] = P.on("tensor", lambda e, half=half, bank=bank: e.matmul(
                    bank[:], st["ones_row"][0:1, 0:128],
                    st["b2row"][0:1, half * 512:(half + 1) * 512], start=False, stop=True),
                    deps=[tbo2], tok=True, mode="k1")
        prev = []
        for half in range(2):
            tr = P.on("vector", lambda e, half=half: e.tensor_tensor(
                wd[:, half * 512:(half + 1) * 512], PB[half][:],
                wc[:, half * 512:(half + 1) * 512], ALU.add),
                deps=[tmm[half], txl], tok=True)
            prev.append(tr)
        prev.append(P.dma("sync", x_out[r, :], wd[:], deps=prev + P.wdeps(xo_name)))


def rms_norm_pair(P, st, src, dst, ncol, bnc0, rstdb, deps):
    ntok = st["ntok"]; PB = st["PB"]; rr = st["rr"]
    nqg = ntok // 512
    tsq = P.on("scalar", lambda e: e.activation(dst, src, AF.Square), deps=deps, tok=True)
    dmas = []
    prev5 = {0: None, 1: None}
    for h in range(2):
        rows = slice(64 * h, 64 * h + 64)
        for qg in range(nqg):
            tm = P.on("tensor", lambda e, h=h, qg=qg, rows=rows: e.matmul(
                PB[2 + h][0:1, :], st["onec"][rows, :], dst[rows, qg * 512:(qg + 1) * 512],
                start=True, stop=True), deps=[tsq, prev5[h]] + (dmas[-1:] if dmas else []),
                tok=True, mode="rms")
            t5 = P.on("scalar", lambda e, h=h: e.activation(rr[:, 0:512], PB[2 + h][0:1, :],
                                                            AF.Ln, bias=RMSEPS, scale=1.0 / 64),
                      deps=[tm], tok=True)
            t6 = P.on("scalar", lambda e: e.activation(rr[:, 0:512], rr[:, 0:512], AF.Exp,
                                                       scale=-0.5), deps=[t5], tok=True)
            prev5[h] = t6
            dmas.append(P.dma("sync", st["BNC"][bnc0 + h:bnc0 + h + 1, qg * 512:(qg + 1) * 512],
                              rr[0:1, 0:512], deps=[t6] + P.wdeps(f"BNCrms{bnc0 + h}")))
    tb = []
    for h in range(2):
        tb.append(P.rd(f"BNCrms{bnc0 + h}",
                       P.dma("sync", rstdb[64 * h:64 * h + 64, :],
                             st["BNC"][bnc0 + h:bnc0 + h + 1, :].partition_broadcast(64),
                             deps=dmas + P.wdeps("rstdb"))))
    tn = P.rd("rstdb", P.on("vector", lambda e: e.scalar_tensor_tensor(dst, src, ncol[:], rstdb,
                                                                      ALU.mult, ALU.mult),
                            deps=tb + st.get("_rmsdep", []), tok=True))
    return tn


def ph_ffnmod(P, st, x_in, x_name, x_out, xo_name, inv):
    T = st["T"]
    ycur, yname = st["YA"], "YA"
    ynext, ynname = st["YB"], "YB"
    P.barrier()
    prev = []
    for tt in range(T):
        r = slice(tt * 128, tt * 128 + 128)
        tld = P.rd(x_name, P.dma("sync", st["wa"][:], x_in[r, :], deps=prev))
        negmu, rstd, tst = ln_stats(P, st, st["wa"], LNEPS, [tld])
        tn = P.on("vector", lambda e: e.tensor_scalar(st["wb"][:], st["wa"][:], negmu, rstd,
                                                      ALU.add, ALU.mult), deps=tst, tok=True)
        prev = [tn, P.dma("sync", ycur[r, :], st["wb"][:], deps=[tn] + P.wdeps(yname))]
    if st["stop"] == "y0":
        return
    for i in range(st["depth"]):
        ph_lattn(P, st, i, ycur, yname, ynext, ynname)
        if st["stop"] in ("lattnA", "lattnB", "lattnC"):
            return
        ycur, yname, ynext, ynname = ynext, ynname, ycur, yname
        ph_lffn(P, st, i, ycur, yname, ynext, ynname, inv)
        if st["stop"] == "lffn":
            return
        ycur, yname, ynext, ynname = ynext, ynname, ycur, yname
    P.barrier()
    prev = []
    for tt in range(T):
        r = slice(tt * 128, tt * 128 + 128)
        t1 = P.rd(x_name, P.dma("sync", st["wa"][:], x_in[r, :], deps=prev))
        t2 = P.rd(yname, P.dma("sync", st["wb"][:], ycur[r, :], deps=prev))
        t3 = P.on("vector", lambda e: e.tensor_tensor(st["wc"][:], st["wa"][:], st["wb"][:],
                                                      ALU.add), deps=[t1, t2], tok=True)
        prev = [t3, P.dma("sync", x_out[r, :], st["wc"][:], deps=[t3] + P.wdeps(xo_name))]


def ph_lattn(P, st, li, yin, yname, yout, yoname):
    din = st["din"]; T = st["T"]; ntok = st["ntok"]
    AR1, AR2, PB = st["AR1"], st["AR2"], st["PB"]
    wa, wb, wc, wd, we = st["wa"], st["wb"], st["wc"], st["wd"], st["we"]
    wbf, vst, ptb = st["wbf"], st["vst"], st["ptb"]
    sml, rr, negC = st["sml"], st["rr"], st["negC"]
    P.barrier()

    wqkv = bf16v(AR1, 0, 8 * 3 * DM)
    wqkv3 = wqkv.rearrange("p (c n) -> p c n", c=8)
    rotc = 8 * 3 * DM
    CS = {}; SS = {}
    for nm in ("q", "k"):
        CS[nm] = f32v(AR1, rotc // 2, T * DH).rearrange("p (t d) -> p t d", d=DH)
        rotc += 2 * T * DH
        SS[nm] = f32v(AR1, rotc // 2, T * DH).rearrange("p (t d) -> p t d", d=DH)
        rotc += 2 * T * DH
    assert rotc <= 44544
    lnf3 = st["wbf3"].ap().rearrange("p (c n) -> p c n", c=8)
    vst3 = vst.ap().rearrange("p (h n) -> p h n", h=NH)

    tw = P.dma("gpsimd", wqkv3, din["f_wqkv"][li])
    tcs = {}
    for nm in ("q", "k"):
        tcs[nm] = [P.dma("sync", CS[nm], din[f"cs_{nm}"].rearrange("(t p) d -> p t d", p=128)),
                   P.dma("sync", SS[nm], din[f"ss_{nm}"].rearrange("(t p) d -> p t d", p=128))]
    tqsb = bcast_col(P, st, din["f_qs"][li:li + 1, :], sml[:, 16:80])
    tksb = bcast_col(P, st, din["f_ks"][li:li + 1, :], sml[:, 80:144])
    tsw = [P.on("vector", lambda e: e.tensor_copy(we[:, 0:32], sml[:, 48:80]),
                deps=[tqsb], tok=True),
           P.on("vector", lambda e: e.tensor_copy(we[:, 32:64], sml[:, 16:48]),
                deps=[tqsb], tok=True),
           P.on("vector", lambda e: e.tensor_copy(we[:, 64:96], sml[:, 112:144]),
                deps=[tksb], tok=True),
           P.on("vector", lambda e: e.tensor_copy(we[:, 96:128], sml[:, 80:112]),
                deps=[tksb], tok=True)]
    tf = {}
    for j, nm in enumerate(("q", "k")):
        scol = sml[:, 16:80] if nm == "q" else sml[:, 80:144]
        swv = we[:, 0:64] if nm == "q" else we[:, 64:128]
        tf[nm] = [P.on("vector", lambda e, nm=nm, scol=scol: e.tensor_tensor(
            CS[nm], CS[nm], scol.unsqueeze(1).broadcast_to((128, T, DH)), ALU.mult),
            deps=[tcs[nm][0], tqsb, tksb], tok=True),
            P.on("vector", lambda e, nm=nm, swv=swv: e.tensor_tensor(
                SS[nm], SS[nm], swv.unsqueeze(1).broadcast_to((128, T, DH)), ALU.mult),
                deps=[tcs[nm][1]] + tsw, tok=True)]
    t1 = P.on("vector", lambda e: e.tensor_reduce(sml[:, 0:1], sml[:, 16:80], AXX, ALU.max,
                                                  apply_absolute_value=True),
              deps=[tqsb] + tf["q"], tok=True)
    t2 = P.on("vector", lambda e: e.tensor_reduce(sml[:, 1:2], sml[:, 80:144], AXX, ALU.max,
                                                  apply_absolute_value=True),
              deps=[tksb] + tf["k"], tok=True)
    t3m = P.on("vector", lambda e: e.tensor_tensor(sml[:, 2:3], sml[:, 0:1], sml[:, 1:2],
                                                   ALU.mult), deps=[t1, t2], tok=True)
    tnc = P.on("vector", lambda e: e.tensor_scalar(negC[:], sml[:, 2:3], -8.0, None,
                                                   ALU.mult, ALU.bypass), deps=[t3m], tok=True)
    tmem = P.on("gpsimd", lambda e: e.memset(vst3[:, :, 64:65], 1.0), tok=True)

    prev_sp = [tmem]
    prev_rot = {"q": None, "k": None}
    for tt in range(T):
        tls = shifted_load(P, st, yin, yname, tt, wa, deps=[prev_rot["k"]])
        tcst = P.on("scalar", lambda e: e.copy(wbf[:], wa[:]), deps=tls, tok=True)
        tsf = transpose_tile(P, st, wbf, lnf3, 6, [tcst])
        tmv = {}
        for g in range(6):
            bank = PB[g]
            for c in range(8):
                tmv[g] = P.on("tensor", lambda e, g=g, c=c, bank=bank: e.matmul(
                    bank[:], lnf3[:, c, :], wqkv3[:, c, g * 512:(g + 1) * 512],
                    start=(c == 0), stop=(c == 7)),
                    deps=([tsf, tw] + prev_sp) if c == 0 else (), tok=(c == 7), mode="full")
        tev = []
        for half in range(2):
            pv = PB[4 + half].ap().rearrange("p (h n) -> p h n", h=8)
            tev.append(P.on("scalar", lambda e, half=half, pv=pv: e.copy(
                vst3[:, half * 8:(half + 1) * 8, 0:64], pv),
                deps=[tmv[4 + half]] + prev_sp, tok=True))
        tvs = P.dma("sync", st["VSP"][tt], vst[:], deps=tev + P.wdeps("VSP"))
        tqr = rotary(P, st, tt, 0, CS["q"], SS["q"], st["QF"], "wbf4",
                     [tmv[0], tmv[1], prev_rot["q"]])
        tkr = rotary(P, st, tt, 2, CS["k"], SS["k"], st["KF"], "wbf5",
                     [tmv[2], tmv[3], tqr["ev"], prev_rot["k"]])
        prev_rot = {"q": tqr["sp"], "k": tkr["sp"]}
        prev_sp = [tvs, tqr["sp"], tkr["sp"]]

    if st["stop"] == "lattnA":
        return
    P.barrier()
    O1 = f32v(AR1, 0, ntok, rows=65)
    O2 = f32v(AR1, ntok, ntok, rows=65)
    rbig = f32v(AR1, 2 * ntok, ntok, rows=64)
    q2 = bf16v(AR2, 0, ntok)
    k2 = bf16v(AR2, ntok, ntok)
    v2p = bf16v(AR2, 2 * ntok, T * 130)
    v2p3 = v2p.rearrange("p (t n) -> p t n", t=T)
    obig = bf16v(AR2, 2 * ntok + T * 130, ntok, rows=64)
    oev = []
    for p in range(8):
        tqd = P.rd("QF", P.dma("sync", q2, st["QF"][p], deps=oev + P.wdeps("q2")))
        tkd = P.rd("KF", P.dma("sync", k2, st["KF"][p], deps=oev + P.wdeps("k2")))
        tvd = P.rd("VSP", P.dma("sync", v2p3,
                                st["VSP"][:, :, 130 * p:130 * (p + 1)].rearrange("t p n -> p t n"),
                                deps=oev + P.wdeps("v2p")))
        owtok = {}
        evtok = {}
        sbank_rd = {0: None, 1: None}
        slot_rd = {}
        txp = {}
        for t in range(T + 1):
            if t < T:
                W = min(384, ntok - 128 * t)
                for h in range(2):
                    sb_ = PB[h]
                    rows = slice(64 * h, 64 * h + 64)
                    sbank_rd[("S", h)] = P.on(
                        "tensor", lambda e, t=t, h=h, sb_=sb_, rows=rows, W=W: e.matmul(
                            sb_[:, 0:W], k2[rows, t * 128:(t + 1) * 128],
                            q2[rows, t * 128:t * 128 + W], start=True, stop=True),
                        deps=([tqd, tkd, tvd, tnc] if t == 0 else []) + [sbank_rd[h]],
                        tok=True, mode="r64")
            if t >= 1:
                tp = t - 1
                Wp = min(384, ntok - 128 * tp)
                nwp = (Wp + 127) // 128
                for h in range(2):
                    pslot = ptb[:, (2 * (tp % 2) + h) * 512:(2 * (tp % 2) + h) * 512 + 384]
                    ws = [w for w in range(tp, tp + nwp) if w < T]
                    for j, w in enumerate(ws):
                        bank = PB[2 + 3 * h + (w % 3)]
                        first = (tp == max(0, w - 2))
                        last = (tp == w)
                        dep = [txp[(h, tp % 2)]]
                        if first and w >= 3:
                            dep.append(evtok[(h, w - 3)])
                        tk_ = P.on("tensor", lambda e, bank=bank, h=h, tp=tp, j=j, first=first,
                                   last=last, pslot=pslot: e.matmul(
                            bank[0:65, 0:128], v2p3[:, tp, 65 * h:65 * h + 65],
                            pslot[:, j * 128:(j + 1) * 128], start=first, stop=last),
                            deps=dep, tok=(last or j == len(ws) - 1), mode="full")
                        if last:
                            owtok[(h, w)] = tk_
                        if j == len(ws) - 1:
                            slot_rd[(h, tp % 2)] = tk_
                    w = tp
                    bank = PB[2 + 3 * h + (w % 3)]
                    Odst = O1 if h == 0 else O2
                    evtok[(h, w)] = P.on("vector", lambda e, bank=bank, w=w, Odst=Odst:
                                         e.tensor_copy(Odst[:, w * 128:(w + 1) * 128],
                                                       bank[0:65, 0:128]),
                                         deps=[owtok[(h, w)]], tok=True)
            if t < T:
                W = min(384, ntok - 128 * t)
                for h in range(2):
                    sb_ = PB[h]
                    tS = sbank_rd[("S", h)]
                    wsf = (wa, wb)[h][:, (t % 2) * 512:(t % 2) * 512 + 384]
                    sbank_rd[h] = P.on("vector", lambda e, sb_=sb_, wsf=wsf, W=W:
                                       e.scalar_tensor_tensor(wsf[:, 0:W], sb_[:, 0:W], 8.0,
                                                              st["maskb"][:, 0:W],
                                                              ALU.mult, ALU.add),
                                       deps=[tS, slot_rd.get((h, t % 2))], tok=True)
                    pslot = ptb[:, (2 * (t % 2) + h) * 512:(2 * (t % 2) + h) * 512 + 384]
                    txp[(h, t % 2)] = P.on(
                        "scalar", lambda e, pslot=pslot, wsf=wsf, W=W: e.activation(
                            pslot[:, 0:W], wsf[:, 0:W], AF.Exp, bias=negC[:], scale=1.0),
                        deps=[sbank_rd[h], slot_rd.get((h, t % 2))], tok=True)
        oev = []
        for h in range(2):
            Osrc = O1 if h == 0 else O2
            tch = []
            for qg in range(ntok // 512):
                tt1 = P.on("vector", lambda e, Osrc=Osrc, qg=qg: e.reciprocal(
                    rr[:, 0:512], Osrc[64:65, qg * 512:(qg + 1) * 512]),
                    deps=[evtok[(h, w)] for w in range(4 * qg, min(T, 4 * qg + 4))]
                    + P.wdeps("lrr"), tok=True)
                tch.append(P.rd("lrr",
                                P.dma("sync", st["BNC"][h:h + 1, qg * 512:(qg + 1) * 512],
                                      rr[0:1, 0:512], deps=[tt1] + P.wdeps(f"LBNC{h}"))))
            tb = P.rd(f"LBNC{h}", P.dma("sync", rbig,
                                        st["BNC"][h:h + 1, :].partition_broadcast(64),
                                        deps=tch + P.wdeps("rbig")))
            t4 = P.rd("rbig", P.on("vector", lambda e, Osrc=Osrc: e.tensor_tensor(
                obig, Osrc[0:64, :], rbig, ALU.mult),
                deps=[tb] + P.wdeps("obig"), tok=True))
            t5 = P.rd("obig", P.dma("sync", st["OF"][p, 64 * h:64 * h + 64, :], obig,
                                    deps=[t4] + P.wdeps("OF")))
            oev += [t4, t5]

    if st["stop"] == "lattnB":
        return
    P.barrier()
    wo = bf16v(AR2, 0, 8 * DM)
    wo3 = wo.rearrange("p (c n) -> p c n", c=8)
    ofm3 = st["wbf3"].ap().rearrange("p (c n) -> p c n", c=8)
    two = P.dma("gpsimd", wo3, din["f_wo"][li])
    prev = []
    for tt in range(T):
        r = slice(tt * 128, tt * 128 + 128)
        tof = P.rd("OF", P.dma("sync", ofm3,
                               st["OF"][:, :, tt * 128:(tt + 1) * 128].rearrange("c p n -> p c n"),
                               deps=prev))
        txl = P.rd(yname, P.dma("sync", wc[:], yin[r, :], deps=prev))
        tmm = {}
        for half in range(2):
            bank = PB[half]
            for c in range(8):
                tmm[half] = P.on("tensor", lambda e, c=c, half=half, bank=bank: e.matmul(
                    bank[:], ofm3[:, c, :], wo3[:, c, half * 512:(half + 1) * 512],
                    start=(c == 0), stop=(c == 7)),
                    deps=([tof, two] + prev) if c == 0 else (), tok=(c == 7), mode="full")
        prev = []
        for half in range(2):
            tr = P.on("vector", lambda e, half=half: e.tensor_tensor(
                wd[:, half * 512:(half + 1) * 512], PB[half][:],
                wc[:, half * 512:(half + 1) * 512], ALU.add),
                deps=[tmm[half], txl], tok=True)
            prev.append(tr)
        prev.append(P.dma("sync", yout[r, :], wd[:], deps=prev + P.wdeps(yoname)))


def rotary(P, st, tt, bank0, CS3, SS3, dstF, stg_name, deps):
    wa, wc, wd, we = st["wa"], st["wc"], st["wd"], st["we"]
    stat = st["stat"]; PB = st["PB"]
    stg = st[stg_name]
    t1 = P.on("scalar", lambda e: e.copy(wc[:, 0:512], PB[bank0][:]), deps=deps, tok=True)
    t2 = P.on("scalar", lambda e: e.copy(wc[:, 512:1024], PB[bank0 + 1][:]), tok=True)
    t3 = P.on("scalar", lambda e: e.activation(wd[:, 0:512], PB[bank0][:], AF.Square), tok=True)
    t4 = P.on("scalar", lambda e: e.activation(wd[:, 512:1024], PB[bank0 + 1][:], AF.Square),
              tok=True)
    wc3 = wc.ap().rearrange("p (h d) -> p h d", d=DH)
    wd3 = wd.ap().rearrange("p (h d) -> p h d", d=DH)
    t5 = P.on("vector", lambda e: e.tensor_reduce(stat[:, 0:16], wd3, AXX, ALU.add),
              deps=[t3, t4], tok=True)
    t6 = P.on("vector", lambda e: e.tensor_scalar(stat[:, 0:16], stat[:, 0:16], 1.42e-24,
                                                  None, ALU.max, ALU.bypass), deps=[t5], tok=True)
    t7 = P.on("scalar", lambda e: e.activation(stat[:, 0:16], stat[:, 0:16], AF.Ln,
                                               scale=1.0), deps=[t6], tok=True)
    t8 = P.on("scalar", lambda e: e.activation(stat[:, 0:16], stat[:, 0:16], AF.Exp,
                                               scale=-0.5), deps=[t7], tok=True)
    rb = stat[:, 0:16].unsqueeze(2).broadcast_to((128, 16, DH))
    tu = P.on("vector", lambda e: e.tensor_tensor(wd3, wc3, rb, ALU.mult),
              deps=[t1, t2, t8], tok=True)
    wd4 = wd.ap().rearrange("p (h s d2) -> p h s d2", s=2, d2=32)
    wa4 = wa.ap().rearrange("p (h s d2) -> p h s d2", s=2, d2=32)
    g1 = P.on("vector", lambda e: e.tensor_tensor(
        we.ap().rearrange("p (h d) -> p h d", d=DH), wd3,
        CS3[:, tt, :].unsqueeze(1).broadcast_to((128, 16, DH)), ALU.mult),
        deps=[tu], tok=True)
    g2 = P.on("vector", lambda e: e.tensor_tensor(
        wa4[:, :, 0, :], wd4[:, :, 1, :],
        SS3[:, tt, 0:32].unsqueeze(1).broadcast_to((128, 16, 32)), ALU.mult),
        deps=[g1], tok=True)
    g3 = P.on("vector", lambda e: e.tensor_tensor(
        wa4[:, :, 1, :], wd4[:, :, 0, :],
        SS3[:, tt, 32:64].unsqueeze(1).broadcast_to((128, 16, 32)), ALU.mult),
        deps=[g2], tok=True)
    g4 = P.on("vector", lambda e: e.tensor_tensor(st["wbf"][:], we[:], wa[:], ALU.add),
              deps=[g1, g2, g3], tok=True)
    tev = transpose_tile(P, st, st["wbf"], stg.ap().rearrange("p (c n) -> p c n", c=8),
                         7, [g4])
    tsp = P.dma("sync", dstF.rearrange("c p n -> p c n")[:, :, tt * 128:(tt + 1) * 128],
                stg.ap().rearrange("p (c n) -> p c n", c=8), deps=[tev])
    return {"sp": tsp, "mid": g4, "ev": tev}


def ph_lffn(P, st, li, yin, yname, yout, yoname, inv):
    din = st["din"]; T = st["T"]; ntok = st["ntok"]
    AR1, AR2, PB = st["AR1"], st["AR2"], st["PB"]
    wa, wb, wc, wd = st["wa"], st["wb"], st["wc"], st["wd"]
    wbf, gsb = st["wbf"], st["gsb"]
    P.barrier()

    w1 = bf16v(AR1, 0, 8 * W1C)
    w13 = w1.rearrange("p (c n) -> p c n", c=8)
    w2 = bf16v(AR2, 0, 22 * DM)
    w23 = w2.rearrange("p (c n) -> p c n", c=22)
    lngb = f32v(AR2, (22 * DM) // 2, DM)
    mst = bf16v(AR2, 22 * DM + 2 * DM, DI)
    mfst = bf16v(AR2, 22 * DM + 2 * DM + DI, 22 * 128)
    assert 22 * DM + 2 * DM + DI + 22 * 128 <= 30208
    lnf3 = st["wbf3"].ap().rearrange("p (c n) -> p c n", c=8)
    mfst3 = mfst.rearrange("p (c n) -> p c n", c=22)

    tw1 = P.dma("gpsimd", w13, din["f_w1"][li])
    tw2 = P.dma("gpsimd", w23, din["f_w2"][li])
    tb2 = P.dma("gpsimd", st["b2row"][:], din["f_b2"][li:li + 1, :])
    tlg = bcast_col(P, st, din["f_lng"][li:li + 1, :], lngb)
    nch = (W1C + 511) // 512
    if inv == 0:
        lnb_src = din["f_lnb"][li:li + 1, :].rearrange("o (c p) -> p (o c)", p=128)
        tlb = P.on("gpsimd", lambda e: e.dma_start(out=st["lnbc"][:], in_=lnb_src,
                                                   allow_slow_non_contiguous=True),
                   tok=True, dma=True)
        twr_last = None
        twr = []
        for j in range(nch):
            Wj = min(512, W1C - 512 * j)
            tbr = P.dma("sync", st["rr"][0:1, 0:Wj],
                        din["f_b1"][li:li + 1, 512 * j:512 * j + Wj], deps=[twr_last])
            tmj = None
            for c in range(8):
                tmj = P.on("tensor", lambda e, j=j, c=c, Wj=Wj: e.matmul(
                    PB[0][0:1, 0:Wj], st["lnbc"][:, c:c + 1],
                    w13[:, c, 512 * j:512 * j + Wj], start=(c == 0), stop=(c == 7)),
                    deps=([tlb, tw1, twr_last]) if c == 0 else (), tok=(c == 7), mode="m1")
            twr_last = P.on("vector", lambda e, j=j, Wj=Wj: e.tensor_tensor(
                st["wrow"][0:1, 512 * j:512 * j + Wj], PB[0][0:1, 0:Wj],
                st["rr"][0:1, 0:Wj], ALU.add), deps=[tmj, tbr], tok=True)
            twr.append(twr_last)
        twr = [P.dma("sync", st["WRD"][li], st["wrow"][:], deps=twr)]
    else:
        twr = [P.dma("sync", st["wrow"][:], st["WRD"][li])]

    prev = []
    prev_aj = []
    prev_mev = []
    for tt in range(T):
        r = slice(tt * 128, tt * 128 + 128)
        tls = shifted_load(P, st, yin, yname, tt, wa, deps=prev)
        negmu, rstd, tst = ln_stats(P, st, wa, LNEPS, tls)
        t1 = P.on("vector", lambda e: e.tensor_scalar(wb[:], wa[:], negmu, rstd,
                                                      ALU.add, ALU.mult), deps=tst, tok=True)
        t2 = P.on("vector", lambda e: e.scalar_tensor_tensor(wbf[:], wb[:], 1.0, lngb,
                                                             ALU.mult, ALU.mult),
                  deps=[t1, tlg], tok=True)
        tsf = transpose_tile(P, st, wbf, lnf3, 6, [t2])
        ajs = []
        for j in range(6):
            a0 = 512 * j
            aW = min(512, DI - a0)
            tmj = {}
            for (idx, c0_) in ((0, a0), (1, DI + a0)):
                bank = PB[2 * (j % 2) + idx]
                bankrel = ajs[j - 2] if j >= 2 else (prev_aj[j + 4] if prev_aj else None)
                for c in range(8):
                    tmj[idx] = P.on("tensor", lambda e, c=c, c0_=c0_, aW=aW, bank=bank: e.matmul(
                        bank[:, 0:aW], lnf3[:, c, :], w13[:, c, c0_:c0_ + aW],
                        start=(c == 0), stop=False),
                        deps=([tsf, tw1, bankrel] + twr + prev)
                        if c == 0 else (), mode="full")
                tmj[idx] = P.on("tensor", lambda e, c0_=c0_, aW=aW, bank=bank: e.matmul(
                    bank[:, 0:aW], st["ones_row"][0:1, 0:128], st["wrow"][0:1, c0_:c0_ + aW],
                    start=False, stop=True), deps=twr, tok=True, mode="k1")
            gb = PB[2 * (j % 2) + 1]
            ab = PB[2 * (j % 2)]
            grel = ajs[j - 1] if j >= 1 else (prev_aj[-1] if prev_aj else None)
            tg = P.on("scalar", lambda e, gb=gb, aW=aW: e.activation(
                gsb[:, 0:aW], gb[:, 0:aW], AF.Gelu), deps=[tmj[1], grel], tok=True)
            ajs.append(P.on("vector", lambda e, ab=ab, a0=a0, aW=aW: e.tensor_tensor(
                mst[:, a0:a0 + aW], ab[:, 0:aW], gsb[:, 0:aW], ALU.mult),
                deps=[tmj[0], tg], tok=True))
        prev_aj = ajs
        tmev = []
        pv = PB[7].ap().bitcast(BF16)
        for grp in range(3):
            ch0 = grp * 8
            nc_ = min(8, 22 - ch0)
            tlast = None
            for cc in range(nc_):
                c2 = ch0 + cc
                cw = min(128, DI - c2 * 128)
                tlast = P.on("tensor", lambda e, cc=cc, c2=c2, cw=cw, pv=pv: e.transpose(
                    pv[0:cw, cc * 128:(cc + 1) * 128], mst[:, c2 * 128:c2 * 128 + cw],
                    st["ident"][:]),
                    deps=(ajs + [st["pt_rd"].get(7)]) if cc == 0 else (),
                    tok=(cc == nc_ - 1), mode="tr")
            pvv = pv.rearrange("p (c n) -> p c n", c=8)[:, 0:nc_, :]
            tmev.append(P.on("scalar", lambda e, ch0=ch0, nc_=nc_, pvv=pvv: e.copy(
                mfst3[:, ch0:ch0 + nc_, :], pvv), deps=[tlast], tok=True))
            st["pt_rd"][7] = tmev[-1]
        prev_mev = tmev
        txl = P.rd(yname, P.dma("sync", wc[:], yin[r, :], deps=prev))
        tm2 = {}
        for half in range(2):
            bank = PB[4 + half]
            for c2 in range(22):
                tm2[half] = P.on("tensor", lambda e, c2=c2, half=half, bank=bank: e.matmul(
                    bank[:], mfst[:, c2 * 128:(c2 + 1) * 128],
                    w23[:, c2, half * 512:(half + 1) * 512],
                    start=(c2 == 0), stop=False),
                    deps=(tmev + [tw2] + prev) if c2 == 0 else (), mode="full")
            tm2[half] = P.on("tensor", lambda e, half=half, bank=bank: e.matmul(
                bank[:], st["ones_row"][0:1, 0:128],
                st["b2row"][0:1, half * 512:(half + 1) * 512],
                start=False, stop=True), deps=[tb2], tok=True, mode="k1")
        prev = []
        for half in range(2):
            tr = P.on("vector", lambda e, half=half: e.tensor_tensor(
                wd[:, half * 512:(half + 1) * 512], PB[4 + half][:],
                wc[:, half * 512:(half + 1) * 512], ALU.add),
                deps=[tm2[half], txl], tok=True)
            prev.append(tr)
        prev.append(P.dma("sync", yout[r, :], wd[:], deps=prev + P.wdeps(yoname)))


_CACHE = {}


def kernel(**inputs):
    x = np.asarray(inputs["L0_attn_a_x"])
    B, N, E = x.shape
    key = (N,)
    if key not in _CACHE:
        nc, P = build(ntok=N)
        _CACHE[key] = (nc, host_tables(N))
    nc, tables = _CACHE[key]
    w = prep_weights(inputs)
    base = {}
    base.update(w)
    base.update(tables)
    in_maps = []
    for c in range(B):
        m = dict(base)
        m["x0"] = np.ascontiguousarray(x[c])
        in_maps.append(m)
    res = run_bass_kernel_spmd(nc, in_maps, core_ids=list(range(B)))
    return np.stack([np.asarray(r["out"]) for r in res.results], axis=0)


# version 04
# speedup vs baseline: 2.1763x; 1.0250x over previous
import numpy as np
import ml_dtypes
import concourse.bass as bass
import concourse.mybir as mybir
from concourse.bass_utils import run_bass_kernel_spmd

F32 = mybir.dt.float32
BF16 = mybir.dt.bfloat16
AF = mybir.ActivationFunctionType
ALU = mybir.AluOpType
AXX = mybir.AxisListType.X
BFNP = ml_dtypes.bfloat16

DM = 1024
NH = 16
DH = 64
DEPTH = 6
DI = 2730
W1C = 5460
WIN = 256
NEG = -1.0e38
LNEPS = 1e-5
RMSEPS = float(np.finfo(np.float32).eps)
C_ATTN_A = 30.0

ENGS = ("sync", "tensor", "vector", "scalar", "gpsimd")


def host_tables(ntok):
    t = {}
    t["ident"] = np.eye(128, dtype=BFNP)
    t["ones_row"] = np.ones((1, 512), dtype=BFNP)
    p = np.arange(128)[:, None]
    c = np.arange(384)[None, :]
    t["maskb"] = np.where((c >= p) & (c <= p + WIN), 0.0, NEG).astype(np.float32)
    u = np.arange(2 * ntok - 1)[None, :]
    t["dist"] = np.abs(u - (ntok - 1) - np.arange(128)[:, None]).astype(np.float32)
    half = np.arange(0, DH, 2, dtype=np.float64)
    inv_freq = 1.0 / 10000.0 ** (half / DH)
    tt = np.arange(ntok, dtype=np.float64)
    freqs = tt[:, None] * inv_freq[None, :]
    freqs = np.concatenate([freqs, freqs], -1)
    base_scale = (half + 0.4 * DH) / (1.4 * DH)
    power = (tt - ntok // 2) / (WIN // 2)
    scale = base_scale[None, :] ** power[:, None]
    scale = np.concatenate([scale, scale], -1)
    cos = np.cos(freqs)
    sin = np.sin(freqs)
    sgn = np.where(np.arange(DH) < DH // 2, -1.0, 1.0)[None, :]
    t["cs_q"] = (cos * scale).astype(np.float32)
    t["ss_q"] = (sin * scale * sgn).astype(np.float32)
    t["cs_k"] = (cos / scale).astype(np.float32)
    t["ss_k"] = (sin / scale * sgn).astype(np.float32)
    return t


def prep_weights(inp, depth=DEPTH):
    w = {}
    cont = np.ascontiguousarray

    def pmaj(a):
        E, F = a.shape
        return cont(a.reshape(E // 128, 128, F).transpose(1, 0, 2))

    w["aw_qkv"] = pmaj(np.asarray(inp["L0_attn_a_Wqkv_w"]).T.copy())
    w["ab_qkv"] = cont(np.asarray(inp["L0_attn_a_Wqkv_b"])[None, :])
    w["aw_out"] = pmaj(np.asarray(inp["L0_attn_a_out_w"]).T.copy())
    w["ab_out"] = cont(np.asarray(inp["L0_attn_a_out_b"])[None, :])
    w["a_dd"] = cont(np.asarray(inp["L0_attn_a_distance_decay"])[None, :])
    w["bw_qkv"] = pmaj(np.concatenate(
        [np.asarray(inp["L0_attn_b_Wq"]).T, np.asarray(inp["L0_attn_b_Wk"]).T,
         np.asarray(inp["L0_attn_b_Wv"]).T], axis=1))
    w["bw_out"] = pmaj(np.asarray(inp["L0_attn_b_Wo"]).T.copy())
    w["b_qn"] = cont(np.asarray(inp["L0_attn_b_qn_w"])[None, :])
    w["b_kn"] = cont(np.asarray(inp["L0_attn_b_kn_w"])[None, :])
    w["f_wqkv"] = np.stack([pmaj(np.asarray(inp["L0_ffn_Wqkv"])[i]) for i in range(depth)])
    w["f_wo"] = np.stack([pmaj(np.asarray(inp["L0_ffn_Wo"])[i]) for i in range(depth)])
    w["f_w1"] = np.stack([pmaj(np.asarray(inp["L0_ffn_W1"])[i]) for i in range(depth)])
    w2 = np.asarray(inp["L0_ffn_W2"])
    w2p = np.zeros((depth, 2816, DM), np.float32)
    w2p[:, :DI] = w2[:depth]
    w["f_w2"] = np.stack([pmaj(w2p[i]) for i in range(depth)])
    w["f_qs"] = cont(np.asarray(inp["L0_ffn_q_scale"])[:depth])
    w["f_ks"] = cont(np.asarray(inp["L0_ffn_k_scale"])[:depth])
    w["f_lng"] = cont(np.asarray(inp["L0_ffn_ln_g"])[:depth])
    w["f_lnb"] = cont(np.asarray(inp["L0_ffn_ln_b"])[:depth])
    w["f_b1"] = cont(np.asarray(inp["L0_ffn_b1"])[:depth])
    w["f_b2"] = cont(np.asarray(inp["L0_ffn_b2"])[:depth])
    return w


class Prog:
    DPOOL = {"sync": 8, "gpsimd": 4, "scalar": 4}
    SEMN = (["sync", "tensor", "vector", "scalar", "gpsimd"]
            + [f"d{q}{i}" for q, n in DPOOL.items() for i in range(n)])
    SAFE_SAME_ENGINE = True

    def __init__(self):
        self.q = {e: [] for e in ENGS}
        self.cnt = {}
        self.waited = {e: {} for e in ENGS}
        self.sems = {}
        self.ninst = 0
        self.scr_rd = {}

    def _dma_sem(self, eng):
        if not hasattr(self, "_dma_rr"):
            self._dma_rr = {}
        i = self._dma_rr.get(eng, 0)
        self._dma_rr[eng] = (i + 1) % self.DPOOL[eng]
        return f"d{eng}{i}"

    def wait(self, eng, deps):
        for d in deps:
            if d is None:
                continue
            s, v = d
            if self.waited[eng].get(s, 0) < v:
                self.waited[eng][s] = v

                def mk(s=s, v=v):
                    return lambda e: e.wait_ge(self.sems[s], v)
                self.q[eng].append(mk())
                self.ninst += 1

    def on(self, eng, fn, deps=(), tok=False, dma=False, mode=None):
        if eng == "tensor" and not dma:
            assert mode is not None, "tensor ops must declare a tiling mode"
            if mode != getattr(self, "_pe_mode", None):
                deps = list(deps) + [getattr(self, "_last_pe", None)]
            self._pe_mode = mode
            tok = True
        self.wait(eng, deps)
        self.ninst += 1
        if not hasattr(self, "last_tok"):
            self.last_tok = {}
            self.clean = {}
        if not tok:
            self.q[eng].append(fn)
            self.clean[eng] = False
            return None
        s = self._dma_sem(eng) if dma else eng
        inc = 16 if dma else 1
        if dma:
            self.wait(eng, [(s, self.cnt.get(s, 0))])
        self.cnt[s] = self.cnt.get(s, 0) + inc
        v = self.cnt[s]

        def mk(fn=fn, s=s, inc=inc):
            return lambda e: fn(e).then_inc(self.sems[s], inc)
        self.q[eng].append(mk())
        if not dma and not self.SAFE_SAME_ENGINE:
            self.waited[eng][s] = v
        if eng == "tensor" and not dma:
            self._last_pe = (s, v)
        if not dma:
            self.last_tok[eng] = (s, v)
            self.clean[eng] = True
        else:
            self.last_tok[eng] = (s, v)
            self.clean[eng] = True
        return (s, v)

    def dma(self, eng, out, in_, deps=()):
        return self.on(eng, lambda e, o=out, i=in_: e.dma_start(out=o, in_=i),
                       deps=deps, tok=True, dma=True)

    def rd(self, name, tok):
        self.scr_rd.setdefault(name, []).append(tok)
        return tok

    def wdeps(self, name):
        return self.scr_rd.pop(name, [])

    def barrier(self):
        if not hasattr(self, "last_tok"):
            self.last_tok = {}
            self.clean = {}
        for eng in ENGS:
            if self.q[eng] and not self.clean.get(eng, True):
                self.on(eng, lambda e: e.nop(), tok=True)
        snap = [(s, self.cnt.get(s, 0)) for s in self.SEMN if self.cnt.get(s, 0) > 0]
        for eng in ENGS:
            self.wait(eng, snap)

    def finish(self, blk):
        for name in ENGS:
            ops = self.q[name]

            def mksec(ops=ops):
                def section(e):
                    for f in ops:
                        f(e)
                return section
            getattr(blk, name)(mksec())


def build(ntok=4096, depth=DEPTH, parts=("a", "m1", "b", "m2"), dbg_out=(), stop=None):
    T = ntok // 128
    nc = bass.Bass("TRN2", target_bir_lowering=False)
    P = Prog()
    din = {}

    def dt_in(name, shape, dtype=F32):
        din[name] = nc.dram_tensor(name, list(shape), dtype, kind="ExternalInput").ap()

    dt_in("x0", (ntok, DM))
    dt_in("aw_qkv", (128, 8, 3 * DM)); dt_in("ab_qkv", (1, 3 * DM))
    dt_in("aw_out", (128, 8, DM)); dt_in("ab_out", (1, DM)); dt_in("a_dd", (1, 1))
    dt_in("bw_qkv", (128, 8, 3 * DM)); dt_in("bw_out", (128, 8, DM))
    dt_in("b_qn", (1, DH)); dt_in("b_kn", (1, DH))
    dt_in("f_wqkv", (depth, 128, 8, 3 * DM)); dt_in("f_wo", (depth, 128, 8, DM))
    dt_in("f_w1", (depth, 128, 8, W1C)); dt_in("f_w2", (depth, 128, 22, DM))
    dt_in("f_qs", (depth, DH)); dt_in("f_ks", (depth, DH))
    dt_in("f_lng", (depth, DM)); dt_in("f_lnb", (depth, DM))
    dt_in("f_b1", (depth, W1C)); dt_in("f_b2", (depth, DM))
    dt_in("ident", (128, 128), BF16); dt_in("ones_row", (1, 512), BF16)
    dt_in("maskb", (128, 384)); dt_in("dist", (128, 2 * ntok - 1))
    dt_in("cs_q", (ntok, DH)); dt_in("ss_q", (ntok, DH))
    dt_in("cs_k", (ntok, DH)); dt_in("ss_k", (ntok, DH))
    dout = nc.dram_tensor("out", [ntok, DM], F32, kind="ExternalOutput").ap()

    def dt_sc(name, shape, dtype=F32):
        kind = "ExternalOutput" if name in dbg_out else "Internal"
        return nc.dram_tensor(name, list(shape), dtype, kind=kind).ap()

    st = dict(ntok=ntok, T=T, depth=depth, nc=nc, din=din, dout=dout, stop=stop)
    st["X1"] = dt_sc("X1", (ntok, DM))
    st["X2"] = dt_sc("X2", (ntok, DM))
    st["X3"] = dt_sc("X3", (ntok, DM))
    st["YA"] = dt_sc("YA", (ntok, DM))
    st["YB"] = dt_sc("YB", (ntok, DM))
    st["QF"] = dt_sc("QF", (8, 128, ntok), BF16)
    st["KF"] = dt_sc("KF", (8, 128, ntok), BF16)
    st["VSP"] = dt_sc("VSP", (T, 128, NH * 65), BF16)
    st["OF"] = dt_sc("OF", (8, 128, ntok), BF16)
    st["BNC"] = dt_sc("BNC", (8, ntok))
    st["WRD"] = dt_sc("WRD", (depth, W1C), BF16)

    from contextlib import ExitStack
    ctx = ExitStack()
    sb = lambda name, shape, dtype: ctx.enter_context(nc.sbuf_tensor("s_" + name, list(shape), dtype))
    with ctx:
        st["AR1"] = sb("AR1", [128, 44544], BF16)
        st["AR2"] = sb("AR2", [128, 30208], BF16)
        st["ident"] = sb("ident", [128, 128], BF16)
        st["ones_row"] = sb("ones_row", [1, 512], BF16)
        st["maskb"] = sb("maskb", [128, 384], F32)
        st["wa"] = sb("wa", [128, 1024], F32)
        st["wb"] = sb("wb", [128, 1024], F32)
        st["wc"] = sb("wc", [128, 1024], F32)
        st["wd"] = sb("wd", [128, 1024], F32)
        st["we"] = sb("we", [128, 1024], F32)
        st["wbf"] = sb("wbf", [128, 1024], BF16)
        st["wbf3"] = sb("wbf3", [128, 1024], BF16)
        st["wbf4"] = sb("wbf4", [128, 1024], BF16)
        st["wbf5"] = sb("wbf5", [128, 1024], BF16)
        st["vst"] = sb("vst", [128, NH * 65], BF16)
        st["gsb"] = sb("gsb", [128, 512], F32)
        st["stat"] = sb("stat", [128, 32], F32)
        st["sml"] = sb("sml", [128, 160], F32)
        st["ptb"] = sb("ptb", [128, 4 * 512], BF16)
        st["wrow"] = sb("wrow", [1, W1C], BF16)
        st["b2row"] = sb("b2row", [1, DM], BF16)
        st["absl"] = sb("absl", [1, 3 * DM], BF16)
        st["lnbc"] = sb("lnbc", [128, 8], BF16)
        st["qnc"] = sb("qnc", [128, 1], F32)
        st["knc"] = sb("knc", [128, 1], F32)
        st["negC"] = sb("negC", [128, 1], F32)
        st["onec"] = sb("onec", [128, 1], BF16)
        st["rr"] = sb("rr", [1, 512], F32)
        st["sconst"] = sb("sconst", [128, 4], F32)
        for col, val in enumerate([LNEPS, RMSEPS, -C_ATTN_A]):
            nc.const_aps.register(F32, float(val), st["sconst"].ap()[:, col:col + 1])
        st["PB"] = [ctx.enter_context(nc.psum_tensor(f"pb{i}", [128, 512], F32))
                    for i in range(8)]
        st["pt_rd"] = {}
        for sname in Prog.SEMN:
            P.sems[sname] = ctx.enter_context(nc.semaphore(sname))
        blk = ctx.enter_context(nc.Block())
        emit_program(P, st, parts)
        P.finish(blk)
    return nc, P


def f32v(arena, c0f, ncols, rows=128, p0=0):
    return arena.ap().bitcast(F32)[p0:p0 + rows, c0f:c0f + ncols]


def bf16v(arena, c0, ncols, rows=128, p0=0):
    return arena[p0:p0 + rows, c0:c0 + ncols]


def ln_stats(P, st, xt, dump, eps, deps, sb=0):
    stat = st["stat"]
    t1 = P.on("vector", lambda e: e.tensor_reduce(stat[:, sb + 0:sb + 1], xt[:], AXX, ALU.add),
              deps=deps, tok=True)
    t2 = P.on("scalar", lambda e: e.activation(dump, xt[:], AF.Square,
                                               accum_out=stat[:, sb + 1:sb + 2]),
              deps=deps, tok=True)
    t4 = P.on("vector", lambda e: e.tensor_scalar(stat[:, sb + 2:sb + 3], stat[:, sb + 0:sb + 1],
                                                  -1.0 / 1024, None, ALU.mult, ALU.bypass),
              deps=[t1], tok=True)
    t5 = P.on("vector", lambda e: e.tensor_tensor(stat[:, sb + 3:sb + 4], stat[:, sb + 2:sb + 3],
                                                  stat[:, sb + 2:sb + 3], ALU.mult),
              deps=[t4], tok=True)
    t6 = P.on("vector", lambda e: e.tensor_scalar(stat[:, sb + 4:sb + 5], stat[:, sb + 1:sb + 2],
                                                  1.0 / 1024, None, ALU.mult, ALU.bypass),
              deps=[t2], tok=True)
    t7 = P.on("vector", lambda e: e.tensor_tensor(stat[:, sb + 5:sb + 6], stat[:, sb + 4:sb + 5],
                                                  stat[:, sb + 3:sb + 4], ALU.subtract),
              deps=[t5, t6], tok=True)
    t8 = P.on("scalar", lambda e: e.activation(stat[:, sb + 6:sb + 7], stat[:, sb + 5:sb + 6],
                                               AF.Ln, bias=float(eps), scale=1.0),
              deps=[t7], tok=True)
    t9 = P.on("scalar", lambda e: e.activation(stat[:, sb + 7:sb + 8], stat[:, sb + 6:sb + 7],
                                               AF.Exp, scale=-0.5), deps=[t8], tok=True)
    return stat[:, sb + 2:sb + 3], stat[:, sb + 7:sb + 8], [t4, t9, t2]


def transpose_tile(P, st, src_bf, dst_strided, bank_idx, deps, nchunk=8, dst_chunks=None):
    bank = st["PB"][bank_idx]
    pv = bank.ap().bitcast(BF16)
    alldeps = list(deps) + [st["pt_rd"].get(bank_idx)]
    tl = None
    for c in range(nchunk):
        tl = P.on("tensor", lambda e, c=c: e.transpose(
            pv[:, c * 128:(c + 1) * 128], src_bf[:, c * 128:(c + 1) * 128], st["ident"][:]),
            deps=alldeps if c == 0 else (), tok=(c == nchunk - 1), mode="tr")
    pvv = pv.rearrange("p (c n) -> p c n", c=8)[:, 0:nchunk, :]
    tev = P.on("scalar", lambda e: e.copy(dst_strided, pvv), deps=[tl], tok=True)
    st["pt_rd"][bank_idx] = tev
    return tev


def shifted_load(P, st, ysrc, yname, tt, dst, deps=()):
    r0 = tt * 128
    t1 = P.rd(yname, P.dma("sync", dst[:, 0:512], ysrc[r0:r0 + 128, 0:512], deps=deps))
    if tt == 0:
        t2 = P.rd(yname, P.dma("sync", dst[1:128, 512:1024], ysrc[0:127, 512:1024]))
        t3 = P.on("gpsimd", lambda e: e.memset(dst[0:1, 512:1024], 0.0), deps=deps, tok=True)
        return [t1, t2, t3]
    t2 = P.rd(yname, P.dma("sync", dst[:, 512:1024], ysrc[r0 - 1:r0 + 127, 512:1024]))
    return [t1, t2]


def bcast_col(P, st, dram_row_ap, dst, deps=()):
    return P.dma("sync", dst, dram_row_ap.partition_broadcast(dst.shape[0]), deps=deps)


def emit_program(P, st, parts):
    din = st["din"]
    P.dma("sync", st["ident"][:], din["ident"][:])
    P.dma("sync", st["ones_row"][:], din["ones_row"][:])
    P.dma("sync", st["maskb"][:], din["maskb"][:])
    P.on("gpsimd", lambda e: e.memset(st["onec"][:], 1.0), tok=True)
    for col, val in enumerate([LNEPS, RMSEPS, -C_ATTN_A]):
        P.on("vector", lambda e, col=col, val=val: e.memset(st["sconst"][:, col:col + 1],
                                                            float(val)), tok=True)
    P.barrier()

    x_cur, x_name = din["x0"], "x0"
    if "a" in parts:
        ph_dense(P, st, "a", x_cur, x_name, st["X1"], "X1")
        x_cur, x_name = st["X1"], "X1"
    if "m1" in parts:
        ph_ffnmod(P, st, x_cur, x_name, st["X2"], "X2", inv=0)
        x_cur, x_name = st["X2"], "X2"
    if "b" in parts:
        ph_dense(P, st, "b", x_cur, x_name, st["X3"], "X3")
        x_cur, x_name = st["X3"], "X3"
    if "m2" in parts:
        ph_ffnmod(P, st, x_cur, x_name, st["dout"], "out", inv=1)
        x_cur = st["dout"]
    if x_cur is not st["dout"]:
        P.barrier()
        prev = []
        for tt in range(st["T"]):
            r = slice(tt * 128, tt * 128 + 128)
            t = P.dma("sync", st["wc"][:], x_cur[r, :], deps=prev)
            tc = P.on("vector", lambda e: e.tensor_copy(st["wd"][:], st["wc"][:]),
                      deps=[t], tok=True)
            prev = [tc, P.dma("sync", st["dout"][r, :], st["wd"][:], deps=[tc])]
    P.wait("tensor", [(s, P.cnt.get(s, 0)) for s in Prog.SEMN if s.startswith("d")])
    P.on("tensor", lambda e: e.nop(), mode="nop")


def ph_dense(P, st, var, x_in, x_name, x_out, xo_name):
    din = st["din"]; T = st["T"]; ntok = st["ntok"]
    AR1, AR2, PB = st["AR1"], st["AR2"], st["PB"]
    wa, wb, wc, wd = st["wa"], st["wb"], st["wc"], st["wd"]
    wbf, vst, ptb = st["wbf"], st["vst"], st["ptb"]
    negC, sml, rr = st["negC"], st["sml"], st["rr"]
    P.barrier()

    lnfm = bf16v(AR1, 0, 8 * ntok)
    lnfm3 = lnfm.rearrange("p (c n) -> p c n", c=8)
    wv = bf16v(AR1, 8 * ntok, 8 * DM)
    wv3 = wv.rearrange("p (c n) -> p c n", c=8)
    wslq = bf16v(AR1, 8 * ntok + 8 * DM, 8 * 128)
    wslk = bf16v(AR1, 8 * ntok + 8 * DM + 8 * 128, 8 * 128)
    wslq3 = wslq.rearrange("p (c n) -> p c n", c=8)
    wslk3 = wslk.rearrange("p (c n) -> p c n", c=8)
    ofm = bf16v(AR1, 8 * ntok + 8 * DM + 16 * 128, 8 * 128)
    ofm3 = ofm.rearrange("p (c n) -> p c n", c=8)
    if var == "a":
        dec = f32v(AR2, 0, 2 * ntok - 1)
        c0 = 2 * (2 * ntok - 1)
        rstdb = None
    else:
        dec = None
        rstdb = f32v(AR2, 0, ntok)
        c0 = 2 * ntok
    q2 = bf16v(AR2, c0, ntok); c0 += ntok
    k2 = bf16v(AR2, c0, ntok); c0 += ntok
    if var == "b":
        tmpn = bf16v(AR2, c0, ntok); c0 += ntok
    else:
        tmpn = None
    v2p = bf16v(AR2, c0, T * 130); c0 += T * 130
    v2p3 = v2p.rearrange("p (t n) -> p t n", t=T)
    rbq = st["wbf3"].ap().bitcast(F32)[0:64, 0:512]
    obf = st["wbf"][0:64, 0:512]
    assert c0 <= 30208, c0

    wq_d = din["aw_qkv" if var == "a" else "bw_qkv"]
    wo_d = din["aw_out" if var == "a" else "bw_out"]

    tbias = None
    if var == "a":
        tdd = bcast_col(P, st, din["a_dd"][0:1, :], sml[:, 8:9])
        tng = P.on("vector", lambda e: e.tensor_scalar(sml[:, 9:10], sml[:, 8:9], -1.0, None,
                                                       ALU.mult, ALU.bypass), deps=[tdd], tok=True)
        tna = P.on("vector", lambda e: e.tensor_tensor(sml[:, 10:11], sml[:, 8:9],
                                                       sml[:, 9:10], ALU.min), deps=[tng], tok=True)
        tdist = P.dma("sync", dec, din["dist"][:])
        tdec = P.on("scalar", lambda e: e.activation(dec, dec, AF.Exp, scale=sml[:, 10:11]),
                    deps=[tna, tdist], tok=True)
        tbias = P.dma("gpsimd", st["absl"][:], din["ab_qkv"][:])
        tbo = P.dma("gpsimd", st["b2row"][:], din["ab_out"][:])
        tnc = None
    else:
        tq1 = P.dma("sync", st["qnc"][0:64, :],
                    din["b_qn"].rearrange("o (p i) -> (o p) i", i=1))
        tq2 = P.dma("sync", st["qnc"][64:128, :],
                    din["b_qn"].rearrange("o (p i) -> (o p) i", i=1))
        tk1 = P.dma("sync", st["knc"][0:64, :],
                    din["b_kn"].rearrange("o (p i) -> (o p) i", i=1))
        tk2 = P.dma("sync", st["knc"][64:128, :],
                    din["b_kn"].rearrange("o (p i) -> (o p) i", i=1))
        tqb = bcast_col(P, st, din["b_qn"][0:1, :], sml[:, 16:80])
        tkb = bcast_col(P, st, din["b_kn"][0:1, :], sml[:, 80:144])
        t1 = P.on("vector", lambda e: e.tensor_reduce(sml[:, 0:1], sml[:, 16:80], AXX, ALU.max,
                                                      apply_absolute_value=True),
                  deps=[tqb], tok=True)
        t2 = P.on("vector", lambda e: e.tensor_reduce(sml[:, 1:2], sml[:, 80:144], AXX, ALU.max,
                                                      apply_absolute_value=True),
                  deps=[tkb], tok=True)
        t3m = P.on("vector", lambda e: e.tensor_tensor(sml[:, 2:3], sml[:, 0:1], sml[:, 1:2],
                                                       ALU.mult), deps=[t1, t2], tok=True)
        tnc = P.on("vector", lambda e: e.tensor_scalar(negC[:], sml[:, 2:3], -8.0, None,
                                                       ALU.mult, ALU.bypass), deps=[t3m], tok=True)
        tdec = None
        st["_rmsdep"] = [tq1, tq2, tk1, tk2]

    prev_tn = {0: None, 1: None}
    prev_ev = None
    for tt in range(T):
        r = slice(tt * 128, tt * 128 + 128)
        par = tt % 2
        ld = (wa, st["we"])[par]
        tld = P.rd(x_name, P.dma("sync", ld[:], x_in[r, :], deps=[prev_tn[par]]))
        negmu, rstd, tst = ln_stats(P, st, ld, (wbf, st["wbf4"])[par][:], LNEPS,
                                    [tld], sb=12 * par)
        prev_tn[par] = P.on("vector", lambda e, ld=ld, par=par, negmu=negmu, rstd=rstd:
                            e.tensor_scalar((wbf, st["wbf4"])[par][:], ld[:], negmu, rstd,
                                            ALU.add, ALU.mult),
                            deps=tst + [prev_ev], tok=True)
        prev_ev = transpose_tile(P, st, (wbf, st["wbf4"])[par],
                                 lnfm3[:, :, tt * 128:(tt + 1) * 128], 6, [prev_tn[par]])

    if st["stop"] == "ph1":
        return
    twv = P.dma("gpsimd", wv3, wq_d[:, :, 2 * DM:3 * DM])
    vst3 = vst.ap().rearrange("p (h n) -> p h n", h=NH)
    tmem = P.on("gpsimd", lambda e: e.memset(vst3[:, :, 64:65], 1.0), tok=True)
    prev_sp = None
    prev_ev = [tmem, prev_ev]
    for tt in range(T):
        tm = {}
        for half in range(2):
            bank = PB[half]
            nmm = 8
            for c in range(8):
                lastmm = (c == 7) and var != "a"
                tm[half] = P.on("tensor", lambda e, c=c, half=half, bank=bank, tt=tt, lastmm=lastmm: e.matmul(
                    bank[:], lnfm3[:, c, tt * 128:(tt + 1) * 128],
                    wv3[:, c, half * 512:(half + 1) * 512],
                    start=(c == 0), stop=lastmm),
                    deps=([twv] + prev_ev) if c == 0 else (), tok=lastmm, mode="full")
            if var == "a":
                tm[half] = P.on("tensor", lambda e, half=half, bank=bank: e.matmul(
                    bank[:], st["ones_row"][0:1, 0:128],
                    st["absl"][0:1, 2 * DM + half * 512:2 * DM + (half + 1) * 512],
                    start=False, stop=True),
                    deps=[tbias], tok=True, mode="k1")
        prev_ev = []
        for half in range(2):
            pv = PB[half].ap().rearrange("p (h n) -> p h n", h=8)
            prev_ev.append(P.on("scalar", lambda e, half=half, pv=pv: e.copy(
                vst3[:, half * 8:(half + 1) * 8, 0:64], pv),
                deps=[tm[half], prev_sp], tok=True))
        prev_sp = P.dma("sync", st["VSP"][tt], vst[:], deps=prev_ev + P.wdeps("VSP"))

    if st["stop"] == "ph2":
        return
    nqg = ntok // 512
    oev = [prev_sp]
    for p in list(range(8))[st.get('_ponly', 0):]:
        tq = P.dma("gpsimd", wslq3, wq_d[:, :, 128 * p:128 * (p + 1)], deps=oev)
        tk = P.dma("gpsimd", wslk3, wq_d[:, :, DM + 128 * p:DM + 128 * (p + 1)], deps=oev)
        tev_q = list(oev)
        for (wsl3, tw_, dst, bofs) in ((wslq3, tq, q2, 128 * p), (wslk3, tk, k2, DM + 128 * p)):
            new_ev = []
            for qg in range(nqg):
                bank = PB[qg % 2]
                tmq = None
                for c in range(8):
                    lastmm = (c == 7) and var != "a"
                    tmq = P.on("tensor", lambda e, c=c, qg=qg, bank=bank, wsl3=wsl3, lastmm=lastmm: e.matmul(
                        bank[:], wsl3[:, c, :], lnfm3[:, c, qg * 512:(qg + 1) * 512],
                        start=(c == 0), stop=lastmm),
                        deps=([tw_] + tev_q) if c == 0 else (), tok=lastmm, mode="full")
                if var == "a":
                    tmq = P.on("tensor", lambda e, qg=qg, bank=bank, bofs=bofs: e.matmul(
                        bank[:], st["absl"][0:1, bofs:bofs + 128],
                        st["ones_row"][0:1, 0:512], start=False, stop=True),
                        deps=[tbias], tok=True, mode="k1")
                new_ev.append(P.on("scalar", lambda e, qg=qg, bank=bank, dst=dst: e.copy(
                    dst[:, qg * 512:(qg + 1) * 512], bank[:]), deps=[tmq], tok=True))
            tev_q = new_ev
        qS, kS = q2, k2
        if st["stop"] == "proj" and p == 0:
            return
        if var == "b":
            t_rms1 = rms_norm_pair(P, st, q2, tmpn, st["qnc"], 2, rstdb, tev_q)
            t_rms2 = rms_norm_pair(P, st, k2, q2, st["knc"], 4, rstdb, [t_rms1])
            qS, kS = tmpn, q2
            tev_q = [t_rms2]
        tv = P.dma("sync", v2p3,
                   st["VSP"][:, :, 130 * p:130 * (p + 1)].rearrange("t p n -> p t n"),
                   deps=tev_q + P.wdeps("v2p"))
        P.rd("VSP", tv)
        slot_rd = {}
        sbank_rd = {}
        for qp0 in range(0, nqg, 2):
            qgs = [qg for qg in (qp0, qp0 + 1) if qg < nqg]
            tdone = {}
            txp = {}
            for t in range(T + 1):
                if t < T:
                    for qI, qg in enumerate(qgs):
                        for h in range(2):
                            sb_ = PB[2 * h + qI]
                            rows = slice(64 * h, 64 * h + 64)
                            tS = P.on("tensor",
                                      lambda e, t=t, h=h, qg=qg, sb_=sb_, rows=rows:
                                      e.matmul(sb_[:], kS[rows, t * 128:(t + 1) * 128],
                                               qS[rows, qg * 512:(qg + 1) * 512],
                                               start=True, stop=True),
                                      deps=([tv, tnc, tdec] + tev_q
                                            + st.get("_oev_par", [])
                                            if (t < 1 and qp0 == 0) else [])
                                      + [sbank_rd.get((h, qI)),
                                         st.get("_oev_par_w", {}).get((qI, h))
                                         if t < 1 else None],
                                      tok=True, mode="r64")
                            sbank_rd[("S", h, qI)] = tS
                if t >= 1:
                    for qI, qg in enumerate(qgs):
                        for h in range(2):
                            ob = PB[4 + 2 * qI + h]
                            slot = 2 * qI + h
                            tdone[(qI, h)] = P.on(
                                "tensor", lambda e, t=t, h=h, ob=ob, slot=slot:
                                e.matmul(ob[0:65, :],
                                         v2p3[:, t - 1, 65 * h:65 * h + 65],
                                         ptb[:, slot * 512:(slot + 1) * 512],
                                         start=(t - 1 == 0), stop=(t - 1 == T - 1)),
                                deps=[txp[(qI, h)]]
                                + (st.get("_oev_par", []) if t == 1 else []),
                                tok=True, mode="full")
                            slot_rd[(qI, h)] = tdone[(qI, h)]
                if t < T:
                    for qI, qg in enumerate(qgs):
                        for h in range(2):
                            sb_ = PB[2 * h + qI]
                            tS = sbank_rd[("S", h, qI)]
                            slot = 2 * qI + h
                            pslot = ptb[:, slot * 512:(slot + 1) * 512]
                            if var == "a":
                                u0 = (ntok - 1) - 128 * t + 512 * qg
                                wsf = (wa, wb)[h][:, qI * 512:qI * 512 + 512]
                                trd = P.on("vector", lambda e, sb_=sb_, wsf=wsf, u0=u0:
                                           e.scalar_tensor_tensor(wsf, sb_[:], 0.125,
                                                                  dec[:, u0:u0 + 512],
                                                                  ALU.mult, ALU.mult),
                                           deps=[tS, slot_rd.get((qI, h))], tok=True)
                                sbank_rd[(h, qI)] = trd
                                tx = P.on("scalar", lambda e, pslot=pslot, wsf=wsf:
                                          e.activation(pslot, wsf, AF.Exp,
                                                       bias=-C_ATTN_A, scale=1.0),
                                          deps=[trd], tok=True)
                            else:
                                tx = P.on("scalar", lambda e, pslot=pslot, sb_=sb_:
                                          e.activation(pslot, sb_[:], AF.Exp,
                                                       bias=negC[:], scale=0.125),
                                          deps=[tS, slot_rd.get((qI, h))], tok=True)
                                sbank_rd[(h, qI)] = tx
                            txp[(qI, h)] = tx
            oev2 = []
            oevw = {}
            for qI, qg in enumerate(qgs):
                for h in range(2):
                    ob = PB[4 + 2 * qI + h]
                    rrh = rr[0:1, 0:512] if h == 0 else st["wc"].ap().bitcast(F32)[0:1, 0:512]
                    rbqh = rbq if h == 0 else st["wd"].ap().bitcast(F32)[0:64, 0:512]
                    obfh = obf if h == 0 else st["wbf4"][0:64, 0:512]
                    t1 = P.on("vector", lambda e, ob=ob, rrh=rrh: e.reciprocal(
                        rrh, ob[64:65, :]),
                        deps=[tdone[(qI, h)]] + P.wdeps(f"rr{h}"), tok=True)
                    t2 = P.rd(f"rr{h}", P.dma("sync", st["BNC"][h:h + 1, 0:512], rrh,
                                              deps=[t1] + P.wdeps(f"BNC{h}")))
                    t3 = P.rd(f"BNC{h}", P.dma("sync", rbqh,
                                               st["BNC"][h:h + 1, 0:512].partition_broadcast(64),
                                               deps=[t2] + P.wdeps(f"rbq{h}")))
                    t4 = P.rd(f"rbq{h}", P.on("vector",
                                              lambda e, ob=ob, obfh=obfh, rbqh=rbqh:
                                              e.tensor_tensor(obfh, ob[0:64, :], rbqh,
                                                              ALU.mult),
                                              deps=[t3] + P.wdeps(f"obf{h}"), tok=True))
                    t5 = P.rd(f"obf{h}", P.dma("sync",
                                               st["OF"][p, 64 * h:64 * h + 64,
                                                        qg * 512:(qg + 1) * 512],
                                               obfh, deps=[t4] + P.wdeps("OF")))
                    oev2 += [t4, t5]
                    oevw[(qI, h)] = t4
            st["_oev_par"] = oev2
            st["_oev_par_w"] = oevw
            oev = oev2
    if st["stop"] == "ph3":
        return
    P.barrier()
    two = P.dma("gpsimd", wv3, wo_d[:, :, :])
    tbo2 = P.dma("gpsimd", st["b2row"][:], din["ab_out"][:]) if var == "a" else None
    prev = []
    for tt in range(T):
        r = slice(tt * 128, tt * 128 + 128)
        tof = P.rd("OF", P.dma("sync", ofm3,
                               st["OF"][:, :, tt * 128:(tt + 1) * 128].rearrange("c p n -> p c n"),
                               deps=prev))
        txl = P.rd(x_name, P.dma("sync", wc[:], x_in[r, :], deps=prev))
        tmm = {}
        for half in range(2):
            bank = PB[half]
            for c in range(8):
                lastmm = (c == 7) and var != "a"
                tmm[half] = P.on("tensor", lambda e, c=c, half=half, bank=bank, lastmm=lastmm: e.matmul(
                    bank[:], ofm3[:, c, :], wv3[:, c, half * 512:(half + 1) * 512],
                    start=(c == 0), stop=lastmm),
                    deps=([tof, two] + prev) if c == 0 else (), tok=lastmm, mode="full")
            if var == "a":
                tmm[half] = P.on("tensor", lambda e, half=half, bank=bank: e.matmul(
                    bank[:], st["ones_row"][0:1, 0:128],
                    st["b2row"][0:1, half * 512:(half + 1) * 512], start=False, stop=True),
                    deps=[tbo2], tok=True, mode="k1")
        prev = []
        for half in range(2):
            tr = P.on("vector", lambda e, half=half: e.tensor_tensor(
                wd[:, half * 512:(half + 1) * 512], PB[half][:],
                wc[:, half * 512:(half + 1) * 512], ALU.add),
                deps=[tmm[half], txl], tok=True)
            prev.append(tr)
        prev.append(P.dma("sync", x_out[r, :], wd[:], deps=prev + P.wdeps(xo_name)))


def rms_norm_pair(P, st, src, dst, ncol, bnc0, rstdb, deps):
    ntok = st["ntok"]; PB = st["PB"]; rr = st["rr"]
    nqg = ntok // 512
    tsq = P.on("scalar", lambda e: e.activation(dst, src, AF.Square), deps=deps, tok=True)
    dmas = []
    prev5 = {0: None, 1: None}
    for h in range(2):
        rows = slice(64 * h, 64 * h + 64)
        for qg in range(nqg):
            tm = P.on("tensor", lambda e, h=h, qg=qg, rows=rows: e.matmul(
                PB[2 + h][0:1, :], st["onec"][rows, :], dst[rows, qg * 512:(qg + 1) * 512],
                start=True, stop=True), deps=[tsq, prev5[h]] + (dmas[-1:] if dmas else []),
                tok=True, mode="rms")
            t5 = P.on("scalar", lambda e, h=h: e.activation(rr[:, 0:512], PB[2 + h][0:1, :],
                                                            AF.Ln, bias=RMSEPS, scale=1.0 / 64),
                      deps=[tm], tok=True)
            t6 = P.on("scalar", lambda e: e.activation(rr[:, 0:512], rr[:, 0:512], AF.Exp,
                                                       scale=-0.5), deps=[t5], tok=True)
            prev5[h] = t6
            dmas.append(P.dma("sync", st["BNC"][bnc0 + h:bnc0 + h + 1, qg * 512:(qg + 1) * 512],
                              rr[0:1, 0:512], deps=[t6] + P.wdeps(f"BNCrms{bnc0 + h}")))
    tb = []
    for h in range(2):
        tb.append(P.rd(f"BNCrms{bnc0 + h}",
                       P.dma("sync", rstdb[64 * h:64 * h + 64, :],
                             st["BNC"][bnc0 + h:bnc0 + h + 1, :].partition_broadcast(64),
                             deps=dmas + P.wdeps("rstdb"))))
    tn = P.rd("rstdb", P.on("vector", lambda e: e.scalar_tensor_tensor(dst, src, ncol[:], rstdb,
                                                                      ALU.mult, ALU.mult),
                            deps=tb + st.get("_rmsdep", []), tok=True))
    return tn


def ph_ffnmod(P, st, x_in, x_name, x_out, xo_name, inv):
    T = st["T"]
    ycur, yname = st["YA"], "YA"
    ynext, ynname = st["YB"], "YB"
    P.barrier()
    prevl = {0: [], 1: []}
    for tt in range(T):
        r = slice(tt * 128, tt * 128 + 128)
        par = tt % 2
        ld = (st["wa"], st["we"])[par]
        dst = (st["wb"], st["wd"])[par]
        tld = P.rd(x_name, P.dma("sync", ld[:], x_in[r, :], deps=prevl[par]))
        negmu, rstd, tst = ln_stats(P, st, ld, dst[:], LNEPS, [tld], sb=12 * par)
        tn = P.on("vector", lambda e, ld=ld, dst=dst, negmu=negmu, rstd=rstd:
                  e.tensor_scalar(dst[:], ld[:], negmu, rstd, ALU.add, ALU.mult),
                  deps=tst, tok=True)
        prevl[par] = [tn, P.dma("sync", ycur[r, :], dst[:], deps=[tn] + P.wdeps(yname))]
    if st["stop"] == "y0":
        return
    for i in range(st["depth"]):
        ph_lattn(P, st, i, ycur, yname, ynext, ynname)
        if st["stop"] in ("lattnA", "lattnB", "lattnC"):
            return
        ycur, yname, ynext, ynname = ynext, ynname, ycur, yname
        ph_lffn(P, st, i, ycur, yname, ynext, ynname, inv)
        if st["stop"] == "lffn":
            return
        ycur, yname, ynext, ynname = ynext, ynname, ycur, yname
    P.barrier()
    prevl = {0: [], 1: []}
    prev_store = None
    for tt in range(T):
        r = slice(tt * 128, tt * 128 + 128)
        par = tt % 2
        la = (st["wa"], st["wd"])[par]
        lb = (st["wb"], st["we"])[par]
        t1 = P.rd(x_name, P.dma("sync", la[:], x_in[r, :], deps=prevl[par]))
        t2 = P.rd(yname, P.dma("sync", lb[:], ycur[r, :], deps=prevl[par]))
        t3 = P.on("vector", lambda e, la=la, lb=lb: e.tensor_tensor(
            st["wc"][:], la[:], lb[:], ALU.add),
            deps=[t1, t2, prev_store], tok=True)
        prev_store = P.dma("sync", x_out[r, :], st["wc"][:], deps=[t3] + P.wdeps(xo_name))
        prevl[par] = [t3, prev_store]


def ph_lattn(P, st, li, yin, yname, yout, yoname):
    din = st["din"]; T = st["T"]; ntok = st["ntok"]
    AR1, AR2, PB = st["AR1"], st["AR2"], st["PB"]
    wa, wb, wc, wd, we = st["wa"], st["wb"], st["wc"], st["wd"], st["we"]
    wbf, vst, ptb = st["wbf"], st["vst"], st["ptb"]
    sml, rr, negC = st["sml"], st["rr"], st["negC"]
    P.barrier()

    wqkv = bf16v(AR1, 0, 8 * 3 * DM)
    wqkv3 = wqkv.rearrange("p (c n) -> p c n", c=8)
    rotc = 8 * 3 * DM
    CS = {}; SS = {}
    for nm in ("q", "k"):
        CS[nm] = f32v(AR1, rotc // 2, T * DH).rearrange("p (t d) -> p t d", d=DH)
        rotc += 2 * T * DH
        SS[nm] = f32v(AR1, rotc // 2, T * DH).rearrange("p (t d) -> p t d", d=DH)
        rotc += 2 * T * DH
    assert rotc <= 44544
    lnf3 = st["wbf3"].ap().rearrange("p (c n) -> p c n", c=8)
    vst3 = vst.ap().rearrange("p (h n) -> p h n", h=NH)

    tw = P.dma("gpsimd", wqkv3, din["f_wqkv"][li])
    tcs = {}
    for nm in ("q", "k"):
        tcs[nm] = [P.dma("sync", CS[nm], din[f"cs_{nm}"].rearrange("(t p) d -> p t d", p=128)),
                   P.dma("sync", SS[nm], din[f"ss_{nm}"].rearrange("(t p) d -> p t d", p=128))]
    tqsb = bcast_col(P, st, din["f_qs"][li:li + 1, :], sml[:, 16:80])
    tksb = bcast_col(P, st, din["f_ks"][li:li + 1, :], sml[:, 80:144])
    tsw = [P.on("vector", lambda e: e.tensor_copy(we[:, 0:32], sml[:, 48:80]),
                deps=[tqsb], tok=True),
           P.on("vector", lambda e: e.tensor_copy(we[:, 32:64], sml[:, 16:48]),
                deps=[tqsb], tok=True),
           P.on("vector", lambda e: e.tensor_copy(we[:, 64:96], sml[:, 112:144]),
                deps=[tksb], tok=True),
           P.on("vector", lambda e: e.tensor_copy(we[:, 96:128], sml[:, 80:112]),
                deps=[tksb], tok=True)]
    tf = {}
    for j, nm in enumerate(("q", "k")):
        scol = sml[:, 16:80] if nm == "q" else sml[:, 80:144]
        swv = we[:, 0:64] if nm == "q" else we[:, 64:128]
        tf[nm] = [P.on("vector", lambda e, nm=nm, scol=scol: e.tensor_tensor(
            CS[nm], CS[nm], scol.unsqueeze(1).broadcast_to((128, T, DH)), ALU.mult),
            deps=[tcs[nm][0], tqsb, tksb], tok=True),
            P.on("vector", lambda e, nm=nm, swv=swv: e.tensor_tensor(
                SS[nm], SS[nm], swv.unsqueeze(1).broadcast_to((128, T, DH)), ALU.mult),
                deps=[tcs[nm][1]] + tsw, tok=True)]
    t1 = P.on("vector", lambda e: e.tensor_reduce(sml[:, 0:1], sml[:, 16:80], AXX, ALU.max,
                                                  apply_absolute_value=True),
              deps=[tqsb] + tf["q"], tok=True)
    t2 = P.on("vector", lambda e: e.tensor_reduce(sml[:, 1:2], sml[:, 80:144], AXX, ALU.max,
                                                  apply_absolute_value=True),
              deps=[tksb] + tf["k"], tok=True)
    t3m = P.on("vector", lambda e: e.tensor_tensor(sml[:, 2:3], sml[:, 0:1], sml[:, 1:2],
                                                   ALU.mult), deps=[t1, t2], tok=True)
    tnc = P.on("vector", lambda e: e.tensor_scalar(negC[:], sml[:, 2:3], -8.0, None,
                                                   ALU.mult, ALU.bypass), deps=[t3m], tok=True)
    tmem = P.on("gpsimd", lambda e: e.memset(vst3[:, :, 64:65], 1.0), tok=True)

    prev_sp = [tmem]
    prev_rot = {"q": None, "k": None}
    for tt in range(T):
        tls = shifted_load(P, st, yin, yname, tt, wa, deps=[prev_rot["k"]])
        tcst = P.on("scalar", lambda e: e.copy(wbf[:], wa[:]), deps=tls, tok=True)
        tsf = transpose_tile(P, st, wbf, lnf3, 6, [tcst])
        tmv = {}
        for g in range(6):
            bank = PB[g]
            for c in range(8):
                tmv[g] = P.on("tensor", lambda e, g=g, c=c, bank=bank: e.matmul(
                    bank[:], lnf3[:, c, :], wqkv3[:, c, g * 512:(g + 1) * 512],
                    start=(c == 0), stop=(c == 7)),
                    deps=([tsf, tw] + prev_sp) if c == 0 else (), tok=(c == 7), mode="full")
        tev = []
        for half in range(2):
            pv = PB[4 + half].ap().rearrange("p (h n) -> p h n", h=8)
            tev.append(P.on("scalar", lambda e, half=half, pv=pv: e.copy(
                vst3[:, half * 8:(half + 1) * 8, 0:64], pv),
                deps=[tmv[4 + half]] + prev_sp, tok=True))
        tvs = P.dma("sync", st["VSP"][tt], vst[:], deps=tev + P.wdeps("VSP"))
        tqr = rotary(P, st, tt, 0, CS["q"], SS["q"], st["QF"], "wbf4",
                     [tmv[0], tmv[1], prev_rot["q"]])
        tkr = rotary(P, st, tt, 2, CS["k"], SS["k"], st["KF"], "wbf5",
                     [tmv[2], tmv[3], tqr["ev"], prev_rot["k"]])
        prev_rot = {"q": tqr["sp"], "k": tkr["sp"]}
        prev_sp = [tvs, tqr["sp"], tkr["sp"]]

    if st["stop"] == "lattnA":
        return
    P.barrier()
    O1 = f32v(AR1, 0, ntok, rows=65)
    O2 = f32v(AR1, ntok, ntok, rows=65)
    rbig = f32v(AR1, 2 * ntok, ntok, rows=64)
    q2 = bf16v(AR2, 0, ntok)
    k2 = bf16v(AR2, ntok, ntok)
    v2p = bf16v(AR2, 2 * ntok, T * 130)
    v2p3 = v2p.rearrange("p (t n) -> p t n", t=T)
    obig = bf16v(AR2, 2 * ntok + T * 130, ntok, rows=64)
    oev = []
    for p in range(8):
        tqd = P.rd("QF", P.dma("sync", q2, st["QF"][p], deps=oev + P.wdeps("q2")))
        tkd = P.rd("KF", P.dma("sync", k2, st["KF"][p], deps=oev + P.wdeps("k2")))
        tvd = P.rd("VSP", P.dma("sync", v2p3,
                                st["VSP"][:, :, 130 * p:130 * (p + 1)].rearrange("t p n -> p t n"),
                                deps=oev + P.wdeps("v2p")))
        owtok = {}
        evtok = {}
        sbank_rd = {0: None, 1: None}
        slot_rd = {}
        txp = {}
        for t in range(T + 1):
            if t < T:
                W = min(384, ntok - 128 * t)
                for h in range(2):
                    sb_ = PB[h]
                    rows = slice(64 * h, 64 * h + 64)
                    sbank_rd[("S", h)] = P.on(
                        "tensor", lambda e, t=t, h=h, sb_=sb_, rows=rows, W=W: e.matmul(
                            sb_[:, 0:W], k2[rows, t * 128:(t + 1) * 128],
                            q2[rows, t * 128:t * 128 + W], start=True, stop=True),
                        deps=([tqd, tkd, tvd, tnc] if t == 0 else []) + [sbank_rd[h]],
                        tok=True, mode="r64")
            if t >= 1:
                tp = t - 1
                Wp = min(384, ntok - 128 * tp)
                nwp = (Wp + 127) // 128
                for h in range(2):
                    pslot = ptb[:, (2 * (tp % 2) + h) * 512:(2 * (tp % 2) + h) * 512 + 384]
                    ws = [w for w in range(tp, tp + nwp) if w < T]
                    for j, w in enumerate(ws):
                        bank = PB[2 + 3 * h + (w % 3)]
                        first = (tp == max(0, w - 2))
                        last = (tp == w)
                        dep = [txp[(h, tp % 2)]]
                        if first and w >= 3:
                            dep.append(evtok[(h, w - 3)])
                        tk_ = P.on("tensor", lambda e, bank=bank, h=h, tp=tp, j=j, first=first,
                                   last=last, pslot=pslot: e.matmul(
                            bank[0:65, 0:128], v2p3[:, tp, 65 * h:65 * h + 65],
                            pslot[:, j * 128:(j + 1) * 128], start=first, stop=last),
                            deps=dep, tok=(last or j == len(ws) - 1), mode="full")
                        if last:
                            owtok[(h, w)] = tk_
                        if j == len(ws) - 1:
                            slot_rd[(h, tp % 2)] = tk_
                    w = tp
                    bank = PB[2 + 3 * h + (w % 3)]
                    Odst = O1 if h == 0 else O2
                    evtok[(h, w)] = P.on("vector", lambda e, bank=bank, w=w, Odst=Odst:
                                         e.tensor_copy(Odst[:, w * 128:(w + 1) * 128],
                                                       bank[0:65, 0:128]),
                                         deps=[owtok[(h, w)]], tok=True)
            if t < T:
                W = min(384, ntok - 128 * t)
                for h in range(2):
                    sb_ = PB[h]
                    tS = sbank_rd[("S", h)]
                    wsf = (wa, wb)[h][:, (t % 2) * 512:(t % 2) * 512 + 384]
                    sbank_rd[h] = P.on("vector", lambda e, sb_=sb_, wsf=wsf, W=W:
                                       e.scalar_tensor_tensor(wsf[:, 0:W], sb_[:, 0:W], 8.0,
                                                              st["maskb"][:, 0:W],
                                                              ALU.mult, ALU.add),
                                       deps=[tS, slot_rd.get((h, t % 2))], tok=True)
                    pslot = ptb[:, (2 * (t % 2) + h) * 512:(2 * (t % 2) + h) * 512 + 384]
                    txp[(h, t % 2)] = P.on(
                        "scalar", lambda e, pslot=pslot, wsf=wsf, W=W: e.activation(
                            pslot[:, 0:W], wsf[:, 0:W], AF.Exp, bias=negC[:], scale=1.0),
                        deps=[sbank_rd[h], slot_rd.get((h, t % 2))], tok=True)
        oev = []
        for h in range(2):
            Osrc = O1 if h == 0 else O2
            tch = []
            for qg in range(ntok // 512):
                tt1 = P.on("vector", lambda e, Osrc=Osrc, qg=qg: e.reciprocal(
                    rr[:, 0:512], Osrc[64:65, qg * 512:(qg + 1) * 512]),
                    deps=[evtok[(h, w)] for w in range(4 * qg, min(T, 4 * qg + 4))]
                    + P.wdeps("lrr"), tok=True)
                tch.append(P.rd("lrr",
                                P.dma("sync", st["BNC"][h:h + 1, qg * 512:(qg + 1) * 512],
                                      rr[0:1, 0:512], deps=[tt1] + P.wdeps(f"LBNC{h}"))))
            tb = P.rd(f"LBNC{h}", P.dma("sync", rbig,
                                        st["BNC"][h:h + 1, :].partition_broadcast(64),
                                        deps=tch + P.wdeps("rbig")))
            t4 = P.rd("rbig", P.on("vector", lambda e, Osrc=Osrc: e.tensor_tensor(
                obig, Osrc[0:64, :], rbig, ALU.mult),
                deps=[tb] + P.wdeps("obig"), tok=True))
            t5 = P.rd("obig", P.dma("sync", st["OF"][p, 64 * h:64 * h + 64, :], obig,
                                    deps=[t4] + P.wdeps("OF")))
            oev += [t4, t5]

    if st["stop"] == "lattnB":
        return
    P.barrier()
    wo = bf16v(AR2, 0, 8 * DM)
    wo3 = wo.rearrange("p (c n) -> p c n", c=8)
    ofm3 = st["wbf3"].ap().rearrange("p (c n) -> p c n", c=8)
    two = P.dma("gpsimd", wo3, din["f_wo"][li])
    prev = []
    for tt in range(T):
        r = slice(tt * 128, tt * 128 + 128)
        tof = P.rd("OF", P.dma("sync", ofm3,
                               st["OF"][:, :, tt * 128:(tt + 1) * 128].rearrange("c p n -> p c n"),
                               deps=prev))
        txl = P.rd(yname, P.dma("sync", wc[:], yin[r, :], deps=prev))
        tmm = {}
        for half in range(2):
            bank = PB[half]
            for c in range(8):
                tmm[half] = P.on("tensor", lambda e, c=c, half=half, bank=bank: e.matmul(
                    bank[:], ofm3[:, c, :], wo3[:, c, half * 512:(half + 1) * 512],
                    start=(c == 0), stop=(c == 7)),
                    deps=([tof, two] + prev) if c == 0 else (), tok=(c == 7), mode="full")
        prev = []
        for half in range(2):
            tr = P.on("vector", lambda e, half=half: e.tensor_tensor(
                wd[:, half * 512:(half + 1) * 512], PB[half][:],
                wc[:, half * 512:(half + 1) * 512], ALU.add),
                deps=[tmm[half], txl], tok=True)
            prev.append(tr)
        prev.append(P.dma("sync", yout[r, :], wd[:], deps=prev + P.wdeps(yoname)))


def rotary(P, st, tt, bank0, CS3, SS3, dstF, stg_name, deps):
    wa, wc, wd, we = st["wa"], st["wc"], st["wd"], st["we"]
    stat = st["stat"]; PB = st["PB"]
    stg = st[stg_name]
    t1 = P.on("scalar", lambda e: e.copy(wc[:, 0:512], PB[bank0][:]), deps=deps, tok=True)
    t2 = P.on("scalar", lambda e: e.copy(wc[:, 512:1024], PB[bank0 + 1][:]), tok=True)
    t3 = P.on("scalar", lambda e: e.activation(wd[:, 0:512], PB[bank0][:], AF.Square), tok=True)
    t4 = P.on("scalar", lambda e: e.activation(wd[:, 512:1024], PB[bank0 + 1][:], AF.Square),
              tok=True)
    wc3 = wc.ap().rearrange("p (h d) -> p h d", d=DH)
    wd3 = wd.ap().rearrange("p (h d) -> p h d", d=DH)
    t5 = P.on("vector", lambda e: e.tensor_reduce(stat[:, 0:16], wd3, AXX, ALU.add),
              deps=[t3, t4], tok=True)
    t6 = P.on("vector", lambda e: e.tensor_scalar(stat[:, 0:16], stat[:, 0:16], 1.42e-24,
                                                  None, ALU.max, ALU.bypass), deps=[t5], tok=True)
    t7 = P.on("scalar", lambda e: e.activation(stat[:, 0:16], stat[:, 0:16], AF.Ln,
                                               scale=1.0), deps=[t6], tok=True)
    t8 = P.on("scalar", lambda e: e.activation(stat[:, 0:16], stat[:, 0:16], AF.Exp,
                                               scale=-0.5), deps=[t7], tok=True)
    rb = stat[:, 0:16].unsqueeze(2).broadcast_to((128, 16, DH))
    tu = P.on("vector", lambda e: e.tensor_tensor(wd3, wc3, rb, ALU.mult),
              deps=[t1, t2, t8], tok=True)
    wd4 = wd.ap().rearrange("p (h s d2) -> p h s d2", s=2, d2=32)
    wa4 = wa.ap().rearrange("p (h s d2) -> p h s d2", s=2, d2=32)
    g1 = P.on("vector", lambda e: e.tensor_tensor(
        we.ap().rearrange("p (h d) -> p h d", d=DH), wd3,
        CS3[:, tt, :].unsqueeze(1).broadcast_to((128, 16, DH)), ALU.mult),
        deps=[tu], tok=True)
    g2 = P.on("vector", lambda e: e.tensor_tensor(
        wa4[:, :, 0, :], wd4[:, :, 1, :],
        SS3[:, tt, 0:32].unsqueeze(1).broadcast_to((128, 16, 32)), ALU.mult),
        deps=[g1], tok=True)
    g3 = P.on("vector", lambda e: e.tensor_tensor(
        wa4[:, :, 1, :], wd4[:, :, 0, :],
        SS3[:, tt, 32:64].unsqueeze(1).broadcast_to((128, 16, 32)), ALU.mult),
        deps=[g2], tok=True)
    g4 = P.on("vector", lambda e: e.tensor_tensor(st["wbf"][:], we[:], wa[:], ALU.add),
              deps=[g1, g2, g3], tok=True)
    tev = transpose_tile(P, st, st["wbf"], stg.ap().rearrange("p (c n) -> p c n", c=8),
                         7, [g4])
    tsp = P.dma("sync", dstF.rearrange("c p n -> p c n")[:, :, tt * 128:(tt + 1) * 128],
                stg.ap().rearrange("p (c n) -> p c n", c=8), deps=[tev])
    return {"sp": tsp, "mid": g4, "ev": tev}


def ph_lffn(P, st, li, yin, yname, yout, yoname, inv):
    din = st["din"]; T = st["T"]; ntok = st["ntok"]
    AR1, AR2, PB = st["AR1"], st["AR2"], st["PB"]
    wa, wb, wc, wd = st["wa"], st["wb"], st["wc"], st["wd"]
    wbf, gsb = st["wbf"], st["gsb"]
    P.barrier()

    w1 = bf16v(AR1, 0, 8 * W1C)
    w13 = w1.rearrange("p (c n) -> p c n", c=8)
    w2 = bf16v(AR2, 0, 22 * DM)
    w23 = w2.rearrange("p (c n) -> p c n", c=22)
    lngb = f32v(AR2, (22 * DM) // 2, DM)
    mst = bf16v(AR2, 22 * DM + 2 * DM, DI)
    mfst = bf16v(AR2, 22 * DM + 2 * DM + DI, 22 * 128)
    assert 22 * DM + 2 * DM + DI + 22 * 128 <= 30208
    mfst3 = mfst.rearrange("p (c n) -> p c n", c=22)

    tw1 = P.dma("gpsimd", w13, din["f_w1"][li])
    tw2 = P.dma("gpsimd", w23, din["f_w2"][li])
    tb2 = P.dma("gpsimd", st["b2row"][:], din["f_b2"][li:li + 1, :])
    tlg = bcast_col(P, st, din["f_lng"][li:li + 1, :], lngb)
    nch = (W1C + 511) // 512
    if inv == 0:
        lnb_src = din["f_lnb"][li:li + 1, :].rearrange("o (c p) -> p (o c)", p=128)
        tlb = P.on("gpsimd", lambda e: e.dma_start(out=st["lnbc"][:], in_=lnb_src,
                                                   allow_slow_non_contiguous=True),
                   tok=True, dma=True)
        twr_last = None
        twr = []
        for j in range(nch):
            Wj = min(512, W1C - 512 * j)
            tbr = P.dma("sync", st["rr"][0:1, 0:Wj],
                        din["f_b1"][li:li + 1, 512 * j:512 * j + Wj], deps=[twr_last])
            tmj = None
            for c in range(8):
                tmj = P.on("tensor", lambda e, j=j, c=c, Wj=Wj: e.matmul(
                    PB[0][0:1, 0:Wj], st["lnbc"][:, c:c + 1],
                    w13[:, c, 512 * j:512 * j + Wj], start=(c == 0), stop=(c == 7)),
                    deps=([tlb, tw1, twr_last]) if c == 0 else (), tok=(c == 7), mode="m1")
            twr_last = P.on("vector", lambda e, j=j, Wj=Wj: e.tensor_tensor(
                st["wrow"][0:1, 512 * j:512 * j + Wj], PB[0][0:1, 0:Wj],
                st["rr"][0:1, 0:Wj], ALU.add), deps=[tmj, tbr], tok=True)
            twr.append(twr_last)
        twr = [P.dma("sync", st["WRD"][li], st["wrow"][:], deps=twr)]
    else:
        twr = [P.dma("sync", st["wrow"][:], st["WRD"][li])]

    prev = []
    prev_aj = []
    prev_mev = []
    prevln = {0: [], 1: []}
    prev_t2 = None
    for tt in range(T):
        r = slice(tt * 128, tt * 128 + 128)
        par = tt % 2
        ld = (wa, st["we"])[par]
        lnfs = (st["wbf3"], st["wbf5"])[par]
        lnf3 = lnfs.ap().rearrange("p (c n) -> p c n", c=8)
        tls = shifted_load(P, st, yin, yname, tt, ld, deps=prevln[par])
        negmu, rstd, tst = ln_stats(P, st, ld, wb[:], LNEPS, tls, sb=12 * par)
        t1 = P.on("vector", lambda e, ld=ld, negmu=negmu, rstd=rstd: e.tensor_scalar(
            wb[:], ld[:], negmu, rstd, ALU.add, ALU.mult), deps=tst, tok=True)
        t2 = P.on("vector", lambda e: e.scalar_tensor_tensor(wbf[:], wb[:], 1.0, lngb,
                                                             ALU.mult, ALU.mult),
                  deps=[t1, tlg, prev_t2], tok=True)
        prev_t2 = t2
        prevln[par] = [t1, tst[2]]
        tsf = transpose_tile(P, st, wbf, lnf3, 6, [t2])
        ajs = []
        for j in range(6):
            a0 = 512 * j
            aW = min(512, DI - a0)
            tmj = {}
            for (idx, c0_) in ((0, a0), (1, DI + a0)):
                bank = PB[2 * (j % 2) + idx]
                bankrel = ajs[j - 2] if j >= 2 else (prev_aj[j + 4] if prev_aj else None)
                for c in range(8):
                    tmj[idx] = P.on("tensor",
                                    lambda e, c=c, c0_=c0_, aW=aW, bank=bank, lnf3=lnf3:
                                    e.matmul(bank[:, 0:aW], lnf3[:, c, :],
                                             w13[:, c, c0_:c0_ + aW],
                                             start=(c == 0), stop=False),
                                    deps=([tsf, tw1, bankrel] + twr + prev)
                                    if c == 0 else (), mode="full")
                tmj[idx] = P.on("tensor", lambda e, c0_=c0_, aW=aW, bank=bank: e.matmul(
                    bank[:, 0:aW], st["ones_row"][0:1, 0:128], st["wrow"][0:1, c0_:c0_ + aW],
                    start=False, stop=True), deps=twr, tok=True, mode="k1")
            gb = PB[2 * (j % 2) + 1]
            ab = PB[2 * (j % 2)]
            grel = ajs[j - 1] if j >= 1 else (prev_aj[-1] if prev_aj else None)
            tg = P.on("scalar", lambda e, gb=gb, aW=aW: e.activation(
                gsb[:, 0:aW], gb[:, 0:aW], AF.Gelu), deps=[tmj[1], grel], tok=True)
            ajs.append(P.on("vector", lambda e, ab=ab, a0=a0, aW=aW: e.tensor_tensor(
                mst[:, a0:a0 + aW], ab[:, 0:aW], gsb[:, 0:aW], ALU.mult),
                deps=[tmj[0], tg], tok=True))
        prev_aj = ajs
        tmev = []
        pv = PB[7].ap().bitcast(BF16)
        for grp in range(3):
            ch0 = grp * 8
            nc_ = min(8, 22 - ch0)
            tlast = None
            for cc in range(nc_):
                c2 = ch0 + cc
                cw = min(128, DI - c2 * 128)
                tlast = P.on("tensor", lambda e, cc=cc, c2=c2, cw=cw, pv=pv: e.transpose(
                    pv[0:cw, cc * 128:(cc + 1) * 128], mst[:, c2 * 128:c2 * 128 + cw],
                    st["ident"][:]),
                    deps=(ajs + [st["pt_rd"].get(7)]) if cc == 0 else (),
                    tok=(cc == nc_ - 1), mode="tr")
            pvv = pv.rearrange("p (c n) -> p c n", c=8)[:, 0:nc_, :]
            tmev.append(P.on("scalar", lambda e, ch0=ch0, nc_=nc_, pvv=pvv: e.copy(
                mfst3[:, ch0:ch0 + nc_, :], pvv), deps=[tlast], tok=True))
            st["pt_rd"][7] = tmev[-1]
        prev_mev = tmev
        txl = P.rd(yname, P.dma("sync", wc[:], yin[r, :], deps=prev))
        tm2 = {}
        for half in range(2):
            bank = PB[4 + half]
            for c2 in range(22):
                tm2[half] = P.on("tensor", lambda e, c2=c2, half=half, bank=bank: e.matmul(
                    bank[:], mfst[:, c2 * 128:(c2 + 1) * 128],
                    w23[:, c2, half * 512:(half + 1) * 512],
                    start=(c2 == 0), stop=False),
                    deps=(tmev + [tw2] + prev) if c2 == 0 else (), mode="full")
            tm2[half] = P.on("tensor", lambda e, half=half, bank=bank: e.matmul(
                bank[:], st["ones_row"][0:1, 0:128],
                st["b2row"][0:1, half * 512:(half + 1) * 512],
                start=False, stop=True), deps=[tb2], tok=True, mode="k1")
        prev = []
        for half in range(2):
            tr = P.on("vector", lambda e, half=half: e.tensor_tensor(
                wd[:, half * 512:(half + 1) * 512], PB[4 + half][:],
                wc[:, half * 512:(half + 1) * 512], ALU.add),
                deps=[tm2[half], txl], tok=True)
            prev.append(tr)
        prev.append(P.dma("sync", yout[r, :], wd[:], deps=prev + P.wdeps(yoname)))


_CACHE = {}


def kernel(**inputs):
    x = np.asarray(inputs["L0_attn_a_x"])
    B, N, E = x.shape
    key = (N,)
    if key not in _CACHE:
        nc, P = build(ntok=N)
        _CACHE[key] = (nc, host_tables(N))
    nc, tables = _CACHE[key]
    w = prep_weights(inputs)
    base = {}
    base.update(w)
    base.update(tables)
    in_maps = []
    for c in range(B):
        m = dict(base)
        m["x0"] = np.ascontiguousarray(x[c])
        in_maps.append(m)
    res = run_bass_kernel_spmd(nc, in_maps, core_ids=list(range(B)))
    return np.stack([np.asarray(r["out"]) for r in res.results], axis=0)


# version 07
# speedup vs baseline: 2.1972x; 1.0096x over previous
import numpy as np
import ml_dtypes
import concourse.bass as bass
import concourse.mybir as mybir
from concourse.bass_utils import run_bass_kernel_spmd

F32 = mybir.dt.float32
BF16 = mybir.dt.bfloat16
AF = mybir.ActivationFunctionType
ALU = mybir.AluOpType
AXX = mybir.AxisListType.X
BFNP = ml_dtypes.bfloat16

DM = 1024
NH = 16
DH = 64
DEPTH = 6
DI = 2730
W1C = 5460
WIN = 256
NEG = -1.0e38
LNEPS = 1e-5
RMSEPS = float(np.finfo(np.float32).eps)
C_ATTN_A = 30.0

ENGS = ("sync", "tensor", "vector", "scalar", "gpsimd")


def host_tables(ntok):
    t = {}
    t["ident"] = np.eye(128, dtype=BFNP)
    t["ones_row"] = np.ones((1, 512), dtype=BFNP)
    p = np.arange(128)[:, None]
    c = np.arange(384)[None, :]
    t["maskb"] = np.where((c >= p) & (c <= p + WIN), 0.0, NEG).astype(np.float32)
    u = np.arange(2 * ntok - 1)[None, :]
    t["dist"] = np.abs(u - (ntok - 1) - np.arange(128)[:, None]).astype(np.float32)
    half = np.arange(0, DH, 2, dtype=np.float64)
    inv_freq = 1.0 / 10000.0 ** (half / DH)
    tt = np.arange(ntok, dtype=np.float64)
    freqs = tt[:, None] * inv_freq[None, :]
    freqs = np.concatenate([freqs, freqs], -1)
    base_scale = (half + 0.4 * DH) / (1.4 * DH)
    power = (tt - ntok // 2) / (WIN // 2)
    scale = base_scale[None, :] ** power[:, None]
    scale = np.concatenate([scale, scale], -1)
    cos = np.cos(freqs)
    sin = np.sin(freqs)
    sgn = np.where(np.arange(DH) < DH // 2, -1.0, 1.0)[None, :]
    t["cs_q"] = (cos * scale).astype(np.float32)
    t["ss_q"] = (sin * scale * sgn).astype(np.float32)
    t["cs_k"] = (cos / scale).astype(np.float32)
    t["ss_k"] = (sin / scale * sgn).astype(np.float32)
    return t


def prep_weights(inp, depth=DEPTH):
    w = {}
    cont = np.ascontiguousarray

    def pmaj(a):
        E, F = a.shape
        return cont(a.reshape(E // 128, 128, F).transpose(1, 0, 2))

    w["aw_qkv"] = pmaj(np.asarray(inp["L0_attn_a_Wqkv_w"]).T.copy())
    w["ab_qkv"] = cont(np.asarray(inp["L0_attn_a_Wqkv_b"])[None, :])
    w["aw_out"] = pmaj(np.asarray(inp["L0_attn_a_out_w"]).T.copy())
    w["ab_out"] = cont(np.asarray(inp["L0_attn_a_out_b"])[None, :])
    w["a_dd"] = cont(np.asarray(inp["L0_attn_a_distance_decay"])[None, :])
    w["bw_qkv"] = pmaj(np.concatenate(
        [np.asarray(inp["L0_attn_b_Wq"]).T, np.asarray(inp["L0_attn_b_Wk"]).T,
         np.asarray(inp["L0_attn_b_Wv"]).T], axis=1))
    w["bw_out"] = pmaj(np.asarray(inp["L0_attn_b_Wo"]).T.copy())
    w["b_qn"] = cont(np.asarray(inp["L0_attn_b_qn_w"])[None, :])
    w["b_kn"] = cont(np.asarray(inp["L0_attn_b_kn_w"])[None, :])
    w["f_wqkv"] = np.stack([pmaj(np.asarray(inp["L0_ffn_Wqkv"])[i]) for i in range(depth)])
    w["f_wo"] = np.stack([pmaj(np.asarray(inp["L0_ffn_Wo"])[i]) for i in range(depth)])
    w["f_w1"] = np.stack([pmaj(np.asarray(inp["L0_ffn_W1"])[i]) for i in range(depth)])
    w2 = np.asarray(inp["L0_ffn_W2"])
    w2p = np.zeros((depth, 2816, DM), np.float32)
    w2p[:, :DI] = w2[:depth]
    w["f_w2"] = np.stack([pmaj(w2p[i]) for i in range(depth)])
    w["f_qs"] = cont(np.asarray(inp["L0_ffn_q_scale"])[:depth])
    w["f_ks"] = cont(np.asarray(inp["L0_ffn_k_scale"])[:depth])
    w["f_lng"] = cont(np.asarray(inp["L0_ffn_ln_g"])[:depth])
    w["f_lnb"] = cont(np.asarray(inp["L0_ffn_ln_b"])[:depth])
    w["f_b1"] = cont(np.asarray(inp["L0_ffn_b1"])[:depth])
    w["f_b2"] = cont(np.asarray(inp["L0_ffn_b2"])[:depth])
    return w


class Prog:
    DPOOL = {"sync": 8, "gpsimd": 4, "scalar": 4}
    SEMN = (["sync", "tensor", "vector", "scalar", "gpsimd"]
            + [f"d{q}{i}" for q, n in DPOOL.items() for i in range(n)])
    SAFE_SAME_ENGINE = False
    SAFE_ENGINES = {"vector"}

    def __init__(self):
        self.q = {e: [] for e in ENGS}
        self.cnt = {}
        self.waited = {e: {} for e in ENGS}
        self.sems = {}
        self.ninst = 0
        self.scr_rd = {}

    def _dma_sem(self, eng):
        if not hasattr(self, "_dma_rr"):
            self._dma_rr = {}
        i = self._dma_rr.get(eng, 0)
        self._dma_rr[eng] = (i + 1) % self.DPOOL[eng]
        return f"d{eng}{i}"

    def wait(self, eng, deps):
        for d in deps:
            if d is None:
                continue
            s, v = d
            if self.waited[eng].get(s, 0) < v:
                self.waited[eng][s] = v

                def mk(s=s, v=v):
                    return lambda e: e.wait_ge(self.sems[s], v)
                self.q[eng].append(mk())
                self.ninst += 1

    def on(self, eng, fn, deps=(), tok=False, dma=False, mode=None):
        if eng == "tensor" and not dma:
            assert mode is not None, "tensor ops must declare a tiling mode"
            if mode != getattr(self, "_pe_mode", None):
                lp = getattr(self, "_last_pe", None)
                if lp is not None:
                    s, v = lp

                    def mkd(s=s, v=v):
                        return lambda e: e.wait_ge(self.sems[s], v)
                    self.q[eng].append(mkd())
                    self.ninst += 1
                    self.waited[eng][s] = max(self.waited[eng].get(s, 0), v)
            self._pe_mode = mode
            tok = True
        self.wait(eng, deps)
        self.ninst += 1
        if not hasattr(self, "last_tok"):
            self.last_tok = {}
            self.clean = {}
        if not tok:
            self.q[eng].append(fn)
            self.clean[eng] = False
            return None
        s = self._dma_sem(eng) if dma else eng
        inc = 16 if dma else 1
        if dma:
            self.wait(eng, [(s, self.cnt.get(s, 0))])
        self.cnt[s] = self.cnt.get(s, 0) + inc
        v = self.cnt[s]

        def mk(fn=fn, s=s, inc=inc):
            return lambda e: fn(e).then_inc(self.sems[s], inc)
        self.q[eng].append(mk())
        if not dma and not self.SAFE_SAME_ENGINE and eng not in self.SAFE_ENGINES:
            self.waited[eng][s] = v
        if eng == "tensor" and not dma:
            self._last_pe = (s, v)
        if not dma:
            self.last_tok[eng] = (s, v)
            self.clean[eng] = True
        else:
            self.last_tok[eng] = (s, v)
            self.clean[eng] = True
        return (s, v)

    def dma(self, eng, out, in_, deps=()):
        return self.on(eng, lambda e, o=out, i=in_: e.dma_start(out=o, in_=i),
                       deps=deps, tok=True, dma=True)

    def rd(self, name, tok):
        self.scr_rd.setdefault(name, []).append(tok)
        return tok

    def wdeps(self, name):
        return self.scr_rd.pop(name, [])

    def barrier(self):
        if not hasattr(self, "last_tok"):
            self.last_tok = {}
            self.clean = {}
        for eng in ENGS:
            if self.q[eng] and not self.clean.get(eng, True):
                self.on(eng, lambda e: e.nop(), tok=True)
        snap = [(s, self.cnt.get(s, 0)) for s in self.SEMN if self.cnt.get(s, 0) > 0]
        for eng in ENGS:
            self.wait(eng, snap)

    def finish(self, blk):
        for name in ENGS:
            ops = self.q[name]

            def mksec(ops=ops):
                def section(e):
                    for f in ops:
                        f(e)
                return section
            getattr(blk, name)(mksec())


def build(ntok=4096, depth=DEPTH, parts=("a", "m1", "b", "m2"), dbg_out=(), stop=None):
    T = ntok // 128
    nc = bass.Bass("TRN2", target_bir_lowering=False)
    P = Prog()
    din = {}

    def dt_in(name, shape, dtype=F32):
        din[name] = nc.dram_tensor(name, list(shape), dtype, kind="ExternalInput").ap()

    dt_in("x0", (ntok, DM))
    dt_in("aw_qkv", (128, 8, 3 * DM)); dt_in("ab_qkv", (1, 3 * DM))
    dt_in("aw_out", (128, 8, DM)); dt_in("ab_out", (1, DM)); dt_in("a_dd", (1, 1))
    dt_in("bw_qkv", (128, 8, 3 * DM)); dt_in("bw_out", (128, 8, DM))
    dt_in("b_qn", (1, DH)); dt_in("b_kn", (1, DH))
    dt_in("f_wqkv", (depth, 128, 8, 3 * DM)); dt_in("f_wo", (depth, 128, 8, DM))
    dt_in("f_w1", (depth, 128, 8, W1C)); dt_in("f_w2", (depth, 128, 22, DM))
    dt_in("f_qs", (depth, DH)); dt_in("f_ks", (depth, DH))
    dt_in("f_lng", (depth, DM)); dt_in("f_lnb", (depth, DM))
    dt_in("f_b1", (depth, W1C)); dt_in("f_b2", (depth, DM))
    dt_in("ident", (128, 128), BF16); dt_in("ones_row", (1, 512), BF16)
    dt_in("maskb", (128, 384)); dt_in("dist", (128, 2 * ntok - 1))
    dt_in("cs_q", (ntok, DH)); dt_in("ss_q", (ntok, DH))
    dt_in("cs_k", (ntok, DH)); dt_in("ss_k", (ntok, DH))
    dout = nc.dram_tensor("out", [ntok, DM], F32, kind="ExternalOutput").ap()

    def dt_sc(name, shape, dtype=F32):
        kind = "ExternalOutput" if name in dbg_out else "Internal"
        return nc.dram_tensor(name, list(shape), dtype, kind=kind).ap()

    st = dict(ntok=ntok, T=T, depth=depth, nc=nc, din=din, dout=dout, stop=stop)
    st["X1"] = dt_sc("X1", (ntok, DM))
    st["X2"] = dt_sc("X2", (ntok, DM))
    st["X3"] = dt_sc("X3", (ntok, DM))
    st["YA"] = dt_sc("YA", (ntok, DM))
    st["YB"] = dt_sc("YB", (ntok, DM))
    st["QF"] = dt_sc("QF", (8, 128, ntok), BF16)
    st["KF"] = dt_sc("KF", (8, 128, ntok), BF16)
    st["VSP"] = dt_sc("VSP", (T, 128, NH * 65), BF16)
    st["OF"] = dt_sc("OF", (8, 128, ntok), BF16)
    st["BNC"] = dt_sc("BNC", (8, ntok))
    st["WRD"] = dt_sc("WRD", (depth, W1C), BF16)

    from contextlib import ExitStack
    ctx = ExitStack()
    sb = lambda name, shape, dtype: ctx.enter_context(nc.sbuf_tensor("s_" + name, list(shape), dtype))
    with ctx:
        st["AR1"] = sb("AR1", [128, 44544], BF16)
        st["AR2"] = sb("AR2", [128, 30208], BF16)
        st["ident"] = sb("ident", [128, 128], BF16)
        st["ones_row"] = sb("ones_row", [1, 512], BF16)
        st["maskb"] = sb("maskb", [128, 384], F32)
        st["wa"] = sb("wa", [128, 1024], F32)
        st["wb"] = sb("wb", [128, 1024], F32)
        st["wc"] = sb("wc", [128, 1024], F32)
        st["wd"] = sb("wd", [128, 1024], F32)
        st["we"] = sb("we", [128, 1024], F32)
        st["wbf"] = sb("wbf", [128, 1024], BF16)
        st["wbf3"] = sb("wbf3", [128, 1024], BF16)
        st["wbf4"] = sb("wbf4", [128, 1024], BF16)
        st["wbf5"] = sb("wbf5", [128, 1024], BF16)
        st["vst"] = sb("vst", [128, NH * 65], BF16)
        st["gsb"] = sb("gsb", [128, 512], F32)
        st["stat"] = sb("stat", [128, 32], F32)
        st["sml"] = sb("sml", [128, 160], F32)
        st["ptb"] = sb("ptb", [128, 4 * 512], BF16)
        st["wrow"] = sb("wrow", [1, W1C], BF16)
        st["b2row"] = sb("b2row", [1, DM], BF16)
        st["absl"] = sb("absl", [1, 3 * DM], BF16)
        st["lnbc"] = sb("lnbc", [128, 8], BF16)
        st["qnc"] = sb("qnc", [128, 1], F32)
        st["knc"] = sb("knc", [128, 1], F32)
        st["negC"] = sb("negC", [128, 1], F32)
        st["onec"] = sb("onec", [128, 1], BF16)
        st["rr"] = sb("rr", [1, 512], F32)
        st["sconst"] = sb("sconst", [128, 4], F32)
        for col, val in enumerate([LNEPS, RMSEPS, -C_ATTN_A]):
            nc.const_aps.register(F32, float(val), st["sconst"].ap()[:, col:col + 1])
        st["PB"] = [ctx.enter_context(nc.psum_tensor(f"pb{i}", [128, 512], F32))
                    for i in range(8)]
        st["pt_rd"] = {}
        for sname in Prog.SEMN:
            P.sems[sname] = ctx.enter_context(nc.semaphore(sname))
        blk = ctx.enter_context(nc.Block())
        emit_program(P, st, parts)
        P.finish(blk)
    return nc, P


def f32v(arena, c0f, ncols, rows=128, p0=0):
    return arena.ap().bitcast(F32)[p0:p0 + rows, c0f:c0f + ncols]


def bf16v(arena, c0, ncols, rows=128, p0=0):
    return arena[p0:p0 + rows, c0:c0 + ncols]


def ln_stats(P, st, xt, dump, eps, deps, sb=0):
    stat = st["stat"]
    t1 = P.on("vector", lambda e: e.tensor_reduce(stat[:, sb + 0:sb + 1], xt[:], AXX, ALU.add),
              deps=deps, tok=True)
    t2 = P.on("scalar", lambda e: e.activation(dump, xt[:], AF.Square,
                                               accum_out=stat[:, sb + 1:sb + 2]),
              deps=deps, tok=True)
    t4 = P.on("vector", lambda e: e.tensor_scalar(stat[:, sb + 2:sb + 3], stat[:, sb + 0:sb + 1],
                                                  -1.0 / 1024, None, ALU.mult, ALU.bypass),
              deps=[t1], tok=True)
    t5 = P.on("vector", lambda e: e.tensor_tensor(stat[:, sb + 3:sb + 4], stat[:, sb + 2:sb + 3],
                                                  stat[:, sb + 2:sb + 3], ALU.mult),
              deps=[t4], tok=True)
    t6 = P.on("vector", lambda e: e.tensor_scalar(stat[:, sb + 4:sb + 5], stat[:, sb + 1:sb + 2],
                                                  1.0 / 1024, None, ALU.mult, ALU.bypass),
              deps=[t2], tok=True)
    t7 = P.on("vector", lambda e: e.tensor_tensor(stat[:, sb + 5:sb + 6], stat[:, sb + 4:sb + 5],
                                                  stat[:, sb + 3:sb + 4], ALU.subtract),
              deps=[t5, t6], tok=True)
    t8 = P.on("scalar", lambda e: e.activation(stat[:, sb + 6:sb + 7], stat[:, sb + 5:sb + 6],
                                               AF.Ln, bias=float(eps), scale=1.0),
              deps=[t7], tok=True)
    t9 = P.on("scalar", lambda e: e.activation(stat[:, sb + 7:sb + 8], stat[:, sb + 6:sb + 7],
                                               AF.Exp, scale=-0.5), deps=[t8], tok=True)
    return stat[:, sb + 2:sb + 3], stat[:, sb + 7:sb + 8], [t4, t9, t2]


def transpose_tile(P, st, src_bf, dst_strided, bank_idx, deps, nchunk=8, dst_chunks=None):
    bank = st["PB"][bank_idx]
    pv = bank.ap().bitcast(BF16)
    alldeps = list(deps) + [st["pt_rd"].get(bank_idx)]
    tl = None
    for c in range(nchunk):
        tl = P.on("tensor", lambda e, c=c: e.transpose(
            pv[:, c * 128:(c + 1) * 128], src_bf[:, c * 128:(c + 1) * 128], st["ident"][:]),
            deps=alldeps if c == 0 else (), tok=(c == nchunk - 1), mode="tr")
    pvv = pv.rearrange("p (c n) -> p c n", c=8)[:, 0:nchunk, :]
    tev = P.on("scalar", lambda e: e.copy(dst_strided, pvv), deps=[tl], tok=True)
    st["pt_rd"][bank_idx] = tev
    return tev


def shifted_load(P, st, ysrc, yname, tt, dst, deps=()):
    r0 = tt * 128
    t1 = P.rd(yname, P.dma("sync", dst[:, 0:512], ysrc[r0:r0 + 128, 0:512], deps=deps))
    if tt == 0:
        t2 = P.rd(yname, P.dma("sync", dst[1:128, 512:1024], ysrc[0:127, 512:1024]))
        t3 = P.on("gpsimd", lambda e: e.memset(dst[0:1, 512:1024], 0.0), deps=deps, tok=True)
        return [t1, t2, t3]
    t2 = P.rd(yname, P.dma("sync", dst[:, 512:1024], ysrc[r0 - 1:r0 + 127, 512:1024]))
    return [t1, t2]


def bcast_col(P, st, dram_row_ap, dst, deps=()):
    return P.dma("sync", dst, dram_row_ap.partition_broadcast(dst.shape[0]), deps=deps)


def emit_program(P, st, parts):
    din = st["din"]
    P.dma("sync", st["ident"][:], din["ident"][:])
    P.dma("sync", st["ones_row"][:], din["ones_row"][:])
    P.dma("sync", st["maskb"][:], din["maskb"][:])
    P.on("gpsimd", lambda e: e.memset(st["onec"][:], 1.0), tok=True)
    for col, val in enumerate([LNEPS, RMSEPS, -C_ATTN_A]):
        P.on("vector", lambda e, col=col, val=val: e.memset(st["sconst"][:, col:col + 1],
                                                            float(val)), tok=True)
    P.barrier()

    x_cur, x_name = din["x0"], "x0"
    if "a" in parts:
        ph_dense(P, st, "a", x_cur, x_name, st["X1"], "X1")
        x_cur, x_name = st["X1"], "X1"
    if "m1" in parts:
        ph_ffnmod(P, st, x_cur, x_name, st["X2"], "X2", inv=0)
        x_cur, x_name = st["X2"], "X2"
    if "b" in parts:
        ph_dense(P, st, "b", x_cur, x_name, st["X3"], "X3")
        x_cur, x_name = st["X3"], "X3"
    if "m2" in parts:
        ph_ffnmod(P, st, x_cur, x_name, st["dout"], "out", inv=1)
        x_cur = st["dout"]
    if x_cur is not st["dout"]:
        P.barrier()
        prev = []
        for tt in range(st["T"]):
            r = slice(tt * 128, tt * 128 + 128)
            t = P.dma("sync", st["wc"][:], x_cur[r, :], deps=prev)
            tc = P.on("vector", lambda e: e.tensor_copy(st["wd"][:], st["wc"][:]),
                      deps=[t], tok=True)
            prev = [tc, P.dma("sync", st["dout"][r, :], st["wd"][:], deps=[tc])]
    P.wait("tensor", [(s, P.cnt.get(s, 0)) for s in Prog.SEMN if s.startswith("d")])
    P.on("tensor", lambda e: e.nop(), mode="nop")


def ph_dense(P, st, var, x_in, x_name, x_out, xo_name):
    din = st["din"]; T = st["T"]; ntok = st["ntok"]
    AR1, AR2, PB = st["AR1"], st["AR2"], st["PB"]
    wa, wb, wc, wd = st["wa"], st["wb"], st["wc"], st["wd"]
    wbf, vst, ptb = st["wbf"], st["vst"], st["ptb"]
    negC, sml, rr = st["negC"], st["sml"], st["rr"]
    P.barrier()

    lnfm = bf16v(AR1, 0, 8 * ntok)
    lnfm3 = lnfm.rearrange("p (c n) -> p c n", c=8)
    wv = bf16v(AR1, 8 * ntok, 8 * DM)
    wv3 = wv.rearrange("p (c n) -> p c n", c=8)
    wslq = bf16v(AR1, 8 * ntok + 8 * DM, 8 * 128)
    wslk = bf16v(AR1, 8 * ntok + 8 * DM + 8 * 128, 8 * 128)
    wslq3 = wslq.rearrange("p (c n) -> p c n", c=8)
    wslk3 = wslk.rearrange("p (c n) -> p c n", c=8)
    ofm = bf16v(AR1, 8 * ntok + 8 * DM + 16 * 128, 8 * 128)
    ofm3 = ofm.rearrange("p (c n) -> p c n", c=8)
    if var == "a":
        dec = f32v(AR2, 0, 2 * ntok - 1)
        c0 = 2 * (2 * ntok - 1)
        rstdb = None
    else:
        dec = None
        rstdb = f32v(AR2, 0, ntok)
        c0 = 2 * ntok
    q2 = bf16v(AR2, c0, ntok); c0 += ntok
    k2 = bf16v(AR2, c0, ntok); c0 += ntok
    if var == "b":
        tmpn = bf16v(AR2, c0, ntok); c0 += ntok
    else:
        tmpn = None
    v2p = bf16v(AR2, c0, T * 130); c0 += T * 130
    v2p3 = v2p.rearrange("p (t n) -> p t n", t=T)
    rbq = st["wbf3"].ap().bitcast(F32)[0:64, 0:512]
    obf = st["wbf"][0:64, 0:512]
    assert c0 <= 30208, c0

    wq_d = din["aw_qkv" if var == "a" else "bw_qkv"]
    wo_d = din["aw_out" if var == "a" else "bw_out"]

    tbias = None
    if var == "a":
        tdd = bcast_col(P, st, din["a_dd"][0:1, :], sml[:, 8:9])
        tng = P.on("vector", lambda e: e.tensor_scalar(sml[:, 9:10], sml[:, 8:9], -1.0, None,
                                                       ALU.mult, ALU.bypass), deps=[tdd], tok=True)
        tna = P.on("vector", lambda e: e.tensor_tensor(sml[:, 10:11], sml[:, 8:9],
                                                       sml[:, 9:10], ALU.min), deps=[tng], tok=True)
        tdist = P.dma("sync", dec, din["dist"][:])
        tdec = P.on("scalar", lambda e: e.activation(dec, dec, AF.Exp, scale=sml[:, 10:11]),
                    deps=[tna, tdist], tok=True)
        tbias = P.dma("gpsimd", st["absl"][:], din["ab_qkv"][:])
        tbo = P.dma("gpsimd", st["b2row"][:], din["ab_out"][:])
        tnc = None
    else:
        tq1 = P.dma("sync", st["qnc"][0:64, :],
                    din["b_qn"].rearrange("o (p i) -> (o p) i", i=1))
        tq2 = P.dma("sync", st["qnc"][64:128, :],
                    din["b_qn"].rearrange("o (p i) -> (o p) i", i=1))
        tk1 = P.dma("sync", st["knc"][0:64, :],
                    din["b_kn"].rearrange("o (p i) -> (o p) i", i=1))
        tk2 = P.dma("sync", st["knc"][64:128, :],
                    din["b_kn"].rearrange("o (p i) -> (o p) i", i=1))
        tqb = bcast_col(P, st, din["b_qn"][0:1, :], sml[:, 16:80])
        tkb = bcast_col(P, st, din["b_kn"][0:1, :], sml[:, 80:144])
        t1 = P.on("vector", lambda e: e.tensor_reduce(sml[:, 0:1], sml[:, 16:80], AXX, ALU.max,
                                                      apply_absolute_value=True),
                  deps=[tqb], tok=True)
        t2 = P.on("vector", lambda e: e.tensor_reduce(sml[:, 1:2], sml[:, 80:144], AXX, ALU.max,
                                                      apply_absolute_value=True),
                  deps=[tkb], tok=True)
        t3m = P.on("vector", lambda e: e.tensor_tensor(sml[:, 2:3], sml[:, 0:1], sml[:, 1:2],
                                                       ALU.mult), deps=[t1, t2], tok=True)
        tnc = P.on("vector", lambda e: e.tensor_scalar(negC[:], sml[:, 2:3], -8.0, None,
                                                       ALU.mult, ALU.bypass), deps=[t3m], tok=True)
        tdec = None
        st["_rmsdep"] = [tq1, tq2, tk1, tk2]

    prev_tn = {0: None, 1: None}
    prev_ev = None
    for tt in range(T):
        r = slice(tt * 128, tt * 128 + 128)
        par = tt % 2
        ld = (wa, st["we"])[par]
        tld = P.rd(x_name, P.dma("sync", ld[:], x_in[r, :], deps=[prev_tn[par]]))
        negmu, rstd, tst = ln_stats(P, st, ld, (wbf, st["wbf4"])[par][:], LNEPS,
                                    [tld], sb=12 * par)
        prev_tn[par] = P.on("vector", lambda e, ld=ld, par=par, negmu=negmu, rstd=rstd:
                            e.tensor_scalar((wbf, st["wbf4"])[par][:], ld[:], negmu, rstd,
                                            ALU.add, ALU.mult),
                            deps=tst + [prev_ev], tok=True)
        prev_ev = transpose_tile(P, st, (wbf, st["wbf4"])[par],
                                 lnfm3[:, :, tt * 128:(tt + 1) * 128], 6, [prev_tn[par]])

    if st["stop"] == "ph1":
        return
    twv = P.dma("gpsimd", wv3, wq_d[:, :, 2 * DM:3 * DM])
    vst3 = vst.ap().rearrange("p (h n) -> p h n", h=NH)
    tmem = P.on("gpsimd", lambda e: e.memset(vst3[:, :, 64:65], 1.0), tok=True)
    prev_sp = None
    prev_ev = [tmem, prev_ev]
    for tt in range(T):
        tm = {}
        for half in range(2):
            bank = PB[half]
            nmm = 8
            for c in range(8):
                lastmm = (c == 7) and var != "a"
                tm[half] = P.on("tensor", lambda e, c=c, half=half, bank=bank, tt=tt, lastmm=lastmm: e.matmul(
                    bank[:], lnfm3[:, c, tt * 128:(tt + 1) * 128],
                    wv3[:, c, half * 512:(half + 1) * 512],
                    start=(c == 0), stop=lastmm),
                    deps=([twv] + prev_ev) if c == 0 else (), tok=lastmm, mode="full")
            if var == "a":
                tm[half] = P.on("tensor", lambda e, half=half, bank=bank: e.matmul(
                    bank[:], st["ones_row"][0:1, 0:128],
                    st["absl"][0:1, 2 * DM + half * 512:2 * DM + (half + 1) * 512],
                    start=False, stop=True),
                    deps=[tbias], tok=True, mode="k1")
        prev_ev = []
        for half in range(2):
            pv = PB[half].ap().rearrange("p (h n) -> p h n", h=8)
            prev_ev.append(P.on("scalar", lambda e, half=half, pv=pv: e.copy(
                vst3[:, half * 8:(half + 1) * 8, 0:64], pv),
                deps=[tm[half], prev_sp], tok=True))
        prev_sp = P.dma("sync", st["VSP"][tt], vst[:], deps=prev_ev + P.wdeps("VSP"))

    if st["stop"] == "ph2":
        return
    nqg = ntok // 512
    oev = [prev_sp]
    for p in list(range(8))[st.get('_ponly', 0):]:
        tq = P.dma("gpsimd", wslq3, wq_d[:, :, 128 * p:128 * (p + 1)], deps=oev)
        tk = P.dma("gpsimd", wslk3, wq_d[:, :, DM + 128 * p:DM + 128 * (p + 1)], deps=oev)
        tev_q = list(oev)
        for (wsl3, tw_, dst, bofs) in ((wslq3, tq, q2, 128 * p), (wslk3, tk, k2, DM + 128 * p)):
            new_ev = []
            for qg in range(nqg):
                bank = PB[qg % 2]
                tmq = None
                for c in range(8):
                    lastmm = (c == 7) and var != "a"
                    tmq = P.on("tensor", lambda e, c=c, qg=qg, bank=bank, wsl3=wsl3, lastmm=lastmm: e.matmul(
                        bank[:], wsl3[:, c, :], lnfm3[:, c, qg * 512:(qg + 1) * 512],
                        start=(c == 0), stop=lastmm),
                        deps=([tw_] + tev_q) if c == 0 else (), tok=lastmm, mode="full")
                if var == "a":
                    tmq = P.on("tensor", lambda e, qg=qg, bank=bank, bofs=bofs: e.matmul(
                        bank[:], st["absl"][0:1, bofs:bofs + 128],
                        st["ones_row"][0:1, 0:512], start=False, stop=True),
                        deps=[tbias], tok=True, mode="k1")
                new_ev.append(P.on("scalar", lambda e, qg=qg, bank=bank, dst=dst: e.copy(
                    dst[:, qg * 512:(qg + 1) * 512], bank[:]), deps=[tmq], tok=True))
            tev_q = new_ev
        qS, kS = q2, k2
        if st["stop"] == "proj" and p == 0:
            return
        if var == "b":
            t_rms1 = rms_norm_pair(P, st, q2, tmpn, st["qnc"], 2, rstdb, tev_q)
            t_rms2 = rms_norm_pair(P, st, k2, q2, st["knc"], 4, rstdb, [t_rms1])
            qS, kS = tmpn, q2
            tev_q = [t_rms2]
        tv = P.dma("sync", v2p3,
                   st["VSP"][:, :, 130 * p:130 * (p + 1)].rearrange("t p n -> p t n"),
                   deps=tev_q + P.wdeps("v2p"))
        P.rd("VSP", tv)
        slot_rd = {}
        sbank_rd = {}
        for qp0 in range(0, nqg, 2):
            qgs = [qg for qg in (qp0, qp0 + 1) if qg < nqg]
            tdone = {}
            txp = {}
            for t in range(T + 1):
                if t < T:
                    for qI, qg in enumerate(qgs):
                        for h in range(2):
                            sb_ = PB[2 * h + qI]
                            rows = slice(64 * h, 64 * h + 64)
                            tS = P.on("tensor",
                                      lambda e, t=t, h=h, qg=qg, sb_=sb_, rows=rows:
                                      e.matmul(sb_[:], kS[rows, t * 128:(t + 1) * 128],
                                               qS[rows, qg * 512:(qg + 1) * 512],
                                               start=True, stop=True),
                                      deps=([tv, tnc, tdec] + tev_q
                                            + st.get("_oev_par", [])
                                            if (t < 1 and qp0 == 0) else [])
                                      + [sbank_rd.get((h, qI)),
                                         st.get("_oev_par_w", {}).get((qI, h))
                                         if t < 1 else None],
                                      tok=True, mode="r64")
                            sbank_rd[("S", h, qI)] = tS
                if t >= 1:
                    for qI, qg in enumerate(qgs):
                        for h in range(2):
                            ob = PB[4 + 2 * qI + h]
                            slot = 2 * qI + h
                            tdone[(qI, h)] = P.on(
                                "tensor", lambda e, t=t, h=h, ob=ob, slot=slot:
                                e.matmul(ob[0:65, :],
                                         v2p3[:, t - 1, 65 * h:65 * h + 65],
                                         ptb[:, slot * 512:(slot + 1) * 512],
                                         start=(t - 1 == 0), stop=(t - 1 == T - 1)),
                                deps=[txp[(qI, h)]]
                                + (st.get("_oev_par", []) if t == 1 else []),
                                tok=True, mode="full")
                            slot_rd[(qI, h)] = tdone[(qI, h)]
                if t < T:
                    for qI, qg in enumerate(qgs):
                        for h in range(2):
                            sb_ = PB[2 * h + qI]
                            tS = sbank_rd[("S", h, qI)]
                            slot = 2 * qI + h
                            pslot = ptb[:, slot * 512:(slot + 1) * 512]
                            if var == "a":
                                u0 = (ntok - 1) - 128 * t + 512 * qg
                                wsf = (wa, wb)[h][:, qI * 512:qI * 512 + 512]
                                trd = P.on("vector", lambda e, sb_=sb_, wsf=wsf, u0=u0:
                                           e.scalar_tensor_tensor(wsf, sb_[:], 0.125,
                                                                  dec[:, u0:u0 + 512],
                                                                  ALU.mult, ALU.mult),
                                           deps=[tS, slot_rd.get((qI, h))], tok=True)
                                sbank_rd[(h, qI)] = trd
                                tx = P.on("scalar", lambda e, pslot=pslot, wsf=wsf:
                                          e.activation(pslot, wsf, AF.Exp,
                                                       bias=-C_ATTN_A, scale=1.0),
                                          deps=[trd], tok=True)
                            else:
                                tx = P.on("scalar", lambda e, pslot=pslot, sb_=sb_:
                                          e.activation(pslot, sb_[:], AF.Exp,
                                                       bias=negC[:], scale=0.125),
                                          deps=[tS, slot_rd.get((qI, h))], tok=True)
                                sbank_rd[(h, qI)] = tx
                            txp[(qI, h)] = tx
            oev2 = []
            oevw = {}
            for qI, qg in enumerate(qgs):
                for h in range(2):
                    ob = PB[4 + 2 * qI + h]
                    rrh = rr[0:1, 0:512] if h == 0 else st["wc"].ap().bitcast(F32)[0:1, 0:512]
                    rbqh = rbq if h == 0 else st["wd"].ap().bitcast(F32)[0:64, 0:512]
                    obfh = obf if h == 0 else st["wbf4"][0:64, 0:512]
                    t1 = P.on("vector", lambda e, ob=ob, rrh=rrh: e.reciprocal(
                        rrh, ob[64:65, :]),
                        deps=[tdone[(qI, h)]] + P.wdeps(f"rr{h}"), tok=True)
                    t2 = P.rd(f"rr{h}", P.dma("sync", st["BNC"][h:h + 1, 0:512], rrh,
                                              deps=[t1] + P.wdeps(f"BNC{h}")))
                    t3 = P.rd(f"BNC{h}", P.dma("sync", rbqh,
                                               st["BNC"][h:h + 1, 0:512].partition_broadcast(64),
                                               deps=[t2] + P.wdeps(f"rbq{h}")))
                    t4 = P.rd(f"rbq{h}", P.on("vector",
                                              lambda e, ob=ob, obfh=obfh, rbqh=rbqh:
                                              e.tensor_tensor(obfh, ob[0:64, :], rbqh,
                                                              ALU.mult),
                                              deps=[t3] + P.wdeps(f"obf{h}"), tok=True))
                    t5 = P.rd(f"obf{h}", P.dma("sync",
                                               st["OF"][p, 64 * h:64 * h + 64,
                                                        qg * 512:(qg + 1) * 512],
                                               obfh, deps=[t4] + P.wdeps("OF")))
                    oev2 += [t4, t5]
                    oevw[(qI, h)] = t4
            st["_oev_par"] = oev2
            st["_oev_par_w"] = oevw
            oev = oev2
    if st["stop"] == "ph3":
        return
    P.barrier()
    two = P.dma("gpsimd", wv3, wo_d[:, :, :])
    tbo2 = P.dma("gpsimd", st["b2row"][:], din["ab_out"][:]) if var == "a" else None
    prev = []
    for tt in range(T):
        r = slice(tt * 128, tt * 128 + 128)
        tof = P.rd("OF", P.dma("sync", ofm3,
                               st["OF"][:, :, tt * 128:(tt + 1) * 128].rearrange("c p n -> p c n"),
                               deps=prev))
        txl = P.rd(x_name, P.dma("sync", wc[:], x_in[r, :], deps=prev))
        tmm = {}
        for half in range(2):
            bank = PB[half]
            for c in range(8):
                lastmm = (c == 7) and var != "a"
                tmm[half] = P.on("tensor", lambda e, c=c, half=half, bank=bank, lastmm=lastmm: e.matmul(
                    bank[:], ofm3[:, c, :], wv3[:, c, half * 512:(half + 1) * 512],
                    start=(c == 0), stop=lastmm),
                    deps=([tof, two] + prev) if c == 0 else (), tok=lastmm, mode="full")
            if var == "a":
                tmm[half] = P.on("tensor", lambda e, half=half, bank=bank: e.matmul(
                    bank[:], st["ones_row"][0:1, 0:128],
                    st["b2row"][0:1, half * 512:(half + 1) * 512], start=False, stop=True),
                    deps=[tbo2], tok=True, mode="k1")
        prev = []
        for half in range(2):
            tr = P.on("vector", lambda e, half=half: e.tensor_tensor(
                wd[:, half * 512:(half + 1) * 512], PB[half][:],
                wc[:, half * 512:(half + 1) * 512], ALU.add),
                deps=[tmm[half], txl], tok=True)
            prev.append(tr)
        prev.append(P.dma("sync", x_out[r, :], wd[:], deps=prev + P.wdeps(xo_name)))


def rms_norm_pair(P, st, src, dst, ncol, bnc0, rstdb, deps):
    ntok = st["ntok"]; PB = st["PB"]; rr = st["rr"]
    nqg = ntok // 512
    tsq = P.on("scalar", lambda e: e.activation(dst, src, AF.Square), deps=deps, tok=True)
    dmas = []
    prev5 = {0: None, 1: None}
    for h in range(2):
        rows = slice(64 * h, 64 * h + 64)
        for qg in range(nqg):
            tm = P.on("tensor", lambda e, h=h, qg=qg, rows=rows: e.matmul(
                PB[2 + h][0:1, :], st["onec"][rows, :], dst[rows, qg * 512:(qg + 1) * 512],
                start=True, stop=True), deps=[tsq, prev5[h]] + (dmas[-1:] if dmas else []),
                tok=True, mode="rms")
            t5 = P.on("scalar", lambda e, h=h: e.activation(rr[:, 0:512], PB[2 + h][0:1, :],
                                                            AF.Ln, bias=RMSEPS, scale=1.0 / 64),
                      deps=[tm], tok=True)
            t6 = P.on("scalar", lambda e: e.activation(rr[:, 0:512], rr[:, 0:512], AF.Exp,
                                                       scale=-0.5), deps=[t5], tok=True)
            prev5[h] = t6
            dmas.append(P.dma("sync", st["BNC"][bnc0 + h:bnc0 + h + 1, qg * 512:(qg + 1) * 512],
                              rr[0:1, 0:512], deps=[t6] + P.wdeps(f"BNCrms{bnc0 + h}")))
    tb = []
    for h in range(2):
        tb.append(P.rd(f"BNCrms{bnc0 + h}",
                       P.dma("sync", rstdb[64 * h:64 * h + 64, :],
                             st["BNC"][bnc0 + h:bnc0 + h + 1, :].partition_broadcast(64),
                             deps=dmas + P.wdeps("rstdb"))))
    tn = P.rd("rstdb", P.on("vector", lambda e: e.scalar_tensor_tensor(dst, src, ncol[:], rstdb,
                                                                      ALU.mult, ALU.mult),
                            deps=tb + st.get("_rmsdep", []), tok=True))
    return tn


def ph_ffnmod(P, st, x_in, x_name, x_out, xo_name, inv):
    T = st["T"]
    ycur, yname = st["YA"], "YA"
    ynext, ynname = st["YB"], "YB"
    P.barrier()
    prevl = {0: [], 1: []}
    for tt in range(T):
        r = slice(tt * 128, tt * 128 + 128)
        par = tt % 2
        ld = (st["wa"], st["we"])[par]
        dst = (st["wb"], st["wd"])[par]
        tld = P.rd(x_name, P.dma("sync", ld[:], x_in[r, :], deps=prevl[par]))
        negmu, rstd, tst = ln_stats(P, st, ld, dst[:], LNEPS, [tld], sb=12 * par)
        tn = P.on("vector", lambda e, ld=ld, dst=dst, negmu=negmu, rstd=rstd:
                  e.tensor_scalar(dst[:], ld[:], negmu, rstd, ALU.add, ALU.mult),
                  deps=tst, tok=True)
        prevl[par] = [tn, P.dma("sync", ycur[r, :], dst[:], deps=[tn] + P.wdeps(yname))]
    if st["stop"] == "y0":
        return
    for i in range(st["depth"]):
        ph_lattn(P, st, i, ycur, yname, ynext, ynname)
        if st["stop"] in ("lattnA", "lattnB", "lattnC"):
            return
        ycur, yname, ynext, ynname = ynext, ynname, ycur, yname
        ph_lffn(P, st, i, ycur, yname, ynext, ynname, inv)
        if st["stop"] == "lffn":
            return
        ycur, yname, ynext, ynname = ynext, ynname, ycur, yname
    P.barrier()
    prevl = {0: [], 1: []}
    prev_store = None
    for tt in range(T):
        r = slice(tt * 128, tt * 128 + 128)
        par = tt % 2
        la = (st["wa"], st["wd"])[par]
        lb = (st["wb"], st["we"])[par]
        t1 = P.rd(x_name, P.dma("sync", la[:], x_in[r, :], deps=prevl[par]))
        t2 = P.rd(yname, P.dma("sync", lb[:], ycur[r, :], deps=prevl[par]))
        t3 = P.on("vector", lambda e, la=la, lb=lb: e.tensor_tensor(
            st["wc"][:], la[:], lb[:], ALU.add),
            deps=[t1, t2, prev_store], tok=True)
        prev_store = P.dma("sync", x_out[r, :], st["wc"][:], deps=[t3] + P.wdeps(xo_name))
        prevl[par] = [t3, prev_store]


def ph_lattn(P, st, li, yin, yname, yout, yoname):
    din = st["din"]; T = st["T"]; ntok = st["ntok"]
    AR1, AR2, PB = st["AR1"], st["AR2"], st["PB"]
    wa, wb, wc, wd, we = st["wa"], st["wb"], st["wc"], st["wd"], st["we"]
    wbf, vst, ptb = st["wbf"], st["vst"], st["ptb"]
    sml, rr, negC = st["sml"], st["rr"], st["negC"]
    P.barrier()

    wqkv = bf16v(AR1, 0, 8 * 3 * DM)
    wqkv3 = wqkv.rearrange("p (c n) -> p c n", c=8)
    rotc = 8 * 3 * DM
    CS = {}; SS = {}
    for nm in ("q", "k"):
        CS[nm] = f32v(AR1, rotc // 2, T * DH).rearrange("p (t d) -> p t d", d=DH)
        rotc += 2 * T * DH
        SS[nm] = f32v(AR1, rotc // 2, T * DH).rearrange("p (t d) -> p t d", d=DH)
        rotc += 2 * T * DH
    assert rotc <= 44544
    lnf3 = st["wbf3"].ap().rearrange("p (c n) -> p c n", c=8)
    vst3 = vst.ap().rearrange("p (h n) -> p h n", h=NH)

    tw = P.dma("gpsimd", wqkv3, din["f_wqkv"][li])
    tcs = {}
    for nm in ("q", "k"):
        tcs[nm] = [P.dma("sync", CS[nm], din[f"cs_{nm}"].rearrange("(t p) d -> p t d", p=128)),
                   P.dma("sync", SS[nm], din[f"ss_{nm}"].rearrange("(t p) d -> p t d", p=128))]
    tqsb = bcast_col(P, st, din["f_qs"][li:li + 1, :], sml[:, 16:80])
    tksb = bcast_col(P, st, din["f_ks"][li:li + 1, :], sml[:, 80:144])
    tsw = [P.on("vector", lambda e: e.tensor_copy(we[:, 0:32], sml[:, 48:80]),
                deps=[tqsb], tok=True),
           P.on("vector", lambda e: e.tensor_copy(we[:, 32:64], sml[:, 16:48]),
                deps=[tqsb], tok=True),
           P.on("vector", lambda e: e.tensor_copy(we[:, 64:96], sml[:, 112:144]),
                deps=[tksb], tok=True),
           P.on("vector", lambda e: e.tensor_copy(we[:, 96:128], sml[:, 80:112]),
                deps=[tksb], tok=True)]
    tf = {}
    for j, nm in enumerate(("q", "k")):
        scol = sml[:, 16:80] if nm == "q" else sml[:, 80:144]
        swv = we[:, 0:64] if nm == "q" else we[:, 64:128]
        tf[nm] = [P.on("vector", lambda e, nm=nm, scol=scol: e.tensor_tensor(
            CS[nm], CS[nm], scol.unsqueeze(1).broadcast_to((128, T, DH)), ALU.mult),
            deps=[tcs[nm][0], tqsb, tksb], tok=True),
            P.on("vector", lambda e, nm=nm, swv=swv: e.tensor_tensor(
                SS[nm], SS[nm], swv.unsqueeze(1).broadcast_to((128, T, DH)), ALU.mult),
                deps=[tcs[nm][1]] + tsw, tok=True)]
    t1 = P.on("vector", lambda e: e.tensor_reduce(sml[:, 0:1], sml[:, 16:80], AXX, ALU.max,
                                                  apply_absolute_value=True),
              deps=[tqsb] + tf["q"], tok=True)
    t2 = P.on("vector", lambda e: e.tensor_reduce(sml[:, 1:2], sml[:, 80:144], AXX, ALU.max,
                                                  apply_absolute_value=True),
              deps=[tksb] + tf["k"], tok=True)
    t3m = P.on("vector", lambda e: e.tensor_tensor(sml[:, 2:3], sml[:, 0:1], sml[:, 1:2],
                                                   ALU.mult), deps=[t1, t2], tok=True)
    tnc = P.on("vector", lambda e: e.tensor_scalar(negC[:], sml[:, 2:3], -8.0, None,
                                                   ALU.mult, ALU.bypass), deps=[t3m], tok=True)
    tmem = P.on("gpsimd", lambda e: e.memset(vst3[:, :, 64:65], 1.0), tok=True)

    prev_sp = [tmem]
    prev_rot = {"q": None, "k": None}
    for tt in range(T):
        tls = shifted_load(P, st, yin, yname, tt, wa, deps=[prev_rot["k"]])
        tcst = P.on("scalar", lambda e: e.copy(wbf[:], wa[:]), deps=tls, tok=True)
        tsf = transpose_tile(P, st, wbf, lnf3, 6, [tcst])
        tmv = {}
        for g in range(6):
            bank = PB[g]
            for c in range(8):
                tmv[g] = P.on("tensor", lambda e, g=g, c=c, bank=bank: e.matmul(
                    bank[:], lnf3[:, c, :], wqkv3[:, c, g * 512:(g + 1) * 512],
                    start=(c == 0), stop=(c == 7)),
                    deps=([tsf, tw] + prev_sp) if c == 0 else (), tok=(c == 7), mode="full")
        tev = []
        for half in range(2):
            pv = PB[4 + half].ap().rearrange("p (h n) -> p h n", h=8)
            tev.append(P.on("scalar", lambda e, half=half, pv=pv: e.copy(
                vst3[:, half * 8:(half + 1) * 8, 0:64], pv),
                deps=[tmv[4 + half]] + prev_sp, tok=True))
        tvs = P.dma("sync", st["VSP"][tt], vst[:], deps=tev + P.wdeps("VSP"))
        tqr = rotary(P, st, tt, 0, CS["q"], SS["q"], st["QF"], "wbf4",
                     [tmv[0], tmv[1], prev_rot["q"]])
        tkr = rotary(P, st, tt, 2, CS["k"], SS["k"], st["KF"], "wbf5",
                     [tmv[2], tmv[3], tqr["ev"], prev_rot["k"]])
        prev_rot = {"q": tqr["sp"], "k": tkr["sp"]}
        prev_sp = [tvs, tqr["sp"], tkr["sp"]]

    if st["stop"] == "lattnA":
        return
    P.barrier()
    O1 = f32v(AR1, 0, ntok, rows=65)
    O2 = f32v(AR1, ntok, ntok, rows=65)
    rbig = f32v(AR1, 2 * ntok, ntok, rows=64)
    q2 = bf16v(AR2, 0, ntok)
    k2 = bf16v(AR2, ntok, ntok)
    v2p = bf16v(AR2, 2 * ntok, T * 130)
    v2p3 = v2p.rearrange("p (t n) -> p t n", t=T)
    obig = bf16v(AR2, 2 * ntok + T * 130, ntok, rows=64)
    oev = []
    for p in range(8):
        tqd = P.rd("QF", P.dma("sync", q2, st["QF"][p], deps=oev + P.wdeps("q2")))
        tkd = P.rd("KF", P.dma("sync", k2, st["KF"][p], deps=oev + P.wdeps("k2")))
        tvd = P.rd("VSP", P.dma("sync", v2p3,
                                st["VSP"][:, :, 130 * p:130 * (p + 1)].rearrange("t p n -> p t n"),
                                deps=oev + P.wdeps("v2p")))
        owtok = {}
        evtok = {}
        sbank_rd = {0: None, 1: None}
        slot_rd = {}
        txp = {}
        for t in range(T + 1):
            if t < T:
                W = min(384, ntok - 128 * t)
                for h in range(2):
                    sb_ = PB[h]
                    rows = slice(64 * h, 64 * h + 64)
                    sbank_rd[("S", h)] = P.on(
                        "tensor", lambda e, t=t, h=h, sb_=sb_, rows=rows, W=W: e.matmul(
                            sb_[:, 0:W], k2[rows, t * 128:(t + 1) * 128],
                            q2[rows, t * 128:t * 128 + W], start=True, stop=True),
                        deps=([tqd, tkd, tvd, tnc] if t == 0 else []) + [sbank_rd[h]],
                        tok=True, mode="r64")
            if t >= 1:
                tp = t - 1
                Wp = min(384, ntok - 128 * tp)
                nwp = (Wp + 127) // 128
                for h in range(2):
                    pslot = ptb[:, (2 * (tp % 2) + h) * 512:(2 * (tp % 2) + h) * 512 + 384]
                    ws = [w for w in range(tp, tp + nwp) if w < T]
                    for j, w in enumerate(ws):
                        bank = PB[2 + 3 * h + (w % 3)]
                        first = (tp == max(0, w - 2))
                        last = (tp == w)
                        dep = [txp[(h, tp % 2)]]
                        if first and w >= 3:
                            dep.append(evtok[(h, w - 3)])
                        tk_ = P.on("tensor", lambda e, bank=bank, h=h, tp=tp, j=j, first=first,
                                   last=last, pslot=pslot: e.matmul(
                            bank[0:65, 0:128], v2p3[:, tp, 65 * h:65 * h + 65],
                            pslot[:, j * 128:(j + 1) * 128], start=first, stop=last),
                            deps=dep, tok=(last or j == len(ws) - 1), mode="full")
                        if last:
                            owtok[(h, w)] = tk_
                        if j == len(ws) - 1:
                            slot_rd[(h, tp % 2)] = tk_
                    w = tp
                    bank = PB[2 + 3 * h + (w % 3)]
                    Odst = O1 if h == 0 else O2
                    evtok[(h, w)] = P.on("vector", lambda e, bank=bank, w=w, Odst=Odst:
                                         e.tensor_copy(Odst[:, w * 128:(w + 1) * 128],
                                                       bank[0:65, 0:128]),
                                         deps=[owtok[(h, w)]], tok=True)
            if t < T:
                W = min(384, ntok - 128 * t)
                for h in range(2):
                    sb_ = PB[h]
                    tS = sbank_rd[("S", h)]
                    wsf = (wa, wb)[h][:, (t % 2) * 512:(t % 2) * 512 + 384]
                    sbank_rd[h] = P.on("vector", lambda e, sb_=sb_, wsf=wsf, W=W:
                                       e.scalar_tensor_tensor(wsf[:, 0:W], sb_[:, 0:W], 8.0,
                                                              st["maskb"][:, 0:W],
                                                              ALU.mult, ALU.add),
                                       deps=[tS, slot_rd.get((h, t % 2))], tok=True)
                    pslot = ptb[:, (2 * (t % 2) + h) * 512:(2 * (t % 2) + h) * 512 + 384]
                    txp[(h, t % 2)] = P.on(
                        "scalar", lambda e, pslot=pslot, wsf=wsf, W=W: e.activation(
                            pslot[:, 0:W], wsf[:, 0:W], AF.Exp, bias=negC[:], scale=1.0),
                        deps=[sbank_rd[h], slot_rd.get((h, t % 2))], tok=True)
        oev = []
        for h in range(2):
            Osrc = O1 if h == 0 else O2
            tch = []
            for qg in range(ntok // 512):
                tt1 = P.on("vector", lambda e, Osrc=Osrc, qg=qg: e.reciprocal(
                    rr[:, 0:512], Osrc[64:65, qg * 512:(qg + 1) * 512]),
                    deps=[evtok[(h, w)] for w in range(4 * qg, min(T, 4 * qg + 4))]
                    + P.wdeps("lrr"), tok=True)
                tch.append(P.rd("lrr",
                                P.dma("sync", st["BNC"][h:h + 1, qg * 512:(qg + 1) * 512],
                                      rr[0:1, 0:512], deps=[tt1] + P.wdeps(f"LBNC{h}"))))
            tb = P.rd(f"LBNC{h}", P.dma("sync", rbig,
                                        st["BNC"][h:h + 1, :].partition_broadcast(64),
                                        deps=tch + P.wdeps("rbig")))
            t4 = P.rd("rbig", P.on("vector", lambda e, Osrc=Osrc: e.tensor_tensor(
                obig, Osrc[0:64, :], rbig, ALU.mult),
                deps=[tb] + P.wdeps("obig"), tok=True))
            t5 = P.rd("obig", P.dma("sync", st["OF"][p, 64 * h:64 * h + 64, :], obig,
                                    deps=[t4] + P.wdeps("OF")))
            oev += [t4, t5]

    if st["stop"] == "lattnB":
        return
    P.barrier()
    wo = bf16v(AR2, 0, 8 * DM)
    wo3 = wo.rearrange("p (c n) -> p c n", c=8)
    ofm3 = st["wbf3"].ap().rearrange("p (c n) -> p c n", c=8)
    two = P.dma("gpsimd", wo3, din["f_wo"][li])
    prev = []
    for tt in range(T):
        r = slice(tt * 128, tt * 128 + 128)
        tof = P.rd("OF", P.dma("sync", ofm3,
                               st["OF"][:, :, tt * 128:(tt + 1) * 128].rearrange("c p n -> p c n"),
                               deps=prev))
        txl = P.rd(yname, P.dma("sync", wc[:], yin[r, :], deps=prev))
        tmm = {}
        for half in range(2):
            bank = PB[half]
            for c in range(8):
                tmm[half] = P.on("tensor", lambda e, c=c, half=half, bank=bank: e.matmul(
                    bank[:], ofm3[:, c, :], wo3[:, c, half * 512:(half + 1) * 512],
                    start=(c == 0), stop=(c == 7)),
                    deps=([tof, two] + prev) if c == 0 else (), tok=(c == 7), mode="full")
        prev = []
        for half in range(2):
            tr = P.on("vector", lambda e, half=half: e.tensor_tensor(
                wd[:, half * 512:(half + 1) * 512], PB[half][:],
                wc[:, half * 512:(half + 1) * 512], ALU.add),
                deps=[tmm[half], txl], tok=True)
            prev.append(tr)
        prev.append(P.dma("sync", yout[r, :], wd[:], deps=prev + P.wdeps(yoname)))


def rotary(P, st, tt, bank0, CS3, SS3, dstF, stg_name, deps):
    wa, wc, wd, we = st["wa"], st["wc"], st["wd"], st["we"]
    stat = st["stat"]; PB = st["PB"]
    stg = st[stg_name]
    t1 = P.on("scalar", lambda e: e.copy(wc[:, 0:512], PB[bank0][:]), deps=deps, tok=True)
    t2 = P.on("scalar", lambda e: e.copy(wc[:, 512:1024], PB[bank0 + 1][:]), tok=True)
    t3 = P.on("scalar", lambda e: e.activation(wd[:, 0:512], PB[bank0][:], AF.Square), tok=True)
    t4 = P.on("scalar", lambda e: e.activation(wd[:, 512:1024], PB[bank0 + 1][:], AF.Square),
              tok=True)
    wc3 = wc.ap().rearrange("p (h d) -> p h d", d=DH)
    wd3 = wd.ap().rearrange("p (h d) -> p h d", d=DH)
    t5 = P.on("vector", lambda e: e.tensor_reduce(stat[:, 0:16], wd3, AXX, ALU.add),
              deps=[t3, t4], tok=True)
    t6 = P.on("vector", lambda e: e.tensor_scalar(stat[:, 0:16], stat[:, 0:16], 1.42e-24,
                                                  None, ALU.max, ALU.bypass), deps=[t5], tok=True)
    t7 = P.on("scalar", lambda e: e.activation(stat[:, 0:16], stat[:, 0:16], AF.Ln,
                                               scale=1.0), deps=[t6], tok=True)
    t8 = P.on("scalar", lambda e: e.activation(stat[:, 0:16], stat[:, 0:16], AF.Exp,
                                               scale=-0.5), deps=[t7], tok=True)
    rb = stat[:, 0:16].unsqueeze(2).broadcast_to((128, 16, DH))
    tu = P.on("vector", lambda e: e.tensor_tensor(wd3, wc3, rb, ALU.mult),
              deps=[t1, t2, t8], tok=True)
    wd4 = wd.ap().rearrange("p (h s d2) -> p h s d2", s=2, d2=32)
    wa4 = wa.ap().rearrange("p (h s d2) -> p h s d2", s=2, d2=32)
    g1 = P.on("vector", lambda e: e.tensor_tensor(
        we.ap().rearrange("p (h d) -> p h d", d=DH), wd3,
        CS3[:, tt, :].unsqueeze(1).broadcast_to((128, 16, DH)), ALU.mult),
        deps=[tu], tok=True)
    g2 = P.on("vector", lambda e: e.tensor_tensor(
        wa4[:, :, 0, :], wd4[:, :, 1, :],
        SS3[:, tt, 0:32].unsqueeze(1).broadcast_to((128, 16, 32)), ALU.mult),
        deps=[g1], tok=True)
    g3 = P.on("vector", lambda e: e.tensor_tensor(
        wa4[:, :, 1, :], wd4[:, :, 0, :],
        SS3[:, tt, 32:64].unsqueeze(1).broadcast_to((128, 16, 32)), ALU.mult),
        deps=[g2], tok=True)
    g4 = P.on("vector", lambda e: e.tensor_tensor(st["wbf"][:], we[:], wa[:], ALU.add),
              deps=[g1, g2, g3], tok=True)
    tev = transpose_tile(P, st, st["wbf"], stg.ap().rearrange("p (c n) -> p c n", c=8),
                         7, [g4])
    tsp = P.dma("sync", dstF.rearrange("c p n -> p c n")[:, :, tt * 128:(tt + 1) * 128],
                stg.ap().rearrange("p (c n) -> p c n", c=8), deps=[tev])
    return {"sp": tsp, "mid": g4, "ev": tev}


def ph_lffn(P, st, li, yin, yname, yout, yoname, inv):
    din = st["din"]; T = st["T"]; ntok = st["ntok"]
    AR1, AR2, PB = st["AR1"], st["AR2"], st["PB"]
    wa, wb, wc, wd = st["wa"], st["wb"], st["wc"], st["wd"]
    wbf, gsb = st["wbf"], st["gsb"]
    P.barrier()

    w1 = bf16v(AR1, 0, 8 * W1C)
    w13 = w1.rearrange("p (c n) -> p c n", c=8)
    w2 = bf16v(AR2, 0, 22 * DM)
    w23 = w2.rearrange("p (c n) -> p c n", c=22)
    lngb = f32v(AR2, (22 * DM) // 2, DM)
    mst = bf16v(AR2, 22 * DM + 2 * DM, DI)
    mfst = bf16v(AR2, 22 * DM + 2 * DM + DI, 22 * 128)
    assert 22 * DM + 2 * DM + DI + 22 * 128 <= 30208
    mfst3 = mfst.rearrange("p (c n) -> p c n", c=22)

    tw1 = P.dma("gpsimd", w13, din["f_w1"][li])
    tw2 = P.dma("gpsimd", w23, din["f_w2"][li])
    tb2 = P.dma("gpsimd", st["b2row"][:], din["f_b2"][li:li + 1, :])
    tlg = bcast_col(P, st, din["f_lng"][li:li + 1, :], lngb)
    nch = (W1C + 511) // 512
    if inv == 0:
        lnb_src = din["f_lnb"][li:li + 1, :].rearrange("o (c p) -> p (o c)", p=128)
        tlb = P.on("gpsimd", lambda e: e.dma_start(out=st["lnbc"][:], in_=lnb_src,
                                                   allow_slow_non_contiguous=True),
                   tok=True, dma=True)
        twr_last = None
        twr = []
        for j in range(nch):
            Wj = min(512, W1C - 512 * j)
            tbr = P.dma("sync", st["rr"][0:1, 0:Wj],
                        din["f_b1"][li:li + 1, 512 * j:512 * j + Wj], deps=[twr_last])
            tmj = None
            for c in range(8):
                tmj = P.on("tensor", lambda e, j=j, c=c, Wj=Wj: e.matmul(
                    PB[0][0:1, 0:Wj], st["lnbc"][:, c:c + 1],
                    w13[:, c, 512 * j:512 * j + Wj], start=(c == 0), stop=(c == 7)),
                    deps=([tlb, tw1, twr_last]) if c == 0 else (), tok=(c == 7), mode="m1")
            twr_last = P.on("vector", lambda e, j=j, Wj=Wj: e.tensor_tensor(
                st["wrow"][0:1, 512 * j:512 * j + Wj], PB[0][0:1, 0:Wj],
                st["rr"][0:1, 0:Wj], ALU.add), deps=[tmj, tbr], tok=True)
            twr.append(twr_last)
        twr = [P.dma("sync", st["WRD"][li], st["wrow"][:], deps=twr)]
    else:
        twr = [P.dma("sync", st["wrow"][:], st["WRD"][li])]

    prev = []
    prev_aj = []
    prev_mev = []
    prevln = {0: [], 1: []}
    prev_t2 = None
    for tt in range(T):
        r = slice(tt * 128, tt * 128 + 128)
        par = tt % 2
        ld = (wa, st["we"])[par]
        lnfs = (st["wbf3"], st["wbf5"])[par]
        lnf3 = lnfs.ap().rearrange("p (c n) -> p c n", c=8)
        tls = shifted_load(P, st, yin, yname, tt, ld, deps=prevln[par])
        negmu, rstd, tst = ln_stats(P, st, ld, wb[:], LNEPS, tls, sb=12 * par)
        t1 = P.on("vector", lambda e, ld=ld, negmu=negmu, rstd=rstd: e.tensor_scalar(
            wb[:], ld[:], negmu, rstd, ALU.add, ALU.mult), deps=tst, tok=True)
        t2 = P.on("vector", lambda e: e.scalar_tensor_tensor(wbf[:], wb[:], 1.0, lngb,
                                                             ALU.mult, ALU.mult),
                  deps=[t1, tlg, prev_t2], tok=True)
        prev_t2 = t2
        prevln[par] = [t1, tst[2]]
        tsf = transpose_tile(P, st, wbf, lnf3, 6, [t2])
        ajs = []
        for j in range(6):
            a0 = 512 * j
            aW = min(512, DI - a0)
            tmj = {}
            for (idx, c0_) in ((0, a0), (1, DI + a0)):
                bank = PB[2 * (j % 2) + idx]
                bankrel = ajs[j - 2] if j >= 2 else (prev_aj[j + 4] if prev_aj else None)
                for c in range(8):
                    tmj[idx] = P.on("tensor",
                                    lambda e, c=c, c0_=c0_, aW=aW, bank=bank, lnf3=lnf3:
                                    e.matmul(bank[:, 0:aW], lnf3[:, c, :],
                                             w13[:, c, c0_:c0_ + aW],
                                             start=(c == 0), stop=False),
                                    deps=([tsf, tw1, bankrel] + twr + prev)
                                    if c == 0 else (), mode="full")
                tmj[idx] = P.on("tensor", lambda e, c0_=c0_, aW=aW, bank=bank: e.matmul(
                    bank[:, 0:aW], st["ones_row"][0:1, 0:128], st["wrow"][0:1, c0_:c0_ + aW],
                    start=False, stop=True), deps=twr, tok=True, mode="k1")
            gb = PB[2 * (j % 2) + 1]
            ab = PB[2 * (j % 2)]
            grel = ajs[j - 1] if j >= 1 else (prev_aj[-1] if prev_aj else None)
            tg = P.on("scalar", lambda e, gb=gb, aW=aW: e.activation(
                gsb[:, 0:aW], gb[:, 0:aW], AF.Gelu), deps=[tmj[1], grel], tok=True)
            ajs.append(P.on("vector", lambda e, ab=ab, a0=a0, aW=aW: e.tensor_tensor(
                mst[:, a0:a0 + aW], ab[:, 0:aW], gsb[:, 0:aW], ALU.mult),
                deps=[tmj[0], tg], tok=True))
        prev_aj = ajs
        tmev = []
        pv = PB[7].ap().bitcast(BF16)
        for grp in range(3):
            ch0 = grp * 8
            nc_ = min(8, 22 - ch0)
            tlast = None
            for cc in range(nc_):
                c2 = ch0 + cc
                cw = min(128, DI - c2 * 128)
                tlast = P.on("tensor", lambda e, cc=cc, c2=c2, cw=cw, pv=pv: e.transpose(
                    pv[0:cw, cc * 128:(cc + 1) * 128], mst[:, c2 * 128:c2 * 128 + cw],
                    st["ident"][:]),
                    deps=(ajs + [st["pt_rd"].get(7)]) if cc == 0 else (),
                    tok=(cc == nc_ - 1), mode="tr")
            pvv = pv.rearrange("p (c n) -> p c n", c=8)[:, 0:nc_, :]
            tmev.append(P.on("scalar", lambda e, ch0=ch0, nc_=nc_, pvv=pvv: e.copy(
                mfst3[:, ch0:ch0 + nc_, :], pvv), deps=[tlast], tok=True))
            st["pt_rd"][7] = tmev[-1]
        prev_mev = tmev
        txl = P.rd(yname, P.dma("sync", wc[:], yin[r, :], deps=prev))
        tm2 = {}
        for half in range(2):
            bank = PB[4 + half]
            for c2 in range(22):
                tm2[half] = P.on("tensor", lambda e, c2=c2, half=half, bank=bank: e.matmul(
                    bank[:], mfst[:, c2 * 128:(c2 + 1) * 128],
                    w23[:, c2, half * 512:(half + 1) * 512],
                    start=(c2 == 0), stop=False),
                    deps=(tmev + [tw2] + prev) if c2 == 0 else (), mode="full")
            tm2[half] = P.on("tensor", lambda e, half=half, bank=bank: e.matmul(
                bank[:], st["ones_row"][0:1, 0:128],
                st["b2row"][0:1, half * 512:(half + 1) * 512],
                start=False, stop=True), deps=[tb2], tok=True, mode="k1")
        prev = []
        for half in range(2):
            tr = P.on("vector", lambda e, half=half: e.tensor_tensor(
                wd[:, half * 512:(half + 1) * 512], PB[4 + half][:],
                wc[:, half * 512:(half + 1) * 512], ALU.add),
                deps=[tm2[half], txl], tok=True)
            prev.append(tr)
        prev.append(P.dma("sync", yout[r, :], wd[:], deps=prev + P.wdeps(yoname)))


_CACHE = {}


def kernel(**inputs):
    x = np.asarray(inputs["L0_attn_a_x"])
    B, N, E = x.shape
    key = (N,)
    if key not in _CACHE:
        nc, P = build(ntok=N)
        _CACHE[key] = (nc, host_tables(N))
    nc, tables = _CACHE[key]
    w = prep_weights(inputs)
    base = {}
    base.update(w)
    base.update(tables)
    in_maps = []
    for c in range(B):
        m = dict(base)
        m["x0"] = np.ascontiguousarray(x[c])
        in_maps.append(m)
    res = run_bass_kernel_spmd(nc, in_maps, core_ids=list(range(B)))
    return np.stack([np.asarray(r["out"]) for r in res.results], axis=0)


# version 08
# speedup vs baseline: 2.2984x; 1.0460x over previous
import numpy as np
import ml_dtypes
import concourse.bass as bass
import concourse.mybir as mybir
from concourse.bass_utils import run_bass_kernel_spmd

F32 = mybir.dt.float32
BF16 = mybir.dt.bfloat16
AF = mybir.ActivationFunctionType
ALU = mybir.AluOpType
AXX = mybir.AxisListType.X
BFNP = ml_dtypes.bfloat16

DM = 1024
NH = 16
DH = 64
DEPTH = 6
DI = 2730
W1C = 5460
WIN = 256
NEG = -1.0e38
LNEPS = 1e-5
RMSEPS = float(np.finfo(np.float32).eps)
C_ATTN_A = 30.0

ENGS = ("sync", "tensor", "vector", "scalar", "gpsimd")


def host_tables(ntok):
    t = {}
    t["ident"] = np.eye(128, dtype=BFNP)
    t["ones_row"] = np.ones((1, 512), dtype=BFNP)
    p = np.arange(128)[:, None]
    c = np.arange(384)[None, :]
    t["maskb"] = np.where((c >= p) & (c <= p + WIN), 0.0, NEG).astype(np.float32)
    u = np.arange(2 * ntok - 1)[None, :]
    t["dist"] = np.abs(u - (ntok - 1) - np.arange(128)[:, None]).astype(np.float32)
    half = np.arange(0, DH, 2, dtype=np.float64)
    inv_freq = 1.0 / 10000.0 ** (half / DH)
    tt = np.arange(ntok, dtype=np.float64)
    freqs = tt[:, None] * inv_freq[None, :]
    freqs = np.concatenate([freqs, freqs], -1)
    base_scale = (half + 0.4 * DH) / (1.4 * DH)
    power = (tt - ntok // 2) / (WIN // 2)
    scale = base_scale[None, :] ** power[:, None]
    scale = np.concatenate([scale, scale], -1)
    cos = np.cos(freqs)
    sin = np.sin(freqs)
    sgn = np.where(np.arange(DH) < DH // 2, -1.0, 1.0)[None, :]
    t["cs_q"] = (cos * scale).astype(np.float32)
    t["ss_q"] = (sin * scale * sgn).astype(np.float32)
    t["cs_k"] = (cos / scale).astype(np.float32)
    t["ss_k"] = (sin / scale * sgn).astype(np.float32)
    return t


def prep_weights(inp, depth=DEPTH):
    w = {}
    cont = np.ascontiguousarray

    def pmaj(a):
        E, F = a.shape
        return cont(a.reshape(E // 128, 128, F).transpose(1, 0, 2))

    w["aw_qkv"] = pmaj(np.asarray(inp["L0_attn_a_Wqkv_w"]).T.copy())
    w["ab_qkv"] = cont(np.asarray(inp["L0_attn_a_Wqkv_b"])[None, :])
    w["aw_out"] = pmaj(np.asarray(inp["L0_attn_a_out_w"]).T.copy())
    w["ab_out"] = cont(np.asarray(inp["L0_attn_a_out_b"])[None, :])
    w["a_dd"] = cont(np.asarray(inp["L0_attn_a_distance_decay"])[None, :])
    w["bw_qkv"] = pmaj(np.concatenate(
        [np.asarray(inp["L0_attn_b_Wq"]).T, np.asarray(inp["L0_attn_b_Wk"]).T,
         np.asarray(inp["L0_attn_b_Wv"]).T], axis=1))
    w["bw_out"] = pmaj(np.asarray(inp["L0_attn_b_Wo"]).T.copy())
    w["b_qn"] = cont(np.asarray(inp["L0_attn_b_qn_w"])[None, :])
    w["b_kn"] = cont(np.asarray(inp["L0_attn_b_kn_w"])[None, :])
    w["f_wqkv"] = np.stack([pmaj(np.asarray(inp["L0_ffn_Wqkv"])[i]) for i in range(depth)])
    w["f_wo"] = np.stack([pmaj(np.asarray(inp["L0_ffn_Wo"])[i]) for i in range(depth)])
    w["f_w1"] = np.stack([pmaj(np.asarray(inp["L0_ffn_W1"])[i]) for i in range(depth)])
    w2 = np.asarray(inp["L0_ffn_W2"])
    w2p = np.zeros((depth, 2816, DM), np.float32)
    w2p[:, :DI] = w2[:depth]
    w["f_w2"] = np.stack([pmaj(w2p[i]) for i in range(depth)])
    w["f_qs"] = cont(np.asarray(inp["L0_ffn_q_scale"])[:depth])
    w["f_ks"] = cont(np.asarray(inp["L0_ffn_k_scale"])[:depth])
    w["f_lng"] = cont(np.asarray(inp["L0_ffn_ln_g"])[:depth])
    w["f_lnb"] = cont(np.asarray(inp["L0_ffn_ln_b"])[:depth])
    w["f_b1"] = cont(np.asarray(inp["L0_ffn_b1"])[:depth])
    w["f_b2"] = cont(np.asarray(inp["L0_ffn_b2"])[:depth])
    return w


class Prog:
    DPOOL = {"sync": 8, "gpsimd": 4, "scalar": 4}
    SEMN = (["sync", "tensor", "vector", "scalar", "gpsimd"]
            + [f"d{q}{i}" for q, n in DPOOL.items() for i in range(n)])
    SAFE_SAME_ENGINE = False
    SAFE_ENGINES = {"vector"}

    def __init__(self):
        self.q = {e: [] for e in ENGS}
        self.cnt = {}
        self.waited = {e: {} for e in ENGS}
        self.sems = {}
        self.ninst = 0
        self.scr_rd = {}

    def _dma_sem(self, eng):
        if not hasattr(self, "_dma_rr"):
            self._dma_rr = {}
        i = self._dma_rr.get(eng, 0)
        self._dma_rr[eng] = (i + 1) % self.DPOOL[eng]
        return f"d{eng}{i}"

    def wait(self, eng, deps):
        best = {}
        for d in deps:
            if d is None:
                continue
            s, v = d
            if v > best.get(s, 0):
                best[s] = v
        for s, v in best.items():
            if self.waited[eng].get(s, 0) < v:
                self.waited[eng][s] = v

                def mk(s=s, v=v):
                    return lambda e: e.wait_ge(self.sems[s], v)
                self.q[eng].append(mk())
                self.ninst += 1

    def on(self, eng, fn, deps=(), tok=False, dma=False, mode=None):
        if eng == "tensor" and not dma:
            assert mode is not None, "tensor ops must declare a tiling mode"
            if mode != getattr(self, "_pe_mode", None):
                lp = getattr(self, "_last_pe", None)
                if lp is not None:
                    s, v = lp

                    def mkd(s=s, v=v):
                        return lambda e: e.wait_ge(self.sems[s], v)
                    self.q[eng].append(mkd())
                    self.ninst += 1
                    self.waited[eng][s] = max(self.waited[eng].get(s, 0), v)
            self._pe_mode = mode
            tok = True
        self.wait(eng, deps)
        self.ninst += 1
        if not hasattr(self, "last_tok"):
            self.last_tok = {}
            self.clean = {}
        if not tok:
            self.q[eng].append(fn)
            self.clean[eng] = False
            return None
        s = self._dma_sem(eng) if dma else eng
        inc = 16 if dma else 1
        if dma:
            self.wait(eng, [(s, self.cnt.get(s, 0))])
        self.cnt[s] = self.cnt.get(s, 0) + inc
        v = self.cnt[s]

        def mk(fn=fn, s=s, inc=inc):
            return lambda e: fn(e).then_inc(self.sems[s], inc)
        self.q[eng].append(mk())
        if not dma and not self.SAFE_SAME_ENGINE and eng not in self.SAFE_ENGINES:
            self.waited[eng][s] = v
        if eng == "tensor" and not dma:
            self._last_pe = (s, v)
        if not dma:
            self.last_tok[eng] = (s, v)
            self.clean[eng] = True
        else:
            self.last_tok[eng] = (s, v)
            self.clean[eng] = True
        return (s, v)

    def dma(self, eng, out, in_, deps=()):
        return self.on(eng, lambda e, o=out, i=in_: e.dma_start(out=o, in_=i),
                       deps=deps, tok=True, dma=True)

    def rd(self, name, tok):
        self.scr_rd.setdefault(name, []).append(tok)
        return tok

    def wdeps(self, name):
        return self.scr_rd.pop(name, [])

    def barrier(self):
        if not hasattr(self, "last_tok"):
            self.last_tok = {}
            self.clean = {}
        for eng in ENGS:
            if self.q[eng] and not self.clean.get(eng, True):
                self.on(eng, lambda e: e.nop(), tok=True)
        snap = [(s, self.cnt.get(s, 0)) for s in self.SEMN if self.cnt.get(s, 0) > 0]
        for eng in ENGS:
            self.wait(eng, snap)

    def finish(self, blk):
        for name in ENGS:
            ops = self.q[name]

            def mksec(ops=ops):
                def section(e):
                    for f in ops:
                        f(e)
                return section
            getattr(blk, name)(mksec())


def build(ntok=4096, depth=DEPTH, parts=("a", "m1", "b", "m2"), dbg_out=(), stop=None):
    T = ntok // 128
    nc = bass.Bass("TRN2", target_bir_lowering=False)
    P = Prog()
    din = {}

    def dt_in(name, shape, dtype=F32):
        din[name] = nc.dram_tensor(name, list(shape), dtype, kind="ExternalInput").ap()

    dt_in("x0", (ntok, DM))
    dt_in("aw_qkv", (128, 8, 3 * DM)); dt_in("ab_qkv", (1, 3 * DM))
    dt_in("aw_out", (128, 8, DM)); dt_in("ab_out", (1, DM)); dt_in("a_dd", (1, 1))
    dt_in("bw_qkv", (128, 8, 3 * DM)); dt_in("bw_out", (128, 8, DM))
    dt_in("b_qn", (1, DH)); dt_in("b_kn", (1, DH))
    dt_in("f_wqkv", (depth, 128, 8, 3 * DM)); dt_in("f_wo", (depth, 128, 8, DM))
    dt_in("f_w1", (depth, 128, 8, W1C)); dt_in("f_w2", (depth, 128, 22, DM))
    dt_in("f_qs", (depth, DH)); dt_in("f_ks", (depth, DH))
    dt_in("f_lng", (depth, DM)); dt_in("f_lnb", (depth, DM))
    dt_in("f_b1", (depth, W1C)); dt_in("f_b2", (depth, DM))
    dt_in("ident", (128, 128), BF16); dt_in("ones_row", (1, 512), BF16)
    dt_in("maskb", (128, 384)); dt_in("dist", (128, 2 * ntok - 1))
    dt_in("cs_q", (ntok, DH)); dt_in("ss_q", (ntok, DH))
    dt_in("cs_k", (ntok, DH)); dt_in("ss_k", (ntok, DH))
    dout = nc.dram_tensor("out", [ntok, DM], F32, kind="ExternalOutput").ap()

    def dt_sc(name, shape, dtype=F32):
        kind = "ExternalOutput" if name in dbg_out else "Internal"
        return nc.dram_tensor(name, list(shape), dtype, kind=kind).ap()

    st = dict(ntok=ntok, T=T, depth=depth, nc=nc, din=din, dout=dout, stop=stop)
    st["X1"] = dt_sc("X1", (ntok, DM))
    st["X2"] = dt_sc("X2", (ntok, DM))
    st["X3"] = dt_sc("X3", (ntok, DM))
    st["YA"] = dt_sc("YA", (ntok, DM))
    st["YB"] = dt_sc("YB", (ntok, DM))
    st["QF"] = dt_sc("QF", (8, 128, ntok), BF16)
    st["KF"] = dt_sc("KF", (8, 128, ntok), BF16)
    st["VSP"] = dt_sc("VSP", (T, 128, NH * 65), BF16)
    st["OF"] = dt_sc("OF", (8, 128, ntok), BF16)
    st["BNC"] = dt_sc("BNC", (8, ntok))
    st["WRD"] = dt_sc("WRD", (depth, W1C), BF16)

    from contextlib import ExitStack
    ctx = ExitStack()
    sb = lambda name, shape, dtype: ctx.enter_context(nc.sbuf_tensor("s_" + name, list(shape), dtype))
    with ctx:
        st["AR1"] = sb("AR1", [128, 43680], BF16)
        st["AR2"] = sb("AR2", [128, 30208], BF16)
        st["ident"] = sb("ident", [128, 128], BF16)
        st["ones_row"] = sb("ones_row", [1, 512], BF16)
        st["maskb"] = sb("maskb", [128, 384], F32)
        st["wab"] = sb("wab", [128, 2048], F32)
        st["wa"] = st["wab"].ap()[:, 0:1024]
        st["wb"] = st["wab"].ap()[:, 1024:2048]
        st["wsf2"] = st["wab"].ap()
        st["wc"] = sb("wc", [128, 1024], F32)
        st["wd"] = sb("wd", [128, 1024], F32)
        st["we"] = sb("we", [128, 1024], F32)
        st["wbf"] = sb("wbf", [128, 1024], BF16)
        st["wbf3"] = sb("wbf3", [128, 1024], BF16)
        st["wbf4"] = sb("wbf4", [128, 1024], BF16)
        st["wbf5"] = sb("wbf5", [128, 1024], BF16)
        st["vst"] = sb("vst", [128, NH * 65], BF16)
        st["gsb"] = sb("gsb", [128, 512], F32)
        st["stat"] = sb("stat", [128, 32], F32)
        st["sml"] = sb("sml", [128, 160], F32)
        st["ptb"] = sb("ptb", [128, 8 * 512], BF16)
        st["wrow"] = sb("wrow", [1, W1C], BF16)
        st["b2row"] = sb("b2row", [1, DM], BF16)
        st["lnbc"] = sb("lnbc", [128, 8], BF16)
        st["qnc"] = sb("qnc", [128, 1], F32)
        st["knc"] = sb("knc", [128, 1], F32)
        st["negC"] = sb("negC", [128, 1], F32)
        st["onec"] = sb("onec", [128, 1], BF16)
        st["rr"] = sb("rr", [1, 512], F32)
        st["sconst"] = sb("sconst", [128, 4], F32)
        for col, val in enumerate([LNEPS, RMSEPS, -C_ATTN_A]):
            nc.const_aps.register(F32, float(val), st["sconst"].ap()[:, col:col + 1])
        st["PBIG"] = [ctx.enter_context(nc.psum_tensor(f"pbig{i}", [128, 2048], F32))
                      for i in range(2)]
        st["PB"] = [st["PBIG"][i // 4].ap()[:, 512 * (i % 4):512 * (i % 4 + 1)]
                    for i in range(8)]
        st["pt_rd"] = {}
        for sname in Prog.SEMN:
            P.sems[sname] = ctx.enter_context(nc.semaphore(sname))
        blk = ctx.enter_context(nc.Block())
        emit_program(P, st, parts)
        P.finish(blk)
    return nc, P


def f32v(arena, c0f, ncols, rows=128, p0=0):
    return arena.ap().bitcast(F32)[p0:p0 + rows, c0f:c0f + ncols]


def bf16v(arena, c0, ncols, rows=128, p0=0):
    return arena[p0:p0 + rows, c0:c0 + ncols]


def ln_stats(P, st, xt, dump, eps, deps, sb=0):
    stat = st["stat"]
    t1 = P.on("vector", lambda e: e.tensor_reduce(stat[:, sb + 0:sb + 1], xt[:], AXX, ALU.add),
              deps=deps, tok=True)
    t2 = P.on("scalar", lambda e: e.activation(dump, xt[:], AF.Square,
                                               accum_out=stat[:, sb + 1:sb + 2]),
              deps=deps, tok=True)
    t4 = P.on("vector", lambda e: e.tensor_scalar(stat[:, sb + 2:sb + 3], stat[:, sb + 0:sb + 1],
                                                  -1.0 / 1024, None, ALU.mult, ALU.bypass),
              deps=[t1], tok=True)
    t5 = P.on("vector", lambda e: e.tensor_tensor(stat[:, sb + 3:sb + 4], stat[:, sb + 2:sb + 3],
                                                  stat[:, sb + 2:sb + 3], ALU.mult),
              deps=[t4], tok=True)
    t6 = P.on("vector", lambda e: e.tensor_scalar(stat[:, sb + 4:sb + 5], stat[:, sb + 1:sb + 2],
                                                  1.0 / 1024, None, ALU.mult, ALU.bypass),
              deps=[t2], tok=True)
    t7 = P.on("vector", lambda e: e.tensor_tensor(stat[:, sb + 5:sb + 6], stat[:, sb + 4:sb + 5],
                                                  stat[:, sb + 3:sb + 4], ALU.subtract),
              deps=[t5, t6], tok=True)
    t8 = P.on("scalar", lambda e: e.activation(stat[:, sb + 6:sb + 7], stat[:, sb + 5:sb + 6],
                                               AF.Ln, bias=float(eps), scale=1.0),
              deps=[t7], tok=True)
    t9 = P.on("scalar", lambda e: e.activation(stat[:, sb + 7:sb + 8], stat[:, sb + 6:sb + 7],
                                               AF.Exp, scale=-0.5), deps=[t8], tok=True)
    return stat[:, sb + 2:sb + 3], stat[:, sb + 7:sb + 8], [t4, t9, t2]


def transpose_tile(P, st, src_bf, dst_strided, bank_idx, deps, nchunk=8, dst_chunks=None):
    bank = st["PB"][bank_idx]
    pv = bank.bitcast(BF16)
    alldeps = list(deps) + [st["pt_rd"].get(bank_idx)]
    tl = None
    for c in range(nchunk):
        tl = P.on("tensor", lambda e, c=c: e.transpose(
            pv[:, c * 128:(c + 1) * 128], src_bf[:, c * 128:(c + 1) * 128], st["ident"][:]),
            deps=alldeps if c == 0 else (), tok=(c == nchunk - 1), mode="tr")
    pvv = pv.rearrange("p (c n) -> p c n", c=8)[:, 0:nchunk, :]
    tev = P.on("scalar", lambda e: e.copy(dst_strided, pvv), deps=[tl], tok=True)
    st["pt_rd"][bank_idx] = tev
    return tev


def shifted_load(P, st, ysrc, yname, tt, dst, deps=()):
    r0 = tt * 128
    t1 = P.rd(yname, P.dma("sync", dst[:, 0:512], ysrc[r0:r0 + 128, 0:512], deps=deps))
    if tt == 0:
        t2 = P.rd(yname, P.dma("sync", dst[1:128, 512:1024], ysrc[0:127, 512:1024]))
        t3 = P.on("gpsimd", lambda e: e.memset(dst[0:1, 512:1024], 0.0), deps=deps, tok=True)
        return [t1, t2, t3]
    t2 = P.rd(yname, P.dma("sync", dst[:, 512:1024], ysrc[r0 - 1:r0 + 127, 512:1024]))
    return [t1, t2]


def bcast_col(P, st, dram_row_ap, dst, deps=()):
    return P.dma("sync", dst, dram_row_ap.partition_broadcast(dst.shape[0]), deps=deps)


def emit_program(P, st, parts):
    din = st["din"]
    P.dma("sync", st["ident"][:], din["ident"][:])
    P.dma("sync", st["ones_row"][:], din["ones_row"][:])
    P.dma("sync", st["maskb"][:], din["maskb"][:])
    P.on("gpsimd", lambda e: e.memset(st["onec"][:], 1.0), tok=True)
    for col, val in enumerate([LNEPS, RMSEPS, -C_ATTN_A]):
        P.on("vector", lambda e, col=col, val=val: e.memset(st["sconst"][:, col:col + 1],
                                                            float(val)), tok=True)
    P.barrier()

    x_cur, x_name = din["x0"], "x0"
    if "a" in parts:
        ph_dense(P, st, "a", x_cur, x_name, st["X1"], "X1")
        x_cur, x_name = st["X1"], "X1"
    if "m1" in parts:
        ph_ffnmod(P, st, x_cur, x_name, st["X2"], "X2", inv=0)
        x_cur, x_name = st["X2"], "X2"
    if "b" in parts:
        ph_dense(P, st, "b", x_cur, x_name, st["X3"], "X3")
        x_cur, x_name = st["X3"], "X3"
    if "m2" in parts:
        ph_ffnmod(P, st, x_cur, x_name, st["dout"], "out", inv=1)
        x_cur = st["dout"]
    if x_cur is not st["dout"]:
        P.barrier()
        prev = []
        for tt in range(st["T"]):
            r = slice(tt * 128, tt * 128 + 128)
            t = P.dma("sync", st["wc"][:], x_cur[r, :], deps=prev)
            tc = P.on("vector", lambda e: e.tensor_copy(st["wd"][:], st["wc"][:]),
                      deps=[t], tok=True)
            prev = [tc, P.dma("sync", st["dout"][r, :], st["wd"][:], deps=[tc])]
    P.wait("tensor", [(s, P.cnt.get(s, 0)) for s in Prog.SEMN if s.startswith("d")])
    P.on("tensor", lambda e: e.nop(), mode="nop")


def ph_dense(P, st, var, x_in, x_name, x_out, xo_name):
    din = st["din"]; T = st["T"]; ntok = st["ntok"]
    AR1, AR2, PB = st["AR1"], st["AR2"], st["PB"]
    wa, wb, wc, wd = st["wa"], st["wb"], st["wc"], st["wd"]
    wbf, vst, ptb = st["wbf"], st["vst"], st["ptb"]
    negC, sml, rr = st["negC"], st["sml"], st["rr"]
    P.barrier()

    lnfm = bf16v(AR1, 0, 8 * ntok)
    lnfm3 = lnfm.rearrange("p (c n) -> p c n", c=8)
    wv = bf16v(AR1, 8 * ntok, 8 * DM)
    wv3 = wv.rearrange("p (c n) -> p c n", c=8)
    wslq = bf16v(AR1, 8 * ntok + 8 * DM, 8 * 128)
    wslk = bf16v(AR1, 8 * ntok + 8 * DM + 8 * 128, 8 * 128)
    wslq3 = wslq.rearrange("p (c n) -> p c n", c=8)
    wslk3 = wslk.rearrange("p (c n) -> p c n", c=8)
    ofm = bf16v(AR1, 8 * ntok + 8 * DM, 8 * 128)
    ofm3 = ofm.rearrange("p (c n) -> p c n", c=8)
    wsfA = [f32v(AR1, 4 * ntok, 2048), f32v(AR1, 4 * ntok + 2048, 2048)]
    if var == "a":
        dec = f32v(AR2, 0, 2 * ntok - 1)
        c0 = 2 * (2 * ntok - 1)
        rstdb = None
    else:
        dec = None
        rstdb = f32v(AR2, 0, ntok)
        c0 = 2 * ntok
    q2 = bf16v(AR2, c0, ntok); c0 += ntok
    k2 = bf16v(AR2, c0, ntok); c0 += ntok
    if var == "b":
        tmpn = bf16v(AR2, c0, ntok); c0 += ntok
    else:
        tmpn = None
    v2p = bf16v(AR2, c0, T * 130); c0 += T * 130
    v2p3 = v2p.rearrange("p (t n) -> p t n", t=T)
    rbq = st["wbf3"].ap().bitcast(F32)[0:64, 0:512]
    obf = st["wbf"][0:64, 0:512]
    assert c0 <= 30208, c0

    wq_d = din["aw_qkv" if var == "a" else "bw_qkv"]
    wo_d = din["aw_out" if var == "a" else "bw_out"]

    tbias = None
    if var == "a":
        tdd = bcast_col(P, st, din["a_dd"][0:1, :], sml[:, 8:9])
        tng = P.on("vector", lambda e: e.tensor_scalar(sml[:, 9:10], sml[:, 8:9], -1.0, None,
                                                       ALU.mult, ALU.bypass), deps=[tdd], tok=True)
        tna = P.on("vector", lambda e: e.tensor_tensor(sml[:, 10:11], sml[:, 8:9],
                                                       sml[:, 9:10], ALU.min), deps=[tng], tok=True)
        tdist = P.dma("sync", dec, din["dist"][:])
        tdec = P.on("scalar", lambda e: e.activation(dec, dec, AF.Exp, scale=sml[:, 10:11]),
                    deps=[tna, tdist], tok=True)
        tbias_v = P.dma("gpsimd", st["b2row"][:], din["ab_qkv"][:, 2 * DM:3 * DM])
        tbias_q = P.dma("gpsimd", st["wrow"][0:1, 0:DM], din["ab_qkv"][:, 0:DM])
        tbias_k = P.dma("gpsimd", st["wrow"][0:1, DM:2 * DM], din["ab_qkv"][:, DM:2 * DM])
        tbias = [tbias_q, tbias_k]
        tnc = None
    else:
        tq1 = P.dma("sync", st["qnc"][0:64, :],
                    din["b_qn"].rearrange("o (p i) -> (o p) i", i=1))
        tq2 = P.dma("sync", st["qnc"][64:128, :],
                    din["b_qn"].rearrange("o (p i) -> (o p) i", i=1))
        tk1 = P.dma("sync", st["knc"][0:64, :],
                    din["b_kn"].rearrange("o (p i) -> (o p) i", i=1))
        tk2 = P.dma("sync", st["knc"][64:128, :],
                    din["b_kn"].rearrange("o (p i) -> (o p) i", i=1))
        tqb = bcast_col(P, st, din["b_qn"][0:1, :], sml[:, 16:80])
        tkb = bcast_col(P, st, din["b_kn"][0:1, :], sml[:, 80:144])
        t1 = P.on("vector", lambda e: e.tensor_reduce(sml[:, 0:1], sml[:, 16:80], AXX, ALU.max,
                                                      apply_absolute_value=True),
                  deps=[tqb], tok=True)
        t2 = P.on("vector", lambda e: e.tensor_reduce(sml[:, 1:2], sml[:, 80:144], AXX, ALU.max,
                                                      apply_absolute_value=True),
                  deps=[tkb], tok=True)
        t3m = P.on("vector", lambda e: e.tensor_tensor(sml[:, 2:3], sml[:, 0:1], sml[:, 1:2],
                                                       ALU.mult), deps=[t1, t2], tok=True)
        tnc = P.on("vector", lambda e: e.tensor_scalar(negC[:], sml[:, 2:3], -8.0, None,
                                                       ALU.mult, ALU.bypass), deps=[t3m], tok=True)
        tdec = None
        st["_rmsdep"] = [tq1, tq2, tk1, tk2]

    prev_tn = {0: None, 1: None}
    prev_ev = None
    for tt in range(T):
        r = slice(tt * 128, tt * 128 + 128)
        par = tt % 2
        ld = (wa, st["we"])[par]
        tld = P.rd(x_name, P.dma("sync", ld[:], x_in[r, :], deps=[prev_tn[par]]))
        negmu, rstd, tst = ln_stats(P, st, ld, (wbf, st["wbf4"])[par][:], LNEPS,
                                    [tld], sb=12 * par)
        prev_tn[par] = P.on("vector", lambda e, ld=ld, par=par, negmu=negmu, rstd=rstd:
                            e.tensor_scalar((wbf, st["wbf4"])[par][:], ld[:], negmu, rstd,
                                            ALU.add, ALU.mult),
                            deps=tst + [prev_ev], tok=True)
        prev_ev = transpose_tile(P, st, (wbf, st["wbf4"])[par],
                                 lnfm3[:, :, tt * 128:(tt + 1) * 128], 6, [prev_tn[par]])

    if st["stop"] == "ph1":
        return
    twv = P.dma("gpsimd", wv3, wq_d[:, :, 2 * DM:3 * DM])
    vst3 = vst.ap().rearrange("p (h n) -> p h n", h=NH)
    tmem = P.on("gpsimd", lambda e: e.memset(vst3[:, :, 64:65], 1.0), tok=True)
    prev_sp = None
    prev_ev = [tmem, prev_ev]
    for tt in range(T):
        tm = {}
        for half in range(2):
            bank = PB[half]
            nmm = 8
            for c in range(8):
                lastmm = (c == 7) and var != "a"
                tm[half] = P.on("tensor", lambda e, c=c, half=half, bank=bank, tt=tt, lastmm=lastmm: e.matmul(
                    bank[:], lnfm3[:, c, tt * 128:(tt + 1) * 128],
                    wv3[:, c, half * 512:(half + 1) * 512],
                    start=(c == 0), stop=lastmm),
                    deps=([twv] + prev_ev) if c == 0 else (), tok=lastmm, mode="full")
            if var == "a":
                tm[half] = P.on("tensor", lambda e, half=half, bank=bank: e.matmul(
                    bank[:], st["ones_row"][0:1, 0:128],
                    st["b2row"][0:1, half * 512:(half + 1) * 512],
                    start=False, stop=True),
                    deps=[tbias_v], tok=True, mode="k1")
        prev_ev = []
        for half in range(2):
            pv = PB[half].rearrange("p (h n) -> p h n", h=8)
            prev_ev.append(P.on("scalar", lambda e, half=half, pv=pv: e.copy(
                vst3[:, half * 8:(half + 1) * 8, 0:64], pv),
                deps=[tm[half], prev_sp], tok=True))
        prev_sp = P.dma("sync", st["VSP"][tt], vst[:], deps=prev_ev + P.wdeps("VSP"))

    if st["stop"] == "ph2":
        return
    nqg = ntok // 512
    oev = [prev_sp]
    for p in list(range(8))[st.get('_ponly', 0):]:
        tq = P.dma("gpsimd", wslq3, wq_d[:, :, 128 * p:128 * (p + 1)], deps=oev)
        tk = P.dma("gpsimd", wslk3, wq_d[:, :, DM + 128 * p:DM + 128 * (p + 1)], deps=oev)
        tev_q = list(oev)
        for (wsl3, tw_, dst, bofs) in ((wslq3, tq, q2, 128 * p), (wslk3, tk, k2, DM + 128 * p)):
            new_ev = []
            for qg in range(nqg):
                bank = PB[qg % 2]
                tmq = None
                for c in range(8):
                    lastmm = (c == 7) and var != "a"
                    tmq = P.on("tensor", lambda e, c=c, qg=qg, bank=bank, wsl3=wsl3, lastmm=lastmm: e.matmul(
                        bank[:], wsl3[:, c, :], lnfm3[:, c, qg * 512:(qg + 1) * 512],
                        start=(c == 0), stop=lastmm),
                        deps=([tw_] + tev_q) if c == 0 else (), tok=lastmm, mode="full")
                if var == "a":
                    tmq = P.on("tensor", lambda e, qg=qg, bank=bank, bofs=bofs: e.matmul(
                        bank[:], st["wrow"][0:1, bofs:bofs + 128],
                        st["ones_row"][0:1, 0:512], start=False, stop=True),
                        deps=tbias, tok=True, mode="k1")
                new_ev.append(P.on("scalar", lambda e, qg=qg, bank=bank, dst=dst: e.copy(
                    dst[:, qg * 512:(qg + 1) * 512], bank[:]), deps=[tmq], tok=True))
            tev_q = new_ev
        qS, kS = q2, k2
        if st["stop"] == "proj" and p == 0:
            return
        if var == "b":
            t_rms1 = rms_norm_pair(P, st, q2, tmpn, st["qnc"], 2, rstdb, tev_q)
            t_rms2 = rms_norm_pair(P, st, k2, q2, st["knc"], 4, rstdb, [t_rms1])
            qS, kS = tmpn, q2
            tev_q = [t_rms2]
        tv = P.dma("sync", v2p3,
                   st["VSP"][:, :, 130 * p:130 * (p + 1)].rearrange("t p n -> p t n"),
                   deps=tev_q + P.wdeps("v2p"))
        P.rd("VSP", tv)
        slot_rd = {}
        sbank_rd = {}
        texp_par = {}
        for qp0 in range(0, nqg, 2):
            qgs = [qg for qg in (qp0, qp0 + 1) if qg < nqg]
            tdone = {}
            txp = {}
            nq = len(qgs)
            SW = nq * 2 * 512
            Sbig = st["PBIG"][0].ap()[:, 0:SW]
            Sbig3 = Sbig.rearrange("p (h n) -> p h n", h=2)
            u0b = (ntok - 1) + 512 * qgs[0]
            for t in range(T + 1):
                par = t % 2
                wsfp = wsfA[par]
                ptbp = ptb[:, par * 2048:par * 2048 + SW]
                if t < T:
                    for h in range(2):
                        for qI, qg in enumerate(qgs):
                            sidx = h * nq + qI
                            rows = slice(64 * h, 64 * h + 64)
                            tS = P.on("tensor",
                                      lambda e, t=t, h=h, qg=qg, sidx=sidx, rows=rows:
                                      e.matmul(st["PBIG"][0].ap()[:, sidx * 512:(sidx + 1) * 512],
                                               kS[rows, t * 128:(t + 1) * 128],
                                               qS[rows, qg * 512:(qg + 1) * 512],
                                               start=True, stop=True),
                                      deps=([tv, tnc, tdec] + tev_q
                                            + st.get("_oev_par", [])
                                            if (t < 1 and qp0 == 0) else [])
                                      + [sbank_rd.get("rel")],
                                      tok=True, mode="r64")
                            sbank_rd["S"] = tS
                avrel_prev = slot_rd.get("last")
                if t >= 1:
                    ppav = (t - 1) % 2
                    for qI, qg in enumerate(qgs):
                        for h in range(2):
                            ob = PB[4 + 2 * qI + h]
                            slot = h * nq + qI
                            tdone[(qI, h)] = P.on(
                                "tensor", lambda e, t=t, h=h, ob=ob, slot=slot, ppav=ppav:
                                e.matmul(ob[0:65, :],
                                         v2p3[:, t - 1, 65 * h:65 * h + 65],
                                         ptb[:, ppav * 2048 + slot * 512:
                                             ppav * 2048 + (slot + 1) * 512],
                                         start=(t - 1 == 0), stop=(t - 1 == T - 1)),
                                deps=[txp.get("x")]
                                + (st.get("_oev_par", []) if t == 1 else []),
                                tok=True, mode="full")
                    slot_rd["last"] = tdone[(nq - 1, 1)]
                if t < T:
                    tS = sbank_rd["S"]
                    if var == "a":
                        u0 = u0b - 128 * t
                        din1 = dec[:, u0:u0 + nq * 512].unsqueeze(1) \
                            .broadcast_to((128, 2, nq * 512))
                        wsf3 = wsfp[:, 0:SW].rearrange("p (h n) -> p h n", h=2)
                        trd = P.on("vector", lambda e, din1=din1, wsf3=wsf3:
                                   e.scalar_tensor_tensor(wsf3, Sbig3, 0.125, din1,
                                                          ALU.mult, ALU.mult),
                                   deps=[tS, texp_par.get(par)], tok=True)
                        sbank_rd["rel"] = trd
                        tx = P.on("scalar", lambda e, wsfp=wsfp, ptbp=ptbp, SW=SW:
                                  e.activation(ptbp, wsfp[:, 0:SW], AF.Exp,
                                               bias=-C_ATTN_A, scale=1.0),
                                  deps=[trd, avrel_prev], tok=True)
                    else:
                        tx = P.on("scalar", lambda e, ptbp=ptbp: e.activation(
                            ptbp, Sbig, AF.Exp, bias=negC[:], scale=0.125),
                            deps=[tS, avrel_prev], tok=True)
                        sbank_rd["rel"] = tx
                    texp_par[par] = tx
                    txp["x"] = tx
            oev2 = []
            oevw = {}
            for qI, qg in enumerate(qgs):
                for h in range(2):
                    ob = PB[4 + 2 * qI + h]
                    rrh = rr[0:1, 0:512] if h == 0 else st["wc"].ap().bitcast(F32)[0:1, 0:512]
                    rbqh = rbq if h == 0 else st["wd"].ap().bitcast(F32)[0:64, 0:512]
                    obfh = obf if h == 0 else st["wbf4"][0:64, 0:512]
                    t1 = P.on("vector", lambda e, ob=ob, rrh=rrh: e.reciprocal(
                        rrh, ob[64:65, :]),
                        deps=[tdone[(qI, h)]] + P.wdeps(f"rr{h}"), tok=True)
                    t2 = P.rd(f"rr{h}", P.dma("sync", st["BNC"][h:h + 1, 0:512], rrh,
                                              deps=[t1] + P.wdeps(f"BNC{h}")))
                    t3 = P.rd(f"BNC{h}", P.dma("sync", rbqh,
                                               st["BNC"][h:h + 1, 0:512].partition_broadcast(64),
                                               deps=[t2] + P.wdeps(f"rbq{h}")))
                    t4 = P.rd(f"rbq{h}", P.on("vector",
                                              lambda e, ob=ob, obfh=obfh, rbqh=rbqh:
                                              e.tensor_tensor(obfh, ob[0:64, :], rbqh,
                                                              ALU.mult),
                                              deps=[t3] + P.wdeps(f"obf{h}"), tok=True))
                    t5 = P.rd(f"obf{h}", P.dma("sync",
                                               st["OF"][p, 64 * h:64 * h + 64,
                                                        qg * 512:(qg + 1) * 512],
                                               obfh, deps=[t4] + P.wdeps("OF")))
                    oev2 += [t4, t5]
                    oevw[(qI, h)] = t4
            st["_oev_par"] = oev2
            st["_oev_par_w"] = oevw
            oev = oev2
    if st["stop"] == "ph3":
        return
    P.barrier()
    two = P.dma("gpsimd", wv3, wo_d[:, :, :])
    tbo2 = P.dma("gpsimd", st["b2row"][:], din["ab_out"][:]) if var == "a" else None
    prev = []
    for tt in range(T):
        r = slice(tt * 128, tt * 128 + 128)
        tof = P.rd("OF", P.dma("sync", ofm3,
                               st["OF"][:, :, tt * 128:(tt + 1) * 128].rearrange("c p n -> p c n"),
                               deps=prev))
        txl = P.rd(x_name, P.dma("sync", wc[:], x_in[r, :], deps=prev))
        tmm = {}
        for half in range(2):
            bank = PB[half]
            for c in range(8):
                lastmm = (c == 7) and var != "a"
                tmm[half] = P.on("tensor", lambda e, c=c, half=half, bank=bank, lastmm=lastmm: e.matmul(
                    bank[:], ofm3[:, c, :], wv3[:, c, half * 512:(half + 1) * 512],
                    start=(c == 0), stop=lastmm),
                    deps=([tof, two] + prev) if c == 0 else (), tok=lastmm, mode="full")
            if var == "a":
                tmm[half] = P.on("tensor", lambda e, half=half, bank=bank: e.matmul(
                    bank[:], st["ones_row"][0:1, 0:128],
                    st["b2row"][0:1, half * 512:(half + 1) * 512], start=False, stop=True),
                    deps=[tbo2], tok=True, mode="k1")
        prev = []
        for half in range(2):
            tr = P.on("vector", lambda e, half=half: e.tensor_tensor(
                wd[:, half * 512:(half + 1) * 512], PB[half][:],
                wc[:, half * 512:(half + 1) * 512], ALU.add),
                deps=[tmm[half], txl], tok=True)
            prev.append(tr)
        prev.append(P.dma("sync", x_out[r, :], wd[:], deps=prev + P.wdeps(xo_name)))


def rms_norm_pair(P, st, src, dst, ncol, bnc0, rstdb, deps):
    ntok = st["ntok"]; PB = st["PB"]; rr = st["rr"]
    nqg = ntok // 512
    tsq = P.on("scalar", lambda e: e.activation(dst, src, AF.Square), deps=deps, tok=True)
    dmas = []
    prev5 = {0: None, 1: None}
    for h in range(2):
        rows = slice(64 * h, 64 * h + 64)
        for qg in range(nqg):
            tm = P.on("tensor", lambda e, h=h, qg=qg, rows=rows: e.matmul(
                PB[2 + h][0:1, :], st["onec"][rows, :], dst[rows, qg * 512:(qg + 1) * 512],
                start=True, stop=True), deps=[tsq, prev5[h]] + (dmas[-1:] if dmas else []),
                tok=True, mode="rms")
            t5 = P.on("scalar", lambda e, h=h: e.activation(rr[:, 0:512], PB[2 + h][0:1, :],
                                                            AF.Ln, bias=RMSEPS, scale=1.0 / 64),
                      deps=[tm], tok=True)
            t6 = P.on("scalar", lambda e: e.activation(rr[:, 0:512], rr[:, 0:512], AF.Exp,
                                                       scale=-0.5), deps=[t5], tok=True)
            prev5[h] = t6
            dmas.append(P.dma("sync", st["BNC"][bnc0 + h:bnc0 + h + 1, qg * 512:(qg + 1) * 512],
                              rr[0:1, 0:512], deps=[t6] + P.wdeps(f"BNCrms{bnc0 + h}")))
    tb = []
    for h in range(2):
        tb.append(P.rd(f"BNCrms{bnc0 + h}",
                       P.dma("sync", rstdb[64 * h:64 * h + 64, :],
                             st["BNC"][bnc0 + h:bnc0 + h + 1, :].partition_broadcast(64),
                             deps=dmas + P.wdeps("rstdb"))))
    tn = P.rd("rstdb", P.on("vector", lambda e: e.scalar_tensor_tensor(dst, src, ncol[:], rstdb,
                                                                      ALU.mult, ALU.mult),
                            deps=tb + st.get("_rmsdep", []), tok=True))
    return tn


def ph_ffnmod(P, st, x_in, x_name, x_out, xo_name, inv):
    T = st["T"]
    ycur, yname = st["YA"], "YA"
    ynext, ynname = st["YB"], "YB"
    P.barrier()
    prevl = {0: [], 1: []}
    for tt in range(T):
        r = slice(tt * 128, tt * 128 + 128)
        par = tt % 2
        ld = (st["wa"], st["we"])[par]
        dst = (st["wb"], st["wd"])[par]
        tld = P.rd(x_name, P.dma("sync", ld[:], x_in[r, :], deps=prevl[par]))
        negmu, rstd, tst = ln_stats(P, st, ld, dst[:], LNEPS, [tld], sb=12 * par)
        tn = P.on("vector", lambda e, ld=ld, dst=dst, negmu=negmu, rstd=rstd:
                  e.tensor_scalar(dst[:], ld[:], negmu, rstd, ALU.add, ALU.mult),
                  deps=tst, tok=True)
        prevl[par] = [tn, P.dma("sync", ycur[r, :], dst[:], deps=[tn] + P.wdeps(yname))]
    if st["stop"] == "y0":
        return
    for i in range(st["depth"]):
        ph_lattn(P, st, i, ycur, yname, ynext, ynname)
        if st["stop"] in ("lattnA", "lattnB", "lattnC"):
            return
        ycur, yname, ynext, ynname = ynext, ynname, ycur, yname
        ph_lffn(P, st, i, ycur, yname, ynext, ynname, inv)
        if st["stop"] == "lffn":
            return
        ycur, yname, ynext, ynname = ynext, ynname, ycur, yname
    P.barrier()
    prevl = {0: [], 1: []}
    prev_store = None
    for tt in range(T):
        r = slice(tt * 128, tt * 128 + 128)
        par = tt % 2
        la = (st["wa"], st["wd"])[par]
        lb = (st["wb"], st["we"])[par]
        t1 = P.rd(x_name, P.dma("sync", la[:], x_in[r, :], deps=prevl[par]))
        t2 = P.rd(yname, P.dma("sync", lb[:], ycur[r, :], deps=prevl[par]))
        t3 = P.on("vector", lambda e, la=la, lb=lb: e.tensor_tensor(
            st["wc"][:], la[:], lb[:], ALU.add),
            deps=[t1, t2, prev_store], tok=True)
        prev_store = P.dma("sync", x_out[r, :], st["wc"][:], deps=[t3] + P.wdeps(xo_name))
        prevl[par] = [t3, prev_store]


def ph_lattn(P, st, li, yin, yname, yout, yoname):
    din = st["din"]; T = st["T"]; ntok = st["ntok"]
    AR1, AR2, PB = st["AR1"], st["AR2"], st["PB"]
    wa, wb, wc, wd, we = st["wa"], st["wb"], st["wc"], st["wd"], st["we"]
    wbf, vst, ptb = st["wbf"], st["vst"], st["ptb"]
    sml, rr, negC = st["sml"], st["rr"], st["negC"]
    P.barrier()

    wqkv = bf16v(AR1, 0, 8 * 3 * DM)
    wqkv3 = wqkv.rearrange("p (c n) -> p c n", c=8)
    rotc = 8 * 3 * DM
    CS = {}; SS = {}
    for nm in ("q", "k"):
        CS[nm] = f32v(AR1, rotc // 2, T * DH).rearrange("p (t d) -> p t d", d=DH)
        rotc += 2 * T * DH
        SS[nm] = f32v(AR1, rotc // 2, T * DH).rearrange("p (t d) -> p t d", d=DH)
        rotc += 2 * T * DH
    assert rotc <= 43680
    lnf3 = st["wbf3"].ap().rearrange("p (c n) -> p c n", c=8)
    vst3 = vst.ap().rearrange("p (h n) -> p h n", h=NH)

    tw = P.dma("gpsimd", wqkv3, din["f_wqkv"][li])
    tcs = {}
    for nm in ("q", "k"):
        tcs[nm] = [P.dma("sync", CS[nm], din[f"cs_{nm}"].rearrange("(t p) d -> p t d", p=128)),
                   P.dma("sync", SS[nm], din[f"ss_{nm}"].rearrange("(t p) d -> p t d", p=128))]
    tqsb = bcast_col(P, st, din["f_qs"][li:li + 1, :], sml[:, 16:80])
    tksb = bcast_col(P, st, din["f_ks"][li:li + 1, :], sml[:, 80:144])
    tsw = [P.on("vector", lambda e: e.tensor_copy(we[:, 0:32], sml[:, 48:80]),
                deps=[tqsb], tok=True),
           P.on("vector", lambda e: e.tensor_copy(we[:, 32:64], sml[:, 16:48]),
                deps=[tqsb], tok=True),
           P.on("vector", lambda e: e.tensor_copy(we[:, 64:96], sml[:, 112:144]),
                deps=[tksb], tok=True),
           P.on("vector", lambda e: e.tensor_copy(we[:, 96:128], sml[:, 80:112]),
                deps=[tksb], tok=True)]
    tf = {}
    for j, nm in enumerate(("q", "k")):
        scol = sml[:, 16:80] if nm == "q" else sml[:, 80:144]
        swv = we[:, 0:64] if nm == "q" else we[:, 64:128]
        tf[nm] = [P.on("vector", lambda e, nm=nm, scol=scol: e.tensor_tensor(
            CS[nm], CS[nm], scol.unsqueeze(1).broadcast_to((128, T, DH)), ALU.mult),
            deps=[tcs[nm][0], tqsb, tksb], tok=True),
            P.on("vector", lambda e, nm=nm, swv=swv: e.tensor_tensor(
                SS[nm], SS[nm], swv.unsqueeze(1).broadcast_to((128, T, DH)), ALU.mult),
                deps=[tcs[nm][1]] + tsw, tok=True)]
    t1 = P.on("vector", lambda e: e.tensor_reduce(sml[:, 0:1], sml[:, 16:80], AXX, ALU.max,
                                                  apply_absolute_value=True),
              deps=[tqsb] + tf["q"], tok=True)
    t2 = P.on("vector", lambda e: e.tensor_reduce(sml[:, 1:2], sml[:, 80:144], AXX, ALU.max,
                                                  apply_absolute_value=True),
              deps=[tksb] + tf["k"], tok=True)
    t3m = P.on("vector", lambda e: e.tensor_tensor(sml[:, 2:3], sml[:, 0:1], sml[:, 1:2],
                                                   ALU.mult), deps=[t1, t2], tok=True)
    tnc = P.on("vector", lambda e: e.tensor_scalar(negC[:], sml[:, 2:3], -8.0, None,
                                                   ALU.mult, ALU.bypass), deps=[t3m], tok=True)
    tmem = P.on("gpsimd", lambda e: e.memset(vst3[:, :, 64:65], 1.0), tok=True)

    prev_sp = [tmem]
    prev_rot = {"q": None, "k": None}
    for tt in range(T):
        tls = shifted_load(P, st, yin, yname, tt, wa, deps=[prev_rot["k"]])
        tcst = P.on("scalar", lambda e: e.copy(wbf[:], wa[:]), deps=tls, tok=True)
        tsf = transpose_tile(P, st, wbf, lnf3, 6, [tcst])
        tmv = {}
        for g in range(6):
            bank = PB[g]
            for c in range(8):
                tmv[g] = P.on("tensor", lambda e, g=g, c=c, bank=bank: e.matmul(
                    bank[:], lnf3[:, c, :], wqkv3[:, c, g * 512:(g + 1) * 512],
                    start=(c == 0), stop=(c == 7)),
                    deps=([tsf, tw] + prev_sp) if c == 0 else (), tok=(c == 7), mode="full")
        tev = []
        for half in range(2):
            pv = PB[4 + half].rearrange("p (h n) -> p h n", h=8)
            tev.append(P.on("scalar", lambda e, half=half, pv=pv: e.copy(
                vst3[:, half * 8:(half + 1) * 8, 0:64], pv),
                deps=[tmv[4 + half]] + prev_sp, tok=True))
        tvs = P.dma("sync", st["VSP"][tt], vst[:], deps=tev + P.wdeps("VSP"))
        tqr = rotary(P, st, tt, 0, CS["q"], SS["q"], st["QF"], "wbf4",
                     [tmv[0], tmv[1], prev_rot["q"]])
        tkr = rotary(P, st, tt, 2, CS["k"], SS["k"], st["KF"], "wbf5",
                     [tmv[2], tmv[3], tqr["ev"], prev_rot["k"]])
        prev_rot = {"q": tqr["sp"], "k": tkr["sp"]}
        prev_sp = [tvs, tqr["sp"], tkr["sp"]]

    if st["stop"] == "lattnA":
        return
    P.barrier()
    O1 = f32v(AR1, 0, ntok, rows=65)
    O2 = f32v(AR1, ntok, ntok, rows=65)
    rbig = f32v(AR1, 2 * ntok, ntok, rows=64)
    q2 = bf16v(AR2, 0, ntok)
    k2 = bf16v(AR2, ntok, ntok)
    v2p = bf16v(AR2, 2 * ntok, T * 130)
    v2p3 = v2p.rearrange("p (t n) -> p t n", t=T)
    obig = bf16v(AR2, 2 * ntok + T * 130, ntok, rows=64)
    oev = []
    for p in range(8):
        tqd = P.rd("QF", P.dma("sync", q2, st["QF"][p], deps=oev + P.wdeps("q2")))
        tkd = P.rd("KF", P.dma("sync", k2, st["KF"][p], deps=oev + P.wdeps("k2")))
        tvd = P.rd("VSP", P.dma("sync", v2p3,
                                st["VSP"][:, :, 130 * p:130 * (p + 1)].rearrange("t p n -> p t n"),
                                deps=oev + P.wdeps("v2p")))
        owtok = {}
        evtok = {}
        sbank_rd = {0: None, 1: None}
        slot_rd = {}
        txp = {}
        for t in range(T + 1):
            if t < T:
                W = min(384, ntok - 128 * t)
                for h in range(2):
                    sb_ = PB[h]
                    rows = slice(64 * h, 64 * h + 64)
                    sbank_rd[("S", h)] = P.on(
                        "tensor", lambda e, t=t, h=h, sb_=sb_, rows=rows, W=W: e.matmul(
                            sb_[:, 0:W], k2[rows, t * 128:(t + 1) * 128],
                            q2[rows, t * 128:t * 128 + W], start=True, stop=True),
                        deps=([tqd, tkd, tvd, tnc] if t == 0 else []) + [sbank_rd[h]],
                        tok=True, mode="r64")
            if t >= 1:
                tp = t - 1
                Wp = min(384, ntok - 128 * tp)
                nwp = (Wp + 127) // 128
                for h in range(2):
                    pslot = ptb[:, (2 * (tp % 2) + h) * 512:(2 * (tp % 2) + h) * 512 + 384]
                    ws = [w for w in range(tp, tp + nwp) if w < T]
                    for j, w in enumerate(ws):
                        bank = PB[2 + 3 * h + (w % 3)]
                        first = (tp == max(0, w - 2))
                        last = (tp == w)
                        dep = [txp[(h, tp % 2)]]
                        if first and w >= 3:
                            dep.append(evtok[(h, w - 3)])
                        tk_ = P.on("tensor", lambda e, bank=bank, h=h, tp=tp, j=j, first=first,
                                   last=last, pslot=pslot: e.matmul(
                            bank[0:65, 0:128], v2p3[:, tp, 65 * h:65 * h + 65],
                            pslot[:, j * 128:(j + 1) * 128], start=first, stop=last),
                            deps=dep, tok=(last or j == len(ws) - 1), mode="full")
                        if last:
                            owtok[(h, w)] = tk_
                        if j == len(ws) - 1:
                            slot_rd[(h, tp % 2)] = tk_
            if t < T:
                W = min(384, ntok - 128 * t)
                par = t % 2
                tS = sbank_rd[("S", 1)]
                Sl3 = st["PBIG"][0].ap()[:, 0:1024] \
                    .rearrange("p (h n) -> p h n", h=2)[:, :, 0:W]
                wsf3 = st["wsf2"][:, par * 1024:(par + 1) * 1024] \
                    .rearrange("p (h n) -> p h n", h=2)[:, :, 0:W]
                mb3 = st["maskb"][:, 0:W].unsqueeze(1).broadcast_to((128, 2, W))
                trd = P.on("vector", lambda e, Sl3=Sl3, wsf3=wsf3, mb3=mb3:
                           e.scalar_tensor_tensor(wsf3, Sl3, 8.0, mb3,
                                                  ALU.mult, ALU.add),
                           deps=[tS, slot_rd.get((0, par)), slot_rd.get((1, par))],
                           tok=True)
                sbank_rd[0] = sbank_rd[1] = trd
                ptb3 = ptb[:, par * 1024:(par + 1) * 1024] \
                    .rearrange("p (h n) -> p h n", h=2)[:, :, 0:W]
                tx = P.on("scalar", lambda e, ptb3=ptb3, wsf3=wsf3: e.activation(
                    ptb3, wsf3, AF.Exp, bias=negC[:], scale=1.0),
                    deps=[trd, slot_rd.get((0, par)), slot_rd.get((1, par))], tok=True)
                txp[(0, par)] = txp[(1, par)] = tx
            if t >= 1:
                tp = t - 1
                for h in range(2):
                    w = tp
                    bank = PB[2 + 3 * h + (w % 3)]
                    Odst = O1 if h == 0 else O2
                    evtok[(h, w)] = P.on("vector", lambda e, bank=bank, w=w, Odst=Odst:
                                         e.tensor_copy(Odst[:, w * 128:(w + 1) * 128],
                                                       bank[0:65, 0:128]),
                                         deps=[owtok[(h, w)]], tok=True)
        oev = []
        for h in range(2):
            Osrc = O1 if h == 0 else O2
            tch = []
            for qg in range(ntok // 512):
                tt1 = P.on("vector", lambda e, Osrc=Osrc, qg=qg: e.reciprocal(
                    rr[:, 0:512], Osrc[64:65, qg * 512:(qg + 1) * 512]),
                    deps=[evtok[(h, w)] for w in range(4 * qg, min(T, 4 * qg + 4))]
                    + P.wdeps("lrr"), tok=True)
                tch.append(P.rd("lrr",
                                P.dma("sync", st["BNC"][h:h + 1, qg * 512:(qg + 1) * 512],
                                      rr[0:1, 0:512], deps=[tt1] + P.wdeps(f"LBNC{h}"))))
            tb = P.rd(f"LBNC{h}", P.dma("sync", rbig,
                                        st["BNC"][h:h + 1, :].partition_broadcast(64),
                                        deps=tch + P.wdeps("rbig")))
            t4 = P.rd("rbig", P.on("vector", lambda e, Osrc=Osrc: e.tensor_tensor(
                obig, Osrc[0:64, :], rbig, ALU.mult),
                deps=[tb] + P.wdeps("obig"), tok=True))
            t5 = P.rd("obig", P.dma("sync", st["OF"][p, 64 * h:64 * h + 64, :], obig,
                                    deps=[t4] + P.wdeps("OF")))
            oev += [t4, t5]

    if st["stop"] == "lattnB":
        return
    P.barrier()
    wo = bf16v(AR2, 0, 8 * DM)
    wo3 = wo.rearrange("p (c n) -> p c n", c=8)
    ofm3 = st["wbf3"].ap().rearrange("p (c n) -> p c n", c=8)
    two = P.dma("gpsimd", wo3, din["f_wo"][li])
    prev = []
    for tt in range(T):
        r = slice(tt * 128, tt * 128 + 128)
        tof = P.rd("OF", P.dma("sync", ofm3,
                               st["OF"][:, :, tt * 128:(tt + 1) * 128].rearrange("c p n -> p c n"),
                               deps=prev))
        txl = P.rd(yname, P.dma("sync", wc[:], yin[r, :], deps=prev))
        tmm = {}
        for half in range(2):
            bank = PB[half]
            for c in range(8):
                tmm[half] = P.on("tensor", lambda e, c=c, half=half, bank=bank: e.matmul(
                    bank[:], ofm3[:, c, :], wo3[:, c, half * 512:(half + 1) * 512],
                    start=(c == 0), stop=(c == 7)),
                    deps=([tof, two] + prev) if c == 0 else (), tok=(c == 7), mode="full")
        prev = []
        for half in range(2):
            tr = P.on("vector", lambda e, half=half: e.tensor_tensor(
                wd[:, half * 512:(half + 1) * 512], PB[half][:],
                wc[:, half * 512:(half + 1) * 512], ALU.add),
                deps=[tmm[half], txl], tok=True)
            prev.append(tr)
        prev.append(P.dma("sync", yout[r, :], wd[:], deps=prev + P.wdeps(yoname)))


def rotary(P, st, tt, bank0, CS3, SS3, dstF, stg_name, deps):
    wa, wc, wd, we = st["wa"], st["wc"], st["wd"], st["we"]
    stat = st["stat"]; PB = st["PB"]
    stg = st[stg_name]
    t1 = P.on("scalar", lambda e: e.copy(wc[:, 0:512], PB[bank0][:]), deps=deps, tok=True)
    t2 = P.on("scalar", lambda e: e.copy(wc[:, 512:1024], PB[bank0 + 1][:]), tok=True)
    t3 = P.on("scalar", lambda e: e.activation(wd[:, 0:512], PB[bank0][:], AF.Square), tok=True)
    t4 = P.on("scalar", lambda e: e.activation(wd[:, 512:1024], PB[bank0 + 1][:], AF.Square),
              tok=True)
    wc3 = wc.ap().rearrange("p (h d) -> p h d", d=DH)
    wd3 = wd.ap().rearrange("p (h d) -> p h d", d=DH)
    t5 = P.on("vector", lambda e: e.tensor_reduce(stat[:, 0:16], wd3, AXX, ALU.add),
              deps=[t3, t4], tok=True)
    t6 = P.on("vector", lambda e: e.tensor_scalar(stat[:, 0:16], stat[:, 0:16], 1.42e-24,
                                                  None, ALU.max, ALU.bypass), deps=[t5], tok=True)
    t7 = P.on("scalar", lambda e: e.activation(stat[:, 0:16], stat[:, 0:16], AF.Ln,
                                               scale=1.0), deps=[t6], tok=True)
    t8 = P.on("scalar", lambda e: e.activation(stat[:, 0:16], stat[:, 0:16], AF.Exp,
                                               scale=-0.5), deps=[t7], tok=True)
    rb = stat[:, 0:16].unsqueeze(2).broadcast_to((128, 16, DH))
    tu = P.on("vector", lambda e: e.tensor_tensor(wd3, wc3, rb, ALU.mult),
              deps=[t1, t2, t8], tok=True)
    wd4 = wd.ap().rearrange("p (h s d2) -> p h s d2", s=2, d2=32)
    wa4 = wa.rearrange("p (h s d2) -> p h s d2", s=2, d2=32)
    g1 = P.on("vector", lambda e: e.tensor_tensor(
        we.ap().rearrange("p (h d) -> p h d", d=DH), wd3,
        CS3[:, tt, :].unsqueeze(1).broadcast_to((128, 16, DH)), ALU.mult),
        deps=[tu], tok=True)
    g2 = P.on("vector", lambda e: e.tensor_tensor(
        wa4[:, :, 0, :], wd4[:, :, 1, :],
        SS3[:, tt, 0:32].unsqueeze(1).broadcast_to((128, 16, 32)), ALU.mult),
        deps=[g1], tok=True)
    g3 = P.on("vector", lambda e: e.tensor_tensor(
        wa4[:, :, 1, :], wd4[:, :, 0, :],
        SS3[:, tt, 32:64].unsqueeze(1).broadcast_to((128, 16, 32)), ALU.mult),
        deps=[g2], tok=True)
    g4 = P.on("vector", lambda e: e.tensor_tensor(st["wbf"][:], we[:], wa[:], ALU.add),
              deps=[g1, g2, g3], tok=True)
    tev = transpose_tile(P, st, st["wbf"], stg.ap().rearrange("p (c n) -> p c n", c=8),
                         7, [g4])
    tsp = P.dma("sync", dstF.rearrange("c p n -> p c n")[:, :, tt * 128:(tt + 1) * 128],
                stg.ap().rearrange("p (c n) -> p c n", c=8), deps=[tev])
    return {"sp": tsp, "mid": g4, "ev": tev}


def ph_lffn(P, st, li, yin, yname, yout, yoname, inv):
    din = st["din"]; T = st["T"]; ntok = st["ntok"]
    AR1, AR2, PB = st["AR1"], st["AR2"], st["PB"]
    wa, wb, wc, wd = st["wa"], st["wb"], st["wc"], st["wd"]
    wbf, gsb = st["wbf"], st["gsb"]
    P.barrier()

    w1 = bf16v(AR1, 0, 8 * W1C)
    w13 = w1.rearrange("p (c n) -> p c n", c=8)
    w2 = bf16v(AR2, 0, 22 * DM)
    w23 = w2.rearrange("p (c n) -> p c n", c=22)
    lngb = f32v(AR2, (22 * DM) // 2, DM)
    mst = bf16v(AR2, 22 * DM + 2 * DM, DI)
    mfst = bf16v(AR2, 22 * DM + 2 * DM + DI, 22 * 128)
    assert 22 * DM + 2 * DM + DI + 22 * 128 <= 30208
    mfst3 = mfst.rearrange("p (c n) -> p c n", c=22)

    tw1 = P.dma("gpsimd", w13, din["f_w1"][li])
    tw2 = P.dma("gpsimd", w23, din["f_w2"][li])
    tb2 = P.dma("gpsimd", st["b2row"][:], din["f_b2"][li:li + 1, :])
    tlg = bcast_col(P, st, din["f_lng"][li:li + 1, :], lngb)
    nch = (W1C + 511) // 512
    if inv == 0:
        lnb_src = din["f_lnb"][li:li + 1, :].rearrange("o (c p) -> p (o c)", p=128)
        tlb = P.on("gpsimd", lambda e: e.dma_start(out=st["lnbc"][:], in_=lnb_src,
                                                   allow_slow_non_contiguous=True),
                   tok=True, dma=True)
        twr_last = None
        twr = []
        for j in range(nch):
            Wj = min(512, W1C - 512 * j)
            tbr = P.dma("sync", st["rr"][0:1, 0:Wj],
                        din["f_b1"][li:li + 1, 512 * j:512 * j + Wj], deps=[twr_last])
            tmj = None
            for c in range(8):
                tmj = P.on("tensor", lambda e, j=j, c=c, Wj=Wj: e.matmul(
                    PB[0][0:1, 0:Wj], st["lnbc"][:, c:c + 1],
                    w13[:, c, 512 * j:512 * j + Wj], start=(c == 0), stop=(c == 7)),
                    deps=([tlb, tw1, twr_last]) if c == 0 else (), tok=(c == 7), mode="m1")
            twr_last = P.on("vector", lambda e, j=j, Wj=Wj: e.tensor_tensor(
                st["wrow"][0:1, 512 * j:512 * j + Wj], PB[0][0:1, 0:Wj],
                st["rr"][0:1, 0:Wj], ALU.add), deps=[tmj, tbr], tok=True)
            twr.append(twr_last)
        twr = [P.dma("sync", st["WRD"][li], st["wrow"][:], deps=twr)]
    else:
        twr = [P.dma("sync", st["wrow"][:], st["WRD"][li])]

    prev = []
    prev_aj = []
    prev_mev = []
    prevln = {0: [], 1: []}
    prev_t2 = None
    for tt in range(T):
        r = slice(tt * 128, tt * 128 + 128)
        par = tt % 2
        ld = (wa, st["we"])[par]
        lnfs = (st["wbf3"], st["wbf5"])[par]
        lnf3 = lnfs.ap().rearrange("p (c n) -> p c n", c=8)
        tls = shifted_load(P, st, yin, yname, tt, ld, deps=prevln[par])
        negmu, rstd, tst = ln_stats(P, st, ld, wb[:], LNEPS, tls, sb=12 * par)
        t1 = P.on("vector", lambda e, ld=ld, negmu=negmu, rstd=rstd: e.tensor_scalar(
            wb[:], ld[:], negmu, rstd, ALU.add, ALU.mult), deps=tst, tok=True)
        t2 = P.on("vector", lambda e: e.scalar_tensor_tensor(wbf[:], wb[:], 1.0, lngb,
                                                             ALU.mult, ALU.mult),
                  deps=[t1, tlg, prev_t2], tok=True)
        prev_t2 = t2
        prevln[par] = [t1, tst[2]]
        tsf = transpose_tile(P, st, wbf, lnf3, 6, [t2])
        ajs = []
        for j in range(6):
            a0 = 512 * j
            aW = min(512, DI - a0)
            tmj = {}
            for (idx, c0_) in ((0, a0), (1, DI + a0)):
                bank = PB[2 * (j % 2) + idx]
                bankrel = ajs[j - 2] if j >= 2 else (prev_aj[j + 4] if prev_aj else None)
                for c in range(8):
                    tmj[idx] = P.on("tensor",
                                    lambda e, c=c, c0_=c0_, aW=aW, bank=bank, lnf3=lnf3:
                                    e.matmul(bank[:, 0:aW], lnf3[:, c, :],
                                             w13[:, c, c0_:c0_ + aW],
                                             start=(c == 0), stop=False),
                                    deps=([tsf, tw1, bankrel] + twr + prev)
                                    if c == 0 else (), mode="full")
                tmj[idx] = P.on("tensor", lambda e, c0_=c0_, aW=aW, bank=bank: e.matmul(
                    bank[:, 0:aW], st["ones_row"][0:1, 0:128], st["wrow"][0:1, c0_:c0_ + aW],
                    start=False, stop=True), deps=twr, tok=True, mode="k1")
            gb = PB[2 * (j % 2) + 1]
            ab = PB[2 * (j % 2)]
            grel = ajs[j - 1] if j >= 1 else (prev_aj[-1] if prev_aj else None)
            tg = P.on("scalar", lambda e, gb=gb, aW=aW: e.activation(
                gsb[:, 0:aW], gb[:, 0:aW], AF.Gelu), deps=[tmj[1], grel], tok=True)
            ajs.append(P.on("vector", lambda e, ab=ab, a0=a0, aW=aW: e.tensor_tensor(
                mst[:, a0:a0 + aW], ab[:, 0:aW], gsb[:, 0:aW], ALU.mult),
                deps=[tmj[0], tg], tok=True))
        prev_aj = ajs
        tmev = []
        pv = PB[7].bitcast(BF16)
        for grp in range(3):
            ch0 = grp * 8
            nc_ = min(8, 22 - ch0)
            tlast = None
            for cc in range(nc_):
                c2 = ch0 + cc
                cw = min(128, DI - c2 * 128)
                tlast = P.on("tensor", lambda e, cc=cc, c2=c2, cw=cw, pv=pv: e.transpose(
                    pv[0:cw, cc * 128:(cc + 1) * 128], mst[:, c2 * 128:c2 * 128 + cw],
                    st["ident"][:]),
                    deps=(ajs + [st["pt_rd"].get(7)]) if cc == 0 else (),
                    tok=(cc == nc_ - 1), mode="tr")
            pvv = pv.rearrange("p (c n) -> p c n", c=8)[:, 0:nc_, :]
            tmev.append(P.on("scalar", lambda e, ch0=ch0, nc_=nc_, pvv=pvv: e.copy(
                mfst3[:, ch0:ch0 + nc_, :], pvv), deps=[tlast], tok=True))
            st["pt_rd"][7] = tmev[-1]
        prev_mev = tmev
        txl = P.rd(yname, P.dma("sync", wc[:], yin[r, :], deps=prev))
        tm2 = {}
        for half in range(2):
            bank = PB[4 + half]
            for c2 in range(22):
                tm2[half] = P.on("tensor", lambda e, c2=c2, half=half, bank=bank: e.matmul(
                    bank[:], mfst[:, c2 * 128:(c2 + 1) * 128],
                    w23[:, c2, half * 512:(half + 1) * 512],
                    start=(c2 == 0), stop=False),
                    deps=(tmev + [tw2] + prev) if c2 == 0 else (), mode="full")
            tm2[half] = P.on("tensor", lambda e, half=half, bank=bank: e.matmul(
                bank[:], st["ones_row"][0:1, 0:128],
                st["b2row"][0:1, half * 512:(half + 1) * 512],
                start=False, stop=True), deps=[tb2], tok=True, mode="k1")
        prev = []
        for half in range(2):
            tr = P.on("vector", lambda e, half=half: e.tensor_tensor(
                wd[:, half * 512:(half + 1) * 512], PB[4 + half][:],
                wc[:, half * 512:(half + 1) * 512], ALU.add),
                deps=[tm2[half], txl], tok=True)
            prev.append(tr)
        prev.append(P.dma("sync", yout[r, :], wd[:], deps=prev + P.wdeps(yoname)))


_CACHE = {}


def kernel(**inputs):
    x = np.asarray(inputs["L0_attn_a_x"])
    B, N, E = x.shape
    key = (N,)
    if key not in _CACHE:
        nc, P = build(ntok=N)
        _CACHE[key] = (nc, host_tables(N))
    nc, tables = _CACHE[key]
    w = prep_weights(inputs)
    base = {}
    base.update(w)
    base.update(tables)
    in_maps = []
    for c in range(B):
        m = dict(base)
        m["x0"] = np.ascontiguousarray(x[c])
        in_maps.append(m)
    res = run_bass_kernel_spmd(nc, in_maps, core_ids=list(range(B)))
    return np.stack([np.asarray(r["out"]) for r in res.results], axis=0)


# version 09
# speedup vs baseline: 2.3619x; 1.0276x over previous
import numpy as np
import ml_dtypes
import concourse.bass as bass
import concourse.mybir as mybir
from concourse.bass_utils import run_bass_kernel_spmd

F32 = mybir.dt.float32
BF16 = mybir.dt.bfloat16
AF = mybir.ActivationFunctionType
ALU = mybir.AluOpType
AXX = mybir.AxisListType.X
BFNP = ml_dtypes.bfloat16

DM = 1024
NH = 16
DH = 64
DEPTH = 6
DI = 2730
W1C = 5460
WIN = 256
NEG = -1.0e38
LNEPS = 1e-5
RMSEPS = float(np.finfo(np.float32).eps)
C_ATTN_A = 30.0

ENGS = ("sync", "tensor", "vector", "scalar", "gpsimd")


def host_tables(ntok):
    t = {}
    t["ident"] = np.eye(128, dtype=BFNP)
    t["ones_row"] = np.ones((1, 512), dtype=BFNP)
    p = np.arange(128)[:, None]
    c = np.arange(384)[None, :]
    t["maskb"] = np.where((c >= p) & (c <= p + WIN), 0.0, NEG).astype(np.float32)
    u = np.arange(2 * ntok - 1)[None, :]
    t["dist"] = np.abs(u - (ntok - 1) - np.arange(128)[:, None]).astype(np.float32)
    half = np.arange(0, DH, 2, dtype=np.float64)
    inv_freq = 1.0 / 10000.0 ** (half / DH)
    tt = np.arange(ntok, dtype=np.float64)
    freqs = tt[:, None] * inv_freq[None, :]
    freqs = np.concatenate([freqs, freqs], -1)
    base_scale = (half + 0.4 * DH) / (1.4 * DH)
    power = (tt - ntok // 2) / (WIN // 2)
    scale = base_scale[None, :] ** power[:, None]
    scale = np.concatenate([scale, scale], -1)
    cos = np.cos(freqs)
    sin = np.sin(freqs)
    sgn = np.where(np.arange(DH) < DH // 2, -1.0, 1.0)[None, :]
    t["cs_q"] = (cos * scale).astype(np.float32)
    t["ss_q"] = (sin * scale * sgn).astype(np.float32)
    t["cs_k"] = (cos / scale).astype(np.float32)
    t["ss_k"] = (sin / scale * sgn).astype(np.float32)
    return t


def prep_weights(inp, depth=DEPTH):
    w = {}
    cont = np.ascontiguousarray

    def pmaj(a):
        E, F = a.shape
        return cont(a.reshape(E // 128, 128, F).transpose(1, 0, 2))

    w["aw_qkv"] = pmaj(np.asarray(inp["L0_attn_a_Wqkv_w"]).T.copy())
    w["ab_qkv"] = cont(np.asarray(inp["L0_attn_a_Wqkv_b"])[None, :])
    w["aw_out"] = pmaj(np.asarray(inp["L0_attn_a_out_w"]).T.copy())
    w["ab_out"] = cont(np.asarray(inp["L0_attn_a_out_b"])[None, :])
    w["a_dd"] = cont(np.asarray(inp["L0_attn_a_distance_decay"])[None, :])
    w["bw_qkv"] = pmaj(np.concatenate(
        [np.asarray(inp["L0_attn_b_Wq"]).T, np.asarray(inp["L0_attn_b_Wk"]).T,
         np.asarray(inp["L0_attn_b_Wv"]).T], axis=1))
    w["bw_out"] = pmaj(np.asarray(inp["L0_attn_b_Wo"]).T.copy())
    w["b_qn"] = cont(np.asarray(inp["L0_attn_b_qn_w"])[None, :])
    w["b_kn"] = cont(np.asarray(inp["L0_attn_b_kn_w"])[None, :])
    w["f_wqkv"] = np.stack([pmaj(np.asarray(inp["L0_ffn_Wqkv"])[i]) for i in range(depth)])
    w["f_wo"] = np.stack([pmaj(np.asarray(inp["L0_ffn_Wo"])[i]) for i in range(depth)])
    w["f_w1"] = np.stack([pmaj(np.asarray(inp["L0_ffn_W1"])[i]) for i in range(depth)])
    w2 = np.asarray(inp["L0_ffn_W2"])
    w2p = np.zeros((depth, 2816, DM), np.float32)
    w2p[:, :DI] = w2[:depth]
    w["f_w2"] = np.stack([pmaj(w2p[i]) for i in range(depth)])
    w["f_qs"] = cont(np.asarray(inp["L0_ffn_q_scale"])[:depth])
    w["f_ks"] = cont(np.asarray(inp["L0_ffn_k_scale"])[:depth])
    w["f_lng"] = cont(np.asarray(inp["L0_ffn_ln_g"])[:depth])
    w["f_lnb"] = cont(np.asarray(inp["L0_ffn_ln_b"])[:depth])
    w["f_b1"] = cont(np.asarray(inp["L0_ffn_b1"])[:depth])
    w["f_b2"] = cont(np.asarray(inp["L0_ffn_b2"])[:depth])
    return w


class Prog:
    DPOOL = {"sync": 8, "gpsimd": 4, "scalar": 4}
    SEMN = (["sync", "tensor", "vector", "scalar", "gpsimd"]
            + [f"d{q}{i}" for q, n in DPOOL.items() for i in range(n)])
    SAFE_SAME_ENGINE = False
    SAFE_ENGINES = {"vector"}

    def __init__(self):
        self.q = {e: [] for e in ENGS}
        self.cnt = {}
        self.waited = {e: {} for e in ENGS}
        self.sems = {}
        self.ninst = 0
        self.scr_rd = {}

    def _dma_sem(self, eng):
        if not hasattr(self, "_dma_rr"):
            self._dma_rr = {}
        i = self._dma_rr.get(eng, 0)
        self._dma_rr[eng] = (i + 1) % self.DPOOL[eng]
        return f"d{eng}{i}"

    def wait(self, eng, deps):
        best = {}
        for d in deps:
            if d is None:
                continue
            s, v = d
            if v > best.get(s, 0):
                best[s] = v
        for s, v in best.items():
            if self.waited[eng].get(s, 0) < v:
                self.waited[eng][s] = v

                def mk(s=s, v=v):
                    return lambda e: e.wait_ge(self.sems[s], v)
                self.q[eng].append(mk())
                self.ninst += 1

    def on(self, eng, fn, deps=(), tok=False, dma=False, mode=None):
        if eng == "tensor" and not dma:
            assert mode is not None, "tensor ops must declare a tiling mode"
            if mode != getattr(self, "_pe_mode", None):
                lp = getattr(self, "_last_pe", None)
                if lp is not None:
                    s, v = lp

                    def mkd(s=s, v=v):
                        return lambda e: e.wait_ge(self.sems[s], v)
                    self.q[eng].append(mkd())
                    self.ninst += 1
                    self.waited[eng][s] = max(self.waited[eng].get(s, 0), v)
            self._pe_mode = mode
            tok = True
        self.wait(eng, deps)
        self.ninst += 1
        if not hasattr(self, "last_tok"):
            self.last_tok = {}
            self.clean = {}
        if not tok:
            self.q[eng].append(fn)
            self.clean[eng] = False
            return None
        s = self._dma_sem(eng) if dma else eng
        inc = 16 if dma else 1
        if dma:
            self.wait(eng, [(s, self.cnt.get(s, 0))])
        self.cnt[s] = self.cnt.get(s, 0) + inc
        v = self.cnt[s]

        def mk(fn=fn, s=s, inc=inc):
            return lambda e: fn(e).then_inc(self.sems[s], inc)
        self.q[eng].append(mk())
        if not dma and not self.SAFE_SAME_ENGINE and eng not in self.SAFE_ENGINES:
            self.waited[eng][s] = v
        if eng == "tensor" and not dma:
            self._last_pe = (s, v)
        if not dma:
            self.last_tok[eng] = (s, v)
            self.clean[eng] = True
        else:
            self.last_tok[eng] = (s, v)
            self.clean[eng] = True
        return (s, v)

    def dma(self, eng, out, in_, deps=()):
        return self.on(eng, lambda e, o=out, i=in_: e.dma_start(out=o, in_=i),
                       deps=deps, tok=True, dma=True)

    def rd(self, name, tok):
        self.scr_rd.setdefault(name, []).append(tok)
        return tok

    def wdeps(self, name):
        return self.scr_rd.pop(name, [])

    def barrier(self):
        if not hasattr(self, "last_tok"):
            self.last_tok = {}
            self.clean = {}
        for eng in ENGS:
            if self.q[eng] and not self.clean.get(eng, True):
                self.on(eng, lambda e: e.nop(), tok=True)
        snap = [(s, self.cnt.get(s, 0)) for s in self.SEMN if self.cnt.get(s, 0) > 0]
        for eng in ENGS:
            self.wait(eng, snap)

    def finish(self, blk):
        for name in ENGS:
            ops = self.q[name]

            def mksec(ops=ops):
                def section(e):
                    for f in ops:
                        f(e)
                return section
            getattr(blk, name)(mksec())


def build(ntok=4096, depth=DEPTH, parts=("a", "m1", "b", "m2"), dbg_out=(), stop=None):
    T = ntok // 128
    nc = bass.Bass("TRN2", target_bir_lowering=False)
    P = Prog()
    din = {}

    def dt_in(name, shape, dtype=F32):
        din[name] = nc.dram_tensor(name, list(shape), dtype, kind="ExternalInput").ap()

    dt_in("x0", (ntok, DM))
    dt_in("aw_qkv", (128, 8, 3 * DM)); dt_in("ab_qkv", (1, 3 * DM))
    dt_in("aw_out", (128, 8, DM)); dt_in("ab_out", (1, DM)); dt_in("a_dd", (1, 1))
    dt_in("bw_qkv", (128, 8, 3 * DM)); dt_in("bw_out", (128, 8, DM))
    dt_in("b_qn", (1, DH)); dt_in("b_kn", (1, DH))
    dt_in("f_wqkv", (depth, 128, 8, 3 * DM)); dt_in("f_wo", (depth, 128, 8, DM))
    dt_in("f_w1", (depth, 128, 8, W1C)); dt_in("f_w2", (depth, 128, 22, DM))
    dt_in("f_qs", (depth, DH)); dt_in("f_ks", (depth, DH))
    dt_in("f_lng", (depth, DM)); dt_in("f_lnb", (depth, DM))
    dt_in("f_b1", (depth, W1C)); dt_in("f_b2", (depth, DM))
    dt_in("ident", (128, 128), BF16); dt_in("ones_row", (1, 512), BF16)
    dt_in("maskb", (128, 384)); dt_in("dist", (128, 2 * ntok - 1))
    dt_in("cs_q", (ntok, DH)); dt_in("ss_q", (ntok, DH))
    dt_in("cs_k", (ntok, DH)); dt_in("ss_k", (ntok, DH))
    dout = nc.dram_tensor("out", [ntok, DM], F32, kind="ExternalOutput").ap()

    def dt_sc(name, shape, dtype=F32):
        kind = "ExternalOutput" if name in dbg_out else "Internal"
        return nc.dram_tensor(name, list(shape), dtype, kind=kind).ap()

    st = dict(ntok=ntok, T=T, depth=depth, nc=nc, din=din, dout=dout, stop=stop)
    st["X1"] = dt_sc("X1", (ntok, DM))
    st["X2"] = dt_sc("X2", (ntok, DM))
    st["X3"] = dt_sc("X3", (ntok, DM))
    st["YA"] = dt_sc("YA", (ntok, DM))
    st["YB"] = dt_sc("YB", (ntok, DM))
    st["QF"] = dt_sc("QF", (8, 128, ntok), BF16)
    st["KF"] = dt_sc("KF", (8, 128, ntok), BF16)
    st["VSP"] = dt_sc("VSP", (T, 128, NH * 65), BF16)
    st["OF"] = dt_sc("OF", (8, 128, ntok), BF16)
    st["BNC"] = dt_sc("BNC", (8, ntok))
    st["WRD"] = dt_sc("WRD", (depth, W1C), BF16)

    from contextlib import ExitStack
    ctx = ExitStack()
    sb = lambda name, shape, dtype: ctx.enter_context(nc.sbuf_tensor("s_" + name, list(shape), dtype))
    with ctx:
        st["AR1"] = sb("AR1", [128, 43680], BF16)
        st["AR2"] = sb("AR2", [128, 30208], BF16)
        st["ident"] = sb("ident", [128, 128], BF16)
        st["ones_row"] = sb("ones_row", [1, 512], BF16)
        st["maskb"] = sb("maskb", [128, 384], F32)
        st["wab"] = sb("wab", [128, 2048], F32)
        st["wa"] = st["wab"].ap()[:, 0:1024]
        st["wb"] = st["wab"].ap()[:, 1024:2048]
        st["wsf2"] = st["wab"].ap()
        st["wc"] = sb("wc", [128, 1024], F32)
        st["wd"] = sb("wd", [128, 1024], F32)
        st["we"] = sb("we", [128, 1024], F32)
        st["wbf"] = sb("wbf", [128, 1024], BF16)
        st["wbf3"] = sb("wbf3", [128, 1024], BF16)
        st["wbf4"] = sb("wbf4", [128, 1024], BF16)
        st["wbf5"] = sb("wbf5", [128, 1024], BF16)
        st["vst"] = sb("vst", [128, NH * 65], BF16)
        st["gsb"] = sb("gsb", [128, 512], F32)
        st["stat"] = sb("stat", [128, 32], F32)
        st["sml"] = sb("sml", [128, 160], F32)
        st["ptb"] = sb("ptb", [128, 8 * 512], BF16)
        st["wrow"] = sb("wrow", [1, W1C], BF16)
        st["b2row"] = sb("b2row", [1, DM], BF16)
        st["lnbc"] = sb("lnbc", [128, 8], BF16)
        st["qnc"] = sb("qnc", [128, 1], F32)
        st["knc"] = sb("knc", [128, 1], F32)
        st["negC"] = sb("negC", [128, 1], F32)
        st["onec"] = sb("onec", [128, 1], BF16)
        st["rr"] = sb("rr", [1, 512], F32)
        st["sconst"] = sb("sconst", [128, 4], F32)
        for col, val in enumerate([LNEPS, RMSEPS, -C_ATTN_A]):
            nc.const_aps.register(F32, float(val), st["sconst"].ap()[:, col:col + 1])
        st["PBIG"] = [ctx.enter_context(nc.psum_tensor(f"pbig{i}", [128, 2048], F32))
                      for i in range(2)]
        st["PB"] = [st["PBIG"][i // 4].ap()[:, 512 * (i % 4):512 * (i % 4 + 1)]
                    for i in range(8)]
        st["pt_rd"] = {}
        for sname in Prog.SEMN:
            P.sems[sname] = ctx.enter_context(nc.semaphore(sname))
        blk = ctx.enter_context(nc.Block())
        emit_program(P, st, parts)
        P.finish(blk)
    return nc, P


def f32v(arena, c0f, ncols, rows=128, p0=0):
    return arena.ap().bitcast(F32)[p0:p0 + rows, c0f:c0f + ncols]


def bf16v(arena, c0, ncols, rows=128, p0=0):
    return arena[p0:p0 + rows, c0:c0 + ncols]


def ln_stats(P, st, xt, dump, eps, deps, sb=0):
    stat = st["stat"]
    t1 = P.on("vector", lambda e: e.tensor_reduce(stat[:, sb + 0:sb + 1], xt[:], AXX, ALU.add),
              deps=deps, tok=True)
    t2 = P.on("scalar", lambda e: e.activation(dump, xt[:], AF.Square,
                                               accum_out=stat[:, sb + 1:sb + 2]),
              deps=deps, tok=True)
    t4 = P.on("vector", lambda e: e.tensor_scalar(stat[:, sb + 2:sb + 3], stat[:, sb + 0:sb + 1],
                                                  -1.0 / 1024, None, ALU.mult, ALU.bypass),
              deps=[t1], tok=True)
    t5 = P.on("vector", lambda e: e.tensor_tensor(stat[:, sb + 3:sb + 4], stat[:, sb + 2:sb + 3],
                                                  stat[:, sb + 2:sb + 3], ALU.mult),
              deps=[t4], tok=True)
    t6 = P.on("vector", lambda e: e.tensor_scalar(stat[:, sb + 4:sb + 5], stat[:, sb + 1:sb + 2],
                                                  1.0 / 1024, None, ALU.mult, ALU.bypass),
              deps=[t2], tok=True)
    t7 = P.on("vector", lambda e: e.tensor_tensor(stat[:, sb + 5:sb + 6], stat[:, sb + 4:sb + 5],
                                                  stat[:, sb + 3:sb + 4], ALU.subtract),
              deps=[t5, t6], tok=True)
    t8 = P.on("scalar", lambda e: e.activation(stat[:, sb + 6:sb + 7], stat[:, sb + 5:sb + 6],
                                               AF.Ln, bias=float(eps), scale=1.0),
              deps=[t7], tok=True)
    t9 = P.on("scalar", lambda e: e.activation(stat[:, sb + 7:sb + 8], stat[:, sb + 6:sb + 7],
                                               AF.Exp, scale=-0.5), deps=[t8], tok=True)
    return stat[:, sb + 2:sb + 3], stat[:, sb + 7:sb + 8], [t4, t9, t2]


def transpose_tile(P, st, src_bf, dst_strided, bank_idx, deps, nchunk=8, dst_chunks=None):
    bank = st["PB"][bank_idx]
    pv = bank.bitcast(BF16)
    alldeps = list(deps) + [st["pt_rd"].get(bank_idx)]
    tl = None
    for c in range(nchunk):
        tl = P.on("tensor", lambda e, c=c: e.transpose(
            pv[:, c * 128:(c + 1) * 128], src_bf[:, c * 128:(c + 1) * 128], st["ident"][:]),
            deps=alldeps if c == 0 else (), tok=(c == nchunk - 1), mode="tr")
    pvv = pv.rearrange("p (c n) -> p c n", c=8)[:, 0:nchunk, :]
    tev = P.on("scalar", lambda e: e.copy(dst_strided, pvv), deps=[tl], tok=True)
    st["pt_rd"][bank_idx] = tev
    return tev


def shifted_load(P, st, ysrc, yname, tt, dst, deps=()):
    r0 = tt * 128
    t1 = P.rd(yname, P.dma("sync", dst[:, 0:512], ysrc[r0:r0 + 128, 0:512], deps=deps))
    if tt == 0:
        t2 = P.rd(yname, P.dma("sync", dst[1:128, 512:1024], ysrc[0:127, 512:1024]))
        t3 = P.on("gpsimd", lambda e: e.memset(dst[0:1, 512:1024], 0.0), deps=deps, tok=True)
        return [t1, t2, t3]
    t2 = P.rd(yname, P.dma("sync", dst[:, 512:1024], ysrc[r0 - 1:r0 + 127, 512:1024]))
    return [t1, t2]


def bcast_col(P, st, dram_row_ap, dst, deps=()):
    return P.dma("sync", dst, dram_row_ap.partition_broadcast(dst.shape[0]), deps=deps)


def emit_program(P, st, parts):
    din = st["din"]
    P.dma("sync", st["ident"][:], din["ident"][:])
    P.dma("sync", st["ones_row"][:], din["ones_row"][:])
    P.dma("sync", st["maskb"][:], din["maskb"][:])
    P.on("gpsimd", lambda e: e.memset(st["onec"][:], 1.0), tok=True)
    for col, val in enumerate([LNEPS, RMSEPS, -C_ATTN_A]):
        P.on("vector", lambda e, col=col, val=val: e.memset(st["sconst"][:, col:col + 1],
                                                            float(val)), tok=True)
    P.barrier()

    x_cur, x_name = din["x0"], "x0"
    if "a" in parts:
        ph_dense(P, st, "a", x_cur, x_name, st["X1"], "X1")
        x_cur, x_name = st["X1"], "X1"
    if "m1" in parts:
        ph_ffnmod(P, st, x_cur, x_name, st["X2"], "X2", inv=0)
        x_cur, x_name = st["X2"], "X2"
    if "b" in parts:
        ph_dense(P, st, "b", x_cur, x_name, st["X3"], "X3")
        x_cur, x_name = st["X3"], "X3"
    if "m2" in parts:
        ph_ffnmod(P, st, x_cur, x_name, st["dout"], "out", inv=1)
        x_cur = st["dout"]
    if x_cur is not st["dout"]:
        P.barrier()
        prev = []
        for tt in range(st["T"]):
            r = slice(tt * 128, tt * 128 + 128)
            t = P.dma("sync", st["wc"][:], x_cur[r, :], deps=prev)
            tc = P.on("vector", lambda e: e.tensor_copy(st["wd"][:], st["wc"][:]),
                      deps=[t], tok=True)
            prev = [tc, P.dma("sync", st["dout"][r, :], st["wd"][:], deps=[tc])]
    P.wait("tensor", [(s, P.cnt.get(s, 0)) for s in Prog.SEMN if s.startswith("d")])
    P.on("tensor", lambda e: e.nop(), mode="nop")


def ph_dense(P, st, var, x_in, x_name, x_out, xo_name):
    din = st["din"]; T = st["T"]; ntok = st["ntok"]
    AR1, AR2, PB = st["AR1"], st["AR2"], st["PB"]
    wa, wb, wc, wd = st["wa"], st["wb"], st["wc"], st["wd"]
    wbf, vst, ptb = st["wbf"], st["vst"], st["ptb"]
    negC, sml, rr = st["negC"], st["sml"], st["rr"]
    P.barrier()

    lnfm = bf16v(AR1, 0, 8 * ntok)
    lnfm3 = lnfm.rearrange("p (c n) -> p c n", c=8)
    wv = bf16v(AR1, 8 * ntok, 8 * DM)
    wv3 = wv.rearrange("p (c n) -> p c n", c=8)
    wslq = bf16v(AR1, 8 * ntok + 8 * DM, 8 * 128)
    wslk = bf16v(AR1, 8 * ntok + 8 * DM + 8 * 128, 8 * 128)
    wslq3 = wslq.rearrange("p (c n) -> p c n", c=8)
    wslk3 = wslk.rearrange("p (c n) -> p c n", c=8)
    ofm = bf16v(AR1, 8 * ntok + 8 * DM, 8 * 128)
    ofm3 = ofm.rearrange("p (c n) -> p c n", c=8)
    wsfA = [f32v(AR1, 4 * ntok, 2048), f32v(AR1, 4 * ntok + 2048, 2048)]
    if var == "a":
        dec = f32v(AR2, 0, 2 * ntok - 1)
        c0 = 2 * (2 * ntok - 1)
        rstdb = None
    else:
        dec = None
        rstdb = f32v(AR2, 0, ntok)
        c0 = 2 * ntok
    q2 = bf16v(AR2, c0, ntok); c0 += ntok
    k2 = bf16v(AR2, c0, ntok); c0 += ntok
    if var == "b":
        tmpn = bf16v(AR2, c0, ntok); c0 += ntok
    else:
        tmpn = None
    v2p = bf16v(AR2, c0, T * 130); c0 += T * 130
    v2p3 = v2p.rearrange("p (t n) -> p t n", t=T)
    rbq = st["wbf3"].ap().bitcast(F32)[0:64, 0:512]
    obf = st["wbf"][0:64, 0:512]
    assert c0 <= 30208, c0

    wq_d = din["aw_qkv" if var == "a" else "bw_qkv"]
    wo_d = din["aw_out" if var == "a" else "bw_out"]

    tbias = None
    if var == "a":
        tdd = bcast_col(P, st, din["a_dd"][0:1, :], sml[:, 8:9])
        tng = P.on("vector", lambda e: e.tensor_scalar(sml[:, 9:10], sml[:, 8:9], -1.0, None,
                                                       ALU.mult, ALU.bypass), deps=[tdd], tok=True)
        tna = P.on("vector", lambda e: e.tensor_tensor(sml[:, 10:11], sml[:, 8:9],
                                                       sml[:, 9:10], ALU.min), deps=[tng], tok=True)
        tdist = P.dma("sync", dec, din["dist"][:])
        tdec = P.on("scalar", lambda e: e.activation(dec, dec, AF.Exp, scale=sml[:, 10:11]),
                    deps=[tna, tdist], tok=True)
        tbias_v = P.dma("gpsimd", st["b2row"][:], din["ab_qkv"][:, 2 * DM:3 * DM])
        tbias_q = P.dma("gpsimd", st["wrow"][0:1, 0:DM], din["ab_qkv"][:, 0:DM])
        tbias_k = P.dma("gpsimd", st["wrow"][0:1, DM:2 * DM], din["ab_qkv"][:, DM:2 * DM])
        tbias = [tbias_q, tbias_k]
        tnc = None
    else:
        tq1 = P.dma("sync", st["qnc"][0:64, :],
                    din["b_qn"].rearrange("o (p i) -> (o p) i", i=1))
        tq2 = P.dma("sync", st["qnc"][64:128, :],
                    din["b_qn"].rearrange("o (p i) -> (o p) i", i=1))
        tk1 = P.dma("sync", st["knc"][0:64, :],
                    din["b_kn"].rearrange("o (p i) -> (o p) i", i=1))
        tk2 = P.dma("sync", st["knc"][64:128, :],
                    din["b_kn"].rearrange("o (p i) -> (o p) i", i=1))
        tqb = bcast_col(P, st, din["b_qn"][0:1, :], sml[:, 16:80])
        tkb = bcast_col(P, st, din["b_kn"][0:1, :], sml[:, 80:144])
        t1 = P.on("vector", lambda e: e.tensor_reduce(sml[:, 0:1], sml[:, 16:80], AXX, ALU.max,
                                                      apply_absolute_value=True),
                  deps=[tqb], tok=True)
        t2 = P.on("vector", lambda e: e.tensor_reduce(sml[:, 1:2], sml[:, 80:144], AXX, ALU.max,
                                                      apply_absolute_value=True),
                  deps=[tkb], tok=True)
        t3m = P.on("vector", lambda e: e.tensor_tensor(sml[:, 2:3], sml[:, 0:1], sml[:, 1:2],
                                                       ALU.mult), deps=[t1, t2], tok=True)
        tnc = P.on("vector", lambda e: e.tensor_scalar(negC[:], sml[:, 2:3], -8.0, None,
                                                       ALU.mult, ALU.bypass), deps=[t3m], tok=True)
        tdec = None
        st["_rmsdep"] = [tq1, tq2, tk1, tk2]

    prev_tn = {0: None, 1: None}
    prev_ev = None
    for tt in range(T):
        r = slice(tt * 128, tt * 128 + 128)
        par = tt % 2
        ld = (wa, st["we"])[par]
        tld = P.rd(x_name, P.dma("sync", ld[:], x_in[r, :], deps=[prev_tn[par]]))
        negmu, rstd, tst = ln_stats(P, st, ld, (wbf, st["wbf4"])[par][:], LNEPS,
                                    [tld], sb=12 * par)
        prev_tn[par] = P.on("vector", lambda e, ld=ld, par=par, negmu=negmu, rstd=rstd:
                            e.tensor_scalar((wbf, st["wbf4"])[par][:], ld[:], negmu, rstd,
                                            ALU.add, ALU.mult),
                            deps=tst + [prev_ev], tok=True)
        prev_ev = transpose_tile(P, st, (wbf, st["wbf4"])[par],
                                 lnfm3[:, :, tt * 128:(tt + 1) * 128], 6, [prev_tn[par]])

    if st["stop"] == "ph1":
        return
    twv = P.dma("gpsimd", wv3, wq_d[:, :, 2 * DM:3 * DM])
    vst3 = vst.ap().rearrange("p (h n) -> p h n", h=NH)
    tmem = P.on("gpsimd", lambda e: e.memset(vst3[:, :, 64:65], 1.0), tok=True)
    prev_sp = None
    prev_ev = [tmem, prev_ev]
    for tt in range(T):
        tm = {}
        for half in range(2):
            bank = PB[half]
            nmm = 8
            for c in range(8):
                lastmm = (c == 7) and var != "a"
                tm[half] = P.on("tensor", lambda e, c=c, half=half, bank=bank, tt=tt, lastmm=lastmm: e.matmul(
                    bank[:], lnfm3[:, c, tt * 128:(tt + 1) * 128],
                    wv3[:, c, half * 512:(half + 1) * 512],
                    start=(c == 0), stop=lastmm),
                    deps=([twv] + prev_ev) if c == 0 else (), tok=lastmm, mode="full")
            if var == "a":
                tm[half] = P.on("tensor", lambda e, half=half, bank=bank: e.matmul(
                    bank[:], st["ones_row"][0:1, 0:128],
                    st["b2row"][0:1, half * 512:(half + 1) * 512],
                    start=False, stop=True),
                    deps=[tbias_v], tok=True, mode="full")
        prev_ev = []
        for half in range(2):
            pv = PB[half].rearrange("p (h n) -> p h n", h=8)
            prev_ev.append(P.on("scalar", lambda e, half=half, pv=pv: e.copy(
                vst3[:, half * 8:(half + 1) * 8, 0:64], pv),
                deps=[tm[half], prev_sp], tok=True))
        prev_sp = P.dma("sync", st["VSP"][tt], vst[:], deps=prev_ev + P.wdeps("VSP"))

    if st["stop"] == "ph2":
        return
    nqg = ntok // 512
    oev = [prev_sp]
    for p in list(range(8))[st.get('_ponly', 0):]:
        tq = P.dma("gpsimd", wslq3, wq_d[:, :, 128 * p:128 * (p + 1)], deps=oev)
        tk = P.dma("gpsimd", wslk3, wq_d[:, :, DM + 128 * p:DM + 128 * (p + 1)], deps=oev)
        tev_q = list(oev)
        for (wsl3, tw_, dst, bofs) in ((wslq3, tq, q2, 128 * p), (wslk3, tk, k2, DM + 128 * p)):
            new_ev = []
            for qg in range(nqg):
                bank = PB[qg % 2]
                tmq = None
                for c in range(8):
                    lastmm = (c == 7) and var != "a"
                    tmq = P.on("tensor", lambda e, c=c, qg=qg, bank=bank, wsl3=wsl3, lastmm=lastmm: e.matmul(
                        bank[:], wsl3[:, c, :], lnfm3[:, c, qg * 512:(qg + 1) * 512],
                        start=(c == 0), stop=lastmm),
                        deps=([tw_] + tev_q) if c == 0 else (), tok=lastmm, mode="full")
                if var == "a":
                    tmq = P.on("tensor", lambda e, qg=qg, bank=bank, bofs=bofs: e.matmul(
                        bank[:], st["wrow"][0:1, bofs:bofs + 128],
                        st["ones_row"][0:1, 0:512], start=False, stop=True),
                        deps=tbias, tok=True, mode="full")
                new_ev.append(P.on("scalar", lambda e, qg=qg, bank=bank, dst=dst: e.copy(
                    dst[:, qg * 512:(qg + 1) * 512], bank[:]), deps=[tmq], tok=True))
            tev_q = new_ev
        qS, kS = q2, k2
        if st["stop"] == "proj" and p == 0:
            return
        if var == "b":
            t_rms1 = rms_norm_pair(P, st, q2, tmpn, st["qnc"], 2, rstdb, tev_q)
            t_rms2 = rms_norm_pair(P, st, k2, q2, st["knc"], 4, rstdb, [t_rms1])
            qS, kS = tmpn, q2
            tev_q = [t_rms2]
        tv = P.dma("sync", v2p3,
                   st["VSP"][:, :, 130 * p:130 * (p + 1)].rearrange("t p n -> p t n"),
                   deps=tev_q + P.wdeps("v2p"))
        P.rd("VSP", tv)
        slot_rd = {}
        sbank_rd = {}
        texp_par = {}
        for qp0 in range(0, nqg, 2):
            qgs = [qg for qg in (qp0, qp0 + 1) if qg < nqg]
            tdone = {}
            txp = {}
            nq = len(qgs)
            SW = nq * 2 * 512
            Sbig = st["PBIG"][0].ap()[:, 0:SW]
            Sbig3 = Sbig.rearrange("p (h n) -> p h n", h=2)
            u0b = (ntok - 1) + 512 * qgs[0]
            for t in range(T + 1):
                par = t % 2
                wsfp = wsfA[par]
                ptbp = ptb[:, par * 2048:par * 2048 + SW]
                if t < T:
                    for h in range(2):
                        for qI, qg in enumerate(qgs):
                            sidx = h * nq + qI
                            rows = slice(64 * h, 64 * h + 64)
                            tS = P.on("tensor",
                                      lambda e, t=t, h=h, qg=qg, sidx=sidx, rows=rows:
                                      e.matmul(st["PBIG"][0].ap()[:, sidx * 512:(sidx + 1) * 512],
                                               kS[rows, t * 128:(t + 1) * 128],
                                               qS[rows, qg * 512:(qg + 1) * 512],
                                               start=True, stop=True),
                                      deps=([tv, tnc, tdec] + tev_q
                                            + st.get("_oev_par", [])
                                            if (t < 1 and qp0 == 0) else [])
                                      + [sbank_rd.get("rel")],
                                      tok=True, mode="r64")
                            sbank_rd["S"] = tS
                avrel_prev = slot_rd.get("last")
                if t >= 1:
                    ppav = (t - 1) % 2
                    for qI, qg in enumerate(qgs):
                        for h in range(2):
                            ob = PB[4 + 2 * qI + h]
                            slot = h * nq + qI
                            tdone[(qI, h)] = P.on(
                                "tensor", lambda e, t=t, h=h, ob=ob, slot=slot, ppav=ppav:
                                e.matmul(ob[0:65, :],
                                         v2p3[:, t - 1, 65 * h:65 * h + 65],
                                         ptb[:, ppav * 2048 + slot * 512:
                                             ppav * 2048 + (slot + 1) * 512],
                                         start=(t - 1 == 0), stop=(t - 1 == T - 1)),
                                deps=[txp.get("x")]
                                + (st.get("_oev_par", []) if t == 1 else []),
                                tok=True, mode="full")
                    slot_rd["last"] = tdone[(nq - 1, 1)]
                if t < T:
                    tS = sbank_rd["S"]
                    if var == "a":
                        u0 = u0b - 128 * t
                        din1 = dec[:, u0:u0 + nq * 512].unsqueeze(1) \
                            .broadcast_to((128, 2, nq * 512))
                        wsf3 = wsfp[:, 0:SW].rearrange("p (h n) -> p h n", h=2)
                        trd = P.on("vector", lambda e, din1=din1, wsf3=wsf3:
                                   e.scalar_tensor_tensor(wsf3, Sbig3, 0.125, din1,
                                                          ALU.mult, ALU.mult),
                                   deps=[tS, texp_par.get(par)], tok=True)
                        sbank_rd["rel"] = trd
                        tx = P.on("scalar", lambda e, wsfp=wsfp, ptbp=ptbp, SW=SW:
                                  e.activation(ptbp, wsfp[:, 0:SW], AF.Exp,
                                               bias=-C_ATTN_A, scale=1.0),
                                  deps=[trd, avrel_prev], tok=True)
                    else:
                        tx = P.on("scalar", lambda e, ptbp=ptbp: e.activation(
                            ptbp, Sbig, AF.Exp, bias=negC[:], scale=0.125),
                            deps=[tS, avrel_prev], tok=True)
                        sbank_rd["rel"] = tx
                    texp_par[par] = tx
                    txp["x"] = tx
            oev2 = []
            oevw = {}
            for qI, qg in enumerate(qgs):
                for h in range(2):
                    ob = PB[4 + 2 * qI + h]
                    rrh = rr[0:1, 0:512] if h == 0 else st["wc"].ap().bitcast(F32)[0:1, 0:512]
                    rbqh = rbq if h == 0 else st["wd"].ap().bitcast(F32)[0:64, 0:512]
                    obfh = obf if h == 0 else st["wbf4"][0:64, 0:512]
                    t1 = P.on("vector", lambda e, ob=ob, rrh=rrh: e.reciprocal(
                        rrh, ob[64:65, :]),
                        deps=[tdone[(qI, h)]] + P.wdeps(f"rr{h}"), tok=True)
                    t2 = P.rd(f"rr{h}", P.dma("sync", st["BNC"][h:h + 1, 0:512], rrh,
                                              deps=[t1] + P.wdeps(f"BNC{h}")))
                    t3 = P.rd(f"BNC{h}", P.dma("sync", rbqh,
                                               st["BNC"][h:h + 1, 0:512].partition_broadcast(64),
                                               deps=[t2] + P.wdeps(f"rbq{h}")))
                    t4 = P.rd(f"rbq{h}", P.on("vector",
                                              lambda e, ob=ob, obfh=obfh, rbqh=rbqh:
                                              e.tensor_tensor(obfh, ob[0:64, :], rbqh,
                                                              ALU.mult),
                                              deps=[t3] + P.wdeps(f"obf{h}"), tok=True))
                    t5 = P.rd(f"obf{h}", P.dma("sync",
                                               st["OF"][p, 64 * h:64 * h + 64,
                                                        qg * 512:(qg + 1) * 512],
                                               obfh, deps=[t4] + P.wdeps("OF")))
                    oev2 += [t4, t5]
                    oevw[(qI, h)] = t4
            st["_oev_par"] = oev2
            st["_oev_par_w"] = oevw
            oev = oev2
    if st["stop"] == "ph3":
        return
    P.barrier()
    two = P.dma("gpsimd", wv3, wo_d[:, :, :])
    tbo2 = P.dma("gpsimd", st["b2row"][:], din["ab_out"][:]) if var == "a" else None
    prev = []
    for tt in range(T):
        r = slice(tt * 128, tt * 128 + 128)
        tof = P.rd("OF", P.dma("sync", ofm3,
                               st["OF"][:, :, tt * 128:(tt + 1) * 128].rearrange("c p n -> p c n"),
                               deps=prev))
        txl = P.rd(x_name, P.dma("sync", wc[:], x_in[r, :], deps=prev))
        tmm = {}
        for half in range(2):
            bank = PB[half]
            for c in range(8):
                lastmm = (c == 7) and var != "a"
                tmm[half] = P.on("tensor", lambda e, c=c, half=half, bank=bank, lastmm=lastmm: e.matmul(
                    bank[:], ofm3[:, c, :], wv3[:, c, half * 512:(half + 1) * 512],
                    start=(c == 0), stop=lastmm),
                    deps=([tof, two] + prev) if c == 0 else (), tok=lastmm, mode="full")
            if var == "a":
                tmm[half] = P.on("tensor", lambda e, half=half, bank=bank: e.matmul(
                    bank[:], st["ones_row"][0:1, 0:128],
                    st["b2row"][0:1, half * 512:(half + 1) * 512], start=False, stop=True),
                    deps=[tbo2], tok=True, mode="full")
        prev = []
        for half in range(2):
            tr = P.on("vector", lambda e, half=half: e.tensor_tensor(
                wd[:, half * 512:(half + 1) * 512], PB[half][:],
                wc[:, half * 512:(half + 1) * 512], ALU.add),
                deps=[tmm[half], txl], tok=True)
            prev.append(tr)
        prev.append(P.dma("sync", x_out[r, :], wd[:], deps=prev + P.wdeps(xo_name)))


def rms_norm_pair(P, st, src, dst, ncol, bnc0, rstdb, deps):
    ntok = st["ntok"]; PB = st["PB"]; rr = st["rr"]
    nqg = ntok // 512
    tsq = P.on("scalar", lambda e: e.activation(dst, src, AF.Square), deps=deps, tok=True)
    dmas = []
    prev5 = {0: None, 1: None}
    for h in range(2):
        rows = slice(64 * h, 64 * h + 64)
        for qg in range(nqg):
            tm = P.on("tensor", lambda e, h=h, qg=qg, rows=rows: e.matmul(
                PB[2 + h][0:1, :], st["onec"][rows, :], dst[rows, qg * 512:(qg + 1) * 512],
                start=True, stop=True), deps=[tsq, prev5[h]] + (dmas[-1:] if dmas else []),
                tok=True, mode="full")
            t5 = P.on("scalar", lambda e, h=h: e.activation(rr[:, 0:512], PB[2 + h][0:1, :],
                                                            AF.Ln, bias=RMSEPS, scale=1.0 / 64),
                      deps=[tm], tok=True)
            t6 = P.on("scalar", lambda e: e.activation(rr[:, 0:512], rr[:, 0:512], AF.Exp,
                                                       scale=-0.5), deps=[t5], tok=True)
            prev5[h] = t6
            dmas.append(P.dma("sync", st["BNC"][bnc0 + h:bnc0 + h + 1, qg * 512:(qg + 1) * 512],
                              rr[0:1, 0:512], deps=[t6] + P.wdeps(f"BNCrms{bnc0 + h}")))
    tb = []
    for h in range(2):
        tb.append(P.rd(f"BNCrms{bnc0 + h}",
                       P.dma("sync", rstdb[64 * h:64 * h + 64, :],
                             st["BNC"][bnc0 + h:bnc0 + h + 1, :].partition_broadcast(64),
                             deps=dmas + P.wdeps("rstdb"))))
    tn = P.rd("rstdb", P.on("vector", lambda e: e.scalar_tensor_tensor(dst, src, ncol[:], rstdb,
                                                                      ALU.mult, ALU.mult),
                            deps=tb + st.get("_rmsdep", []), tok=True))
    return tn


def ph_ffnmod(P, st, x_in, x_name, x_out, xo_name, inv):
    T = st["T"]
    ycur, yname = st["YA"], "YA"
    ynext, ynname = st["YB"], "YB"
    P.barrier()
    prevl = {0: [], 1: []}
    for tt in range(T):
        r = slice(tt * 128, tt * 128 + 128)
        par = tt % 2
        ld = (st["wa"], st["we"])[par]
        dst = (st["wb"], st["wd"])[par]
        tld = P.rd(x_name, P.dma("sync", ld[:], x_in[r, :], deps=prevl[par]))
        negmu, rstd, tst = ln_stats(P, st, ld, dst[:], LNEPS, [tld], sb=12 * par)
        tn = P.on("vector", lambda e, ld=ld, dst=dst, negmu=negmu, rstd=rstd:
                  e.tensor_scalar(dst[:], ld[:], negmu, rstd, ALU.add, ALU.mult),
                  deps=tst, tok=True)
        prevl[par] = [tn, P.dma("sync", ycur[r, :], dst[:], deps=[tn] + P.wdeps(yname))]
    if st["stop"] == "y0":
        return
    for i in range(st["depth"]):
        ph_lattn(P, st, i, ycur, yname, ynext, ynname)
        if st["stop"] in ("lattnA", "lattnB", "lattnC"):
            return
        ycur, yname, ynext, ynname = ynext, ynname, ycur, yname
        ph_lffn(P, st, i, ycur, yname, ynext, ynname, inv)
        if st["stop"] == "lffn":
            return
        ycur, yname, ynext, ynname = ynext, ynname, ycur, yname
    P.barrier()
    prevl = {0: [], 1: []}
    prev_store = None
    for tt in range(T):
        r = slice(tt * 128, tt * 128 + 128)
        par = tt % 2
        la = (st["wa"], st["wd"])[par]
        lb = (st["wb"], st["we"])[par]
        t1 = P.rd(x_name, P.dma("sync", la[:], x_in[r, :], deps=prevl[par]))
        t2 = P.rd(yname, P.dma("sync", lb[:], ycur[r, :], deps=prevl[par]))
        t3 = P.on("vector", lambda e, la=la, lb=lb: e.tensor_tensor(
            st["wc"][:], la[:], lb[:], ALU.add),
            deps=[t1, t2, prev_store], tok=True)
        prev_store = P.dma("sync", x_out[r, :], st["wc"][:], deps=[t3] + P.wdeps(xo_name))
        prevl[par] = [t3, prev_store]


def ph_lattn(P, st, li, yin, yname, yout, yoname):
    din = st["din"]; T = st["T"]; ntok = st["ntok"]
    AR1, AR2, PB = st["AR1"], st["AR2"], st["PB"]
    wa, wb, wc, wd, we = st["wa"], st["wb"], st["wc"], st["wd"], st["we"]
    wbf, vst, ptb = st["wbf"], st["vst"], st["ptb"]
    sml, rr, negC = st["sml"], st["rr"], st["negC"]
    P.barrier()

    wqkv = bf16v(AR1, 0, 8 * 3 * DM)
    wqkv3 = wqkv.rearrange("p (c n) -> p c n", c=8)
    rotc = 8 * 3 * DM
    CS = {}; SS = {}
    for nm in ("q", "k"):
        CS[nm] = f32v(AR1, rotc // 2, T * DH).rearrange("p (t d) -> p t d", d=DH)
        rotc += 2 * T * DH
        SS[nm] = f32v(AR1, rotc // 2, T * DH).rearrange("p (t d) -> p t d", d=DH)
        rotc += 2 * T * DH
    assert rotc <= 43680
    lnf3 = st["wbf3"].ap().rearrange("p (c n) -> p c n", c=8)
    vst3 = vst.ap().rearrange("p (h n) -> p h n", h=NH)

    tw = P.dma("gpsimd", wqkv3, din["f_wqkv"][li])
    tcs = {}
    for nm in ("q", "k"):
        tcs[nm] = [P.dma("sync", CS[nm], din[f"cs_{nm}"].rearrange("(t p) d -> p t d", p=128)),
                   P.dma("sync", SS[nm], din[f"ss_{nm}"].rearrange("(t p) d -> p t d", p=128))]
    tqsb = bcast_col(P, st, din["f_qs"][li:li + 1, :], sml[:, 16:80])
    tksb = bcast_col(P, st, din["f_ks"][li:li + 1, :], sml[:, 80:144])
    tsw = [P.on("vector", lambda e: e.tensor_copy(we[:, 0:32], sml[:, 48:80]),
                deps=[tqsb], tok=True),
           P.on("vector", lambda e: e.tensor_copy(we[:, 32:64], sml[:, 16:48]),
                deps=[tqsb], tok=True),
           P.on("vector", lambda e: e.tensor_copy(we[:, 64:96], sml[:, 112:144]),
                deps=[tksb], tok=True),
           P.on("vector", lambda e: e.tensor_copy(we[:, 96:128], sml[:, 80:112]),
                deps=[tksb], tok=True)]
    tf = {}
    for j, nm in enumerate(("q", "k")):
        scol = sml[:, 16:80] if nm == "q" else sml[:, 80:144]
        swv = we[:, 0:64] if nm == "q" else we[:, 64:128]
        tf[nm] = [P.on("vector", lambda e, nm=nm, scol=scol: e.tensor_tensor(
            CS[nm], CS[nm], scol.unsqueeze(1).broadcast_to((128, T, DH)), ALU.mult),
            deps=[tcs[nm][0], tqsb, tksb], tok=True),
            P.on("vector", lambda e, nm=nm, swv=swv: e.tensor_tensor(
                SS[nm], SS[nm], swv.unsqueeze(1).broadcast_to((128, T, DH)), ALU.mult),
                deps=[tcs[nm][1]] + tsw, tok=True)]
    t1 = P.on("vector", lambda e: e.tensor_reduce(sml[:, 0:1], sml[:, 16:80], AXX, ALU.max,
                                                  apply_absolute_value=True),
              deps=[tqsb] + tf["q"], tok=True)
    t2 = P.on("vector", lambda e: e.tensor_reduce(sml[:, 1:2], sml[:, 80:144], AXX, ALU.max,
                                                  apply_absolute_value=True),
              deps=[tksb] + tf["k"], tok=True)
    t3m = P.on("vector", lambda e: e.tensor_tensor(sml[:, 2:3], sml[:, 0:1], sml[:, 1:2],
                                                   ALU.mult), deps=[t1, t2], tok=True)
    tnc = P.on("vector", lambda e: e.tensor_scalar(negC[:], sml[:, 2:3], -8.0, None,
                                                   ALU.mult, ALU.bypass), deps=[t3m], tok=True)
    tmem = P.on("gpsimd", lambda e: e.memset(vst3[:, :, 64:65], 1.0), tok=True)

    prev_sp = [tmem]
    prev_rot = {"q": None, "k": None}
    for tt in range(T):
        tls = shifted_load(P, st, yin, yname, tt, wa, deps=[prev_rot["k"]])
        tcst = P.on("scalar", lambda e: e.copy(wbf[:], wa[:]), deps=tls, tok=True)
        tsf = transpose_tile(P, st, wbf, lnf3, 6, [tcst])
        tmv = {}
        for g in range(6):
            bank = PB[g]
            for c in range(8):
                tmv[g] = P.on("tensor", lambda e, g=g, c=c, bank=bank: e.matmul(
                    bank[:], lnf3[:, c, :], wqkv3[:, c, g * 512:(g + 1) * 512],
                    start=(c == 0), stop=(c == 7)),
                    deps=([tsf, tw] + prev_sp) if c == 0 else (), tok=(c == 7), mode="full")
        tev = []
        for half in range(2):
            pv = PB[4 + half].rearrange("p (h n) -> p h n", h=8)
            tev.append(P.on("scalar", lambda e, half=half, pv=pv: e.copy(
                vst3[:, half * 8:(half + 1) * 8, 0:64], pv),
                deps=[tmv[4 + half]] + prev_sp, tok=True))
        tvs = P.dma("sync", st["VSP"][tt], vst[:], deps=tev + P.wdeps("VSP"))
        tqr = rotary(P, st, tt, 0, CS["q"], SS["q"], st["QF"], "wbf4",
                     [tmv[0], tmv[1], prev_rot["q"]])
        tkr = rotary(P, st, tt, 2, CS["k"], SS["k"], st["KF"], "wbf5",
                     [tmv[2], tmv[3], tqr["ev"], prev_rot["k"]])
        prev_rot = {"q": tqr["sp"], "k": tkr["sp"]}
        prev_sp = [tvs, tqr["sp"], tkr["sp"]]

    if st["stop"] == "lattnA":
        return
    P.barrier()
    O1 = f32v(AR1, 0, ntok, rows=65)
    O2 = f32v(AR1, ntok, ntok, rows=65)
    rbig = f32v(AR1, 2 * ntok, ntok, rows=64)
    q2 = bf16v(AR2, 0, ntok)
    k2 = bf16v(AR2, ntok, ntok)
    v2p = bf16v(AR2, 2 * ntok, T * 130)
    v2p3 = v2p.rearrange("p (t n) -> p t n", t=T)
    obig = bf16v(AR2, 2 * ntok + T * 130, ntok, rows=64)
    oev = []
    for p in range(8):
        tqd = P.rd("QF", P.dma("sync", q2, st["QF"][p], deps=oev + P.wdeps("q2")))
        tkd = P.rd("KF", P.dma("sync", k2, st["KF"][p], deps=oev + P.wdeps("k2")))
        tvd = P.rd("VSP", P.dma("sync", v2p3,
                                st["VSP"][:, :, 130 * p:130 * (p + 1)].rearrange("t p n -> p t n"),
                                deps=oev + P.wdeps("v2p")))
        owtok = {}
        evtok = {}
        sbank_rd = {0: None, 1: None}
        slot_rd = {}
        txp = {}
        for t in range(T + 1):
            if t < T:
                W = min(384, ntok - 128 * t)
                for h in range(2):
                    sb_ = PB[h]
                    rows = slice(64 * h, 64 * h + 64)
                    sbank_rd[("S", h)] = P.on(
                        "tensor", lambda e, t=t, h=h, sb_=sb_, rows=rows, W=W: e.matmul(
                            sb_[:, 0:W], k2[rows, t * 128:(t + 1) * 128],
                            q2[rows, t * 128:t * 128 + W], start=True, stop=True),
                        deps=([tqd, tkd, tvd, tnc] if t == 0 else []) + [sbank_rd[h]],
                        tok=True, mode="r64")
            if t >= 1:
                tp = t - 1
                Wp = min(384, ntok - 128 * tp)
                nwp = (Wp + 127) // 128
                for h in range(2):
                    pslot = ptb[:, (2 * (tp % 2) + h) * 512:(2 * (tp % 2) + h) * 512 + 384]
                    ws = [w for w in range(tp, tp + nwp) if w < T]
                    for j, w in enumerate(ws):
                        bank = PB[2 + 3 * h + (w % 3)]
                        first = (tp == max(0, w - 2))
                        last = (tp == w)
                        dep = [txp[(h, tp % 2)]]
                        if first and w >= 3:
                            dep.append(evtok[(h, w - 3)])
                        tk_ = P.on("tensor", lambda e, bank=bank, h=h, tp=tp, j=j, first=first,
                                   last=last, pslot=pslot: e.matmul(
                            bank[0:65, 0:128], v2p3[:, tp, 65 * h:65 * h + 65],
                            pslot[:, j * 128:(j + 1) * 128], start=first, stop=last),
                            deps=dep, tok=(last or j == len(ws) - 1), mode="full")
                        if last:
                            owtok[(h, w)] = tk_
                        if j == len(ws) - 1:
                            slot_rd[(h, tp % 2)] = tk_
            if t < T:
                W = min(384, ntok - 128 * t)
                par = t % 2
                tS = sbank_rd[("S", 1)]
                Sl3 = st["PBIG"][0].ap()[:, 0:1024] \
                    .rearrange("p (h n) -> p h n", h=2)[:, :, 0:W]
                wsf3 = st["wsf2"][:, par * 1024:(par + 1) * 1024] \
                    .rearrange("p (h n) -> p h n", h=2)[:, :, 0:W]
                mb3 = st["maskb"][:, 0:W].unsqueeze(1).broadcast_to((128, 2, W))
                trd = P.on("vector", lambda e, Sl3=Sl3, wsf3=wsf3, mb3=mb3:
                           e.scalar_tensor_tensor(wsf3, Sl3, 8.0, mb3,
                                                  ALU.mult, ALU.add),
                           deps=[tS, slot_rd.get((0, par)), slot_rd.get((1, par))],
                           tok=True)
                sbank_rd[0] = sbank_rd[1] = trd
                ptb3 = ptb[:, par * 1024:(par + 1) * 1024] \
                    .rearrange("p (h n) -> p h n", h=2)[:, :, 0:W]
                tx = P.on("scalar", lambda e, ptb3=ptb3, wsf3=wsf3: e.activation(
                    ptb3, wsf3, AF.Exp, bias=negC[:], scale=1.0),
                    deps=[trd, slot_rd.get((0, par)), slot_rd.get((1, par))], tok=True)
                txp[(0, par)] = txp[(1, par)] = tx
            if t >= 1:
                tp = t - 1
                for h in range(2):
                    w = tp
                    bank = PB[2 + 3 * h + (w % 3)]
                    Odst = O1 if h == 0 else O2
                    evtok[(h, w)] = P.on("vector", lambda e, bank=bank, w=w, Odst=Odst:
                                         e.tensor_copy(Odst[:, w * 128:(w + 1) * 128],
                                                       bank[0:65, 0:128]),
                                         deps=[owtok[(h, w)]], tok=True)
        oev = []
        for h in range(2):
            Osrc = O1 if h == 0 else O2
            tch = []
            for qg in range(ntok // 512):
                tt1 = P.on("vector", lambda e, Osrc=Osrc, qg=qg: e.reciprocal(
                    rr[:, 0:512], Osrc[64:65, qg * 512:(qg + 1) * 512]),
                    deps=[evtok[(h, w)] for w in range(4 * qg, min(T, 4 * qg + 4))]
                    + P.wdeps("lrr"), tok=True)
                tch.append(P.rd("lrr",
                                P.dma("sync", st["BNC"][h:h + 1, qg * 512:(qg + 1) * 512],
                                      rr[0:1, 0:512], deps=[tt1] + P.wdeps(f"LBNC{h}"))))
            tb = P.rd(f"LBNC{h}", P.dma("sync", rbig,
                                        st["BNC"][h:h + 1, :].partition_broadcast(64),
                                        deps=tch + P.wdeps("rbig")))
            t4 = P.rd("rbig", P.on("vector", lambda e, Osrc=Osrc: e.tensor_tensor(
                obig, Osrc[0:64, :], rbig, ALU.mult),
                deps=[tb] + P.wdeps("obig"), tok=True))
            t5 = P.rd("obig", P.dma("sync", st["OF"][p, 64 * h:64 * h + 64, :], obig,
                                    deps=[t4] + P.wdeps("OF")))
            oev += [t4, t5]

    if st["stop"] == "lattnB":
        return
    P.barrier()
    wo = bf16v(AR2, 0, 8 * DM)
    wo3 = wo.rearrange("p (c n) -> p c n", c=8)
    ofm3 = st["wbf3"].ap().rearrange("p (c n) -> p c n", c=8)
    two = P.dma("gpsimd", wo3, din["f_wo"][li])
    prev = []
    for tt in range(T):
        r = slice(tt * 128, tt * 128 + 128)
        tof = P.rd("OF", P.dma("sync", ofm3,
                               st["OF"][:, :, tt * 128:(tt + 1) * 128].rearrange("c p n -> p c n"),
                               deps=prev))
        txl = P.rd(yname, P.dma("sync", wc[:], yin[r, :], deps=prev))
        tmm = {}
        for half in range(2):
            bank = PB[half]
            for c in range(8):
                tmm[half] = P.on("tensor", lambda e, c=c, half=half, bank=bank: e.matmul(
                    bank[:], ofm3[:, c, :], wo3[:, c, half * 512:(half + 1) * 512],
                    start=(c == 0), stop=(c == 7)),
                    deps=([tof, two] + prev) if c == 0 else (), tok=(c == 7), mode="full")
        prev = []
        for half in range(2):
            tr = P.on("vector", lambda e, half=half: e.tensor_tensor(
                wd[:, half * 512:(half + 1) * 512], PB[half][:],
                wc[:, half * 512:(half + 1) * 512], ALU.add),
                deps=[tmm[half], txl], tok=True)
            prev.append(tr)
        prev.append(P.dma("sync", yout[r, :], wd[:], deps=prev + P.wdeps(yoname)))


def rotary(P, st, tt, bank0, CS3, SS3, dstF, stg_name, deps):
    wa, wc, wd, we = st["wa"], st["wc"], st["wd"], st["we"]
    stat = st["stat"]; PB = st["PB"]
    stg = st[stg_name]
    t1 = P.on("scalar", lambda e: e.copy(wc[:, 0:512], PB[bank0][:]), deps=deps, tok=True)
    t2 = P.on("scalar", lambda e: e.copy(wc[:, 512:1024], PB[bank0 + 1][:]), tok=True)
    t3 = P.on("scalar", lambda e: e.activation(wd[:, 0:512], PB[bank0][:], AF.Square), tok=True)
    t4 = P.on("scalar", lambda e: e.activation(wd[:, 512:1024], PB[bank0 + 1][:], AF.Square),
              tok=True)
    wc3 = wc.ap().rearrange("p (h d) -> p h d", d=DH)
    wd3 = wd.ap().rearrange("p (h d) -> p h d", d=DH)
    t5 = P.on("vector", lambda e: e.tensor_reduce(stat[:, 0:16], wd3, AXX, ALU.add),
              deps=[t3, t4], tok=True)
    t6 = P.on("vector", lambda e: e.tensor_scalar(stat[:, 0:16], stat[:, 0:16], 1.42e-24,
                                                  None, ALU.max, ALU.bypass), deps=[t5], tok=True)
    t7 = P.on("scalar", lambda e: e.activation(stat[:, 0:16], stat[:, 0:16], AF.Ln,
                                               scale=1.0), deps=[t6], tok=True)
    t8 = P.on("scalar", lambda e: e.activation(stat[:, 0:16], stat[:, 0:16], AF.Exp,
                                               scale=-0.5), deps=[t7], tok=True)
    rb = stat[:, 0:16].unsqueeze(2).broadcast_to((128, 16, DH))
    tu = P.on("vector", lambda e: e.tensor_tensor(wd3, wc3, rb, ALU.mult),
              deps=[t1, t2, t8], tok=True)
    wd4 = wd.ap().rearrange("p (h s d2) -> p h s d2", s=2, d2=32)
    wa4 = wa.rearrange("p (h s d2) -> p h s d2", s=2, d2=32)
    g1 = P.on("vector", lambda e: e.tensor_tensor(
        we.ap().rearrange("p (h d) -> p h d", d=DH), wd3,
        CS3[:, tt, :].unsqueeze(1).broadcast_to((128, 16, DH)), ALU.mult),
        deps=[tu], tok=True)
    g2 = P.on("vector", lambda e: e.tensor_tensor(
        wa4[:, :, 0, :], wd4[:, :, 1, :],
        SS3[:, tt, 0:32].unsqueeze(1).broadcast_to((128, 16, 32)), ALU.mult),
        deps=[g1], tok=True)
    g3 = P.on("vector", lambda e: e.tensor_tensor(
        wa4[:, :, 1, :], wd4[:, :, 0, :],
        SS3[:, tt, 32:64].unsqueeze(1).broadcast_to((128, 16, 32)), ALU.mult),
        deps=[g2], tok=True)
    g4 = P.on("vector", lambda e: e.tensor_tensor(st["wbf"][:], we[:], wa[:], ALU.add),
              deps=[g1, g2, g3], tok=True)
    tev = transpose_tile(P, st, st["wbf"], stg.ap().rearrange("p (c n) -> p c n", c=8),
                         7, [g4])
    tsp = P.dma("sync", dstF.rearrange("c p n -> p c n")[:, :, tt * 128:(tt + 1) * 128],
                stg.ap().rearrange("p (c n) -> p c n", c=8), deps=[tev])
    return {"sp": tsp, "mid": g4, "ev": tev}


def ph_lffn(P, st, li, yin, yname, yout, yoname, inv):
    din = st["din"]; T = st["T"]; ntok = st["ntok"]
    AR1, AR2, PB = st["AR1"], st["AR2"], st["PB"]
    wa, wb, wc, wd = st["wa"], st["wb"], st["wc"], st["wd"]
    wbf, gsb = st["wbf"], st["gsb"]
    P.barrier()

    w1 = bf16v(AR1, 0, 8 * W1C)
    w13 = w1.rearrange("p (c n) -> p c n", c=8)
    w2 = bf16v(AR2, 0, 22 * DM)
    w23 = w2.rearrange("p (c n) -> p c n", c=22)
    lngb = f32v(AR2, (22 * DM) // 2, DM)
    mst = bf16v(AR2, 22 * DM + 2 * DM, DI)
    mfst = bf16v(AR2, 22 * DM + 2 * DM + DI, 22 * 128)
    assert 22 * DM + 2 * DM + DI + 22 * 128 <= 30208
    mfst3 = mfst.rearrange("p (c n) -> p c n", c=22)

    tw1 = P.dma("gpsimd", w13, din["f_w1"][li])
    tw2 = P.dma("gpsimd", w23, din["f_w2"][li])
    tb2 = P.dma("gpsimd", st["b2row"][:], din["f_b2"][li:li + 1, :])
    tlg = bcast_col(P, st, din["f_lng"][li:li + 1, :], lngb)
    nch = (W1C + 511) // 512
    if inv == 0:
        lnb_src = din["f_lnb"][li:li + 1, :].rearrange("o (c p) -> p (o c)", p=128)
        tlb = P.on("gpsimd", lambda e: e.dma_start(out=st["lnbc"][:], in_=lnb_src,
                                                   allow_slow_non_contiguous=True),
                   tok=True, dma=True)
        twr_last = None
        twr = []
        for j in range(nch):
            Wj = min(512, W1C - 512 * j)
            tbr = P.dma("sync", st["rr"][0:1, 0:Wj],
                        din["f_b1"][li:li + 1, 512 * j:512 * j + Wj], deps=[twr_last])
            tmj = None
            for c in range(8):
                tmj = P.on("tensor", lambda e, j=j, c=c, Wj=Wj: e.matmul(
                    PB[0][0:1, 0:Wj], st["lnbc"][:, c:c + 1],
                    w13[:, c, 512 * j:512 * j + Wj], start=(c == 0), stop=(c == 7)),
                    deps=([tlb, tw1, twr_last]) if c == 0 else (), tok=(c == 7), mode="full")
            twr_last = P.on("vector", lambda e, j=j, Wj=Wj: e.tensor_tensor(
                st["wrow"][0:1, 512 * j:512 * j + Wj], PB[0][0:1, 0:Wj],
                st["rr"][0:1, 0:Wj], ALU.add), deps=[tmj, tbr], tok=True)
            twr.append(twr_last)
        twr = [P.dma("sync", st["WRD"][li], st["wrow"][:], deps=twr)]
    else:
        twr = [P.dma("sync", st["wrow"][:], st["WRD"][li])]

    prev = []
    prev_aj = []
    prev_mev = []
    prevln = {0: [], 1: []}
    prev_t2 = None
    for tt in range(T):
        r = slice(tt * 128, tt * 128 + 128)
        par = tt % 2
        ld = (wa, st["we"])[par]
        lnfs = (st["wbf3"], st["wbf5"])[par]
        lnf3 = lnfs.ap().rearrange("p (c n) -> p c n", c=8)
        tls = shifted_load(P, st, yin, yname, tt, ld, deps=prevln[par])
        negmu, rstd, tst = ln_stats(P, st, ld, wb[:], LNEPS, tls, sb=12 * par)
        t1 = P.on("vector", lambda e, ld=ld, negmu=negmu, rstd=rstd: e.tensor_scalar(
            wb[:], ld[:], negmu, rstd, ALU.add, ALU.mult), deps=tst, tok=True)
        t2 = P.on("vector", lambda e: e.scalar_tensor_tensor(wbf[:], wb[:], 1.0, lngb,
                                                             ALU.mult, ALU.mult),
                  deps=[t1, tlg, prev_t2], tok=True)
        prev_t2 = t2
        prevln[par] = [t1, tst[2]]
        tsf = transpose_tile(P, st, wbf, lnf3, 6, [t2])
        ajs = []
        for j in range(6):
            a0 = 512 * j
            aW = min(512, DI - a0)
            tmj = {}
            for (idx, c0_) in ((0, a0), (1, DI + a0)):
                bank = PB[2 * (j % 2) + idx]
                bankrel = ajs[j - 2] if j >= 2 else (prev_aj[j + 4] if prev_aj else None)
                for c in range(8):
                    tmj[idx] = P.on("tensor",
                                    lambda e, c=c, c0_=c0_, aW=aW, bank=bank, lnf3=lnf3:
                                    e.matmul(bank[:, 0:aW], lnf3[:, c, :],
                                             w13[:, c, c0_:c0_ + aW],
                                             start=(c == 0), stop=False),
                                    deps=([tsf, tw1, bankrel] + twr + prev)
                                    if c == 0 else (), mode="full")
                tmj[idx] = P.on("tensor", lambda e, c0_=c0_, aW=aW, bank=bank: e.matmul(
                    bank[:, 0:aW], st["ones_row"][0:1, 0:128], st["wrow"][0:1, c0_:c0_ + aW],
                    start=False, stop=True), deps=twr, tok=True, mode="full")
            gb = PB[2 * (j % 2) + 1]
            ab = PB[2 * (j % 2)]
            grel = ajs[j - 1] if j >= 1 else (prev_aj[-1] if prev_aj else None)
            tg = P.on("scalar", lambda e, gb=gb, aW=aW: e.activation(
                gsb[:, 0:aW], gb[:, 0:aW], AF.Gelu), deps=[tmj[1], grel], tok=True)
            ajs.append(P.on("vector", lambda e, ab=ab, a0=a0, aW=aW: e.tensor_tensor(
                mst[:, a0:a0 + aW], ab[:, 0:aW], gsb[:, 0:aW], ALU.mult),
                deps=[tmj[0], tg], tok=True))
        prev_aj = ajs
        tmev = []
        pv = PB[7].bitcast(BF16)
        for grp in range(3):
            ch0 = grp * 8
            nc_ = min(8, 22 - ch0)
            tlast = None
            for cc in range(nc_):
                c2 = ch0 + cc
                cw = min(128, DI - c2 * 128)
                tlast = P.on("tensor", lambda e, cc=cc, c2=c2, cw=cw, pv=pv: e.transpose(
                    pv[0:cw, cc * 128:(cc + 1) * 128], mst[:, c2 * 128:c2 * 128 + cw],
                    st["ident"][:]),
                    deps=(ajs + [st["pt_rd"].get(7)]) if cc == 0 else (),
                    tok=(cc == nc_ - 1), mode="tr")
            pvv = pv.rearrange("p (c n) -> p c n", c=8)[:, 0:nc_, :]
            tmev.append(P.on("scalar", lambda e, ch0=ch0, nc_=nc_, pvv=pvv: e.copy(
                mfst3[:, ch0:ch0 + nc_, :], pvv), deps=[tlast], tok=True))
            st["pt_rd"][7] = tmev[-1]
        prev_mev = tmev
        txl = P.rd(yname, P.dma("sync", wc[:], yin[r, :], deps=prev))
        tm2 = {}
        for half in range(2):
            bank = PB[4 + half]
            for c2 in range(22):
                tm2[half] = P.on("tensor", lambda e, c2=c2, half=half, bank=bank: e.matmul(
                    bank[:], mfst[:, c2 * 128:(c2 + 1) * 128],
                    w23[:, c2, half * 512:(half + 1) * 512],
                    start=(c2 == 0), stop=False),
                    deps=(tmev + [tw2] + prev) if c2 == 0 else (), mode="full")
            tm2[half] = P.on("tensor", lambda e, half=half, bank=bank: e.matmul(
                bank[:], st["ones_row"][0:1, 0:128],
                st["b2row"][0:1, half * 512:(half + 1) * 512],
                start=False, stop=True), deps=[tb2], tok=True, mode="full")
        prev = []
        for half in range(2):
            tr = P.on("vector", lambda e, half=half: e.tensor_tensor(
                wd[:, half * 512:(half + 1) * 512], PB[4 + half][:],
                wc[:, half * 512:(half + 1) * 512], ALU.add),
                deps=[tm2[half], txl], tok=True)
            prev.append(tr)
        prev.append(P.dma("sync", yout[r, :], wd[:], deps=prev + P.wdeps(yoname)))


_CACHE = {}


def kernel(**inputs):
    x = np.asarray(inputs["L0_attn_a_x"])
    B, N, E = x.shape
    key = (N,)
    if key not in _CACHE:
        nc, P = build(ntok=N)
        _CACHE[key] = (nc, host_tables(N))
    nc, tables = _CACHE[key]
    w = prep_weights(inputs)
    base = {}
    base.update(w)
    base.update(tables)
    in_maps = []
    for c in range(B):
        m = dict(base)
        m["x0"] = np.ascontiguousarray(x[c])
        in_maps.append(m)
    res = run_bass_kernel_spmd(nc, in_maps, core_ids=list(range(B)))
    return np.stack([np.asarray(r["out"]) for r in res.results], axis=0)
